# version 01
# baseline (speedup 1.0000x reference)
import numpy as np
import contextlib
import concourse.bass as bass
import concourse.mybir as mybir

F32 = mybir.dt.float32
BF16 = mybir.dt.bfloat16
U32 = mybir.dt.uint32
AF = mybir.ActivationFunctionType
OP = mybir.AluOpType
AX = mybir.AxisListType

D = 1024
NCH = 8
NT = 16
TOK = 2048


class Ctx:
    def __init__(self, nc, es):
        self.nc = nc
        self.es = es
        self.block = es.enter_context(nc.Block())
        self.sems = {}
        self.vals = {}

    def sem(self, name):
        if name not in self.sems:
            self.sems[name] = self.es.enter_context(self.nc.semaphore(name))
            self.vals[name] = 0
        return self.sems[name]

    def val(self, name):
        self.sem(name)
        return self.vals[name]

    def bump(self, name, amt=1):
        self.sem(name)
        self.vals[name] += amt
        return self.vals[name]

    def v(self, fn): self.block.vector(fn)
    def s(self, fn): self.block.scalar(fn)
    def t(self, fn): self.block.tensor(fn)
    def g(self, fn): self.block.gpsimd(fn)
    def sy(self, fn): self.block.sync(fn)

    def sb(self, name, shape, dt=F32):
        return self.es.enter_context(self.nc.sbuf_tensor(name, shape, dt))

    def ps(self, name, shape, dt=F32):
        return self.es.enter_context(self.nc.psum_tensor(name, shape, dt))


def build_L1():
    nc = bass.Bass("TRN2", target_bir_lowering=False)
    i_x = nc.dram_tensor("xloc", [128, NT * D], F32, kind="ExternalInput").ap()
    i_xh = nc.dram_tensor("xhalo", [128, D], F32, kind="ExternalInput").ap()
    i_xp = nc.dram_tensor("xpfx", [128, NT * D], F32, kind="ExternalInput").ap()
    i_gw1 = nc.dram_tensor("gw1", [D, 256], F32, kind="ExternalInput").ap()
    i_gb1 = nc.dram_tensor("gb1", [128, 2], F32, kind="ExternalInput").ap()
    i_gw2 = nc.dram_tensor("gw2", [256, 32], F32, kind="ExternalInput").ap()
    i_gb2 = nc.dram_tensor("gb2", [32, 1], F32, kind="ExternalInput").ap()
    i_uw = nc.dram_tensor("uw", [D, 32], F32, kind="ExternalInput").ap()
    i_vw = nc.dram_tensor("vw", [D, 32], F32, kind="ExternalInput").ap()
    i_cvw = nc.dram_tensor("cvw", [128, 24], F32, kind="ExternalInput").ap()
    i_ow = nc.dram_tensor("ow", [32, D], F32, kind="ExternalInput").ap()
    i_obr = nc.dram_tensor("obr", [1, D], F32, kind="ExternalInput").ap()
    i_gw = nc.dram_tensor("gwf", [D, 32], F32, kind="ExternalInput").ap()
    i_gbr = nc.dram_tensor("gbr", [1, 32], F32, kind="ExternalInput").ap()
    i_w1 = nc.dram_tensor("w1x", [34, D, 512], F32, kind="ExternalInput").ap()
    i_w3 = nc.dram_tensor("w3x", [34, D, 512], F32, kind="ExternalInput").ap()
    i_w2 = nc.dram_tensor("w2x", [34, 512, D], F32, kind="ExternalInput").ap()
    i_b1 = nc.dram_tensor("b1x", [34, 512], F32, kind="ExternalInput").ap()
    i_b3 = nc.dram_tensor("b3x", [34, 512], F32, kind="ExternalInput").ap()
    i_b2 = nc.dram_tensor("b2s", [32, D], F32, kind="ExternalInput").ap()
    i_sb2 = nc.dram_tensor("sb2r", [1, D], F32, kind="ExternalInput").ap()
    i_idn = nc.dram_tensor("idn", [128, 128], BF16, kind="ExternalInput").ap()
    i_onr = nc.dram_tensor("onesr", [1, 128], BF16, kind="ExternalInput").ap()
    i_onc = nc.dram_tensor("onesc", [128, 1], BF16, kind="ExternalInput").ap()
    i_idnf = nc.dram_tensor("idnf", [128, 128], F32, kind="ExternalInput").ap()
    o_x = nc.dram_tensor("xout", [128, NT * D], F32, kind="ExternalOutput").ap()
    o_wk = nc.dram_tensor("dbg_wk", [128, NCH * TOK], BF16, kind="ExternalOutput").ap()
    o_xn1 = nc.dram_tensor("dbg_xn1", [128, NCH * TOK], BF16, kind="ExternalOutput").ap()
    o_xn2 = nc.dram_tensor("dbg_xn2", [128, NCH * TOK], BF16, kind="ExternalOutput").ap()
    o_cb = nc.dram_tensor("dbg_cb", [128, NT * 8], F32, kind="ExternalOutput").ap()
    o_q = nc.dram_tensor("dbg_q", [128, NCH * TOK], BF16, kind="ExternalOutput").ap()
    o_xn = nc.dram_tensor("dbg_xn", [128, NCH * 2050], BF16, kind="ExternalOutput").ap()
    o_xa = nc.dram_tensor("dbg_xattn", [128, NT * D], F32, kind="ExternalOutput").ap()
    o_cw = nc.dram_tensor("dbg_cw", [128, NT * 32], F32, kind="ExternalOutput").ap()
    o_ss = nc.dram_tensor("dbg_ss", [32, TOK], BF16, kind="ExternalOutput").ap()
    o_gt = nc.dram_tensor("dbg_gt", [32, TOK], BF16, kind="ExternalOutput").ap()
    o_uu = nc.dram_tensor("dbg_uu", [32, TOK], BF16, kind="ExternalOutput").ap()
    o_vv = nc.dram_tensor("dbg_vv", [32, TOK], BF16, kind="ExternalOutput").ap()
    o_s0 = nc.dram_tensor("dbg_s0", [32, 1], F32, kind="ExternalOutput").ap()
    o_xsr = nc.dram_tensor("dbg_xsr", [1, D], BF16, kind="ExternalOutput").ap()
    o_xsf = nc.dram_tensor("dbg_xsf", [128, 8], BF16, kind="ExternalOutput").ap()
    o_s0r = nc.dram_tensor("dbg_s0r", [1, 32], F32, kind="ExternalOutput").ap()
    o_pxb = nc.dram_tensor("dbg_pxb", [128, D], BF16, kind="ExternalOutput").ap()
    o_onc = nc.dram_tensor("dbg_onc", [128, 1], BF16, kind="ExternalOutput").ap()

    es = contextlib.ExitStack()
    with es:
        c = Ctx(nc, es)
        W = 2050
        X = c.sb("X", [128, NT * D], F32)
        XN = c.sb("XN", [128, NCH * W], BF16)
        XNT = c.sb("XNT", [128, 2 * D], BF16)
        TRASH_AP = None
        MV = c.sb("MV", [128, 36], F32)
        RS = c.sb("RS", [128, 18], F32)
        SQ = c.sb("SQ", [128, 18 * 2], F32)
        PXT = c.sb("PXT", [128, D], F32)
        PXB = c.sb("PXB", [128, D], BF16)
        XSR = c.sb("XSR", [1, D], BF16)
        XSF = c.sb("XSF", [128, 8], BF16)
        S0R = c.sb("S0R", [1, 32], F32)
        S0 = c.sb("S0", [32, 1], F32)
        G1W = c.sb("G1W", [128, NCH * 256], BF16)
        GB1 = c.sb("GB1", [128, 2], F32)
        G2W = c.sb("G2W", [128, 2 * 32], BF16)
        GB2 = c.sb("GB2", [32, 1], F32)
        UW = c.sb("UW", [128, NCH * 32], BF16)
        VW = c.sb("VW", [128, NCH * 32], BF16)
        CVW = c.sb("CVW", [128, 24], F32)
        OW = c.sb("OW", [32, D], BF16)
        OBR = c.sb("OBR", [1, D], BF16)
        GWr = c.sb("GWr", [128, NCH * 32], BF16)
        GBr = c.sb("GBr", [1, 32], BF16)
        H1S = c.sb("H1S", [128, 2 * 512], BF16)
        GT = c.sb("GT", [32, TOK], BF16)
        UU = c.sb("UU", [32, TOK], BF16)
        VV = c.sb("VV", [32, TOK], BF16)
        SS = c.sb("SS", [32, TOK], BF16)
        T1G = VV
        GLBB = GT
        CT1 = c.sb("CT1", [128, 512], F32)
        CT2 = CT1
        CT3 = c.sb("CT3", [128, 512], BF16)
        SC = c.sb("SC", [128, 32], F32)
        NMX = c.sb("NMX", [128, 1], F32)
        DEN = c.sb("DEN", [128, 1], F32)
        RDEN = c.sb("RDEN", [128, 1], F32)
        PS6 = c.sb("PS6", [128, 8], F32)
        GS = c.sb("GS", [128, 8], F32)
        GS8 = c.sb("GS8", [128, 8], F32)
        IDX8 = c.sb("IDX8", [128, 8], U32)
        KM = c.sb("KM", [128, 32], mybir.dt.uint8)
        KG = c.sb("KG", [128, 8], mybir.dt.uint8)
        MSK = c.sb("MSK", [128, 32], F32)
        NEG = c.sb("NEG", [128, 32], F32)
        M8 = c.sb("M8", [128, 8], F32)
        IND = c.sb("IND", [128, 32], F32)
        CWall = c.sb("CWall", [128, NT * 32], F32)
        CWT = c.sb("CWT", [32, NT * 128], BF16)
        W1E = c.sb("W1E", [128, 2 * NCH * 512], BF16)
        W3E = c.sb("W3E", [128, 2 * NCH * 512], BF16)
        W2E = c.sb("W2E", [128, 2 * 4 * D], BF16)
        B1E = c.sb("B1E", [1, 2 * 512], BF16)
        B3E = c.sb("B3E", [1, 2 * 512], BF16)
        B2S = c.sb("B2S", [32, D], BF16)
        SB2R = c.sb("SB2R", [1, D], BF16)
        SA = c.sb("SA", [128, 512], BF16)
        HS = c.sb("HS", [128, 512], BF16)
        HF = c.sb("HF", [128, 4 * 128], BF16)
        IDN = c.sb("IDN", [128, 128], BF16)
        IDNF = c.sb("IDNF", [128, 128], F32)
        ONR = c.sb("ONR", [1, 128], BF16)
        ONC = c.sb("ONC", [128, 1], BF16)
        EPS = c.sb("EPS", [128, 1], F32)
        PP0 = c.ps("PP0", [128, 512], F32)
        PP1 = c.ps("PP1", [128, 512], F32)
        PT = c.ps("PT", [128, 1024], BF16)
        YP = c.ps("YP", [128, 1024], F32)
        SP = c.ps("SP", [128, 512], F32)
        PXS = c.ps("PXS", [128, 1024], F32)

        sem, bump, val = c.sem, c.bump, c.val

        def vedge(v, instr):
            n = bump("V")
            instr.then_inc(sem("V"), 1)
            v.wait_ge(sem("V"), n)

        def sedge(s_, instr):
            n = bump("S")
            instr.then_inc(sem("S"), 1)
            s_.wait_ge(sem("S"), n)

        def ph0_sy(sy):
            sy.dma_start(out=X[:], in_=i_x).then_inc(sem("ldx"), 16)
            sy.dma_start(out=PXT[:], in_=i_xh).then_inc(sem("ldx"), 16)
            sy.dma_start(out=GB1[:], in_=i_gb1).then_inc(sem("ldx"), 16)
            sy.dma_start(out=GB2[:], in_=i_gb2).then_inc(sem("ldx"), 16)
            sy.dma_start(out=CVW[:], in_=i_cvw).then_inc(sem("ldx"), 16)
        c.sy(ph0_sy)
        LDX = bump("ldx", 5 * 16)

        def ph0_g(g):
            for dst, src in [(IDN, i_idn), (IDNF, i_idnf), (ONR, i_onr), (ONC, i_onc), (OW, i_ow),
                             (OBR, i_obr), (GBr, i_gbr), (B2S, i_b2), (SB2R, i_sb2)]:
                g.dma_start(out=dst[:], in_=src).then_inc(sem("ldw"), 16)
            for dst, src, m in [(G1W, i_gw1, 256), (G2W, i_gw2, 32), (UW, i_uw, 32),
                                (VW, i_vw, 32), (GWr, i_gw, 32)]:
                g.dma_start(out=dst[:].rearrange("p (c m) -> p c m", m=m),
                            in_=src.rearrange("(c p) m -> p c m", p=128)).then_inc(sem("ldw"), 16)
            g.memset(NEG[:], -1e9)
            g.memset(EPS[:], 1e-5).then_inc(sem("ldw"), 1)
        c.g(ph0_g)
        LDW = bump("ldw", 14 * 16 + 1)

        def get_xt(i):
            return X[:, D * i:D * (i + 1)] if i < 16 else PXT[:]

        def ph1_ssq(s_):
            s_.wait_ge(sem("ldx"), LDX)
            s_.wait_ge(sem("ldw"), LDW)
            for i in range(17):
                s_.activation(out=XNT[:, 0:D], in_=get_xt(i), func=AF.Square,
                              accum_out=SQ[:, 18 + i:19 + i])
            last = s_.copy(RS[:, 17:18], SQ[:, 34:35])
            last.then_inc(sem("S"), 1)
        c.s(ph1_ssq)
        s_sq = bump("S")

        def ph1_vstats(v):
            v.wait_ge(sem("ldx"), LDX)
            v.wait_ge(sem("ldw"), LDW)
            for i in range(17):
                v.reduce_sum(SQ[:, i:i + 1], get_xt(i), axis=AX.X)
            i1 = v.tensor_scalar(MV[:, 0:17], SQ[:, 0:17], 1.0 / D, None, OP.mult)
            vedge(v, i1)
            v.wait_ge(sem("S"), s_sq)
            i2 = v.tensor_scalar(MV[:, 18:35], SQ[:, 18:35], 1.0 / D, None, OP.mult)
            vedge(v, i2)
            i3 = v.tensor_tensor(RS[:, 0:17], MV[:, 0:17], MV[:, 0:17], OP.mult)
            vedge(v, i3)
            v.tensor_tensor(MV[:, 18:35], MV[:, 18:35], RS[:, 0:17], OP.subtract).then_inc(sem("V"), 1)
        c.v(ph1_vstats)
        v_stats = val("V") + 1
        bump("V")

        def ph1_srs(s_):
            s_.wait_ge(sem("V"), v_stats)
            i1 = s_.activation(out=RS[:, 0:17], in_=MV[:, 18:35], func=AF.Ln, bias=EPS[:])
            sedge(s_, i1)
            s_.activation(out=RS[:, 0:17], in_=RS[:, 0:17], func=AF.Exp, scale=-0.5).then_inc(sem("S"), 1)
        c.s(ph1_srs)
        s_rs = bump("S")

        v_apply = [0] * 17
        t_tr = [0] * 17
        v_evac = [0] * 17
        XNv = XN[:].rearrange("p (ch w) -> p ch w", w=W)
        PTv = PT[:].rearrange("p (ch w) -> p ch w", w=128)
        for i in range(17):
            buf = i % 2

            def f_ap(v, i=i, buf=buf):
                v.wait_ge(sem("S"), s_rs)
                if i >= 2:
                    v.wait_ge(sem("T"), t_tr[i - 2])
                v.tensor_scalar(XNT[:, D * buf:D * (buf + 1)], get_xt(i),
                                MV[:, i:i + 1], RS[:, i:i + 1],
                                OP.subtract, OP.mult).then_inc(sem("V"), 1)
            c.v(f_ap)
            v_apply[i] = bump("V")

            def f_tr(t, i=i, buf=buf):
                t.wait_ge(sem("V"), v_apply[i])
                if i == 0:
                    t.wait_ge(sem("ldw"), LDW)
                if i >= 1:
                    t.wait_ge(sem("V"), v_evac[i - 1])
                for ch in range(NCH):
                    mm = t.transpose(PT[:, 128 * ch:128 * (ch + 1)],
                                     XNT[:, D * buf + 128 * ch:D * buf + 128 * (ch + 1)], IDN[:])
                mm.then_inc(sem("T"), 1)
            c.t(f_tr)
            t_tr[i] = bump("T")

            def f_ev(v, i=i):
                v.wait_ge(sem("T"), t_tr[i])
                if i < 16:
                    v.tensor_copy(XNv[:, :, 1 + 128 * i:1 + 128 * (i + 1)], PTv).then_inc(sem("V"), 1)
                else:
                    v.tensor_copy(XNv[:, :, 0:1], PTv[:, :, 0:1])
                    v.tensor_copy(XNv[:, :, W - 1:W], PTv[:, :, 1:2]).then_inc(sem("V"), 1)
            c.v(f_ev)
            v_evac[i] = bump("V")

        def dbg1_sy(sy):
            sy.wait_ge(sem("V"), v_evac[16])
            sy.dma_start(out=o_xn, in_=XN[:]).then_inc(sem("st"), 16)
        c.sy(dbg1_sy)
        bump("st", 16)

        def xn(ch, a, b, off=1):
            return XN[:, W * ch + off + a:W * ch + off + b]

        pv_ap = [0] * 16
        pxs_t = [0] * 16
        for i in range(16):
            def f_dma(sy, i=i):
                if i == 0:
                    sy.wait_ge(sem("V"), v_apply[16])
                if i >= 1:
                    sy.wait_ge(sem("V"), pv_ap[i - 1])
                sy.dma_start(out=PXT[:],
                             in_=i_xp[:, D * i:D * (i + 1)]).then_inc(sem("ldp"), 16)
            c.sy(f_dma)
            ld_i = bump("ldp", 16)

            def f_s0(s_, ld_i=ld_i, i=i):
                if i == 0:
                    s_.wait_ge(sem("T"), t_tr[16])
                s_.wait_ge(sem("ldp"), ld_i)
                s_.activation(out=XNT[:, 0:D], in_=PXT[:], func=AF.Square,
                              accum_out=SQ[:, 35:36])
                s_.copy(RS[:, 17:18], SQ[:, 35:36]).then_inc(sem("S"), 1)
            c.s(f_s0)
            s_q = bump("S")

            def f_v1(v, ld_i=ld_i, s_q=s_q):
                v.wait_ge(sem("ldp"), ld_i)
                v.wait_ge(sem("S"), s_q)
                i0 = v.reduce_sum(SQ[:, 17:18], PXT[:], axis=AX.X)
                vedge(v, i0)
                i1 = v.tensor_scalar(MV[:, 17:18], SQ[:, 17:18], 1.0 / D, None, OP.mult)
                i2 = v.tensor_scalar(MV[:, 35:36], SQ[:, 35:36], 1.0 / D, None, OP.mult)
                vedge(v, i2)
                i3 = v.tensor_tensor(RS[:, 17:18], MV[:, 17:18], MV[:, 17:18], OP.mult)
                vedge(v, i3)
                v.tensor_tensor(MV[:, 35:36], MV[:, 35:36], RS[:, 17:18], OP.subtract).then_inc(sem("V"), 1)
            c.v(f_v1)
            v_st = bump("V")

            def f_s(s_, v_st=v_st):
                s_.wait_ge(sem("V"), v_st)
                i1 = s_.activation(out=RS[:, 17:18], in_=MV[:, 35:36], func=AF.Ln, bias=EPS[:])
                sedge(s_, i1)
                s_.activation(out=RS[:, 17:18], in_=RS[:, 17:18], func=AF.Exp, scale=-0.5).then_inc(sem("S"), 1)
            c.s(f_s)
            s_i = bump("S")

            def f_v2(v, s_i=s_i, i=i):
                v.wait_ge(sem("S"), s_i)
                if i >= 1:
                    v.wait_ge(sem("T"), pxs_t[i - 1])
                v.tensor_scalar(PXB[:], PXT[:], MV[:, 17:18], RS[:, 17:18],
                                OP.subtract, OP.mult).then_inc(sem("V"), 1)
            c.v(f_v2)
            pv_ap[i] = bump("V")

            def f_t(t, i=i):
                t.wait_ge(sem("V"), pv_ap[i])
                t.matmul(PXS[0:1, 0:512], ONC[:], PXB[:, 0:512], start=(i == 0), stop=(i == 15))
                mm = t.matmul(PXS[0:1, 512:1024], ONC[:], PXB[:, 512:1024], start=(i == 0), stop=(i == 15))
                mm.then_inc(sem("T"), 1)
            c.t(f_t)
            pxs_t[i] = bump("T")
        t_pxs = val("T")

        def ph2c_v(v):
            v.wait_ge(sem("T"), t_pxs)
            v.tensor_copy(XSR[:], PXS[0:1, :]).then_inc(sem("V"), 1)
        c.v(ph2c_v)
        v_xsr = bump("V")

        def ph2c_t(t):
            t.wait_ge(sem("V"), v_xsr)
            for ch in range(NCH):
                mm = t.transpose(PT[:, 2 * ch:2 * ch + 1],
                                 XSR[:, 128 * ch:128 * (ch + 1)], IDN[0:1, 0:1])
            mm.then_inc(sem("T"), 1)
        c.t(ph2c_t)
        t_xf = bump("T")

        def ph2c_v2(v):
            v.wait_ge(sem("T"), t_xf)
            v.tensor_copy(XSF[:], PT[:].rearrange("p (a b) -> p a b", b=2)[:, 0:8, 0], allow_slow_non_contiguous=True) if False else None
            v.tensor_copy(XSF[:].rearrange("p (a b) -> p a b", b=1), PT[:].rearrange("p (a b) -> p a b", b=2)[:, 0:8, 0:1]).then_inc(sem("V"), 1)
        c.v(ph2c_v2)
        v_xsf = bump("V")

        def ph2c_t2(t):
            t.wait_ge(sem("V"), v_xsf)
            for ch in range(NCH):
                mm = t.matmul(SP[0:1, 0:32], XSF[:, ch:ch + 1], VW[:, 32 * ch:32 * (ch + 1)],
                              start=(ch == 0), stop=(ch == 7))
            mm.then_inc(sem("T"), 1)
        c.t(ph2c_t2)
        t_s0r = bump("T")

        def ph2c_v3(v):
            v.wait_ge(sem("T"), t_s0r)
            v.tensor_copy(S0R[:], SP[0:1, 0:32]).then_inc(sem("V"), 1)
        c.v(ph2c_v3)
        v_s0r = bump("V")

        def ph2c_t3(t):
            t.wait_ge(sem("V"), v_s0r)
            t.transpose(SP[0:32, 0:1], S0R[:], IDNF[0:1, 0:1]).then_inc(sem("T"), 1)
        c.t(ph2c_t3)
        t_s0 = bump("T")

        def ph2c_v4(v):
            v.wait_ge(sem("T"), t_s0)
            v.tensor_copy(S0[:], SP[0:32, 0:1]).then_inc(sem("V"), 1)
        c.v(ph2c_v4)
        v_s0 = bump("V")

        ev_done = {}
        for s4 in range(4):
            a, b = 512 * s4, 512 * (s4 + 1)
            for m in range(2):
                PB = PP0 if m == 0 else PP1

                def f_h1(t, m=m, s4=s4, a=a, b=b, PB=PB):
                    if (s4, m) != (0, 0):
                        pass
                    key = ("h1e", s4 - 1, m)
                    if key in ev_done:
                        t.wait_ge(sem("S"), ev_done[key])
                    for ch in range(NCH):
                        mm = t.matmul(PB[:], G1W[:, 256 * ch + 128 * m:256 * ch + 128 * (m + 1)],
                                      xn(ch, a, b), start=(ch == 0), stop=(ch == 7))
                    mm.then_inc(sem("T"), 1)
                c.t(f_h1)
                t_h1 = bump("T")

                def f_h1e(s, m=m, t_h1=t_h1, s4=s4):
                    if (s4, m) == (0, 0):
                        s.wait_ge(sem("ldx"), LDX)
                    s.wait_ge(sem("T"), t_h1)
                    s.activation(out=H1S[:, 512 * m:512 * (m + 1)], in_=(PP0 if m == 0 else PP1)[:],
                                 func=AF.Relu, bias=GB1[:, m:m + 1]).then_inc(sem("S"), 1)
                c.s(f_h1e)
                ev_done[("h1e", s4, m)] = bump("S")

            def f_g(t, s4=s4):
                t.wait_ge(sem("S"), ev_done[("h1e", s4, 1)])
                if s4 == 0:
                    t.wait_ge(sem("V"), v_s0)
                else:
                    t.wait_ge(sem("V"), ev_done[("ve", s4 - 1)])
                for ch in range(2):
                    mm = t.matmul(SP[0:32, 0:512], G2W[:, 32 * ch:32 * (ch + 1)],
                                  H1S[:, 512 * ch:512 * (ch + 1)], start=(ch == 0), stop=(ch == 1))
                mm.then_inc(sem("T"), 1)
            c.t(f_g)
            t_g = bump("T")

            def f_ge(s, s4=s4, a=a, t_g=t_g):
                s.wait_ge(sem("T"), t_g)
                s.activation(out=GT[0:32, a:a + 512], in_=SP[0:32, 0:512],
                             func=AF.Sigmoid, bias=GB2[:]).then_inc(sem("S"), 1)
            c.s(f_ge)
            ev_done[("ge", s4)] = bump("S")

            for nm, WT, DST in (("u", UW, UU), ("v", VW, VV)):
                def f_uv(t, WT=WT, a=a, b=b, s4=s4, nm=nm):
                    t.wait_ge(sem("S" if nm == "u" else "V"),
                              ev_done[("ge", s4)] if nm == "u" else ev_done[("ue", s4)])
                    for ch in range(NCH):
                        mm = t.matmul(SP[0:32, 0:512], WT[:, 32 * ch:32 * (ch + 1)],
                                      xn(ch, a, b), start=(ch == 0), stop=(ch == 7))
                    mm.then_inc(sem("T"), 1)
                c.t(f_uv)
                t_uv = bump("T")

                def f_uve(v, DST=DST, a=a, t_uv=t_uv):
                    v.wait_ge(sem("T"), t_uv)
                    v.tensor_copy(DST[0:32, a:a + 512], SP[0:32, 0:512]).then_inc(sem("V"), 1)
                c.v(f_uve)
                ev_done[(nm + "e", s4)] = bump("V")

        def dbg2_sy(sy):
            sy.wait_ge(sem("V"), ev_done[("ve", 3)])
            sy.wait_ge(sem("V"), v_s0)
            sy.dma_start(out=o_gt, in_=GT[0:32, :]).then_inc(sem("st"), 16)
            sy.dma_start(out=o_uu, in_=UU[0:32, :]).then_inc(sem("st"), 16)
            sy.dma_start(out=o_vv, in_=VV[0:32, :]).then_inc(sem("st"), 16)
            sy.dma_start(out=o_s0, in_=S0[:]).then_inc(sem("st"), 16)
            sy.dma_start(out=o_xsr, in_=XSR[:]).then_inc(sem("st"), 16)
            sy.dma_start(out=o_xsf, in_=XSF[:]).then_inc(sem("st"), 16)
            sy.dma_start(out=o_s0r, in_=S0R[:]).then_inc(sem("st"), 16)
            sy.dma_start(out=o_pxb, in_=PXB[:]).then_inc(sem("st"), 16)
            sy.dma_start(out=o_onc, in_=ONC[:]).then_inc(sem("st"), 16)
        c.sy(dbg2_sy)
        st_dbg2 = bump("st", 144)

        def f_scan(v):
            v.wait_ge(sem("V"), ev_done[("ve", 3)])
            v.wait_ge(sem("V"), v_s0)
            v.wait_ge(sem("st"), st_dbg2)
            i1 = v.tensor_tensor_scan(SS[:], VV[:], VV[:], S0[:], OP.add, OP.bypass)
            i2 = v.tensor_tensor(T1G[:], GT[:], UU[:], OP.mult)
            vedge(v, i1)
            vedge(v, i2)
            v.tensor_tensor(GLBB[:], T1G[:], SS[:], OP.mult).then_inc(sem("V"), 1)
        c.v(f_scan)
        v_glb = bump("V")

        yg_ev = {}
        for i in range(NT):
            for n in range(2):
                PB = PP0 if n == 0 else PP1

                def f_yg(t, i=i, n=n, PB=PB):
                    if (i, n) == (0, 0):
                        t.wait_ge(sem("V"), v_glb)
                    key = (i - 1, n)
                    if key in yg_ev:
                        t.wait_ge(sem("V"), yg_ev[key])
                    t.matmul(PB[:], GLBB[:, 128 * i:128 * (i + 1)], OW[:, 512 * n:512 * (n + 1)],
                             start=True, stop=False)
                    t.matmul(PB[:], ONR[:], OBR[:, 512 * n:512 * (n + 1)],
                             start=False, stop=True).then_inc(sem("T"), 1)
                c.t(f_yg)
                t_yg = bump("T")

                def f_yge(v, i=i, n=n, PB=PB, t_yg=t_yg):
                    v.wait_ge(sem("T"), t_yg)
                    sl = X[:, D * i + 512 * n:D * i + 512 * (n + 1)]
                    v.tensor_tensor(sl, sl, PB[:], OP.add).then_inc(sem("V"), 1)
                c.v(f_yge)
                yg_ev[(i, n)] = bump("V")

        cv_ev = {}
        for ch in range(NCH):
            for s4 in range(4):
                def f_cv(v, ch=ch, s4=s4):
                    a = 512 * s4
                    i1 = v.tensor_scalar(CT1[:], xn(ch, a, a + 512, off=0), CVW[:, 3 * ch:3 * ch + 1], None, OP.mult)
                    vedge(v, i1)
                    i2 = v.scalar_tensor_tensor(CT2[:], xn(ch, a, a + 512, off=1), CVW[:, 3 * ch + 1:3 * ch + 2],
                                                CT1[:], OP.mult, OP.add)
                    vedge(v, i2)
                    v.scalar_tensor_tensor(CT3[:], xn(ch, a, a + 512, off=2), CVW[:, 3 * ch + 2:3 * ch + 3],
                                           CT2[:], OP.mult, OP.add).then_inc(sem("V"), 1)
                c.v(f_cv)
                v_ct = bump("V")

                def f_cvt(t, ch=ch, s4=s4, v_ct=v_ct):
                    t.wait_ge(sem("V"), v_ct)
                    key = (ch, s4 - 1) if s4 > 0 else (ch - 1, 3)
                    if key in cv_ev:
                        t.wait_ge(sem("V"), cv_ev[key])
                    for j in range(4):
                        mm = t.transpose(PT[:, 128 * j:128 * (j + 1)],
                                         CT3[:, 128 * j:128 * (j + 1)], IDN[:])
                    mm.then_inc(sem("T"), 1)
                c.t(f_cvt)
                t_ct = bump("T")

                def f_cva(v, ch=ch, s4=s4, t_ct=t_ct):
                    v.wait_ge(sem("T"), t_ct)
                    dst = X[:].rearrange("p (t f) -> p t f", f=D)[:, 4 * s4:4 * s4 + 4, 128 * ch:128 * (ch + 1)]
                    src = PT[:].rearrange("p (j w) -> p j w", w=128)[:, 0:4, :]
                    v.tensor_tensor(dst, dst, src, OP.add).then_inc(sem("V"), 1)
                c.v(f_cva)
                cv_ev[(ch, s4)] = bump("V")

        rt_sexp = {}
        for i in range(NT):
            def f_lg(t, i=i):
                if i == 0:
                    t.wait_ge(sem("V"), cv_ev[(NCH - 1, 3)])
                else:
                    t.wait_ge(sem("S"), rt_sexp[i - 1])
                for ch in range(NCH):
                    t.matmul(SP[:, 0:32], xn(ch, 128 * i, 128 * (i + 1)),
                             GWr[:, 32 * ch:32 * (ch + 1)], start=(ch == 0), stop=False)
                t.matmul(SP[:, 0:32], ONR[:], GBr[:], start=False, stop=True).then_inc(sem("T"), 1)
            c.t(f_lg)
            t_lg = bump("T")

            def f_rmax(v, t_lg=t_lg):
                v.wait_ge(sem("T"), t_lg)
                v.reduce_max(NMX[:], SP[:, 0:32], axis=AX.X, negate=True).then_inc(sem("V"), 1)
            c.v(f_rmax)
            v_mx = bump("V")

            def f_exp(s, v_mx=v_mx):
                s.wait_ge(sem("V"), v_mx)
                s.activation(out=SC[:], in_=SP[:, 0:32], func=AF.Exp, bias=NMX[:],
                             accum_out=DEN[:])
                s.copy(DEN[:], DEN[:]).then_inc(sem("S"), 1)
            c.s(f_exp)
            s_exp = bump("S")
            rt_sexp[i] = s_exp

            def f_rt(v, i=i, s_exp=s_exp):
                v.wait_ge(sem("S"), s_exp)
                i0 = v.reciprocal(RDEN[:], DEN[:])
                vedge(v, i0)
                i1 = v.tensor_scalar(SC[:], SC[:], RDEN[:], None, OP.mult)
                vedge(v, i1)
                SCg = SC[:].rearrange("p (g k) -> p g k", k=4)
                i2 = v.tensor_tensor(PS6[:], SCg[:, :, 0], SCg[:, :, 1], OP.add)
                i3 = v.tensor_tensor(GS[:], SCg[:, :, 0], SCg[:, :, 2], OP.add)
                vedge(v, i2); vedge(v, i3)
                i4 = v.tensor_tensor(GS[:], GS[:], PS6[:], OP.max)
                i5 = v.tensor_tensor(PS6[:], SCg[:, :, 0], SCg[:, :, 3], OP.add)
                vedge(v, i4); vedge(v, i5)
                i4 = v.tensor_tensor(GS[:], GS[:], PS6[:], OP.max)
                i5 = v.tensor_tensor(PS6[:], SCg[:, :, 1], SCg[:, :, 2], OP.add)
                vedge(v, i4); vedge(v, i5)
                i4 = v.tensor_tensor(GS[:], GS[:], PS6[:], OP.max)
                i5 = v.tensor_tensor(PS6[:], SCg[:, :, 1], SCg[:, :, 3], OP.add)
                vedge(v, i4); vedge(v, i5)
                i4 = v.tensor_tensor(GS[:], GS[:], PS6[:], OP.max)
                i5 = v.tensor_tensor(PS6[:], SCg[:, :, 2], SCg[:, :, 3], OP.add)
                vedge(v, i4); vedge(v, i5)
                i4 = v.tensor_tensor(GS[:], GS[:], PS6[:], OP.max)
                vedge(v, i4)
                v.max_with_indices(GS8[:], IDX8[:], GS[:])
                i6 = v.tensor_copy(PS6[:, 0:1], GS8[:, 0:1])
                vedge(v, i6)
                i7 = v.tensor_scalar(KG[:], GS[:], GS8[:, 3:4], None, OP.is_ge)
                vedge(v, i7)
                KMg = KM[:].rearrange("p (g k) -> p g k", k=4)
                for k in range(4):
                    i8 = v.tensor_copy(KMg[:, :, k], KG[:])
                vedge(v, i8)
                i9 = v.select(MSK[:], KM[:], SC[:], NEG[:], add_drain=True)
                vedge(v, i9)
                v.max_with_indices(M8[:], IDX8[:], MSK[:])
                i10 = v.tensor_copy(PS6[:, 0:1], M8[:, 0:1])
                vedge(v, i10)
                i11 = v.tensor_scalar(IND[:], MSK[:], M8[:, 3:4], None, OP.is_ge)
                vedge(v, i11)
                v.tensor_tensor(CWall[:, 32 * i:32 * (i + 1)], SC[:], IND[:], OP.mult).then_inc(sem("V"), 1)
            c.v(f_rt)
            bump("V", 1)
            v_cw = val("V")

            def f_cwt(t, i=i, v_cw=v_cw):
                t.wait_ge(sem("V"), v_cw)
                t.transpose(PXS[0:32, 0:128], CWall[:, 32 * i:32 * (i + 1)], IDNF[:]).then_inc(sem("T"), 1)
            c.t(f_cwt)
            t_cw = bump("T")

            def f_cwte(v, i=i, t_cw=t_cw):
                v.wait_ge(sem("T"), t_cw)
                v.tensor_copy(CWT[:, 128 * i:128 * (i + 1)], PXS[0:32, 0:128]).then_inc(sem("V"), 1)
            c.v(f_cwte)
            bump("V")
        v_route = val("V")

        def dbg3_sy(sy):
            sy.wait_ge(sem("V"), v_route)
            sy.dma_start(out=o_ss, in_=SS[0:32, :]).then_inc(sem("st"), 16)
            sy.dma_start(out=o_xa, in_=X[:]).then_inc(sem("st"), 16)
            sy.dma_start(out=o_cw, in_=CWall[:]).then_inc(sem("st"), 16)
        c.sy(dbg3_sy)
        st_dbg3 = bump("st", 48)

        t_lastuse = [0] * 34
        prev = {}
        g_emitted = [0]

        def emit_wdma(e):
            def f_wd(g, e=e):
                if e >= 2:
                    g.wait_ge(sem("T"), t_lastuse[e - 2])
                sl = e % 2
                g.dma_start(out=W1E[:, 4096 * sl:4096 * (sl + 1)].rearrange("p (ch m) -> p ch m", m=512),
                            in_=i_w1[e].rearrange("(ch p) m -> p ch m", p=128)).then_inc(sem("lde"), 16)
                g.dma_start(out=W3E[:, 4096 * sl:4096 * (sl + 1)].rearrange("p (ch m) -> p ch m", m=512),
                            in_=i_w3[e].rearrange("(ch p) m -> p ch m", p=128)).then_inc(sem("lde"), 16)
                g.dma_start(out=W2E[:, 4096 * sl:4096 * (sl + 1)].rearrange("p (ic m) -> p ic m", m=D),
                            in_=i_w2[e].rearrange("(ic p) m -> p ic m", p=128)).then_inc(sem("lde"), 16)
                g.dma_start(out=B1E[:, 512 * sl:512 * (sl + 1)], in_=i_b1[e:e + 1, :]).then_inc(sem("lde"), 16)
                g.dma_start(out=B3E[:, 512 * sl:512 * (sl + 1)], in_=i_b3[e:e + 1, :]).then_inc(sem("lde"), 16)
            c.g(f_wd)
            bump("lde", 5 * 16)

        def emit_expert(e):
            sl = e % 2
            shared = e >= 32
            lde_need = 80 * (e + 1)
            for i in range(NT):
                def f_a(t, e=e, i=i, sl=sl, lde_need=lde_need):
                    if i == 0:
                        t.wait_ge(sem("lde"), lde_need)
                        if e == 0:
                            t.wait_ge(sem("V"), v_route)
                            t.wait_ge(sem("st"), st_dbg3)
                    k = ("sil", e, i - 1) if i > 0 else ("sil", e - 1, NT - 1)
                    if k in prev:
                        t.wait_ge(sem("S"), prev[k])
                    k2 = ("hs", e, i - 1) if i > 0 else ("hs", e - 1, NT - 1)
                    if k2 in prev:
                        t.wait_ge(sem("V"), prev[k2])
                    for ch in range(NCH):
                        t.matmul(PP0[:], xn(ch, 128 * i, 128 * (i + 1)),
                                 W1E[:, 4096 * sl + 512 * ch:4096 * sl + 512 * (ch + 1)],
                                 start=(ch == 0), stop=False)
                    t.matmul(PP0[:], ONR[:], B1E[:, 512 * sl:512 * (sl + 1)],
                             start=False, stop=True).then_inc(sem("T"), 1)
                c.t(f_a)
                t_a = bump("T")

                def f_sil(s, t_a=t_a):
                    s.wait_ge(sem("T"), t_a)
                    s.activation(out=SA[:], in_=PP0[:], func=AF.Silu).then_inc(sem("S"), 1)
                c.s(f_sil)
                prev[("sil", e, i)] = bump("S")

                def f_b(t, e=e, i=i, sl=sl):
                    k = ("hs", e, i - 1) if i > 0 else ("hs", e - 1, NT - 1)
                    if k in prev:
                        t.wait_ge(sem("V"), prev[k])
                    for ch in range(NCH):
                        t.matmul(PP1[:], xn(ch, 128 * i, 128 * (i + 1)),
                                 W3E[:, 4096 * sl + 512 * ch:4096 * sl + 512 * (ch + 1)],
                                 start=(ch == 0), stop=False)
                    t.matmul(PP1[:], ONR[:], B3E[:, 512 * sl:512 * (sl + 1)],
                             start=False, stop=True).then_inc(sem("T"), 1)
                c.t(f_b)
                t_b = bump("T")

                def f_hs(v, e=e, i=i, t_b=t_b, shared=shared):
                    v.wait_ge(sem("T"), t_b)
                    v.wait_ge(sem("S"), prev[("sil", e, i)])
                    if shared:
                        v.tensor_tensor(HS[:], SA[:], PP1[:], OP.mult).then_inc(sem("V"), 1)
                    else:
                        v.scalar_tensor_tensor(HS[:], PP1[:], CWall[:, 32 * i + e:32 * i + e + 1],
                                               SA[:], OP.mult, OP.mult).then_inc(sem("V"), 1)
                c.v(f_hs)
                prev[("hs", e, i)] = bump("V")

                def f_ht(t, e=e, i=i):
                    t.wait_ge(sem("V"), prev[("hs", e, i)])
                    k = ("hf", e, i - 1) if i > 0 else ("hf", e - 1, NT - 1)
                    if k in prev:
                        t.wait_ge(sem("V"), prev[k])
                    for j in range(4):
                        mm = t.transpose(PT[:, 128 * j:128 * (j + 1)],
                                         HS[:, 128 * j:128 * (j + 1)], IDN[:])
                    mm.then_inc(sem("T"), 1)
                c.t(f_ht)
                t_ht = bump("T")

                def f_hf(v, t_ht=t_ht):
                    v.wait_ge(sem("T"), t_ht)
                    v.tensor_copy(HF[:], PT[:, 0:512]).then_inc(sem("V"), 1)
                c.v(f_hf)
                prev[("hf", e, i)] = bump("V")

                def f_y(t, e=e, i=i, sl=sl, shared=shared):
                    t.wait_ge(sem("V"), prev[("hf", e, i)])
                    k = ("ya", e, i - 1) if i > 0 else ("ya", e - 1, NT - 1)
                    if k in prev:
                        t.wait_ge(sem("V"), prev[k])
                    for n in range(2):
                        for ic in range(4):
                            mm = t.matmul(YP[:, 512 * n:512 * (n + 1)], HF[:, 128 * ic:128 * (ic + 1)],
                                          W2E[:, 4096 * sl + 1024 * ic + 512 * n:4096 * sl + 1024 * ic + 512 * (n + 1)],
                                          start=(ic == 0), stop=(e != 33 and ic == 3))
                        if e == 33:
                            t.matmul(YP[:, 512 * n:512 * (n + 1)], CWT[:, 128 * i:128 * (i + 1)],
                                     B2S[:, 512 * n:512 * (n + 1)], start=False, stop=False)
                            mm = t.matmul(YP[:, 512 * n:512 * (n + 1)], ONR[:],
                                          SB2R[:, 512 * n:512 * (n + 1)], start=False, stop=True)
                    mm.then_inc(sem("T"), 1)
                c.t(f_y)
                t_y = bump("T")

                def f_ya(v, i=i, t_y=t_y):
                    v.wait_ge(sem("T"), t_y)
                    for n in range(2):
                        sl_ = X[:, D * i + 512 * n:D * i + 512 * (n + 1)]
                        mmv = v.tensor_tensor(sl_, sl_, YP[:, 512 * n:512 * (n + 1)], OP.add)
                    mmv.then_inc(sem("V"), 1)
                c.v(f_ya)
                prev[("ya", e, i)] = bump("V")
            t_lastuse[e] = val("T")

        emit_wdma(0)
        emit_wdma(1)
        for e in range(34):
            if e + 2 < 34:
                emit_expert(e)
                emit_wdma(e + 2)
            else:
                emit_expert(e)

        v_fin = val("V")

        FINAL_ST = [0]
        def ph6_sy(sy):
            sy.wait_ge(sem("V"), v_fin)
            sy.dma_start(out=o_x, in_=X[:]).then_inc(sem("st"), 16)
            sy.wait_ge(sem("st"), FINAL_ST[0])
        FINAL_ST[0] = val("st") + 16
        c.sy(ph6_sy)
        bump("st", 16)
    return nc


def build_L2():
    nc = bass.Bass("TRN2", target_bir_lowering=False)
    i_x = nc.dram_tensor("xloc", [128, NT * D], F32, kind="ExternalInput").ap()
    i_xo = nc.dram_tensor("xoth", [128, NT * D], F32, kind="ExternalInput").ap()
    i_wqm = nc.dram_tensor("wqm", [NCH, 128, NCH * 128], F32, kind="ExternalInput").ap()
    i_wkm = nc.dram_tensor("wkm", [NCH, 128, NCH * 128], F32, kind="ExternalInput").ap()
    i_wam = nc.dram_tensor("wam", [NCH, 128, NCH * 128], F32, kind="ExternalInput").ap()
    i_wvn = nc.dram_tensor("wvn", [2, 128, NCH * 512], F32, kind="ExternalInput").ap()
    i_bq = nc.dram_tensor("bqc", [128, 8], F32, kind="ExternalInput").ap()
    i_bk = nc.dram_tensor("bkc", [128, 8], F32, kind="ExternalInput").ap()
    i_ba = nc.dram_tensor("bac", [128, 8], F32, kind="ExternalInput").ap()
    i_bvr = nc.dram_tensor("bvr", [1, D], F32, kind="ExternalInput").ap()
    i_gwT = nc.dram_tensor("gwT", [D, 8], F32, kind="ExternalInput").ap()
    i_bbr = nc.dram_tensor("bbr", [1, 8], F32, kind="ExternalInput").ap()
    i_w1 = nc.dram_tensor("w1t", [8, D, 2048], F32, kind="ExternalInput").ap()
    i_w2t = nc.dram_tensor("w2t", [8, 2048, D], F32, kind="ExternalInput").ap()
    i_idn = nc.dram_tensor("idn", [128, 128], BF16, kind="ExternalInput").ap()
    i_onr = nc.dram_tensor("onesr", [1, 128], BF16, kind="ExternalInput").ap()
    i_onc = nc.dram_tensor("onesc", [128, 1], BF16, kind="ExternalInput").ap()
    i_pmsk = nc.dram_tensor("pmsk", [128, 1], F32, kind="ExternalInput").ap()
    i_smsk = nc.dram_tensor("smsk", [128, 1], F32, kind="ExternalInput").ap()
    i_mcau = nc.dram_tensor("mcau", [128, 128], BF16, kind="ExternalInput").ap()
    o_x = nc.dram_tensor("xout", [128, NT * D], F32, kind="ExternalOutput").ap()
    o_wk = nc.dram_tensor("dbg_wk", [128, NCH * TOK], BF16, kind="ExternalOutput").ap()
    o_xn1 = nc.dram_tensor("dbg_xn1", [128, NCH * TOK], BF16, kind="ExternalOutput").ap()
    o_xn2 = nc.dram_tensor("dbg_xn2", [128, NCH * TOK], BF16, kind="ExternalOutput").ap()
    o_cb = nc.dram_tensor("dbg_cb", [128, NT * 8], F32, kind="ExternalOutput").ap()
    o_q = nc.dram_tensor("dbg_q", [128, NCH * TOK], BF16, kind="ExternalOutput").ap()
    d_np = nc.dram_tensor("nscr", [128, 16 * (2 * D + 16)], BF16, kind="ExternalOutput").ap()
    d_no = nc.dram_tensor("noscr", [128, NT * D], BF16, kind="ExternalOutput").ap()

    es = contextlib.ExitStack()
    with es:
        c = Ctx(nc, es)
        sem, bump, val = c.sem, c.bump, c.val

        def vedge(v, instr):
            n = bump("V"); instr.then_inc(sem("V"), 1); v.wait_ge(sem("V"), n)

        def sedge(s_, instr):
            n = bump("S"); instr.then_inc(sem("S"), 1); s_.wait_ge(sem("S"), n)

        XN = c.sb("XN", [128, NCH * TOK], BF16)
        Q = c.sb("Q", [128, NCH * TOK], BF16)
        BQ = c.sb("BQ", [128, 8], F32)
        BK = c.sb("BK", [128, 8], F32)
        BA = c.sb("BA", [128, 8], F32)
        BVR = c.sb("BVR", [1, D], BF16)
        GWT = c.sb("GWT", [128, NCH * 8], BF16)
        BBR = c.sb("BBR", [1, 8], BF16)
        CSO = c.sb("CSO", [128, 8], F32)
        CSX = c.sb("CSX", [128, 8], F32)
        SFX = c.sb("SFX", [128, 8], F32)
        MV = c.sb("MV", [128, 36], F32)
        RS = c.sb("RS", [128, 18], F32)
        SQ = c.sb("SQ", [128, 36], F32)
        XNT = c.sb("XNT", [128, 2 * D], BF16)
        IDN = c.sb("IDN", [128, 128], BF16)
        ONR = c.sb("ONR", [1, 128], BF16)
        ONC = c.sb("ONC", [128, 1], BF16)
        PMSK = c.sb("PMSK", [128, 1], F32)
        SMSK = c.sb("SMSK", [128, 1], F32)
        MCAU = c.sb("MCAU", [128, 128], BF16)
        EPS = c.sb("EPS", [128, 1], F32)
        EPS6 = c.sb("EPS6", [128, 1], F32)
        M87 = c.sb("M87", [128, 1], F32)
        NOB = c.sb("NOB", [128, 2 * D + 16], BF16)
        DENR = c.sb("DENR", [128, 8], F32)
        FINT = c.sb("FINT", [128, 512], F32)
        IDXS = c.sb("IDXS", [128, 8], U32)
        LACs = c.sb("LACs", [128, TOK], F32)
        KCs = c.sb("KCs", [128, TOK], BF16)
        EGCs = c.sb("EGCs", [128, TOK], BF16)
        WMB = c.sb("WMB", [128, 2 * NCH * 128], BF16)
        WMB2 = c.sb("WMB2", [128, 2 * NCH * 128], BF16)
        WVB = c.sb("WVB", [128, NCH * 512], BF16)
        PP0 = c.ps("PP0", [128, 512], F32)
        PP1 = c.ps("PP1", [128, 512], F32)
        NUM0 = c.ps("NUM0", [128, 1024], F32)
        NUM1 = c.ps("NUM1", [128, 1024], F32)
        DEN01 = c.ps("DEN01", [128, 512], F32)
        SPS = c.ps("SPS", [128, 512], F32)
        PT = NUM0[:].bitcast(BF16)

        def ph0_sy(sy):
            for dst, src in [(BQ, i_bq), (BK, i_bk), (BA, i_ba), (PMSK, i_pmsk), (SMSK, i_smsk)]:
                sy.dma_start(out=dst[:], in_=src).then_inc(sem("ldx"), 16)
        c.sy(ph0_sy)
        LDX = bump("ldx", 5 * 16)

        def ph0_g(g):
            for dst, src in [(IDN, i_idn), (ONR, i_onr), (ONC, i_onc), (BVR, i_bvr),
                             (BBR, i_bbr), (MCAU, i_mcau)]:
                g.dma_start(out=dst[:], in_=src).then_inc(sem("ldw"), 16)
            g.dma_start(out=GWT[:].rearrange("p (c m) -> p c m", m=8),
                        in_=i_gwT.rearrange("(c p) m -> p c m", p=128)).then_inc(sem("ldw"), 16)
            g.memset(EPS[:], 1e-5)
            g.memset(EPS6[:], 1e-6)
            g.memset(M87[:], -87.0).then_inc(sem("ldw"), 1)
        c.g(ph0_g)
        LDW = bump("ldw", 7 * 16 + 1)

        def emit_ln(XSRC, ldsem, ldval, DEST):
            def f_ssq(s_, tw=val("T")):
                s_.wait_ge(sem(ldsem), ldval)
                s_.wait_ge(sem("ldw"), LDW)
                s_.wait_ge(sem("T"), tw)
                for i in range(16):
                    s_.activation(out=XNT[:, 0:D], in_=XSRC[:, D * i:D * (i + 1)],
                                  func=AF.Square, accum_out=SQ[:, 18 + i:19 + i])
                s_.copy(RS[:, 17:18], SQ[:, 33:34]).then_inc(sem("S"), 1)
            c.s(f_ssq)
            s_sq = bump("S")

            def f_vst(v):
                v.wait_ge(sem(ldsem), ldval)
                v.wait_ge(sem("ldw"), LDW)
                for i in range(16):
                    v.reduce_sum(SQ[:, i:i + 1], XSRC[:, D * i:D * (i + 1)], axis=AX.X)
                i1 = v.tensor_scalar(MV[:, 0:16], SQ[:, 0:16], 1.0 / D, None, OP.mult)
                vedge(v, i1)
                v.wait_ge(sem("S"), s_sq)
                i2 = v.tensor_scalar(MV[:, 18:34], SQ[:, 18:34], 1.0 / D, None, OP.mult)
                vedge(v, i2)
                i3 = v.tensor_tensor(RS[:, 0:16], MV[:, 0:16], MV[:, 0:16], OP.mult)
                vedge(v, i3)
                v.tensor_tensor(MV[:, 18:34], MV[:, 18:34], RS[:, 0:16], OP.subtract).then_inc(sem("V"), 1)
            c.v(f_vst)
            v_st = bump("V")

            def f_rs(s_):
                s_.wait_ge(sem("V"), v_st)
                i1 = s_.activation(out=RS[:, 0:16], in_=MV[:, 18:34], func=AF.Ln, bias=EPS[:])
                sedge(s_, i1)
                s_.activation(out=RS[:, 0:16], in_=RS[:, 0:16], func=AF.Exp, scale=-0.5).then_inc(sem("S"), 1)
            c.s(f_rs)
            s_rs = bump("S")

            v_ap = [0] * 16
            t_tr = [0] * 16
            v_ev = [0] * 16
            DV = DEST[:].rearrange("p (ch w) -> p ch w", w=TOK)
            PTv = PT.rearrange("p (ch w) -> p ch w", w=128)
            for i in range(16):
                buf = i % 2

                def f_ap(v, i=i, buf=buf):
                    v.wait_ge(sem("S"), s_rs)
                    if i >= 2:
                        v.wait_ge(sem("T"), t_tr[i - 2])
                    v.tensor_scalar(XNT[:, D * buf:D * (buf + 1)], XSRC[:, D * i:D * (i + 1)],
                                    MV[:, i:i + 1], RS[:, i:i + 1],
                                    OP.subtract, OP.mult).then_inc(sem("V"), 1)
                c.v(f_ap)
                v_ap[i] = bump("V")

                def f_tr(t, i=i, buf=buf):
                    t.wait_ge(sem("V"), v_ap[i])
                    if i >= 1:
                        t.wait_ge(sem("V"), v_ev[i - 1])
                    for ch in range(NCH):
                        mm = t.transpose(PT[:, 128 * ch:128 * (ch + 1)],
                                         XNT[:, D * buf + 128 * ch:D * buf + 128 * (ch + 1)], IDN[:])
                    mm.then_inc(sem("T"), 1)
                c.t(f_tr)
                t_tr[i] = bump("T")

                def f_ev(v, i=i):
                    v.wait_ge(sem("T"), t_tr[i])
                    v.tensor_copy(DV[:, :, 128 * i:128 * (i + 1)], PTv[:, 0:NCH, :]).then_inc(sem("V"), 1)
                c.v(f_ev)
                v_ev[i] = bump("V")
            return val("V")

        def emit_proj_fm_stream(SRC, i_wm, WBUF, BCOL, DEST_FN, act, lds):
            ev = {}
            tm = {}
            for mch in range(NCH):
                def f_wd(g, mch=mch, lds=lds):
                    if mch >= 2 and (lds, mch - 2) in tm:
                        g.wait_ge(sem("T"), tm[(lds, mch - 2)])
                    sl = mch % 2
                    g.dma_start(out=WBUF[:, 1024 * sl:1024 * (sl + 1)],
                                in_=i_wm[mch]).then_inc(sem(lds), 16)
                c.g(f_wd)
                ld_m = bump(lds, 16)
                for s4 in range(4):
                    PB = PP0 if (mch * 4 + s4) % 2 == 0 else PP1

                    def f_mm(t, mch=mch, s4=s4, PB=PB, ld_m=ld_m):
                        key = mch * 4 + s4 - 2
                        if key in ev:
                            t.wait_ge(sem("S"), ev[key])
                        if s4 == 0:
                            t.wait_ge(sem(lds), ld_m)
                        sl = mch % 2
                        for cc in range(NCH):
                            mm = t.matmul(PB[:], WBUF[:, 1024 * sl + 128 * cc:1024 * sl + 128 * (cc + 1)],
                                          SRC[:, TOK * cc + 512 * s4:TOK * cc + 512 * (s4 + 1)],
                                          start=(cc == 0), stop=(cc == 7))
                        mm.then_inc(sem("T"), 1)
                    c.t(f_mm)
                    tm[(lds, mch)] = bump("T")

                    def f_ev(s_, mch=mch, s4=s4, PB=PB):
                        s_.wait_ge(sem("T"), tm[(lds, mch)])
                        s_.activation(out=DEST_FN(mch, s4), in_=PB[:], func=act,
                                      bias=BCOL[:, mch:mch + 1]).then_inc(sem("S"), 1)
                    c.s(f_ev)
                    ev[mch * 4 + s4] = bump("S")
            return val("S"), val("T")

        def emit_half(SRC_FM, WK, wk_off, VT, vt_off, CS_OUT, sfx_fn, lds):
            tm = {}
            sv = {}
            for ch in range(NCH):
                def f_wd(g, ch=ch):
                    if ch >= 2 and ("h", ch - 2) in tm:
                        g.wait_ge(sem("T"), tm[("h", ch - 2)])
                    sl = ch % 2
                    g.dma_start(out=WMB[:, 1024 * sl:1024 * (sl + 1)], in_=i_wam[ch]).then_inc(sem(lds), 16)
                    g.dma_start(out=WMB2[:, 1024 * sl:1024 * (sl + 1)], in_=i_wkm[ch]).then_inc(sem(lds), 16)
                c.g(f_wd)
                ld_m = bump(lds, 32)
                sl = ch % 2
                s_la0 = val("S")
                for s4 in range(4):
                    def f_amm(t, ch=ch, s4=s4, sl=sl, ld_m=ld_m, vw=val("V")):
                        t.wait_ge(sem("S"), val("S"))
                        if s4 == 0:
                            t.wait_ge(sem(lds), ld_m)
                            t.wait_ge(sem("V"), vw)
                        for cc in range(NCH):
                            mm = t.matmul(PP0[:], WMB[:, 1024 * sl + 128 * cc:1024 * sl + 128 * (cc + 1)],
                                          SRC_FM[:, TOK * cc + 512 * s4:TOK * cc + 512 * (s4 + 1)],
                                          start=(cc == 0), stop=(cc == 7))
                        mm.then_inc(sem("T"), 1)
                    c.t(f_amm)
                    t_a = bump("T")

                    def f_sig(s_, ch=ch, s4=s4, t_a=t_a):
                        s_.wait_ge(sem("T"), t_a)
                        i1 = s_.activation(out=LACs[:, 512 * s4:512 * (s4 + 1)], in_=PP0[:],
                                           func=AF.Sigmoid, bias=BA[:, ch:ch + 1])
                        sedge(s_, i1)
                        s_.activation(out=LACs[:, 512 * s4:512 * (s4 + 1)],
                                      in_=LACs[:, 512 * s4:512 * (s4 + 1)],
                                      func=AF.Ln, bias=EPS6[:]).then_inc(sem("S"), 1)
                    c.s(f_sig)
                    bump("S")
                s_la = val("S")

                def f_cs(v, ch=ch, s_la=s_la):
                    v.wait_ge(sem("S"), s_la)
                    i0 = v.reduce_sum(CS_OUT[:, ch:ch + 1], LACs[:], axis=AX.X)
                    vedge(v, i0)
                    i1 = v.tensor_tensor_scan(LACs[:, ::-1], LACs[:, ::-1], LACs[:, ::-1],
                                              0.0, OP.add, OP.bypass)
                    vedge(v, i1)
                    sfx = sfx_fn(v, ch)
                    v.tensor_scalar(LACs[:], LACs[:], sfx, M87[:], OP.add, OP.max).then_inc(sem("V"), 1)
                c.v(f_cs)
                v_lg = bump("V")

                def f_eg(s_, v_lg=v_lg):
                    s_.wait_ge(sem("V"), v_lg)
                    s_.activation(out=EGCs[:], in_=LACs[:], func=AF.Exp).then_inc(sem("S"), 1)
                c.s(f_eg)
                s_eg = bump("S")

                for s4 in range(4):
                    def f_kmm(t, ch=ch, s4=s4, sl=sl, vw=val("V")):
                        t.wait_ge(sem("S"), val("S"))
                        if s4 == 0:
                            t.wait_ge(sem("V"), vw)
                        for cc in range(NCH):
                            mm = t.matmul(PP1[:], WMB2[:, 1024 * sl + 128 * cc:1024 * sl + 128 * (cc + 1)],
                                          SRC_FM[:, TOK * cc + 512 * s4:TOK * cc + 512 * (s4 + 1)],
                                          start=(cc == 0), stop=(cc == 7))
                        mm.then_inc(sem("T"), 1)
                    c.t(f_kmm)
                    t_k = bump("T")

                    def f_rel(s_, ch=ch, s4=s4, t_k=t_k):
                        s_.wait_ge(sem("T"), t_k)
                        s_.activation(out=KCs[:, 512 * s4:512 * (s4 + 1)], in_=PP1[:],
                                      func=AF.Relu, bias=BK[:, ch:ch + 1]).then_inc(sem("S"), 1)
                    c.s(f_rel)
                    bump("S")
                s_k = val("S")

                def f_wk(v, ch=ch, s_k=s_k, s_eg=s_eg):
                    v.wait_ge(sem("S"), s_k)
                    v.wait_ge(sem("S"), s_eg)
                    v.tensor_tensor(WK[:, TOK * ch + wk_off:TOK * ch + wk_off + TOK],
                                    KCs[:], EGCs[:], OP.mult).then_inc(sem("V"), 1)
                c.v(f_wk)
                tm[("h", ch)] = val("T")
                bump("V")
            v_wk = val("V")

            for n in range(2):
                def f_wvd(g, n=n):
                    if n == 1:
                        g.wait_ge(sem("T"), sv[(15, 0)])
                    g.dma_start(out=WVB[:], in_=i_wvn[n]).then_inc(sem(lds), 16)
                c.g(f_wvd)
                ld_v = bump(lds, 16)
                for i in range(16):
                    PB = PP0 if i % 2 == 0 else PP1

                    def f_vmm(t, i=i, n=n, PB=PB, ld_v=ld_v):
                        if i == 0:
                            t.wait_ge(sem(lds), ld_v)
                        key = (i - 2, n)
                        if key in sv:
                            pass
                        t.wait_ge(sem("V"), val("V"))
                        for cc in range(NCH):
                            t.matmul(PB[:], SRC_FM[:, TOK * cc + 128 * i:TOK * cc + 128 * (i + 1)],
                                     WVB[:, 512 * cc:512 * (cc + 1)],
                                     start=(cc == 0), stop=False)
                        mm = t.matmul(PB[:], ONR[:], BVR[:, 512 * n:512 * (n + 1)],
                                      start=False, stop=True)
                        mm.then_inc(sem("T"), 1)
                    c.t(f_vmm)
                    t_v = bump("T")
                    sv[(i, n)] = t_v

                    def f_vev(v, i=i, n=n, PB=PB, t_v=t_v):
                        v.wait_ge(sem("T"), t_v)
                        v.tensor_copy(VT[:, D * (vt_off + i) + 512 * n:D * (vt_off + i) + 512 * (n + 1)],
                                      PB[:]).then_inc(sem("V"), 1)
                    c.v(f_vev)
                    bump("V")
            return val("V")

        def emit_attn_pass(WK, VT, nk_fn, is_final, diag_base_fn, PPART=None):
            for qg in range(8):
                nk = nk_fn(qg)
                v_sct_prev = [None]
                t_num_prev = [None]
                for kt in range(nk):
                    def f_sc(t, qg=qg, kt=kt):
                        if v_sct_prev[0] is not None:
                            t.wait_ge(sem("V"), v_sct_prev[0])
                        else:
                            t.wait_ge(sem("V"), val("V"))
                        for cc in range(NCH):
                            mm = t.matmul(SPS[:, 0:256], WK[:, TOK * cc + 128 * kt:TOK * cc + 128 * (kt + 1)],
                                          Q[:, TOK * cc + 256 * qg:TOK * cc + 256 * (qg + 1)],
                                          start=(cc == 0), stop=(cc == 7))
                        mm.then_inc(sem("T"), 1)
                    c.t(f_sc)
                    t_sc = bump("T")

                    def f_sev(v, qg=qg, kt=kt, t_sc=t_sc):
                        v.wait_ge(sem("T"), t_sc)
                        if t_num_prev[0] is not None:
                            v.wait_ge(sem("T"), t_num_prev[0])
                        SCT = XNT
                        db = diag_base_fn(qg)
                        if db is None:
                            i1 = v.tensor_scalar(SCT[:, 0:256], SPS[:, 0:256], PMSK[:], None, OP.mult)
                        else:
                            m = kt
                            for j in range(2):
                                sub = SPS[:, 128 * j:128 * (j + 1)]
                                dst = SCT[:, 128 * j:128 * (j + 1)]
                                if m < 2 * qg + j:
                                    i1 = v.tensor_copy(dst, sub)
                                elif m == 2 * qg + j:
                                    i1 = v.tensor_tensor(dst, sub, MCAU[:], OP.mult)
                                else:
                                    i1 = v.memset(dst, 0.0)
                        i1.then_inc(sem("V"), 1)
                    c.v(f_sev)
                    v_sct_prev[0] = bump("V")

                    def f_num(t, qg=qg, kt=kt, nk=nk, vw=v_sct_prev[0]):
                        t.wait_ge(sem("V"), vw)
                        SCT = XNT
                        for j in range(2):
                            NP = NUM0 if j == 0 else NUM1
                            for n in range(2):
                                t.matmul(NP[:, 512 * n:512 * (n + 1)], SCT[:, 128 * j:128 * (j + 1)],
                                         VT[:, D * kt + 512 * n:D * kt + 512 * (n + 1)],
                                         start=(kt == 0), stop=(kt == nk - 1))
                            mm = t.matmul(DEN01[:, 256 * j:256 * j + 1], SCT[:, 128 * j:128 * (j + 1)],
                                          ONC[:], start=(kt == 0 and j == 0), stop=(kt == nk - 1),
                                          skip_group_check=True)
                        mm.then_inc(sem("T"), 1)
                    c.t(f_num)
                    t_num_prev[0] = bump("T")

                t_last = val("T")
                if not is_final:
                    def f_p1ev(v, qg=qg, t_last=t_last):
                        v.wait_ge(sem("T"), t_last)
                        for j in range(2):
                            NP = NUM0 if j == 0 else NUM1
                            v.tensor_copy(NOB[:, D * j:D * (j + 1)], NP[:])
                            i1 = v.tensor_copy(NOB[:, 2 * D + j:2 * D + j + 1], DEN01[:, 256 * j:256 * j + 1])
                        i1.then_inc(sem("V"), 1)
                    c.v(f_p1ev)
                    v_nob = bump("V")

                    def f_nst(sy, qg=qg, v_nob=v_nob):
                        sy.wait_ge(sem("V"), v_nob)
                        sy.dma_start(out=d_np[:, (2 * D + 16) * qg:(2 * D + 16) * qg + 2 * D],
                                     in_=NOB[:, 0:2 * D]).then_inc(sem("st"), 16)
                        sy.dma_start(out=d_np[:, (2 * D + 16) * qg + 2 * D:(2 * D + 16) * qg + 2 * D + 2],
                                     in_=NOB[:, 2 * D:2 * D + 2]).then_inc(sem("st"), 16)
                    c.sy(f_nst)
                    bump("st", 32)
                else:
                    def f_ppl(sy, qg=qg, t_last=t_last):
                        sy.wait_ge(sem("V"), val("V"))
                        sy.dma_start(out=PPART[:, 0:2 * D],
                                     in_=d_np[:, (2 * D + 16) * qg:(2 * D + 16) * qg + 2 * D]).then_inc(sem("ldp"), 16)
                        sy.dma_start(out=PPART[:, 2 * D:2 * D + 2],
                                     in_=d_np[:, (2 * D + 16) * qg + 2 * D:(2 * D + 16) * qg + 2 * D + 2]).then_inc(sem("ldp"), 16)
                    c.sy(f_ppl)
                    ld_pp = bump("ldp", 32)

                    def f_fin(v, qg=qg, t_last=t_last, ld_pp=ld_pp):
                        v.wait_ge(sem("T"), t_last)
                        v.wait_ge(sem("ldp"), ld_pp)
                        for j in range(2):
                            i0 = v.tensor_tensor(DENR[:, j:j + 1], DEN01[:, 256 * j:256 * j + 1],
                                                 PPART[:, 2 * D + j:2 * D + j + 1], OP.add)
                            vedge(v, i0)
                            i1 = v.tensor_scalar(DENR[:, 2 + j:3 + j], DENR[:, j:j + 1], EPS6[:], None, OP.add)
                            vedge(v, i1)
                            i2 = v.reciprocal(DENR[:, 4 + j:5 + j], DENR[:, 2 + j:3 + j])
                            vedge(v, i2)
                        for j in range(2):
                            NP = NUM0 if j == 0 else NUM1
                            for n in range(2):
                                i3 = v.tensor_tensor(FINT[:], NP[:, 512 * n:512 * (n + 1)],
                                                     PPART[:, D * j + 512 * n:D * j + 512 * (n + 1)], OP.add)
                                vedge(v, i3)
                                i4 = v.tensor_scalar(NOB[:, D * j + 512 * n:D * j + 512 * (n + 1)],
                                                     FINT[:], DENR[:, 4 + j:5 + j], None, OP.mult)
                                vedge(v, i4)
                        v.tensor_copy(DENR[:, 6:7], DENR[:, 4:5]).then_inc(sem("V"), 1)
                    c.v(f_fin)
                    v_fin = bump("V")

                    def f_fst(sy, qg=qg, v_fin=v_fin):
                        sy.wait_ge(sem("V"), v_fin)
                        sy.dma_start(out=d_no[:, 2 * D * qg:2 * D * (qg + 1)],
                                     in_=NOB[:, 0:2 * D]).then_inc(sem("st"), 16)
                    c.sy(f_fst)
                    bump("st", 16)

        with nc.sbuf_tensor("X", [128, NT * D], F32) as X:
            def ldx_sy(sy):
                sy.dma_start(out=X[:], in_=i_x).then_inc(sem("ldx"), 16)
            c.sy(ldx_sy)
            LDX2 = bump("ldx", 16)
            v_lnx = emit_ln(X, "ldx", LDX2, XN)

            def dbgxn1(sy):
                sy.wait_ge(sem("V"), v_lnx)
                sy.dma_start(out=o_xn1, in_=XN[:]).then_inc(sem("st"), 16)
            c.sy(dbgxn1)
            bump("st", 16)

        s_q, t_q = emit_proj_fm_stream(
            XN, i_wqm, WMB, BQ,
            lambda mch, s4: Q[:, TOK * mch + 512 * s4:TOK * mch + 512 * (s4 + 1)],
            AF.Relu, "ldq")

        def dest_null(mch, s4):
            return LACs[:, 512 * s4:512 * (s4 + 1)]
        ev_cso = {}
        for mch in range(NCH):
            def f_wd(g, mch=mch):
                if ("cso", mch - 2) in ev_cso:
                    g.wait_ge(sem("T"), ev_cso[("cso", mch - 2)])
                sl = mch % 2
                g.dma_start(out=WMB2[:, 1024 * sl:1024 * (sl + 1)], in_=i_wam[mch]).then_inc(sem("lda"), 16)
            c.g(f_wd)
            ld_m = bump("lda", 16)
            sl = mch % 2
            for s4 in range(4):
                def f_mm(t, mch=mch, s4=s4, sl=sl, ld_m=ld_m, vw=val("V")):
                    t.wait_ge(sem("S"), val("S"))
                    if s4 == 0:
                        t.wait_ge(sem("lda"), ld_m)
                        t.wait_ge(sem("V"), vw)
                    for cc in range(NCH):
                        mm = t.matmul(PP0[:], WMB2[:, 1024 * sl + 128 * cc:1024 * sl + 128 * (cc + 1)],
                                      XN[:, TOK * cc + 512 * s4:TOK * cc + 512 * (s4 + 1)],
                                      start=(cc == 0), stop=(cc == 7))
                    mm.then_inc(sem("T"), 1)
                c.t(f_mm)
                ev_cso[("cso", mch)] = bump("T")
                t_m = val("T")

                def f_sig(s_, mch=mch, s4=s4, t_m=t_m):
                    s_.wait_ge(sem("T"), t_m)
                    i1 = s_.activation(out=LACs[:, 512 * s4:512 * (s4 + 1)], in_=PP0[:],
                                       func=AF.Sigmoid, bias=BA[:, mch:mch + 1])
                    sedge(s_, i1)
                    s_.activation(out=LACs[:, 512 * s4:512 * (s4 + 1)],
                                  in_=LACs[:, 512 * s4:512 * (s4 + 1)],
                                  func=AF.Ln, bias=EPS6[:]).then_inc(sem("S"), 1)
                c.s(f_sig)
                bump("S")
            s_la = val("S")

            def f_cs(v, mch=mch, s_la=s_la):
                v.wait_ge(sem("S"), s_la)
                v.reduce_sum(CSO[:, mch:mch + 1], LACs[:], axis=AX.X).then_inc(sem("V"), 1)
            c.v(f_cs)
            bump("V")
        v_cso = val("V")

        CSD = c.sb("CSD", [128, 8], F32)
        with nc.sbuf_tensor("XNO", [128, NCH * TOK], BF16) as XNO:
            with nc.sbuf_tensor("XO", [128, NT * D], F32) as XO:
                def ldxo_sy(sy):
                    sy.wait_ge(sem("V"), v_cso)
                    sy.dma_start(out=XO[:], in_=i_xo).then_inc(sem("ldx"), 16)
                c.sy(ldxo_sy)
                LDX3 = bump("ldx", 16)
                emit_ln(XO, "ldx", LDX3, XNO)
            import os as _os
            if _os.environ.get("L3CUT") == "2":
                def fcut(sy, vv=val("V")):
                    sy.wait_ge(sem("V"), vv)
                    sy.dma_start(out=o_xn0, in_=XN[:]).then_inc(sem("st"), 16)
                    sy.wait_ge(sem("st"), val("st") + 16)
                c.sy(fcut)
                bump("st", 16)
                return nc
            with nc.sbuf_tensor("WKP", [128, NCH * TOK], BF16) as WKP, \
                 nc.sbuf_tensor("VTP", [128, 16 * D], BF16) as VTP:
                def sfx_oth(v, ch):
                    return CSO[:, ch:ch + 1]
                emit_half(XNO, WKP, 0, VTP, 0, CSX, sfx_oth, "ldh1")
                emit_attn_pass(WKP, VTP, lambda qg: 16, False, lambda qg: None)
        v_p1 = val("V")

        with nc.sbuf_tensor("WKO", [128, NCH * TOK], BF16) as WKO, \
             nc.sbuf_tensor("VTO", [128, 16 * D], BF16) as VTO, \
             nc.sbuf_tensor("PPART", [128, 2 * D + 16], BF16) as PPART:
            def sfx_own(v, ch):
                i1 = v.tensor_scalar(SFX[:, ch:ch + 1], CSX[:, ch:ch + 1], SMSK[:], None, OP.mult)
                vedge(v, i1)
                return SFX[:, ch:ch + 1]
            emit_half(XN, WKO, 0, VTO, 0, CSD, sfx_own, "ldh2")
            v_wko = val("V")

            def f_dbgwk(sy):
                sy.wait_ge(sem("V"), v_wko)
                sy.dma_start(out=o_wk, in_=WKO[:]).then_inc(sem("st"), 16)
                sy.dma_start(out=o_q, in_=Q[:]).then_inc(sem("st"), 16)
            c.sy(f_dbgwk)
            st_wk = bump("st", 32)

            def f_dbgwk_hold(t):
                t.wait_ge(sem("st"), st_wk)
            c.t(f_dbgwk_hold)
            emit_attn_pass(WKO, VTO, lambda qg: 2 * qg + 2, True, lambda qg: qg, PPART=PPART)
        v_attn = val("V")
        st_attn = val("st")

        with nc.sbuf_tensor("X2", [128, NT * D], F32) as X2, \
             nc.sbuf_tensor("NOL", [128, 2 * D], BF16) as NOL, \
             nc.sbuf_tensor("W1S", [128, NCH * 512], BF16) as W1S, \
             nc.sbuf_tensor("W2S", [128, 4 * 1024], BF16) as W2S, \
             nc.sbuf_tensor("CBall", [128, NT * 8], F32) as CBall, \
             nc.sbuf_tensor("SA2", [128, 512], BF16) as SA2, \
             nc.sbuf_tensor("HS2", [128, 512], BF16) as HS2, \
             nc.sbuf_tensor("HF2", [128, 512], BF16) as HF2, \
             nc.sbuf_tensor("RT", [128, 16], F32) as RT:
            def ldx2_sy(sy):
                sy.wait_ge(sem("st"), st_attn)
                sy.wait_ge(sem("V"), v_attn)
                sy.dma_start(out=X2[:], in_=i_x).then_inc(sem("ldf"), 16)
            c.sy(ldx2_sy)
            LDF = bump("ldf", 16)
            v_na = [0]
            for i in range(NT):
                def f_nld(sy, i=i):
                    sy.wait_ge(sem("ldf"), LDF)
                    if i >= 2:
                        sy.wait_ge(sem("V"), v_na[0] - 1)
                    sy.dma_start(out=NOL[:, D * (i % 2):D * (i % 2 + 1)],
                                 in_=d_no[:, D * i:D * (i + 1)]).then_inc(sem("ldf"), 16)
                c.sy(f_nld)
                ld_n = bump("ldf", 16)

                def f_nadd(v, i=i, ld_n=ld_n):
                    v.wait_ge(sem("ldf"), ld_n)
                    v.tensor_tensor(X2[:, D * i:D * (i + 1)], X2[:, D * i:D * (i + 1)],
                                    NOL[:, D * (i % 2):D * (i % 2 + 1)], OP.add).then_inc(sem("V"), 1)
                c.v(f_nadd)
                v_na[0] = bump("V")
            v_xadd = val("V")

            rt_prev = {}
            rt_vtop = {}
            for i in range(NT):
                def f_lg(t, i=i):
                    if i == 0:
                        t.wait_ge(sem("V"), v_xadd)
                    if (i - 1) in rt_prev:
                        t.wait_ge(sem("S"), rt_prev[i - 1])
                    if (i - 1) in rt_vtop:
                        t.wait_ge(sem("V"), rt_vtop[i - 1])
                    for cc in range(NCH):
                        t.matmul(SPS[:, 0:8], XN[:, TOK * cc + 128 * i:TOK * cc + 128 * (i + 1)],
                                 GWT[:, 8 * cc:8 * (cc + 1)], start=(cc == 0), stop=False)
                    mm = t.matmul(SPS[:, 0:8], ONR[:], BBR[:], start=False, stop=True)
                    mm.then_inc(sem("T"), 1)
                c.t(f_lg)
                t_lg = bump("T")

                def f_sg(s_, i=i, t_lg=t_lg):
                    s_.wait_ge(sem("T"), t_lg)
                    s_.activation(out=RT[:, 0:8], in_=SPS[:, 0:8], func=AF.Sigmoid).then_inc(sem("S"), 1)
                c.s(f_sg)
                rt_prev[i] = bump("S")
                s_sg = rt_prev[i]

                def f_top(v, i=i, s_sg=s_sg):
                    v.wait_ge(sem("S"), s_sg)
                    v.max_with_indices(MV[:, 0:8], IDXS[:], RT[:, 0:8])
                    i1 = v.tensor_copy(DENR[:, 7:8], MV[:, 1:2])
                    vedge(v, i1)
                    i2 = v.tensor_scalar(RT[:, 8:16], RT[:, 0:8], MV[:, 1:2], None, OP.is_ge)
                    vedge(v, i2)
                    i3 = v.tensor_tensor(RT[:, 8:16], RT[:, 0:8], RT[:, 8:16], OP.mult)
                    vedge(v, i3)
                    i4 = v.reduce_sum(DENR[:, 0:1], RT[:, 8:16], axis=AX.X)
                    vedge(v, i4)
                    i5 = v.tensor_scalar(DENR[:, 1:2], DENR[:, 0:1], EPS6[:], None, OP.add)
                    vedge(v, i5)
                    i6 = v.reciprocal(DENR[:, 2:3], DENR[:, 1:2])
                    vedge(v, i6)
                    v.tensor_scalar(CBall[:, 8 * i:8 * (i + 1)], RT[:, 8:16], DENR[:, 2:3],
                                    None, OP.mult).then_inc(sem("V"), 1)
                c.v(f_top)
                rt_vtop[i] = bump("V")
            v_route = val("V")

            prevv = {}
            for e in range(8):
                for isl in range(4):
                    def f_wde(g, e=e, isl=isl):
                        k2 = e * 4 + isl - 1
                        if k2 in prevv:
                            g.wait_ge(sem("T"), prevv[k2])
                        g.dma_start(out=W1S[:].rearrange("p (cc m) -> p cc m", m=512),
                                    in_=i_w1.rearrange("e (cc p) (g i) -> e g p cc i", p=128, g=4)[e, isl]).then_inc(sem("lde"), 16)
                        g.dma_start(out=W2S[:].rearrange("p (ic m) -> p ic m", m=1024),
                                    in_=i_w2t.rearrange("e (g ic p) d -> e g p ic d", g=4, p=128)[e, isl]).then_inc(sem("lde"), 16)
                    c.g(f_wde)
                    ld_e = bump("lde", 32)
                    for i in range(NT):
                        def f_a(t, e=e, isl=isl, i=i, ld_e=ld_e):
                            if (e, isl, i) == (0, 0, 0):
                                t.wait_ge(sem("V"), v_route)
                            if i == 0:
                                t.wait_ge(sem("lde"), ld_e)
                            k = ("sil", e * 64 + isl * 16 + i - 1)
                            if k in prevv:
                                t.wait_ge(sem("S"), prevv[k])
                            k2 = ("hs", e * 64 + isl * 16 + i - 1)
                            if k2 in prevv:
                                t.wait_ge(sem("V"), prevv[k2])
                            for cc in range(NCH):
                                mm = t.matmul(PP0[:], XN[:, TOK * cc + 128 * i:TOK * cc + 128 * (i + 1)],
                                              W1S[:, 512 * cc:512 * (cc + 1)],
                                              start=(cc == 0), stop=(cc == 7))
                            mm.then_inc(sem("T"), 1)
                        c.t(f_a)
                        t_a = bump("T")

                        def f_sil(s_, t_a=t_a):
                            s_.wait_ge(sem("T"), t_a)
                            s_.activation(out=SA2[:], in_=PP0[:], func=AF.Silu).then_inc(sem("S"), 1)
                        c.s(f_sil)
                        prevv[("sil", e * 64 + isl * 16 + i)] = bump("S")
                        s_sil = prevv[("sil", e * 64 + isl * 16 + i)]

                        def f_hs(v, e=e, i=i, s_sil=s_sil):
                            v.wait_ge(sem("S"), s_sil)
                            v.tensor_scalar(HS2[:], SA2[:], CBall[:, 8 * i + e:8 * i + e + 1],
                                            None, OP.mult).then_inc(sem("V"), 1)
                        c.v(f_hs)
                        v_hs = bump("V")
                        prevv[("hs", e * 64 + isl * 16 + i)] = v_hs

                        def f_ht(t, v_hs=v_hs):
                            t.wait_ge(sem("V"), v_hs)
                            t.wait_ge(sem("V"), val("V"))
                            for j in range(4):
                                mm = t.transpose(PT[:, 128 * j:128 * (j + 1)],
                                                 HS2[:, 128 * j:128 * (j + 1)], IDN[:])
                            mm.then_inc(sem("T"), 1)
                        c.t(f_ht)
                        t_ht = bump("T")

                        def f_hf(v, t_ht=t_ht):
                            v.wait_ge(sem("T"), t_ht)
                            v.tensor_copy(HF2[:], PT[:, 0:512]).then_inc(sem("V"), 1)
                        c.v(f_hf)
                        v_hf = bump("V")

                        def f_y(t, e=e, isl=isl, i=i, v_hf=v_hf):
                            t.wait_ge(sem("V"), v_hf)
                            t.wait_ge(sem("V"), val("V"))
                            for n in range(2):
                                for ic in range(4):
                                    mm = t.matmul(NUM1[:, 512 * n:512 * (n + 1)], HF2[:, 128 * ic:128 * (ic + 1)],
                                                  W2S[:, 1024 * ic + 512 * n:1024 * ic + 512 * (n + 1)],
                                                  start=(ic == 0), stop=(ic == 3))
                            mm.then_inc(sem("T"), 1)
                        c.t(f_y)
                        t_y = bump("T")
                        prevv[e * 4 + isl] = t_y

                        def f_yev(v, i=i, t_y=t_y):
                            v.wait_ge(sem("T"), t_y)
                            v.tensor_tensor(X2[:, D * i:D * (i + 1)], X2[:, D * i:D * (i + 1)],
                                            NUM1[:, 0:1024], OP.add).then_inc(sem("V"), 1)
                        c.v(f_yev)
                        bump("V")
            v_ffn = val("V")

            def ph9_sy(sy):
                sy.wait_ge(sem("V"), v_ffn)
                sy.dma_start(out=o_xn2, in_=XN[:]).then_inc(sem("st"), 16)
                sy.dma_start(out=o_cb, in_=CBall[:]).then_inc(sem("st"), 16)
                sy.dma_start(out=o_x, in_=X2[:]).then_inc(sem("st"), 16)
                sy.wait_ge(sem("st"), st_attn + 48)
            c.sy(ph9_sy)
            bump("st", 16)
    return nc


def build_L3():
    nc = bass.Bass("TRN2", target_bir_lowering=False)
    i_x = nc.dram_tensor("xloc", [128, NT * D], F32, kind="ExternalInput").ap()
    i_xo = nc.dram_tensor("xoth", [128, NT * D], F32, kind="ExternalInput").ap()
    i_wqm = nc.dram_tensor("wqm", [NCH, 128, NCH * 128], F32, kind="ExternalInput").ap()
    i_wkm = nc.dram_tensor("wkm", [NCH, 128, NCH * 128], F32, kind="ExternalInput").ap()
    i_wvg = nc.dram_tensor("wvg", [4, 128, NCH * 256], F32, kind="ExternalInput").ap()
    i_wog = nc.dram_tensor("wog", [4, 64, 4 * D], F32, kind="ExternalInput").ap()
    i_qcos = nc.dram_tensor("qcos", [128, TOK], F32, kind="ExternalInput").ap()
    i_qsin = nc.dram_tensor("qsin", [128, TOK], F32, kind="ExternalInput").ap()
    i_kcos = nc.dram_tensor("kcos", [128, 2 * TOK], F32, kind="ExternalInput").ap()
    i_ksin = nc.dram_tensor("ksin", [128, 2 * TOK], F32, kind="ExternalInput").ap()
    i_lng = nc.dram_tensor("lngr", [1, D], F32, kind="ExternalInput").ap()
    i_lnb = nc.dram_tensor("lnbr", [1, D], F32, kind="ExternalInput").ap()
    i_idn = nc.dram_tensor("idn", [128, 128], BF16, kind="ExternalInput").ap()
    i_onr = nc.dram_tensor("onesr", [1, 128], BF16, kind="ExternalInput").ap()
    i_amsk = nc.dram_tensor("amsk", [128, 1], F32, kind="ExternalInput").ap()
    i_mca = nc.dram_tensor("mcadd", [128, 128], BF16, kind="ExternalInput").ap()
    o_x = nc.dram_tensor("xout", [128, NT * D], F32, kind="ExternalOutput").ap()
    d_ao = nc.dram_tensor("aoscr", [128, NT * D], BF16, kind="ExternalOutput").ap()
    o_scc = nc.dram_tensor("dbg_scc", [128, 32 * 128], BF16, kind="ExternalOutput").ap()
    o_den = nc.dram_tensor("dbg_den", [128, 4], F32, kind="ExternalOutput").ap()
    o_qf = nc.dram_tensor("dbg_qf", [128, 2 * TOK], BF16, kind="ExternalOutput").ap()
    o_kf = nc.dram_tensor("dbg_kf", [128, 4 * TOK], BF16, kind="ExternalOutput").ap()
    o_qpre = nc.dram_tensor("dbg_qpre", [128, 2 * TOK], BF16, kind="ExternalOutput").ap()
    o_xn3 = nc.dram_tensor("dbg_xn3", [128, NCH * TOK], BF16, kind="ExternalOutput").ap()
    o_xn0 = nc.dram_tensor("dbg_xn0", [128, NCH * TOK], BF16, kind="ExternalOutput").ap()

    es = contextlib.ExitStack()
    with es:
        c = Ctx(nc, es)
        sem, bump, val = c.sem, c.bump, c.val

        def vedge(v, instr):
            n = bump("V"); instr.then_inc(sem("V"), 1); v.wait_ge(sem("V"), n)

        def sedge(s_, instr):
            n = bump("S"); instr.then_inc(sem("S"), 1); s_.wait_ge(sem("S"), n)

        MV = c.sb("MV", [128, 36], F32)
        RS = c.sb("RS", [128, 18], F32)
        SQ = c.sb("SQ", [128, 36], F32)
        XNT = c.sb("XNT", [128, 2 * D], BF16)
        IDN = c.sb("IDN", [128, 128], BF16)
        ONR = c.sb("ONR", [1, 128], BF16)
        AMSK = c.sb("AMSK", [128, 1], F32)
        MCA = c.sb("MCA", [128, 128], BF16)
        EPS = c.sb("EPS", [128, 1], F32)
        Z0 = c.sb("Z0", [128, 1], F32)
        NMX = c.sb("NMX", [128, 1], F32)
        DEN = c.sb("DEN", [128, 1], F32)
        RDEN = c.sb("RDEN", [128, 1], F32)
        ATT = c.sb("ATT", [128, 512], BF16)
        CTXS = c.sb("CTXS", [64, 128], BF16)
        LNGR = c.sb("LNGR", [1, D], F32)
        LNBR = c.sb("LNBR", [1, D], F32)
        LNGB = c.sb("LNGB", [1, D], BF16)
        LNBB = c.sb("LNBB", [1, D], BF16)
        PP0 = c.ps("PP0", [128, 512], F32)
        PP1 = c.ps("PP1", [128, 512], F32)
        SPS = c.ps("SPS", [128, 512], F32)
        OUTP = c.ps("OUTP", [128, 1024], F32)
        CTXP = c.ps("CTXP", [128, 128], F32)
        PTT = c.ps("PTT", [128, 1024], BF16)
        PT = PTT[:]

        def ph0_sy(sy):
            for dst, src in [(AMSK, i_amsk), (LNGR, i_lng), (LNBR, i_lnb)]:
                sy.dma_start(out=dst[:], in_=src).then_inc(sem("ldx"), 16)
        c.sy(ph0_sy)
        LDX = bump("ldx", 3 * 16)

        def ph0_g(g):
            for dst, src in [(IDN, i_idn), (ONR, i_onr), (MCA, i_mca)]:
                g.dma_start(out=dst[:], in_=src).then_inc(sem("ldw"), 16)
            g.memset(EPS[:], 1e-5)
            g.memset(Z0[:], 0.0).then_inc(sem("ldw"), 1)
        c.g(ph0_g)
        LDW = bump("ldw", 3 * 16 + 1)

        def emit_ln(XSRC, ldsem, ldval, DEST):
            def f_ssq(s_, tw=val("T")):
                s_.wait_ge(sem(ldsem), ldval)
                s_.wait_ge(sem("ldw"), LDW)
                s_.wait_ge(sem("T"), tw)
                for i in range(16):
                    s_.activation(out=XNT[:, 0:D], in_=XSRC[:, D * i:D * (i + 1)],
                                  func=AF.Square, accum_out=SQ[:, 18 + i:19 + i])
                s_.copy(RS[:, 17:18], SQ[:, 33:34]).then_inc(sem("S"), 1)
            c.s(f_ssq)
            s_sq = bump("S")

            def f_vst(v):
                v.wait_ge(sem(ldsem), ldval)
                v.wait_ge(sem("ldw"), LDW)
                for i in range(16):
                    v.reduce_sum(SQ[:, i:i + 1], XSRC[:, D * i:D * (i + 1)], axis=AX.X)
                i1 = v.tensor_scalar(MV[:, 0:16], SQ[:, 0:16], 1.0 / D, None, OP.mult)
                vedge(v, i1)
                v.wait_ge(sem("S"), s_sq)
                i2 = v.tensor_scalar(MV[:, 18:34], SQ[:, 18:34], 1.0 / D, None, OP.mult)
                vedge(v, i2)
                i3 = v.tensor_tensor(RS[:, 0:16], MV[:, 0:16], MV[:, 0:16], OP.mult)
                vedge(v, i3)
                v.tensor_tensor(MV[:, 18:34], MV[:, 18:34], RS[:, 0:16], OP.subtract).then_inc(sem("V"), 1)
            c.v(f_vst)
            v_st = bump("V")

            def f_rs(s_):
                s_.wait_ge(sem("V"), v_st)
                i1 = s_.activation(out=RS[:, 0:16], in_=MV[:, 18:34], func=AF.Ln, bias=EPS[:])
                sedge(s_, i1)
                s_.activation(out=RS[:, 0:16], in_=RS[:, 0:16], func=AF.Exp, scale=-0.5).then_inc(sem("S"), 1)
            c.s(f_rs)
            s_rs = bump("S")

            v_ap = [0] * 16
            t_tr = [0] * 16
            v_ev = [0] * 16
            DV = DEST[:].rearrange("p (ch w) -> p ch w", w=TOK)
            PTv = PT.rearrange("p (ch w) -> p ch w", w=128)
            for i in range(16):
                buf = i % 2

                def f_ap(v, i=i, buf=buf):
                    v.wait_ge(sem("S"), s_rs)
                    if i >= 2:
                        v.wait_ge(sem("T"), t_tr[i - 2])
                    v.tensor_scalar(XNT[:, D * buf:D * (buf + 1)], XSRC[:, D * i:D * (i + 1)],
                                    MV[:, i:i + 1], RS[:, i:i + 1],
                                    OP.subtract, OP.mult).then_inc(sem("V"), 1)
                c.v(f_ap)
                v_ap[i] = bump("V")

                def f_tr(t, i=i, buf=buf):
                    t.wait_ge(sem("V"), v_ap[i])
                    if i >= 1:
                        t.wait_ge(sem("V"), v_ev[i - 1])
                    for ch in range(NCH):
                        mm = t.transpose(PT[:, 128 * ch:128 * (ch + 1)],
                                         XNT[:, D * buf + 128 * ch:D * buf + 128 * (ch + 1)], IDN[:])
                    mm.then_inc(sem("T"), 1)
                c.t(f_tr)
                t_tr[i] = bump("T")

                def f_ev(v, i=i):
                    v.wait_ge(sem("T"), t_tr[i])
                    v.tensor_copy(DV[:, :, 128 * i:128 * (i + 1)], PTv[:, 0:NCH, :]).then_inc(sem("V"), 1)
                c.v(f_ev)
                v_ev[i] = bump("V")
            return val("V")

        with nc.sbuf_tensor("XN", [128, NCH * TOK], BF16) as XN, \
             nc.sbuf_tensor("XNO", [128, NCH * TOK], BF16) as XNO:
            with nc.sbuf_tensor("X", [128, NT * D], F32) as X:
                def ldx_sy(sy):
                    sy.dma_start(out=X[:], in_=i_x).then_inc(sem("ldx"), 16)
                c.sy(ldx_sy)
                LDX2 = bump("ldx", 16)
                emit_ln(X, "ldx", LDX2, XN)

                def f_dxn0(sy, vv=val("V")):
                    sy.wait_ge(sem("V"), vv)
                    sy.dma_start(out=o_xn0, in_=XN[:]).then_inc(sem("st"), 16)
                c.sy(f_dxn0)
                st0 = bump("st", 16)

                def f_dxn0h(v):
                    v.wait_ge(sem("st"), st0)
                c.v(f_dxn0h)
                import os
                if os.environ.get("L3CUT") == "1":
                    return nc
            with nc.sbuf_tensor("XO", [128, NT * D], F32) as XO:
                def ldxo_sy(sy, vv=val("V")):
                    sy.wait_ge(sem("V"), vv)
                    sy.dma_start(out=XO[:], in_=i_xo).then_inc(sem("ldx"), 16)
                c.sy(ldxo_sy)
                LDX3 = bump("ldx", 16)
                emit_ln(XO, "ldx", LDX3, XNO)
            import os as _os
            if _os.environ.get("L3CUT") == "2":
                def fcut(sy, vv=val("V")):
                    sy.wait_ge(sem("V"), vv)
                    sy.dma_start(out=o_xn0, in_=XN[:]).then_inc(sem("st"), 16)
                    sy.wait_ge(sem("st"), val("st") + 16)
                c.sy(fcut)
                bump("st", 16)
                return nc

            with nc.sbuf_tensor("QCOS", [128, TOK], BF16) as QCOS, \
                 nc.sbuf_tensor("QSIN", [128, TOK], BF16) as QSIN, \
                 nc.sbuf_tensor("KCOS", [128, 2 * TOK], BF16) as KCOS, \
                 nc.sbuf_tensor("KSIN", [128, 2 * TOK], BF16) as KSIN, \
                 nc.sbuf_tensor("SCC", [128, 32 * 128], BF16) as SCC, \
                 nc.sbuf_tensor("WMB", [128, 2 * NCH * 128], BF16) as WMB, \
                 nc.sbuf_tensor("WVB", [128, NCH * 256], BF16) as WVB, \
                 nc.sbuf_tensor("WOTG", [64, 4 * D], BF16) as WOTG, \
                 nc.sbuf_tensor("RSC", [128, 2 * TOK], BF16) as RSC, \
                 nc.sbuf_tensor("RC2", [128, 2 * TOK], BF16) as RC2, \
                 nc.sbuf_tensor("AOB", [128, D], BF16) as AOB, \
                 nc.sbuf_tensor("AOL", [128, D], BF16) as AOL:
                def f_tbl(g, vv=val("V")):
                    g.wait_ge(sem("V"), vv)
                    for dst, src in [(QCOS, i_qcos), (QSIN, i_qsin), (KCOS, i_kcos), (KSIN, i_ksin)]:
                        g.dma_start(out=dst[:], in_=src).then_inc(sem("ldw"), 16)
                c.g(f_tbl)
                LDT = bump("ldw", 4 * 16)
                import os as _osA
                if _osA.environ.get("L3CUT") == "3a":
                    def fcutA(sy, vv=val("V")):
                        sy.wait_ge(sem("V"), vv)
                        sy.wait_ge(sem("ldw"), LDT)
                        sy.dma_start(out=o_xn0, in_=XN[:]).then_inc(sem("st"), 16)
                        sy.wait_ge(sem("st"), val("st") + 16)
                    c.sy(fcutA)
                    bump("st", 16)
                    return nc

                for hg in range(4):
                        hg_es = contextlib.ExitStack()
                        QF = hg_es.enter_context(nc.sbuf_tensor(f"QF{hg}", [128, 2 * TOK], BF16))
                        KF = hg_es.enter_context(nc.sbuf_tensor(f"KF{hg}", [128, 2 * 2 * TOK], BF16))
                        VT = hg_es.enter_context(nc.sbuf_tensor(f"VT{hg}", [128, 32 * 256], BF16))
                        for which, i_wm in (("q", i_wqm), ("k", i_wkm)):
                            for mci in range(2):
                                mch = 2 * hg + mci

                                def f_wd(g, mch=mch, mci=mci, i_wm=i_wm, tw=val("T")):
                                    g.wait_ge(sem("T"), tw)
                                    g.dma_start(out=WMB[:, 1024 * mci:1024 * (mci + 1)],
                                                in_=i_wm[mch]).then_inc(sem("ldm"), 16)
                                c.g(f_wd)
                                ld_m = bump("ldm", 16)
                                srcs = [(XN, TOK, TOK)] if which == "q" else [(XNO, 0, TOK), (XN, TOK, TOK)]
                                if which == "q":
                                    srcs = [(XN, 0, TOK)]
                                for SRC, off, wid in srcs:
                                    for s4 in range(wid // 512):
                                        PB = PP0 if s4 % 2 == 0 else PP1

                                        def f_mm(t, mci=mci, s4=s4, PB=PB, ld_m=ld_m, SRC=SRC, sw=val("S")):
                                            t.wait_ge(sem("S"), sw)
                                            if s4 == 0:
                                                t.wait_ge(sem("ldm"), ld_m)
                                            for cc in range(NCH):
                                                mm = t.matmul(PB[:], WMB[:, 1024 * mci + 128 * cc:1024 * mci + 128 * (cc + 1)],
                                                              SRC[:, TOK * cc + 512 * s4:TOK * cc + 512 * (s4 + 1)],
                                                              start=(cc == 0), stop=(cc == 7))
                                            mm.then_inc(sem("T"), 1)
                                        c.t(f_mm)
                                        t_mm = bump("T")

                                        def f_ev(s_, mci=mci, s4=s4, PB=PB, t_mm=t_mm, which=which, off=off):
                                            s_.wait_ge(sem("T"), t_mm)
                                            if which == "q":
                                                dst = QF[:, TOK * mci + 512 * s4:TOK * mci + 512 * (s4 + 1)]
                                            else:
                                                dst = KF[:, 2 * TOK * mci + off + 512 * s4:2 * TOK * mci + off + 512 * (s4 + 1)]
                                            s_.activation(out=dst, in_=PB[:], func=AF.Identity,
                                                          bias=Z0[:]).then_inc(sem("S"), 1)
                                        c.s(f_ev)
                                        bump("S")
                        s_prj = val("S")
                        import os as _os2
                        if _os2.environ.get("L3CUT") == "3" and hg == 0:
                            def fcut3(sy, sp=s_prj):
                                sy.wait_ge(sem("S"), sp)
                                sy.dma_start(out=o_xn0, in_=XN[:]).then_inc(sem("st"), 16)
                                sy.wait_ge(sem("st"), val("st") + 16)
                            c.sy(fcut3)
                            bump("st", 16)
                            hg_es.close()
                            return nc
                        if hg == 0:
                            def f_dbgq(sy, s_prj=s_prj):
                                sy.wait_ge(sem("S"), s_prj)
                                sy.dma_start(out=o_qpre, in_=QF[:]).then_inc(sem("st"), 16)
                                sy.dma_start(out=o_xn3, in_=XN[:]).then_inc(sem("st"), 16)
                            c.sy(f_dbgq)
                            stq = bump("st", 32)

                            def f_dbgq_hold(v):
                                v.wait_ge(sem("st"), stq)
                            c.v(f_dbgq_hold)

                        def f_rope(v):
                            v.wait_ge(sem("S"), s_prj)
                            v.wait_ge(sem("ldw"), LDT)
                            last = None
                            for mci in range(2):
                                for pb in (0, 64):
                                    qv = QF[pb:pb + 64, TOK * mci:TOK * (mci + 1)]
                                    i0 = v.tensor_copy(RSC[pb:pb + 32, 0:TOK],
                                                       QF[pb + 32:pb + 64, TOK * mci:TOK * (mci + 1)])
                                    i1 = v.tensor_copy(RSC[pb + 32:pb + 64, 0:TOK],
                                                       QF[pb:pb + 32, TOK * mci:TOK * (mci + 1)])
                                    vedge(v, i0); vedge(v, i1)
                                    i2 = v.tensor_tensor(RSC[pb:pb + 64, 0:TOK], RSC[pb:pb + 64, 0:TOK],
                                                         QSIN[pb:pb + 64, :], OP.mult)
                                    i3 = v.tensor_tensor(RC2[pb:pb + 64, 0:TOK], qv, QCOS[pb:pb + 64, :], OP.mult)
                                    vedge(v, i2); vedge(v, i3)
                                    i4 = v.tensor_tensor(qv, RC2[pb:pb + 64, 0:TOK], RSC[pb:pb + 64, 0:TOK], OP.add)
                                    vedge(v, i4)
                                    kv = KF[pb:pb + 64, 2 * TOK * mci:2 * TOK * (mci + 1)]
                                    i5 = v.tensor_copy(RSC[pb:pb + 32, :],
                                                       KF[pb + 32:pb + 64, 2 * TOK * mci:2 * TOK * (mci + 1)])
                                    i6 = v.tensor_copy(RSC[pb + 32:pb + 64, :],
                                                       KF[pb:pb + 32, 2 * TOK * mci:2 * TOK * (mci + 1)])
                                    vedge(v, i5); vedge(v, i6)
                                    i7 = v.tensor_tensor(RSC[pb:pb + 64, :], RSC[pb:pb + 64, :],
                                                         KSIN[pb:pb + 64, :], OP.mult)
                                    i8 = v.tensor_tensor(RC2[pb:pb + 64, :], kv, KCOS[pb:pb + 64, :], OP.mult)
                                    vedge(v, i7); vedge(v, i8)
                                    i9 = v.tensor_tensor(kv, RC2[pb:pb + 64, :], RSC[pb:pb + 64, :], OP.add)
                                    vedge(v, i9)
                            v.tensor_copy(MV[:, 0:1], MV[:, 0:1]).then_inc(sem("V"), 1)
                        c.v(f_rope)
                        v_rope = bump("V")
                        import os as _os3
                        if _os3.environ.get("L3CUT") == "4" and hg == 0:
                            def fcut4(sy, vr=v_rope):
                                sy.wait_ge(sem("V"), vr)
                                sy.dma_start(out=o_xn0, in_=XN[:]).then_inc(sem("st"), 16)
                                sy.dma_start(out=o_qf, in_=QF[:]).then_inc(sem("st"), 16)
                                sy.dma_start(out=o_kf, in_=KF[:]).then_inc(sem("st"), 16)
                                sy.wait_ge(sem("st"), val("st") + 48)
                            c.sy(fcut4)
                            bump("st", 48)
                            hg_es.close()
                            return nc

                        def f_wvd(g, hg=hg, tw=val("T")):
                            g.wait_ge(sem("T"), tw)
                            g.dma_start(out=WVB[:].rearrange("p (cc m) -> p cc m", m=256),
                                        in_=i_wvg[hg]).then_inc(sem("ldm"), 16)
                            g.dma_start(out=WOTG[:], in_=i_wog[hg]).then_inc(sem("ldm"), 16)
                        c.g(f_wvd)
                        ld_v = bump("ldm", 32)
                        for half, SRC, toff in (("pfx", XNO, 0), ("own", XN, 16)):
                            for i in range(16):
                                PB = PP0 if i % 2 == 0 else PP1

                                def f_vmm(t, i=i, PB=PB, SRC=SRC, ld_v=ld_v, sw=val("S")):
                                    t.wait_ge(sem("S"), sw)
                                    if i == 0:
                                        t.wait_ge(sem("ldm"), ld_v)
                                    for cc in range(NCH):
                                        mm = t.matmul(PB[:, 0:256], SRC[:, TOK * cc + 128 * i:TOK * cc + 128 * (i + 1)],
                                                      WVB[:, 256 * cc:256 * (cc + 1)],
                                                      start=(cc == 0), stop=(cc == 7))
                                    mm.then_inc(sem("T"), 1)
                                c.t(f_vmm)
                                t_v = bump("T")

                                def f_vev(s_, i=i, toff=toff, PB=PB, t_v=t_v):
                                    s_.wait_ge(sem("T"), t_v)
                                    s_.activation(out=VT[:, 256 * (toff + i):256 * (toff + i + 1)], in_=PB[:, 0:256],
                                                  func=AF.Identity, bias=Z0[:]).then_inc(sem("S"), 1)
                                c.s(f_vev)
                                bump("S")
                        s_kv = val("S")

                        for qt in range(16):
                            nkt = 16 + qt + 1
                            if hg > 0:
                                def f_aol(sy, qt=qt, vv=val("V")):
                                    sy.wait_ge(sem("V"), vv)
                                    sy.dma_start(out=AOL[:],
                                                 in_=d_ao[:, D * qt:D * (qt + 1)]).then_inc(sem("lda"), 16)
                                c.sy(f_aol)
                                ld_ao = bump("lda", 16)
                            else:
                                ld_ao = None
                            for hi in range(4):
                                mci = hi // 2
                                pb = 64 * (hi % 2)
                                prev_scev = [None]
                                for kb in range((nkt + 3) // 4):
                                    k0 = 4 * kb
                                    kw = min(4, nkt - k0) * 128

                                    def f_sc(t, qt=qt, mci=mci, pb=pb, k0=k0, kw=kw, pv=prev_scev[0], s_kv=s_kv, v_rope=v_rope):
                                        if pv is not None:
                                            t.wait_ge(sem("V"), pv)
                                        else:
                                            t.wait_ge(sem("V"), v_rope)
                                            t.wait_ge(sem("S"), s_kv)
                                        mm = t.matmul(SPS[:, 0:kw],
                                                      QF[pb:pb + 64, TOK * mci + 128 * qt:TOK * mci + 128 * (qt + 1)],
                                                      KF[pb:pb + 64, 2 * TOK * mci + 128 * k0:2 * TOK * mci + 128 * k0 + kw],
                                                      start=True, stop=True)
                                        mm.then_inc(sem("T"), 1)
                                    c.t(f_sc)
                                    t_sc = bump("T")

                                    def f_scev(v, k0=k0, kw=kw, t_sc=t_sc, nkt=nkt):
                                        v.wait_ge(sem("T"), t_sc)
                                        for j in range(kw // 128):
                                            kt = k0 + j
                                            src = SPS[:, 128 * j:128 * (j + 1)]
                                            dst = SCC[:, 128 * kt:128 * (kt + 1)]
                                            if kt < 16:
                                                i1 = v.tensor_scalar(dst, src, AMSK[:], None, OP.add)
                                            elif kt == nkt - 1:
                                                i1 = v.tensor_tensor(dst, src, MCA[:], OP.add)
                                            else:
                                                i1 = v.tensor_copy(dst, src)
                                        i1.then_inc(sem("V"), 1)
                                    c.v(f_scev)
                                    prev_scev[0] = bump("V")

                                def f_sm(v, nkt=nkt, pv=prev_scev[0]):
                                    v.wait_ge(sem("V"), pv)
                                    i1 = v.reduce_max(NMX[:], SCC[:, 0:128 * nkt], axis=AX.X, negate=True)
                                    vedge(v, i1)
                                    v.tensor_scalar(NMX[:], NMX[:], 0.125, None, OP.mult).then_inc(sem("V"), 1)
                                c.v(f_sm)
                                v_mx = bump("V")

                                def f_exp(s_, nkt=nkt, v_mx=v_mx):
                                    s_.wait_ge(sem("V"), v_mx)
                                    s_.activation(out=SCC[:, 0:128 * nkt], in_=SCC[:, 0:128 * nkt], func=AF.Exp,
                                                  bias=NMX[:], scale=0.125, accum_out=DEN[:])
                                    s_.copy(DEN[:], DEN[:]).then_inc(sem("S"), 1)
                                c.s(f_exp)
                                s_exp = bump("S")

                                def f_nrm(v, nkt=nkt, s_exp=s_exp):
                                    v.wait_ge(sem("S"), s_exp)
                                    i1 = v.reciprocal(RDEN[:], DEN[:])
                                    vedge(v, i1)
                                    v.tensor_scalar(SCC[:, 0:128 * nkt], SCC[:, 0:128 * nkt], RDEN[:],
                                                    None, OP.mult).then_inc(sem("V"), 1)
                                c.v(f_nrm)
                                v_nrm = bump("V")

                                prev_ctx = [None]
                                for kb in range((nkt + 3) // 4):
                                    k0 = 4 * kb
                                    nt_ = min(4, nkt - k0)

                                    def f_att(t, k0=k0, nt_=nt_, v_nrm=v_nrm, pv=prev_ctx[0]):
                                        t.wait_ge(sem("V"), pv if pv is not None else v_nrm)
                                        for j in range(nt_):
                                            mm = t.transpose(PT[:, 128 * j:128 * (j + 1)],
                                                             SCC[:, 128 * (k0 + j):128 * (k0 + j + 1)], IDN[:])
                                        mm.then_inc(sem("T"), 1)
                                    c.t(f_att)
                                    t_at = bump("T")

                                    def f_ate(v, nt_=nt_, t_at=t_at):
                                        v.wait_ge(sem("T"), t_at)
                                        v.tensor_copy(ATT[:, 0:128 * nt_], PT[:, 0:128 * nt_]).then_inc(sem("V"), 1)
                                    c.v(f_ate)
                                    v_ate = bump("V")
                                    prev_ctx[0] = v_ate

                                    def f_ctx(t, k0=k0, nt_=nt_, hi=hi, v_ate=v_ate, nkt=nkt):
                                        t.wait_ge(sem("V"), v_ate)
                                        for j in range(nt_):
                                            kt = k0 + j
                                            mm = t.matmul(CTXP[0:64, 0:128],
                                                          VT[:, 256 * kt + 64 * hi:256 * kt + 64 * (hi + 1)],
                                                          ATT[:, 128 * j:128 * (j + 1)],
                                                          start=(kt == 0), stop=(kt == nkt - 1))
                                        mm.then_inc(sem("T"), 1)
                                    c.t(f_ctx)
                                    t_ctx = bump("T")

                                def f_cte(v, t_ctx=val("T")):
                                    v.wait_ge(sem("T"), t_ctx)
                                    v.tensor_copy(CTXS[:], CTXP[0:64, 0:128]).then_inc(sem("V"), 1)
                                c.v(f_cte)
                                v_cte = bump("V")

                                def f_wo(t, hi=hi, v_cte=v_cte):
                                    t.wait_ge(sem("V"), v_cte)
                                    for n in range(2):
                                        mm = t.matmul(OUTP[:, 512 * n:512 * (n + 1)], CTXS[:],
                                                      WOTG[:, 1024 * hi + 512 * n:1024 * hi + 512 * (n + 1)],
                                                      start=(hi == 0), stop=(hi == 3 and n == 1),
                                                      skip_group_check=True)
                                    mm.then_inc(sem("T"), 1)
                                c.t(f_wo)
                                bump("T")
                            t_wo = val("T")

                            def f_oev(v, qt=qt, hg=hg, t_wo=t_wo, ld_ao=ld_ao):
                                v.wait_ge(sem("T"), t_wo)
                                if hg == 0:
                                    mmv = v.tensor_copy(AOB[:], OUTP[:, 0:1024])
                                else:
                                    v.wait_ge(sem("lda"), ld_ao)
                                    mmv = v.tensor_tensor(AOB[:], AOL[:], OUTP[:, 0:1024], OP.add)
                                mmv.then_inc(sem("V"), 1)
                            c.v(f_oev)
                            v_ao = bump("V")

                            def f_ast(sy, qt=qt, v_ao=v_ao):
                                sy.wait_ge(sem("V"), v_ao)
                                sy.dma_start(out=d_ao[:, D * qt:D * (qt + 1)],
                                             in_=AOB[:]).then_inc(sem("st"), 16)
                            c.sy(f_ast)
                            bump("st", 16)
                        if hg == 0:
                            def f_dbg(sy, vv=val("V")):
                                sy.wait_ge(sem("V"), vv)
                                sy.dma_start(out=o_scc, in_=SCC[:]).then_inc(sem("st"), 16)
                                sy.dma_start(out=o_den[:, 0:1], in_=DEN[:], allow_slow_non_contiguous=True).then_inc(sem("st"), 16)
                                sy.dma_start(out=o_den[:, 1:2], in_=NMX[:], allow_slow_non_contiguous=True).then_inc(sem("st"), 16)
                                sy.dma_start(out=o_qf, in_=QF[:]).then_inc(sem("st"), 16)
                                sy.dma_start(out=o_kf, in_=KF[:]).then_inc(sem("st"), 16)
                            c.sy(f_dbg)
                            stdbg = bump("st", 80)

                            def f_dbg_hold(t):
                                t.wait_ge(sem("st"), stdbg)
                            c.t(f_dbg_hold)
                        hg_es.close()
        v_attn = val("V")
        st_attn = val("st")

        with nc.sbuf_tensor("X3", [128, NT * D], F32) as X3, \
             nc.sbuf_tensor("GREP", [128, D], F32) as GREP, \
             nc.sbuf_tensor("BREP", [128, D], F32) as BREP, \
             nc.sbuf_tensor("FFT", [128, D], F32) as FFT, \
             nc.sbuf_tensor("FFT2", [128, D], F32) as FFT2, \
             nc.sbuf_tensor("AOL2", [128, 2 * D], BF16) as AOL2:
            def ldx3_sy(sy):
                sy.wait_ge(sem("st"), st_attn)
                sy.dma_start(out=X3[:], in_=i_x).then_inc(sem("ldf"), 16)
            c.sy(ldx3_sy)
            LDF = bump("ldf", 16)

            def f_rowcast(v):
                v.wait_ge(sem("ldx"), LDX)
                i1 = v.tensor_copy(LNGB[:], LNGR[:])
                i2 = v.tensor_copy(LNBB[:], LNBR[:])
                vedge(v, i1)
                i2.then_inc(sem("V"), 1)
            c.v(f_rowcast)
            v_rows = bump("V")

            def f_rep2(t):
                t.wait_ge(sem("V"), v_rows)
                t.wait_ge(sem("V"), v_attn)
                t.matmul(OUTP[:, 0:512], ONR[:], LNGB[:, 0:512], start=True, stop=True)
                t.matmul(OUTP[:, 512:1024], ONR[:], LNGB[:, 512:1024], start=True, stop=True)
                t.matmul(PTT[:, 0:512].bitcast(F32) if False else CTXP[:, 0:128], ONR[:], LNBB[:, 0:128], start=True, stop=True)
                mm = t.matmul(SPS[:, 0:512], ONR[:], LNBB[:, 0:512], start=True, stop=True)
                mm.then_inc(sem("T"), 1)
            def f_rep3(t):
                t.wait_ge(sem("V"), v_rows)
                t.wait_ge(sem("V"), v_attn)
                t.matmul(OUTP[:, 0:512], ONR[:], LNGB[:, 0:512], start=True, stop=True)
                t.matmul(OUTP[:, 512:1024], ONR[:], LNGB[:, 512:1024], start=True, stop=True)
                t.matmul(SPS[:, 0:512], ONR[:], LNBB[:, 0:512], start=True, stop=True)
                mm = t.matmul(PP0[:, 0:512], ONR[:], LNBB[:, 512:1024], start=True, stop=True)
                mm.then_inc(sem("T"), 1)
            c.t(f_rep3)
            t_rep = bump("T")

            def f_grep(v):
                v.wait_ge(sem("T"), t_rep)
                i1 = v.tensor_copy(GREP[:], OUTP[:, 0:1024])
                i2 = v.tensor_copy(BREP[:, 0:512], SPS[:, 0:512])
                i3 = v.tensor_copy(BREP[:, 512:1024], PP0[:, 0:512])
                vedge(v, i1); vedge(v, i2)
                i3.then_inc(sem("V"), 1)
            c.v(f_grep)
            v_grep = bump("V")

            v_pa = [0]
            for i in range(16):
                def f_aol2(sy, i=i):
                    sy.wait_ge(sem("ldf"), LDF)
                    if i >= 2:
                        sy.wait_ge(sem("V"), v_pa[0] - 1)
                    sy.dma_start(out=AOL2[:, D * (i % 2):D * (i % 2 + 1)],
                                 in_=d_ao[:, D * i:D * (i + 1)]).then_inc(sem("ldf"), 16)
                c.sy(f_aol2)
                ld_n = bump("ldf", 16)

                def f_f3(v, i=i, ld_n=ld_n):
                    if i == 0:
                        v.wait_ge(sem("V"), v_grep)
                    v.wait_ge(sem("ldf"), ld_n)
                    x = X3[:, D * i:D * (i + 1)]
                    i0 = v.reduce_sum(SQ[:, 0:1], x, axis=AX.X)
                    vedge(v, i0)
                    i1 = v.tensor_tensor(FFT[:], x, x, OP.mult)
                    vedge(v, i1)
                    i2 = v.reduce_sum(SQ[:, 1:2], FFT[:], axis=AX.X)
                    vedge(v, i2)
                    i3 = v.tensor_scalar(MV[:, 0:2], SQ[:, 0:2], 1.0 / D, None, OP.mult)
                    vedge(v, i3)
                    i4 = v.tensor_tensor(RS[:, 0:1], MV[:, 0:1], MV[:, 0:1], OP.mult)
                    vedge(v, i4)
                    v.tensor_tensor(MV[:, 1:2], MV[:, 1:2], RS[:, 0:1], OP.subtract).then_inc(sem("V"), 1)
                c.v(f_f3)
                v_st1 = bump("V")

                def f_f3s(s_, v_st1=v_st1):
                    s_.wait_ge(sem("V"), v_st1)
                    i1 = s_.activation(out=RS[:, 0:1], in_=MV[:, 1:2], func=AF.Ln, bias=EPS[:])
                    sedge(s_, i1)
                    s_.activation(out=RS[:, 0:1], in_=RS[:, 0:1], func=AF.Exp, scale=-0.5).then_inc(sem("S"), 1)
                c.s(f_f3s)
                s_r1 = bump("S")

                def f_f3b(v, i=i, s_r1=s_r1):
                    v.wait_ge(sem("S"), s_r1)
                    x = X3[:, D * i:D * (i + 1)]
                    i0 = v.tensor_scalar(FFT[:], x, MV[:, 0:1], RS[:, 0:1], OP.subtract, OP.mult)
                    vedge(v, i0)
                    i1 = v.reduce_sum(SQ[:, 2:3], FFT[:], axis=AX.X)
                    vedge(v, i1)
                    i2 = v.tensor_tensor(FFT2[:], FFT[:], FFT[:], OP.mult)
                    vedge(v, i2)
                    i3 = v.reduce_sum(SQ[:, 3:4], FFT2[:], axis=AX.X)
                    vedge(v, i3)
                    i4 = v.tensor_scalar(MV[:, 2:4], SQ[:, 2:4], 1.0 / D, None, OP.mult)
                    vedge(v, i4)
                    i5 = v.tensor_tensor(RS[:, 1:2], MV[:, 2:3], MV[:, 2:3], OP.mult)
                    vedge(v, i5)
                    v.tensor_tensor(MV[:, 3:4], MV[:, 3:4], RS[:, 1:2], OP.subtract).then_inc(sem("V"), 1)
                c.v(f_f3b)
                v_st2 = bump("V")

                def f_f3s2(s_, v_st2=v_st2):
                    s_.wait_ge(sem("V"), v_st2)
                    i1 = s_.activation(out=RS[:, 1:2], in_=MV[:, 3:4], func=AF.Ln, bias=EPS[:])
                    sedge(s_, i1)
                    s_.activation(out=RS[:, 1:2], in_=RS[:, 1:2], func=AF.Exp, scale=-0.5).then_inc(sem("S"), 1)
                c.s(f_f3s2)
                s_r2 = bump("S")

                def f_f3c(v, i=i, s_r2=s_r2):
                    v.wait_ge(sem("S"), s_r2)
                    x = X3[:, D * i:D * (i + 1)]
                    i0 = v.tensor_scalar(FFT[:], FFT[:], MV[:, 2:3], RS[:, 1:2], OP.subtract, OP.mult)
                    vedge(v, i0)
                    i1 = v.tensor_tensor(FFT[:], FFT[:], GREP[:], OP.mult)
                    vedge(v, i1)
                    i2 = v.tensor_tensor(FFT[:], FFT[:], BREP[:], OP.add)
                    vedge(v, i2)
                    i3 = v.tensor_tensor(x, x, FFT[:], OP.add)
                    vedge(v, i3)
                    v.tensor_tensor(x, x, AOL2[:, D * (i % 2):D * (i % 2 + 1)], OP.add).then_inc(sem("V"), 1)
                c.v(f_f3c)
                v_pa[0] = bump("V")
            v_fin = val("V")

            def ph9_sy(sy):
                sy.wait_ge(sem("V"), v_fin)
                sy.dma_start(out=o_x, in_=X3[:]).then_inc(sem("st"), 16)
                sy.wait_ge(sem("st"), val("st") + 16)
            c.sy(ph9_sy)
            bump("st", 16)
    return nc


DBG0 = False


def build_L0a():
    nc = bass.Bass("TRN2", target_bir_lowering=False)
    i_x = nc.dram_tensor("xloc", [128, NT * D], F32, kind="ExternalInput").ap()
    i_xh = nc.dram_tensor("xhk", [128, D], F32, kind="ExternalInput").ap()
    i_wqm = nc.dram_tensor("wqm", [NCH, 128, NCH * 128], F32, kind="ExternalInput").ap()
    i_wkm = nc.dram_tensor("wkm", [NCH, 128, NCH * 128], F32, kind="ExternalInput").ap()
    i_wvg = nc.dram_tensor("wvg", [4, 128, NCH * 256], F32, kind="ExternalInput").ap()
    i_wot = nc.dram_tensor("wot", [64, 16 * D], F32, kind="ExternalInput").ap()
    i_bq = nc.dram_tensor("bqc", [128, 8], F32, kind="ExternalInput").ap()
    i_bk = nc.dram_tensor("bkc", [128, 8], F32, kind="ExternalInput").ap()
    i_bvr = nc.dram_tensor("bvr", [1, D], F32, kind="ExternalInput").ap()
    i_bor = nc.dram_tensor("bor", [1, D], F32, kind="ExternalInput").ap()
    i_qcos = nc.dram_tensor("qcos", [128, TOK], F32, kind="ExternalInput").ap()
    i_qsin = nc.dram_tensor("qsin", [128, TOK], F32, kind="ExternalInput").ap()
    i_kcos = nc.dram_tensor("kcos", [128, TOK + 128], F32, kind="ExternalInput").ap()
    i_ksin = nc.dram_tensor("ksin", [128, TOK + 128], F32, kind="ExternalInput").ap()
    i_mprev = nc.dram_tensor("mprev", [128, 128], BF16, kind="ExternalInput").ap()
    i_mown = nc.dram_tensor("mown", [128, 128], BF16, kind="ExternalInput").ap()
    i_am0 = nc.dram_tensor("amsk0", [128, 128], BF16, kind="ExternalInput").ap()
    i_idn = nc.dram_tensor("idn", [128, 128], BF16, kind="ExternalInput").ap()
    i_onr = nc.dram_tensor("onesr", [1, 128], BF16, kind="ExternalInput").ap()
    o_y = nc.dram_tensor("ypart", [128, NT * D], F32, kind="ExternalOutput").ap()
    o_z = nc.dram_tensor("z0", [128, NCH * TOK], BF16, kind="ExternalOutput").ap()

    KVW = TOK + 128

    es = contextlib.ExitStack()
    with es:
        c = Ctx(nc, es)
        sem, bump, val = c.sem, c.bump, c.val

        def vedge(v, instr):
            n = bump("V"); instr.then_inc(sem("V"), 1); v.wait_ge(sem("V"), n)

        def sedge(s_, instr):
            n = bump("S"); instr.then_inc(sem("S"), 1); s_.wait_ge(sem("S"), n)

        MV = c.sb("MV", [128, 40], F32)
        RS = c.sb("RS", [128, 19], F32)
        SQ = c.sb("SQ", [128, 40], F32)
        XNT = c.sb("XNT", [128, 2 * D], BF16)
        IDN = c.sb("IDN", [128, 128], BF16)
        ONR = c.sb("ONR", [1, 128], BF16)
        EPS = c.sb("EPS", [128, 1], F32)
        Z0C = c.sb("Z0C", [128, 1], F32)
        NMX = c.sb("NMX", [128, 1], F32)
        DEN = c.sb("DEN", [128, 1], F32)
        RDEN = c.sb("RDEN", [128, 1], F32)
        ATT = c.sb("ATT", [128, 256], BF16)
        CTXS = c.sb("CTXS", [64, 128], BF16)
        SCB = c.sb("SCB", [128, 256], BF16)
        MPREV = c.sb("MPREV", [128, 128], BF16)
        MOWN = c.sb("MOWN", [128, 128], BF16)
        AM0 = c.sb("AM0", [128, 128], BF16)
        XN = c.sb("XN", [128, NCH * KVW], BF16)
        BQ = c.sb("BQ", [128, 8], F32)
        BK = c.sb("BK", [128, 8], F32)
        BVR = c.sb("BVR", [1, D], BF16)
        BOR = c.sb("BOR", [1, D], BF16)
        YSTG = c.sb("YSTG", [128, 2 * D], F32)
        PP0 = c.ps("PP0", [128, 512], F32)
        PP1 = c.ps("PP1", [128, 512], F32)
        SPS = c.ps("SPS", [128, 512], F32)
        OUTP = c.ps("OUTP", [128, 1024], F32)
        CTXP = c.ps("CTXP", [128, 128], F32)
        PTT = c.ps("PTT", [128, 1024], BF16)
        PT = PTT[:]

        def ph0_g(g):
            for dst, src in [(IDN, i_idn), (ONR, i_onr), (MPREV, i_mprev), (MOWN, i_mown),
                             (AM0, i_am0), (BVR, i_bvr), (BOR, i_bor)]:
                g.dma_start(out=dst[:], in_=src).then_inc(sem("ldw"), 16)
            g.memset(EPS[:], 1e-5)
            g.memset(Z0C[:], 0.0).then_inc(sem("ldw"), 1)
        c.g(ph0_g)
        LDW = bump("ldw", 7 * 16 + 1)

        def ph0_sy(sy):
            sy.dma_start(out=BQ[:], in_=i_bq).then_inc(sem("ldx"), 16)
            sy.dma_start(out=BK[:], in_=i_bk).then_inc(sem("ldx"), 16)
        c.sy(ph0_sy)
        LDX = bump("ldx", 32)

        with nc.sbuf_tensor("X", [128, NT * D], F32) as X, \
             nc.sbuf_tensor("XH", [128, D], F32) as XH:
            def ldx_sy(sy):
                sy.dma_start(out=X[:], in_=i_x).then_inc(sem("ldx"), 16)
                sy.dma_start(out=XH[:], in_=i_xh).then_inc(sem("ldx"), 16)
            c.sy(ldx_sy)
            LDX2 = bump("ldx", 32)

            def get_xt(i):
                return XH[:] if i == 0 else X[:, D * (i - 1):D * i]

            def f_ssq(s_):
                s_.wait_ge(sem("ldx"), LDX2)
                s_.wait_ge(sem("ldw"), LDW)
                for i in range(17):
                    s_.activation(out=XNT[:, 0:D], in_=get_xt(i), func=AF.Square,
                                  accum_out=SQ[:, 20 + i:21 + i])
                s_.copy(RS[:, 18:19], SQ[:, 36:37]).then_inc(sem("S"), 1)
            c.s(f_ssq)
            s_sq = bump("S")

            def f_vst(v):
                v.wait_ge(sem("ldx"), LDX2)
                v.wait_ge(sem("ldw"), LDW)
                for i in range(17):
                    v.reduce_sum(SQ[:, i:i + 1], get_xt(i), axis=AX.X)
                i1 = v.tensor_scalar(MV[:, 0:17], SQ[:, 0:17], 1.0 / D, None, OP.mult)
                vedge(v, i1)
                v.wait_ge(sem("S"), s_sq)
                i2 = v.tensor_scalar(MV[:, 20:37], SQ[:, 20:37], 1.0 / D, None, OP.mult)
                vedge(v, i2)
                i3 = v.tensor_tensor(RS[:, 0:17], MV[:, 0:17], MV[:, 0:17], OP.mult)
                vedge(v, i3)
                v.tensor_tensor(MV[:, 20:37], MV[:, 20:37], RS[:, 0:17], OP.subtract).then_inc(sem("V"), 1)
            c.v(f_vst)
            v_st = bump("V")

            def f_rs(s_):
                s_.wait_ge(sem("V"), v_st)
                i1 = s_.activation(out=RS[:, 0:17], in_=MV[:, 20:37], func=AF.Ln, bias=EPS[:])
                sedge(s_, i1)
                s_.activation(out=RS[:, 0:17], in_=RS[:, 0:17], func=AF.Exp, scale=-0.5).then_inc(sem("S"), 1)
            c.s(f_rs)
            s_rs = bump("S")

            v_ap = [0] * 17
            t_tr = [0] * 17
            v_ev = [0] * 17
            XNv = XN[:].rearrange("p (ch w) -> p ch w", w=KVW)
            PTv = PT.rearrange("p (ch w) -> p ch w", w=128)
            for i in range(17):
                buf = i % 2

                def f_ap(v, i=i, buf=buf):
                    v.wait_ge(sem("S"), s_rs)
                    if i >= 2:
                        v.wait_ge(sem("T"), t_tr[i - 2])
                    v.tensor_scalar(XNT[:, D * buf:D * (buf + 1)], get_xt(i),
                                    MV[:, i:i + 1], RS[:, i:i + 1],
                                    OP.subtract, OP.mult).then_inc(sem("V"), 1)
                c.v(f_ap)
                v_ap[i] = bump("V")

                def f_tr(t, i=i, buf=buf):
                    t.wait_ge(sem("V"), v_ap[i])
                    if i >= 1:
                        t.wait_ge(sem("V"), v_ev[i - 1])
                    for ch in range(NCH):
                        mm = t.transpose(PT[:, 128 * ch:128 * (ch + 1)],
                                         XNT[:, D * buf + 128 * ch:D * buf + 128 * (ch + 1)], IDN[:])
                    mm.then_inc(sem("T"), 1)
                c.t(f_tr)
                t_tr[i] = bump("T")

                def f_ev(v, i=i):
                    v.wait_ge(sem("T"), t_tr[i])
                    v.tensor_copy(XNv[:, :, 128 * i:128 * (i + 1)], PTv[:, 0:NCH, :]).then_inc(sem("V"), 1)
                c.v(f_ev)
                v_ev[i] = bump("V")
            v_ln = val("V")

        QF = c.sb("QF", [128, NCH * TOK], BF16)
        KF = c.sb("KF", [128, NCH * KVW], BF16)
        VT = c.sb("VT", [128, 17 * D], BF16)
        prod_es = contextlib.ExitStack()
        QCOS = prod_es.enter_context(nc.sbuf_tensor("QCOS", [128, TOK], BF16))
        QSIN = prod_es.enter_context(nc.sbuf_tensor("QSIN", [128, TOK], BF16))
        KCOS = prod_es.enter_context(nc.sbuf_tensor("KCOS", [128, KVW], BF16))
        KSIN = prod_es.enter_context(nc.sbuf_tensor("KSIN", [128, KVW], BF16))
        RSC = prod_es.enter_context(nc.sbuf_tensor("RSC", [128, KVW], BF16))
        RC2 = prod_es.enter_context(nc.sbuf_tensor("RC2", [128, KVW], BF16))
        WMB = prod_es.enter_context(nc.sbuf_tensor("WMB", [128, 2 * NCH * 128], BF16))
        WVB = prod_es.enter_context(nc.sbuf_tensor("WVB", [128, NCH * 256], BF16))

        def f_tbl2(g, vv=val("V")):
            g.wait_ge(sem("V"), vv)
            for dst, src in [(QCOS, i_qcos), (QSIN, i_qsin), (KCOS, i_kcos), (KSIN, i_ksin)]:
                g.dma_start(out=dst[:], in_=src).then_inc(sem("ldw"), 16)
        c.g(f_tbl2)
        LDT = bump("ldw", 64)

        def f_z0(sy):
            sy.wait_ge(sem("V"), v_ln)
            sy.dma_start(out=o_z.rearrange("p (ch w) -> p ch w", w=TOK),
                         in_=XN[:].rearrange("p (ch w) -> p ch w", w=KVW)[:, :, 128:KVW]).then_inc(sem("st"), 16)
        c.sy(f_z0)
        bump("st", 16)

        prev_ev = {}
        for which, i_wm, DST, off, wid, BC in (("q", i_wqm, QF, 128, TOK, BQ), ("k", i_wkm, KF, 0, KVW, BK)):
            for mch in range(NCH):
                def f_wd(g, mch=mch, i_wm=i_wm, tw=val("T"), vv=val("V")):
                    g.wait_ge(sem("T"), tw)
                    if mch == 0:
                        g.wait_ge(sem("V"), vv)
                    g.dma_start(out=WMB[:, 1024 * (mch % 2):1024 * (mch % 2 + 1)],
                                in_=i_wm[mch]).then_inc(sem("ldm"), 16)
                c.g(f_wd)
                ld_m = bump("ldm", 16)
                ns = (wid + 511) // 512
                for s4 in range(ns):
                    a = 512 * s4
                    w_ = min(512, wid - a)
                    PB = PP0 if s4 % 2 == 0 else PP1

                    def f_mm(t, mch=mch, s4=s4, a=a, w_=w_, PB=PB, ld_m=ld_m, off=off, sw=val("S")):
                        t.wait_ge(sem("S"), sw)
                        if s4 == 0:
                            t.wait_ge(sem("ldm"), ld_m)
                        src_a = a if off == 0 else a + 128
                        for cc in range(NCH):
                            mm = t.matmul(PB[:, 0:w_],
                                          WMB[:, 1024 * (mch % 2) + 128 * cc:1024 * (mch % 2) + 128 * (cc + 1)],
                                          XN[:, KVW * cc + (0 if off == 0 else 128) + a:KVW * cc + (0 if off == 0 else 128) + a + w_],
                                          start=(cc == 0), stop=(cc == 7))
                        mm.then_inc(sem("T"), 1)
                    c.t(f_mm)
                    t_mm = bump("T")

                    def f_ev(s_, mch=mch, a=a, w_=w_, PB=PB, t_mm=t_mm, DST=DST, wid=wid, BC=BC):
                        s_.wait_ge(sem("T"), t_mm)
                        s_.activation(out=DST[:, wid * mch + a:wid * mch + a + w_], in_=PB[:, 0:w_],
                                      func=AF.Identity, bias=BC[:, mch:mch + 1]).then_inc(sem("S"), 1)
                    c.s(f_ev)
                    bump("S")
        s_prj = val("S")

        def f_rope(v):
            v.wait_ge(sem("S"), s_prj)
            v.wait_ge(sem("ldw"), LDT)
            for mci in range(NCH):
                for pb in (0, 64):
                    qv = QF[pb:pb + 64, TOK * mci:TOK * (mci + 1)]
                    i0 = v.tensor_copy(RSC[pb:pb + 32, 0:TOK], QF[pb + 32:pb + 64, TOK * mci:TOK * (mci + 1)])
                    i1 = v.tensor_copy(RSC[pb + 32:pb + 64, 0:TOK], QF[pb:pb + 32, TOK * mci:TOK * (mci + 1)])
                    vedge(v, i0); vedge(v, i1)
                    i2 = v.tensor_tensor(RSC[pb:pb + 64, 0:TOK], RSC[pb:pb + 64, 0:TOK], QSIN[pb:pb + 64, :], OP.mult)
                    i3 = v.tensor_tensor(RC2[pb:pb + 64, 0:TOK], qv, QCOS[pb:pb + 64, :], OP.mult)
                    vedge(v, i2); vedge(v, i3)
                    i4 = v.tensor_tensor(qv, RC2[pb:pb + 64, 0:TOK], RSC[pb:pb + 64, 0:TOK], OP.add)
                    vedge(v, i4)
                    kv = KF[pb:pb + 64, KVW * mci:KVW * (mci + 1)]
                    i5 = v.tensor_copy(RSC[pb:pb + 32, :], KF[pb + 32:pb + 64, KVW * mci:KVW * (mci + 1)])
                    i6 = v.tensor_copy(RSC[pb + 32:pb + 64, :], KF[pb:pb + 32, KVW * mci:KVW * (mci + 1)])
                    vedge(v, i5); vedge(v, i6)
                    i7 = v.tensor_tensor(RSC[pb:pb + 64, :], RSC[pb:pb + 64, :], KSIN[pb:pb + 64, :], OP.mult)
                    i8 = v.tensor_tensor(RC2[pb:pb + 64, :], kv, KCOS[pb:pb + 64, :], OP.mult)
                    vedge(v, i7); vedge(v, i8)
                    i9 = v.tensor_tensor(kv, RC2[pb:pb + 64, :], RSC[pb:pb + 64, :], OP.add)
                    vedge(v, i9)
            v.tensor_copy(MV[:, 0:1], MV[:, 0:1]).then_inc(sem("V"), 1)
        c.v(f_rope)
        v_rope = bump("V")

        for hg in range(4):
            def f_wvd(g, hg=hg, tw=val("T")):
                g.wait_ge(sem("T"), tw)
                g.dma_start(out=WVB[:].rearrange("p (cc m) -> p cc m", m=256),
                            in_=i_wvg[hg]).then_inc(sem("ldm"), 16)
            c.g(f_wvd)
            ld_v = bump("ldm", 16)
            for i in range(17):
                PB = PP0 if i % 2 == 0 else PP1

                def f_vmm(t, i=i, hg=hg, PB=PB, ld_v=ld_v, sw=val("S")):
                    t.wait_ge(sem("S"), sw)
                    if i == 0:
                        t.wait_ge(sem("ldm"), ld_v)
                    for cc in range(NCH):
                        t.matmul(PB[:, 0:256], XN[:, KVW * cc + 128 * i:KVW * cc + 128 * (i + 1)],
                                 WVB[:, 256 * cc:256 * (cc + 1)],
                                 start=(cc == 0), stop=False)
                    mm = t.matmul(PB[:, 0:256], ONR[:], BVR[:, 256 * hg:256 * (hg + 1)],
                                  start=False, stop=True)
                    mm.then_inc(sem("T"), 1)
                c.t(f_vmm)
                t_v = bump("T")

                def f_vev(s_, i=i, hg=hg, PB=PB, t_v=t_v):
                    s_.wait_ge(sem("T"), t_v)
                    s_.activation(out=VT[:, D * i + 256 * hg:D * i + 256 * (hg + 1)], in_=PB[:, 0:256],
                                  func=AF.Identity, bias=Z0C[:]).then_inc(sem("S"), 1)
                c.s(f_vev)
                bump("S")
        s_kv = val("S")
        prod_es.close()
        WOT = c.sb("WOT", [64, 16 * D], BF16)

        def f_wotld(g, tw=val("T")):
            g.wait_ge(sem("T"), tw)
            g.wait_ge(sem("V"), v_rope)
            g.dma_start(out=WOT[:], in_=i_wot).then_inc(sem("ldw"), 16)
        c.g(f_wotld)
        LDWOT = bump("ldw", 16)

        with nc.sbuf_tensor("X2", [128, 2 * D], F32) as X2:
            v_ao = [0]
            styst = {}
            for qt in range(16):
                def f_x2(sy, qt=qt, vv=val("V"), tv=val("T")):
                    if qt < 2:
                        sy.wait_ge(sem("V"), vv)
                        sy.wait_ge(sem("T"), tv)
                    if qt >= 2:
                        sy.wait_ge(sem("V"), v_ao[0] - 1)
                    sy.dma_start(out=X2[:, D * (qt % 2):D * (qt % 2 + 1)],
                                 in_=i_x[:, D * qt:D * (qt + 1)]).then_inc(sem("ldq"), 16)
                c.sy(f_x2)
                ld_x2 = bump("ldq", 16)
                for hi in range(16):
                    mci = hi // 2
                    pb = 64 * (hi % 2)

                    def f_sc(t, qt=qt, mci=mci, pb=pb, vw=v_rope, sw=s_kv, pv=val("V")):
                        t.wait_ge(sem("V"), vw)
                        t.wait_ge(sem("S"), sw)
                        t.wait_ge(sem("V"), pv)
                        mm = t.matmul(SPS[:, 0:256],
                                      QF[pb:pb + 64, TOK * mci + 128 * qt:TOK * mci + 128 * (qt + 1)],
                                      KF[pb:pb + 64, KVW * mci + 128 * qt:KVW * mci + 128 * qt + 256],
                                      start=True, stop=True)
                        mm.then_inc(sem("T"), 1)
                    c.t(f_sc)
                    t_sc = bump("T")

                    def f_scev(v, qt=qt, t_sc=t_sc):
                        v.wait_ge(sem("T"), t_sc)
                        MP = AM0 if qt == 0 else MPREV
                        i1 = v.tensor_tensor(SCB[:, 0:128], SPS[:, 0:128], MP[:], OP.add)
                        i2 = v.tensor_tensor(SCB[:, 128:256], SPS[:, 128:256], MOWN[:], OP.add)
                        vedge(v, i1); vedge(v, i2)
                        i3 = v.reduce_max(NMX[:], SCB[:, 0:256], axis=AX.X, negate=True)
                        vedge(v, i3)
                        v.tensor_scalar(NMX[:], NMX[:], 0.125, None, OP.mult).then_inc(sem("V"), 1)
                    c.v(f_scev)
                    v_mx = bump("V")

                    def f_exp(s_, v_mx=v_mx):
                        s_.wait_ge(sem("V"), v_mx)
                        s_.activation(out=SCB[:, 0:256], in_=SCB[:, 0:256], func=AF.Exp,
                                      bias=NMX[:], scale=0.125, accum_out=DEN[:])
                        s_.copy(DEN[:], DEN[:]).then_inc(sem("S"), 1)
                    c.s(f_exp)
                    s_exp = bump("S")

                    def f_nrm(v, s_exp=s_exp):
                        v.wait_ge(sem("S"), s_exp)
                        i1 = v.reciprocal(RDEN[:], DEN[:])
                        vedge(v, i1)
                        v.tensor_scalar(SCB[:, 0:256], SCB[:, 0:256], RDEN[:], None, OP.mult).then_inc(sem("V"), 1)
                    c.v(f_nrm)
                    v_nrm = bump("V")

                    def f_att(t, v_nrm=v_nrm):
                        t.wait_ge(sem("V"), v_nrm)
                        t.transpose(PT[:, 0:128], SCB[:, 0:128], IDN[:])
                        mm = t.transpose(PT[:, 128:256], SCB[:, 128:256], IDN[:])
                        mm.then_inc(sem("T"), 1)
                    c.t(f_att)
                    t_at = bump("T")

                    def f_ate(v, t_at=t_at):
                        v.wait_ge(sem("T"), t_at)
                        v.tensor_copy(ATT[:], PT[:, 0:256]).then_inc(sem("V"), 1)
                    c.v(f_ate)
                    v_at = bump("V")

                    def f_ctx(t, qt=qt, hi=hi, v_at=v_at):
                        t.wait_ge(sem("V"), v_at)
                        t.matmul(CTXP[0:64, 0:128], VT[:, D * qt + 64 * hi:D * qt + 64 * (hi + 1)],
                                 ATT[:, 0:128], start=True, stop=False)
                        mm = t.matmul(CTXP[0:64, 0:128], VT[:, D * (qt + 1) + 64 * hi:D * (qt + 1) + 64 * (hi + 1)],
                                      ATT[:, 128:256], start=False, stop=True)
                        mm.then_inc(sem("T"), 1)
                    c.t(f_ctx)
                    t_ctx = bump("T")

                    def f_cte(v, t_ctx=t_ctx):
                        v.wait_ge(sem("T"), t_ctx)
                        v.tensor_copy(CTXS[:], CTXP[0:64, 0:128]).then_inc(sem("V"), 1)
                    c.v(f_cte)
                    v_cte = bump("V")

                    def f_wo(t, hi=hi, v_cte=v_cte):
                        t.wait_ge(sem("V"), v_cte)
                        if hi == 0:
                            t.wait_ge(sem("ldw"), LDWOT)
                        mm = None
                        for n in range(2):
                            mm = t.matmul(OUTP[:, 512 * n:512 * (n + 1)], CTXS[:],
                                          WOT[:, D * hi + 512 * n:D * hi + 512 * (n + 1)],
                                          start=(hi == 0), stop=False, skip_group_check=True)
                        if hi == 15:
                            t.matmul(OUTP[:, 0:512], ONR[:], BOR[:, 0:512],
                                     start=False, stop=False, skip_group_check=True)
                            mm = t.matmul(OUTP[:, 512:1024], ONR[:], BOR[:, 512:1024],
                                          start=False, stop=True, skip_group_check=True)
                            mm.then_inc(sem("T"), 1)
                    c.t(f_wo)
                    if hi == 15:
                        t_wo = bump("T")

                def f_yev(v, qt=qt, t_wo=t_wo, ld_x2=ld_x2):
                    v.wait_ge(sem("T"), t_wo)
                    v.wait_ge(sem("ldq"), ld_x2)
                    if qt >= 2 and (qt - 2) in styst:
                        v.wait_ge(sem("st"), styst[qt - 2])
                    v.tensor_tensor(YSTG[:, D * (qt % 2):D * (qt % 2 + 1)],
                                    X2[:, D * (qt % 2):D * (qt % 2 + 1)],
                                    OUTP[:, 0:1024], OP.add).then_inc(sem("V"), 1)
                c.v(f_yev)
                v_ao[0] = bump("V")

                def f_yst(sy, qt=qt, vv=v_ao[0]):
                    sy.wait_ge(sem("V"), vv)
                    sy.dma_start(out=o_y[:, D * qt:D * (qt + 1)],
                                 in_=YSTG[:, D * (qt % 2):D * (qt % 2 + 1)]).then_inc(sem("st"), 16)
                c.sy(f_yst)
                styst[qt] = bump("st", 16)

        if DBG0:
            o_dxn = nc.dram_tensor("dxn", [128, NCH * KVW], BF16, kind="ExternalOutput").ap()
            o_dqf = nc.dram_tensor("dqf", [128, NCH * TOK], BF16, kind="ExternalOutput").ap()
            o_dkf = nc.dram_tensor("dkf", [128, NCH * KVW], BF16, kind="ExternalOutput").ap()
            o_dvt = nc.dram_tensor("dvt", [128, 17 * D], BF16, kind="ExternalOutput").ap()

            def dbg_sy(sy, sv=val("st")):
                sy.wait_ge(sem("st"), sv)
                sy.dma_start(out=o_dxn, in_=XN[:]).then_inc(sem("st"), 16)
                sy.dma_start(out=o_dqf, in_=QF[:]).then_inc(sem("st"), 16)
                sy.dma_start(out=o_dkf, in_=KF[:]).then_inc(sem("st"), 16)
                sy.dma_start(out=o_dvt, in_=VT[:]).then_inc(sem("st"), 16)
            c.sy(dbg_sy)
            bump("st", 64)

        def fin_sy(sy):
            sy.wait_ge(sem("st"), val("st"))
        c.sy(fin_sy)
    return nc


L0F_PHASE = 4
L0F_NCI = 22
L0F_GELU = True


def build_L0f(first=False):
    nc = bass.Bass("TRN2", target_bir_lowering=False)
    ZDT = BF16 if first else F32
    QW = 17 * 128
    KW = 19 * 128
    ZW = 20 * 128
    i_zl = nc.dram_tensor("zloc", [128, NCH * TOK], ZDT, kind="ExternalInput").ap()
    i_zh = nc.dram_tensor("zhalo", [128, NCH * 512], ZDT, kind="ExternalInput").ap()
    i_zr = nc.dram_tensor("zres", [17, 128, D], ZDT, kind="ExternalInput").ap()
    i_wqh = nc.dram_tensor("wqh", [2, 128, 4096], F32, kind="ExternalInput").ap()
    i_wkh = nc.dram_tensor("wkh", [2, 128, 4096], F32, kind="ExternalInput").ap()
    i_wvh = nc.dram_tensor("wvh", [2, 128, 4096], F32, kind="ExternalInput").ap()
    i_wot = nc.dram_tensor("wot", [64, 16 * D], F32, kind="ExternalInput").ap()
    i_w1s = nc.dram_tensor("w1s", [22, 128, 2048], F32, kind="ExternalInput").ap()
    i_w2s = nc.dram_tensor("w2s", [128, 22 * D], F32, kind="ExternalInput").ap()
    i_b1c = nc.dram_tensor("b1c", [128, 44], F32, kind="ExternalInput").ap()
    i_b2c = nc.dram_tensor("b2c", [128, 8], F32, kind="ExternalInput").ap()
    i_gbc = nc.dram_tensor("gbc", [128, 16], F32, kind="ExternalInput").ap()
    i_qsc4 = nc.dram_tensor("qsc4", [128, 4], F32, kind="ExternalInput").ap()
    i_qcs = nc.dram_tensor("qcs", [128, QW], F32, kind="ExternalInput").ap()
    i_qss = nc.dram_tensor("qss", [128, QW], F32, kind="ExternalInput").ap()
    i_kcs = nc.dram_tensor("kcs", [128, KW], F32, kind="ExternalInput").ap()
    i_kss = nc.dram_tensor("kss", [128, KW], F32, kind="ExternalInput").ap()
    i_mb4 = nc.dram_tensor("mb4", [128, 4 * 384], BF16, kind="ExternalInput").ap()
    i_shz = nc.dram_tensor("shz", [128, 1], F32, kind="ExternalInput").ap()
    i_idn = nc.dram_tensor("idn", [128, 128], BF16, kind="ExternalInput").ap()
    i_idnf = nc.dram_tensor("idnf", [128, 128], F32, kind="ExternalInput").ap()
    i_onec = nc.dram_tensor("onec", [128, 1], BF16, kind="ExternalInput").ap()
    i_onr = nc.dram_tensor("onesr", [1, 128], BF16, kind="ExternalInput").ap()
    o_z = nc.dram_tensor("zout", [128, NCH * TOK], F32, kind="ExternalOutput").ap()
    d_ysc = nc.dram_tensor("yscr", [128, NCH * QW], F32, kind="Internal").ap()
    yscv = d_ysc.rearrange("p (c w) -> p c w", w=QW)

    es = contextlib.ExitStack()
    with es:
        c = Ctx(nc, es)
        sem, bump, val = c.sem, c.bump, c.val

        def vedge(v, instr):
            n = bump("V"); instr.then_inc(sem("V"), 1); v.wait_ge(sem("V"), n)

        MB4 = c.sb("MB4", [128, 4 * 384], BF16)
        QSC4 = c.sb("QSC4", [128, 4], F32)
        GBC = c.sb("GBC", [128, 16], F32)
        B1C = c.sb("B1C", [128, 44], F32)
        B2C = c.sb("B2C", [128, 8], F32)
        SHZ = c.sb("SHZ", [128, 1], F32)
        IDN = c.sb("IDN", [128, 128], BF16)
        IDNF = c.sb("IDNF", [128, 128], F32)
        ONEC = c.sb("ONEC", [128, 1], BF16)
        ONR = c.sb("ONR", [1, 128], BF16)
        EPS = c.sb("EPS", [128, 1], F32)
        BTINY = c.sb("BTINY", [128, 1], F32)
        NMX = c.sb("NMX", [128, 1], F32)
        DEN = c.sb("DEN", [128, 1], F32)
        RDEN = c.sb("RDEN", [128, 1], F32)
        SSQ = c.sb("SSQ", [128, 16], F32)
        NRM = c.sb("NRM", [128, 16], F32)
        NRM2 = c.sb("NRM2", [128, 16], F32)
        TRW = c.sb("TRW", [128, 64], F32)
        CTXS = c.sb("CTXS", [64, 128], BF16)
        SCB = c.sb("SCB", [128, 384], BF16)
        ATT = c.sb("ATT", [128, 384], BF16)

        PP0 = c.ps("PP0", [128, 512], F32)
        PP1 = c.ps("PP1", [128, 512], F32)
        OUTP = c.ps("OUTP", [128, 1024], F32)
        PTT = c.ps("PTT", [128, 1024], BF16)
        PTF = c.ps("PTF", [128, 1024], F32)
        PSB = c.ps("PSB", [128, 512], F32)
        SPS = PP0[:, 0:384]
        CTXP = PP1[0:64, 0:128]
        PTB = PTT[:, 0:384]

        def ph0_g(g):
            for dst, src in [(MB4, i_mb4), (QSC4, i_qsc4), (GBC, i_gbc), (B1C, i_b1c),
                             (B2C, i_b2c), (SHZ, i_shz), (IDN, i_idn), (IDNF, i_idnf),
                             (ONEC, i_onec), (ONR, i_onr)]:
                g.dma_start(out=dst[:], in_=src).then_inc(sem("ldw"), 16)
            g.memset(EPS[:], 1e-5)
            g.memset(BTINY[:], 1e-30).then_inc(sem("ldw"), 1)
        c.g(ph0_g)
        LDW0 = bump("ldw", 10 * 16 + 1)

        scopeQ = contextlib.ExitStack()
        QF = scopeQ.enter_context(nc.sbuf_tensor("QF", [128, NCH * QW], BF16))
        KF = scopeQ.enter_context(nc.sbuf_tensor("KF", [128, NCH * KW], BF16))
        VT = scopeQ.enter_context(nc.sbuf_tensor("VT", [128, 19 * D], BF16))
        QFv = QF[:].rearrange("p (c w) -> p c w", w=QW)
        KFv = KF[:].rearrange("p (c w) -> p c w", w=KW)

        with nc.sbuf_tensor("ZB", [128, NCH * ZW], BF16) as ZB, \
             nc.sbuf_tensor("WQKB", [128, 2 * 4096], BF16) as WQKB:
            ZBv = ZB[:].rearrange("p (c w) -> p c w", w=ZW)

            def ph0z(g):
                g.dma_start(out=ZBv[:, :, 0:512], in_=i_zh.rearrange("p (c w) -> p c w", w=512)).then_inc(sem("ldw"), 16)
                g.dma_start(out=ZBv[:, :, 512:ZW], in_=i_zl.rearrange("p (c w) -> p c w", w=TOK)).then_inc(sem("ldw"), 16)
            c.g(ph0z)
            LDZB = bump("ldw", 32)

            def zsh(fc, c0, w):
                off = 0 if fc < 4 else -1
                return ZBv[:, fc, c0 + off:c0 + off + w]

            projs = [("q", i_wqh, 17, 384), ("k", i_wkh, 19, 128), ("v", i_wvh, 19, 128)]
            bi = 0
            last_mm_for_buf = {0: 0, 1: 0}
            last_ev_for_pp = {0: 0, 1: 0}
            ppi = 0
            for nm, i_w, NTIL, BASE in projs:
                for n in (0, 1):
                    buf = bi % 2

                    def f_wld(g, i_w=i_w, n=n, buf=buf, tw=last_mm_for_buf[buf]):
                        g.wait_ge(sem("T"), tw)
                        g.dma_start(out=WQKB[:, 4096 * buf:4096 * (buf + 1)],
                                    in_=i_w[n]).then_inc(sem("ldm"), 16)
                    c.g(f_wld)
                    ld = bump("ldm", 16)
                    for m in range(NTIL):
                        PB = PP0 if ppi % 2 == 0 else PP1
                        pkey = ppi % 2

                        def f_mm(t, nm=nm, n=n, buf=buf, m=m, PB=PB, ld=ld, BASE=BASE,
                                 sw=last_ev_for_pp[pkey]):
                            if m == 0:
                                t.wait_ge(sem("ldm"), ld)
                                t.wait_ge(sem("ldw"), LDZB)
                            t.wait_ge(sem("S"), sw)
                            mm = None
                            for fc in range(NCH):
                                mm = t.matmul(PB[:, 0:512], zsh(fc, BASE + 128 * m, 128),
                                              WQKB[:, 4096 * buf + 512 * fc:4096 * buf + 512 * (fc + 1)],
                                              start=(fc == 0), stop=(fc == 7))
                            mm.then_inc(sem("T"), 1)
                        c.t(f_mm)
                        t_mm = bump("T")
                        last_mm_for_buf[buf] = t_mm

                        def f_ev(s_, nm=nm, n=n, m=m, PB=PB, t_mm=t_mm):
                            s_.wait_ge(sem("T"), t_mm)
                            if nm == "v":
                                s_.activation(out=VT[:, D * m + 512 * n:D * m + 512 * (n + 1)],
                                              in_=PB[:, 0:512], func=AF.Identity).then_inc(sem("S"), 1)
                            else:
                                DV = QFv if nm == "q" else KFv
                                s_.activation(out=DV[:, 4 * n:4 * (n + 1), 128 * m:128 * (m + 1)],
                                              in_=PB[:].rearrange("p (a b) -> p a b", b=128),
                                              func=AF.Identity).then_inc(sem("S"), 1)
                        c.s(f_ev)
                        last_ev_for_pp[pkey] = bump("S")
                        ppi += 1
                    bi += 1
            s_st1 = val("S")
            t_st1 = val("T")

        if L0F_PHASE == 1:
            def dbg1(g, sw=val("S")):
                g.wait_ge(sem("S"), sw)
                g.dma_start(out=o_z, in_=QF[:, 0:NCH * TOK]).then_inc(sem("st"), 16)
            c.g(dbg1)
            bump("st", 16)

            def fin1(sy):
                sy.wait_ge(sem("st"), val("st"))
            c.sy(fin1)
            scopeQ.close()
            return nc

        with nc.sbuf_tensor("QCS", [128, QW], BF16) as QCS, \
             nc.sbuf_tensor("QSS", [128, QW], BF16) as QSS, \
             nc.sbuf_tensor("KCS", [128, KW], BF16) as KCS, \
             nc.sbuf_tensor("KSS", [128, KW], BF16) as KSS, \
             nc.sbuf_tensor("RSC", [128, KW], BF16) as RSC, \
             nc.sbuf_tensor("RC2", [128, KW], BF16) as RC2, \
             nc.sbuf_tensor("QTB", [128, D], BF16) as QTB:

            def f_tbl(g, tw=t_st1, sw=s_st1):
                g.wait_ge(sem("T"), tw)
                g.wait_ge(sem("S"), sw)
                for dst, src in [(QCS, i_qcs), (QSS, i_qss), (KCS, i_kcs), (KSS, i_kss)]:
                    g.dma_start(out=dst[:], in_=src).then_inc(sem("ldm"), 16)
            c.g(f_tbl)
            ld_tb = bump("ldm", 64)

            v_ts_prev = 0
            t_tr_prev = 0
            v_ev_prev = 0
            for nm, SRCv, WID, NTIL in (("q", QFv, QW, 17), ("k", KFv, KW, 19)):
                for m in range(NTIL):
                    def f_sq(s_, SRCv=SRCv, m=m, first_it=(nm == "q" and m == 0), vp=v_ts_prev):
                        if first_it:
                            s_.wait_ge(sem("S"), s_st1)
                        s_.wait_ge(sem("V"), vp)
                        for h in range(16):
                            s_.activation(out=TRW[:], in_=SRCv[:, h // 2, 128 * m + 64 * (h % 2):128 * m + 64 * (h % 2) + 64],
                                          func=AF.Square, accum_out=SSQ[:, h:h + 1])
                        i1 = s_.activation(out=NRM[:], in_=SSQ[:], func=AF.Ln, bias=BTINY[:])
                        n0 = bump("S"); i1.then_inc(sem("S"), 1); s_.wait_ge(sem("S"), n0)
                        s_.activation(out=NRM[:], in_=NRM[:], func=AF.Exp, scale=0.5).then_inc(sem("S"), 1)
                    c.s(f_sq)
                    s_nr = bump("S")

                    def f_ts(v, SRCv=SRCv, m=m, s_nr=s_nr, tp=t_tr_prev):
                        v.wait_ge(sem("S"), s_nr)
                        v.wait_ge(sem("T"), tp)
                        i1 = v.tensor_scalar(NRM2[:], NRM[:], 1e-12, None, OP.max)
                        vedge(v, i1)
                        i2 = v.reciprocal(NRM2[:], NRM2[:])
                        vedge(v, i2)
                        for h in range(16):
                            i3 = v.tensor_scalar(QTB[:, 64 * h:64 * (h + 1)],
                                                 SRCv[:, h // 2, 128 * m + 64 * (h % 2):128 * m + 64 * (h % 2) + 64],
                                                 NRM2[:, h:h + 1], None, OP.mult)
                        i3.then_inc(sem("V"), 1)
                    c.v(f_ts)
                    v_ts_prev = bump("V")

                    def f_tr(t, v_ts=v_ts_prev, vep=v_ev_prev):
                        t.wait_ge(sem("V"), v_ts)
                        t.wait_ge(sem("V"), vep)
                        t.wait_ge(sem("ldw"), LDW0)
                        for ch in range(NCH):
                            mm = t.transpose(PTT[:, 128 * ch:128 * (ch + 1)],
                                             QTB[:, 128 * ch:128 * (ch + 1)], IDN[:])
                        mm.then_inc(sem("T"), 1)
                    c.t(f_tr)
                    t_tr_prev = bump("T")

                    def f_ev(v, SRCv=SRCv, m=m, t_tr=t_tr_prev):
                        v.wait_ge(sem("T"), t_tr)
                        v.tensor_copy(SRCv[:, :, 128 * m:128 * (m + 1)],
                                      PTT[:].rearrange("p (a b) -> p a b", b=128)[:, 0:NCH, :]).then_inc(sem("V"), 1)
                    c.v(f_ev)
                    v_ev_prev = bump("V")
            t_st2 = val("T")

            def f_fold(v, ld_tb=ld_tb):
                v.wait_ge(sem("ldm"), ld_tb)
                v.wait_ge(sem("ldw"), LDW0)
                i1 = v.tensor_scalar(QCS[:], QCS[:], QSC4[:, 0:1], None, OP.mult)
                i2 = v.tensor_scalar(QSS[:], QSS[:], QSC4[:, 1:2], None, OP.mult)
                i3 = v.tensor_scalar(KCS[:], KCS[:], QSC4[:, 2:3], None, OP.mult)
                vedge(v, i1); vedge(v, i2); vedge(v, i3)
                v.tensor_scalar(KSS[:], KSS[:], QSC4[:, 3:4], None, OP.mult).then_inc(sem("V"), 1)
            c.v(f_fold)
            bump("V")

            for mci in range(NCH):
                def f_rope(v, mci=mci):
                    qv = QF[:, QW * mci:QW * (mci + 1)]
                    v.tensor_copy(RSC[0:32, 0:QW], qv[32:64])
                    v.tensor_copy(RSC[32:64, 0:QW], qv[0:32])
                    v.tensor_copy(RSC[64:96, 0:QW], qv[96:128])
                    i1 = v.tensor_copy(RSC[96:128, 0:QW], qv[64:96])
                    vedge(v, i1)
                    i2 = v.tensor_tensor(RSC[:, 0:QW], RSC[:, 0:QW], QSS[:], OP.mult)
                    i3 = v.tensor_tensor(RC2[:, 0:QW], qv, QCS[:], OP.mult)
                    vedge(v, i2); vedge(v, i3)
                    i4 = v.tensor_tensor(qv, RC2[:, 0:QW], RSC[:, 0:QW], OP.add)
                    vedge(v, i4)
                    kv = KF[:, KW * mci:KW * (mci + 1)]
                    v.tensor_copy(RSC[0:32, :], kv[32:64])
                    v.tensor_copy(RSC[32:64, :], kv[0:32])
                    v.tensor_copy(RSC[64:96, :], kv[96:128])
                    i5 = v.tensor_copy(RSC[96:128, :], kv[64:96])
                    vedge(v, i5)
                    i6 = v.tensor_tensor(RSC[:], RSC[:], KSS[:], OP.mult)
                    i7 = v.tensor_tensor(RC2[:], kv, KCS[:], OP.mult)
                    vedge(v, i6); vedge(v, i7)
                    v.tensor_tensor(kv, RC2[:], RSC[:], OP.add).then_inc(sem("V"), 1)
                c.v(f_rope)
                bump("V")
            v_rope = val("V")

        if L0F_PHASE == 2:
            def dbg2(g, vw=v_rope):
                g.wait_ge(sem("V"), vw)
                g.dma_start(out=o_z, in_=QF[:, 0:NCH * TOK]).then_inc(sem("st"), 16)
            c.g(dbg2)
            bump("st", 16)

            def fin2(sy):
                sy.wait_ge(sem("st"), val("st"))
            c.sy(fin2)
            scopeQ.close()
            return nc

        with nc.sbuf_tensor("WOT", [64, 16 * D], BF16) as WOT, \
             nc.sbuf_tensor("YTM", [128, 2 * D], F32) as YTM, \
             nc.sbuf_tensor("ZRES", [128, 2 * D], F32) as ZRES, \
             nc.sbuf_tensor("YFS", [128, 2 * D], F32) as YFS:

            def f_wotld(g, vw=v_rope, tw=t_st2):
                g.wait_ge(sem("V"), vw)
                g.wait_ge(sem("T"), tw)
                g.dma_start(out=WOT[:], in_=i_wot).then_inc(sem("ldm"), 16)
            c.g(f_wotld)
            ld_wot = bump("ldm", 16)

            styst = {}
            v_yfm = {}
            ld_zres = {}
            s_oev_prev = 0
            t_ytr = {}
            for qt in range(17):
                def f_zres(g, qt=qt, vw=v_rope, tw=t_st2):
                    if qt < 2:
                        g.wait_ge(sem("V"), vw)
                        g.wait_ge(sem("T"), tw)
                    else:
                        g.wait_ge(sem("V"), v_yfm.get(qt - 2, 0))
                    g.dma_start(out=ZRES[:, D * (qt % 2):D * (qt % 2 + 1)],
                                in_=i_zr[qt]).then_inc(sem("ldq"), 16)
                c.g(f_zres)
                ld_zres[qt] = bump("ldq", 16)

                mi = min(qt, 3)
                for hi in range(16):
                    mci = hi // 2
                    pb = 64 * (hi % 2)

                    def f_sc(t, qt=qt, mci=mci, pb=pb, vw=v_rope, pv=val("V")):
                        t.wait_ge(sem("V"), vw)
                        t.wait_ge(sem("V"), pv)
                        mm = t.matmul(SPS, QF[pb:pb + 64, QW * mci + 128 * qt:QW * mci + 128 * (qt + 1)],
                                      KF[pb:pb + 64, KW * mci + 128 * qt:KW * mci + 128 * qt + 384],
                                      start=True, stop=True)
                        mm.then_inc(sem("T"), 1)
                    c.t(f_sc)
                    t_sc = bump("T")

                    def f_scev(v, mi=mi, t_sc=t_sc):
                        v.wait_ge(sem("T"), t_sc)
                        i1 = v.tensor_tensor(SCB[:], SPS, MB4[:, 384 * mi:384 * (mi + 1)], OP.add)
                        vedge(v, i1)
                        i2 = v.reduce_max(NMX[:], SCB[:], axis=AX.X, negate=True)
                        vedge(v, i2)
                        v.tensor_scalar(NMX[:], NMX[:], 8.0, None, OP.mult).then_inc(sem("V"), 1)
                    c.v(f_scev)
                    v_mx = bump("V")

                    def f_exp(s_, v_mx=v_mx):
                        s_.wait_ge(sem("V"), v_mx)
                        s_.activation(out=SCB[:], in_=SCB[:], func=AF.Exp,
                                      bias=NMX[:], scale=8.0, accum_out=DEN[:])
                        s_.copy(DEN[:], DEN[:]).then_inc(sem("S"), 1)
                    c.s(f_exp)
                    s_exp = bump("S")

                    def f_nrm(v, s_exp=s_exp):
                        v.wait_ge(sem("S"), s_exp)
                        i1 = v.reciprocal(RDEN[:], DEN[:])
                        vedge(v, i1)
                        v.tensor_scalar(SCB[:], SCB[:], RDEN[:], None, OP.mult).then_inc(sem("V"), 1)
                    c.v(f_nrm)
                    v_nrm = bump("V")

                    def f_att(t, v_nrm=v_nrm):
                        t.wait_ge(sem("V"), v_nrm)
                        for kk in range(3):
                            mm = t.transpose(PTB[:, 128 * kk:128 * (kk + 1)],
                                             SCB[:, 128 * kk:128 * (kk + 1)], IDN[:])
                        mm.then_inc(sem("T"), 1)
                    c.t(f_att)
                    t_at = bump("T")

                    def f_ate(v, t_at=t_at):
                        v.wait_ge(sem("T"), t_at)
                        v.tensor_copy(ATT[:], PTB).then_inc(sem("V"), 1)
                    c.v(f_ate)
                    v_at = bump("V")

                    def f_ctx(t, qt=qt, hi=hi, v_at=v_at):
                        t.wait_ge(sem("V"), v_at)
                        for kk in range(3):
                            mm = t.matmul(CTXP, VT[:, D * (qt + kk) + 64 * hi:D * (qt + kk) + 64 * (hi + 1)],
                                          ATT[:, 128 * kk:128 * (kk + 1)],
                                          start=(kk == 0), stop=(kk == 2))
                        mm.then_inc(sem("T"), 1)
                    c.t(f_ctx)
                    t_ctx = bump("T")

                    def f_cte(v, t_ctx=t_ctx):
                        v.wait_ge(sem("T"), t_ctx)
                        v.tensor_copy(CTXS[:], CTXP).then_inc(sem("V"), 1)
                    c.v(f_cte)
                    v_cte = bump("V")

                    def f_wo(t, hi=hi, v_cte=v_cte, sp=s_oev_prev):
                        t.wait_ge(sem("V"), v_cte)
                        if hi == 0:
                            t.wait_ge(sem("ldm"), ld_wot)
                            t.wait_ge(sem("S"), sp)
                        for n in range(2):
                            mm = t.matmul(OUTP[:, 512 * n:512 * (n + 1)], CTXS[:],
                                          WOT[:, D * hi + 512 * n:D * hi + 512 * (n + 1)],
                                          start=(hi == 0), stop=(hi == 15), skip_group_check=True)
                        if hi == 15:
                            mm.then_inc(sem("T"), 1)
                    c.t(f_wo)
                    if hi == 15:
                        t_wo = bump("T")

                def f_oev(s_, qt=qt, t_wo=t_wo, tp=t_ytr.get(qt - 2, 0)):
                    s_.wait_ge(sem("T"), t_wo)
                    s_.wait_ge(sem("T"), tp)
                    s_.activation(out=YTM[:, D * (qt % 2):D * (qt % 2 + 1)], in_=OUTP[:],
                                  func=AF.Identity).then_inc(sem("S"), 1)
                c.s(f_oev)
                s_oev_prev = bump("S")

                def f_ytr(t, qt=qt, s_o=s_oev_prev, vp=v_yfm.get(qt - 1, 0)):
                    t.wait_ge(sem("S"), s_o)
                    t.wait_ge(sem("V"), vp)
                    for ch in range(NCH):
                        mm = t.transpose(PTF[:, 128 * ch:128 * (ch + 1)],
                                         YTM[:, D * (qt % 2) + 128 * ch:D * (qt % 2) + 128 * (ch + 1)],
                                         IDNF[:])
                    mm.then_inc(sem("T"), 1)
                c.t(f_ytr)
                t_ytr[qt] = bump("T")

                def f_yfm(v, qt=qt, t_y=t_ytr[qt]):
                    v.wait_ge(sem("T"), t_y)
                    v.wait_ge(sem("ldq"), ld_zres[qt])
                    if (qt - 2) in styst:
                        v.wait_ge(sem("st"), styst[qt - 2])
                    v.tensor_tensor(YFS[:, D * (qt % 2):D * (qt % 2 + 1)], PTF[:],
                                    ZRES[:, D * (qt % 2):D * (qt % 2 + 1)], OP.add).then_inc(sem("V"), 1)
                c.v(f_yfm)
                v_yfm[qt] = bump("V")

                def f_yst(sy, qt=qt, vv=v_yfm[qt]):
                    sy.wait_ge(sem("V"), vv)
                    sy.dma_start(out=yscv[:, :, 128 * qt:128 * (qt + 1)],
                                 in_=YFS[:, D * (qt % 2):D * (qt % 2 + 1)].rearrange(
                                     "p (a b) -> p a b", b=128)).then_inc(sem("st"), 16)
                c.sy(f_yst)
                styst[qt] = bump("st", 16)

        scopeQ.close()
        if L0F_PHASE == 3:
            def dbg3(g, sv=val("st")):
                g.wait_ge(sem("st"), sv)
                g.dma_start(out=o_z, in_=d_ysc[:, 0:NCH * TOK]).then_inc(sem("st"), 16)
            c.g(dbg3)
            bump("st", 16)

            def fin3(sy):
                sy.wait_ge(sem("st"), val("st"))
            c.sy(fin3)
            return nc

        bT, bV, bS, bst = val("T"), val("V"), val("S"), val("st")

        with nc.sbuf_tensor("XNF", [128, NCH * TOK], BF16) as XNF, \
             nc.sbuf_tensor("W2B", [128, 22 * D], BF16) as W2B:

            def f_bld(g):
                g.wait_ge(sem("st"), bst)
                g.wait_ge(sem("V"), bV)
                g.wait_ge(sem("S"), bS)
                g.wait_ge(sem("T"), bT)
                g.dma_start(out=W2B[:], in_=i_w2s).then_inc(sem("ldm"), 16)
            c.g(f_bld)
            ld_w2 = bump("ldm", 16)

            scopeB1 = contextlib.ExitStack()
            YB = scopeB1.enter_context(nc.sbuf_tensor("YB", [128, NCH * 2052], BF16))
            SQB = scopeB1.enter_context(nc.sbuf_tensor("SQB", [128, 2 * 512], BF16))
            MUR = scopeB1.enter_context(nc.sbuf_tensor("MUR", [128, TOK], BF16))
            RSR = scopeB1.enter_context(nc.sbuf_tensor("RSR", [128, TOK], BF16))
            MUROW = scopeB1.enter_context(nc.sbuf_tensor("MUROW", [1, TOK], BF16))
            RSROW = scopeB1.enter_context(nc.sbuf_tensor("RSROW", [1, TOK], BF16))
            MURF = scopeB1.enter_context(nc.sbuf_tensor("MURF", [1, TOK], F32))
            VRW = scopeB1.enter_context(nc.sbuf_tensor("VRW", [1, TOK], F32))
            TVR = scopeB1.enter_context(nc.sbuf_tensor("TVR", [1, 512], F32))
            XT1 = scopeB1.enter_context(nc.sbuf_tensor("XT1", [128, 512], BF16))
            YBv = YB[:].rearrange("p (c w) -> p c w", w=2052)

            def f_ybld(g):
                g.dma_start(out=YB[:], in_=yscv[:, :, 124:QW]).then_inc(sem("ldm"), 16)
            c.g(f_ybld)
            ld_yb = bump("ldm", 16)

            def f_shz(v, ld_yb=ld_yb):
                v.wait_ge(sem("ldm"), ld_yb)
                v.wait_ge(sem("ldw"), LDW0)
                for ch in range(4, 8):
                    i1 = v.tensor_scalar(YBv[:, ch, 3:4], YBv[:, ch, 3:4], SHZ[:], None, OP.mult)
                i1.then_inc(sem("V"), 1)
            c.v(f_shz)
            v_shz = bump("V")

            def ybsh(fc, a, w):
                off = 4 if fc < 4 else 3
                return YBv[:, fc, off + a:off + a + w]

            SUMP = PTF[0:1, 0:512]
            SSQP = PTF[0:1, 512:1024]
            v_repe2 = {}
            s_sqr = {}
            t_stat = {}
            v_mu_prev = {}
            for t4 in range(4):
                for fc in range(NCH):
                    def f_sqr(s_, t4=t4, fc=fc, tp=t_stat.get((t4 * 8 + fc) - 2, 0)):
                        if t4 == 0 and fc == 0:
                            s_.wait_ge(sem("V"), v_shz)
                        s_.wait_ge(sem("T"), tp)
                        s_.activation(out=SQB[:, 512 * (fc % 2):512 * (fc % 2 + 1)],
                                      in_=ybsh(fc, 512 * t4, 512), func=AF.Square).then_inc(sem("S"), 1)
                    c.s(f_sqr)
                    s_sqr[(t4, fc)] = bump("S")

                    def f_stat(t, t4=t4, fc=fc, sw=s_sqr[(t4, fc)], vm=v_mu_prev.get(t4 - 1, 0)):
                        t.wait_ge(sem("S"), sw)
                        if fc == 0:
                            t.wait_ge(sem("V"), vm)
                        t.matmul(SUMP, ONEC[:], ybsh(fc, 512 * t4, 512),
                                 start=(fc == 0), stop=(fc == 7), skip_group_check=True)
                        mm = t.matmul(SSQP, ONEC[:], SQB[:, 512 * (fc % 2):512 * (fc % 2 + 1)],
                                      start=(fc == 0), stop=(fc == 7), skip_group_check=True)
                        if fc == 7:
                            mm.then_inc(sem("T"), 1)
                    c.t(f_stat)
                    t_stat[t4 * 8 + fc] = val("T") + (1 if fc == 7 else 0)
                    if fc == 7:
                        bump("T")
                t_st = val("T")

                def f_mu(v, t4=t4, t_st=t_st):
                    v.wait_ge(sem("T"), t_st)
                    a = 512 * t4
                    i1 = v.tensor_scalar(MURF[0:1, a:a + 512], SUMP, 1.0 / D, None, OP.mult)
                    i2 = v.tensor_scalar(VRW[0:1, a:a + 512], SSQP, 1.0 / D, None, OP.mult)
                    vedge(v, i1); vedge(v, i2)
                    i3 = v.tensor_copy(MUROW[0:1, a:a + 512], MURF[0:1, a:a + 512])
                    i4 = v.tensor_tensor(TVR[0:1, :], MURF[0:1, a:a + 512], MURF[0:1, a:a + 512], OP.mult)
                    vedge(v, i3); vedge(v, i4)
                    v.tensor_tensor(VRW[0:1, a:a + 512], VRW[0:1, a:a + 512], TVR[0:1, :],
                                    OP.subtract).then_inc(sem("V"), 1)
                c.v(f_mu)
                v_mu = bump("V")
                v_mu_prev[t4] = v_mu

                def f_rst(s_, t4=t4, v_mu=v_mu):
                    s_.wait_ge(sem("V"), v_mu)
                    a = 512 * t4
                    i1 = s_.activation(out=VRW[0:1, a:a + 512], in_=VRW[0:1, a:a + 512],
                                       func=AF.Ln, bias=EPS[0:1, :])
                    n0 = bump("S"); i1.then_inc(sem("S"), 1); s_.wait_ge(sem("S"), n0)
                    s_.activation(out=RSROW[0:1, a:a + 512], in_=VRW[0:1, a:a + 512],
                                  func=AF.Exp, scale=-0.5).then_inc(sem("S"), 1)
                c.s(f_rst)
                s_rst = bump("S")

                def f_rep1(t, t4=t4, sw=s_rst, vp=v_repe2.get(t4 - 1, 0)):
                    t.wait_ge(sem("S"), sw)
                    t.wait_ge(sem("V"), vp)
                    t.matmul(PSB[:], ONR[:], MUROW[0:1, 512 * t4:512 * (t4 + 1)],
                             start=True, stop=True).then_inc(sem("T"), 1)
                c.t(f_rep1)
                t_r1 = bump("T")

                def f_re1(v, t4=t4, t_r1=t_r1):
                    v.wait_ge(sem("T"), t_r1)
                    v.tensor_copy(MUR[:, 512 * t4:512 * (t4 + 1)], PSB[:]).then_inc(sem("V"), 1)
                c.v(f_re1)
                v_r1 = bump("V")

                def f_rep2(t, t4=t4, v_r1=v_r1):
                    t.wait_ge(sem("V"), v_r1)
                    t.matmul(PSB[:], ONR[:], RSROW[0:1, 512 * t4:512 * (t4 + 1)],
                             start=True, stop=True).then_inc(sem("T"), 1)
                c.t(f_rep2)
                t_r2 = bump("T")

                def f_re2(v, t4=t4, t_r2=t_r2):
                    v.wait_ge(sem("T"), t_r2)
                    v.tensor_copy(RSR[:, 512 * t4:512 * (t4 + 1)], PSB[:]).then_inc(sem("V"), 1)
                c.v(f_re2)
                v_repe2[t4] = bump("V")

            if L0F_PHASE == 35:
                def dbg35(g, vv=val("V")):
                    g.wait_ge(sem("V"), vv)
                    g.dma_start(out=o_z, in_=d_ysc[:, 0:NCH * TOK]).then_inc(sem("st"), 16)
                c.g(dbg35)
                bump("st", 16)

                def fin35(sy):
                    sy.wait_ge(sem("st"), val("st"))
                c.sy(fin35)
                scopeB1.close()
                return nc

            for t4 in range(4):
                for fc in range(NCH):
                    def f_xnf(v, t4=t4, fc=fc, vr=v_repe2[t4]):
                        v.wait_ge(sem("V"), vr)
                        a = 512 * t4
                        i1 = v.tensor_tensor(XT1[:], ybsh(fc, a, 512), MUR[:, a:a + 512], OP.subtract)
                        vedge(v, i1)
                        i2 = v.tensor_tensor(XT1[:], XT1[:], RSR[:, a:a + 512], OP.mult)
                        vedge(v, i2)
                        v.tensor_scalar(XNF[:, TOK * fc + a:TOK * fc + a + 512], XT1[:],
                                        GBC[:, fc:fc + 1], GBC[:, 8 + fc:8 + fc + 1],
                                        OP.mult, OP.add).then_inc(sem("V"), 1)
                    c.v(f_xnf)
                    bump("V")
            v_xnf_fin = val("V")
            scopeB1.close()
            if L0F_PHASE == 36:
                def dbg36(g, vv=val("V")):
                    g.wait_ge(sem("V"), vv)
                    g.dma_start(out=o_z, in_=d_ysc[:, 0:NCH * TOK]).then_inc(sem("st"), 16)
                c.g(dbg36)
                bump("st", 16)

                def fin36(sy):
                    sy.wait_ge(sem("st"), val("st"))
                c.sy(fin36)
                return nc

            with nc.sbuf_tensor("H", [128, 22 * TOK], BF16) as H, \
                 nc.sbuf_tensor("W1B", [128, 2 * TOK], BF16) as W1B, \
                 nc.sbuf_tensor("ABUF", [128, 2 * 512], BF16) as ABUF, \
                 nc.sbuf_tensor("GBUF", [128, 2 * 512], BF16) as GBUF, \
                 nc.sbuf_tensor("YFSL", [128, 2 * 512], F32) as YFSL, \
                 nc.sbuf_tensor("ZOW", [128, 2 * 512], F32) as ZOW:

                w1_last_mm = {0: 0, 1: 0}
                s_gel_prev = {0: 0, 1: 0}
                s_gel_last = [0]
                v_hev = {}
                for ci in range(L0F_NCI):
                    buf = ci % 2

                    def f_w1ld(g, ci=ci, buf=buf, tw=w1_last_mm[buf], vx=v_xnf_fin):
                        if ci < 2:
                            g.wait_ge(sem("V"), vx)
                        g.wait_ge(sem("T"), tw)
                        g.dma_start(out=W1B[:, TOK * buf:TOK * (buf + 1)],
                                    in_=i_w1s[ci]).then_inc(sem("ldm"), 16)
                    c.g(f_w1ld)
                    ld_w1 = bump("ldm", 16)
                    for t4 in range(4):
                        def f_hmm(t, ci=ci, buf=buf, t4=t4, ld_w1=ld_w1,
                                  sp=s_gel_last[0], vx=v_xnf_fin):
                            if t4 == 0:
                                t.wait_ge(sem("ldm"), ld_w1)
                            if ci == 0 and t4 == 0:
                                t.wait_ge(sem("V"), vx)
                            t.wait_ge(sem("S"), sp)
                            for fc in range(NCH):
                                t.matmul(PP0[:], W1B[:, TOK * buf + 256 * fc:TOK * buf + 256 * fc + 128],
                                         XNF[:, TOK * fc + 512 * t4:TOK * fc + 512 * (t4 + 1)],
                                         start=(fc == 0), stop=(fc == 7), skip_group_check=True)
                                mm = t.matmul(PP1[:], W1B[:, TOK * buf + 256 * fc + 128:TOK * buf + 256 * (fc + 1)],
                                              XNF[:, TOK * fc + 512 * t4:TOK * fc + 512 * (t4 + 1)],
                                              start=(fc == 0), stop=(fc == 7), skip_group_check=True)
                            mm.then_inc(sem("T"), 1)
                        c.t(f_hmm)
                        t_hmm = bump("T")
                        w1_last_mm[buf] = t_hmm

                        def f_gel(s_, ci=ci, t4=t4, t_hmm=t_hmm, vp=v_hev.get((ci, t4 - 2) if t4 >= 2 else (ci - 1, t4 + 2), 0)):
                            s_.wait_ge(sem("T"), t_hmm)
                            s_.wait_ge(sem("V"), vp)
                            i1 = s_.activation(out=ABUF[:, 512 * (t4 % 2):512 * (t4 % 2 + 1)], in_=PP0[:],
                                               func=AF.Identity, bias=B1C[:, ci:ci + 1])
                            n0 = bump("S"); i1.then_inc(sem("S"), 1); s_.wait_ge(sem("S"), n0)
                            s_.activation(out=GBUF[:, 512 * (t4 % 2):512 * (t4 % 2 + 1)], in_=PP1[:],
                                          func=(AF.Gelu if L0F_GELU else AF.Identity), bias=B1C[:, 22 + ci:23 + ci]).then_inc(sem("S"), 1)
                        c.s(f_gel)
                        s_gel_prev[t4 % 2] = bump("S")
                        s_gel_last[0] = s_gel_prev[t4 % 2]

                        def f_hev(v, ci=ci, t4=t4, sw=s_gel_prev[t4 % 2]):
                            v.wait_ge(sem("S"), sw)
                            v.tensor_tensor(H[:, TOK * ci + 512 * t4:TOK * ci + 512 * (t4 + 1)],
                                            ABUF[:, 512 * (t4 % 2):512 * (t4 % 2 + 1)],
                                            GBUF[:, 512 * (t4 % 2):512 * (t4 % 2 + 1)],
                                            OP.mult).then_inc(sem("V"), 1)
                        c.v(f_hev)
                        v_hev[(ci, t4)] = bump("V")
                v_hev_fin = val("V")
                if L0F_PHASE == 37:
                    def dbg37(g, vv=val("V")):
                        g.wait_ge(sem("V"), vv)
                        g.dma_start(out=o_z, in_=d_ysc[:, 0:NCH * TOK]).then_inc(sem("st"), 16)
                    c.g(dbg37)
                    bump("st", 16)

                    def fin37(sy):
                        sy.wait_ge(sem("st"), val("st"))
                    c.sy(fin37)
                    return nc

                v_zev = {}
                zst = {}
                ld_yfl = {}
                for fc in range(NCH):
                    for t4 in range(4):
                        gi = fc * 4 + t4
                        buf = gi % 2
                        PZ = OUTP[:, 512 * buf:512 * (buf + 1)]

                        def f_yfl(g, fc=fc, t4=t4, gi=gi, buf=buf, vh=v_hev_fin):
                            if gi < 2:
                                g.wait_ge(sem("V"), vh)
                            else:
                                g.wait_ge(sem("V"), v_zev.get(gi - 2, 0))
                            g.dma_start(out=YFSL[:, 512 * buf:512 * (buf + 1)],
                                        in_=yscv[:, fc, 128 + 512 * t4:128 + 512 * (t4 + 1)]).then_inc(sem("ldq"), 16)
                        c.g(f_yfl)
                        ld_yfl[gi] = bump("ldq", 16)

                        def f_zmm(t, fc=fc, t4=t4, gi=gi, PZ=PZ, vh=v_hev_fin, ld_w2=ld_w2):
                            if gi == 0:
                                t.wait_ge(sem("V"), vh)
                                t.wait_ge(sem("ldm"), ld_w2)
                            t.wait_ge(sem("V"), v_zev.get(gi - 2, 0))
                            for ci in range(22):
                                mm = t.matmul(PZ, W2B[:, D * ci + 128 * fc:D * ci + 128 * (fc + 1)],
                                              H[:, TOK * ci + 512 * t4:TOK * ci + 512 * (t4 + 1)],
                                              start=(ci == 0), stop=(ci == 21), skip_group_check=True)
                            mm.then_inc(sem("T"), 1)
                        c.t(f_zmm)
                        t_zmm = bump("T")

                        def f_zev(v, fc=fc, gi=gi, buf=buf, PZ=PZ, t_zmm=t_zmm):
                            v.wait_ge(sem("T"), t_zmm)
                            v.wait_ge(sem("ldq"), ld_yfl[gi])
                            if (gi - 2) in zst:
                                v.wait_ge(sem("st"), zst[gi - 2])
                            v.scalar_tensor_tensor(ZOW[:, 512 * buf:512 * (buf + 1)], PZ,
                                                   B2C[:, fc:fc + 1],
                                                   YFSL[:, 512 * buf:512 * (buf + 1)],
                                                   OP.add, OP.add).then_inc(sem("V"), 1)
                        c.v(f_zev)
                        v_zev[gi] = bump("V")

                        def f_zst(sy, fc=fc, t4=t4, buf=buf, vv=v_zev[gi]):
                            sy.wait_ge(sem("V"), vv)
                            sy.dma_start(out=o_z[:, TOK * fc + 512 * t4:TOK * fc + 512 * (t4 + 1)],
                                         in_=ZOW[:, 512 * buf:512 * (buf + 1)]).then_inc(sem("st"), 16)
                        c.sy(f_zst)
                        zst[gi] = bump("st", 16)

        def fin_sy(sy):
            sy.wait_ge(sem("st"), val("st"))
        c.sy(fin_sy)
    return nc


def build_comb():
    nc = bass.Bass("TRN2", target_bir_lowering=False)
    i_a = nc.dram_tensor("ain", [128, NT * D], F32, kind="ExternalInput").ap()
    i_b = nc.dram_tensor("bin", [128, NT * D], F32, kind="ExternalInput").ap()
    o_x = nc.dram_tensor("xout", [128, NT * D], F32, kind="ExternalOutput").ap()
    es = contextlib.ExitStack()
    with es:
        c = Ctx(nc, es)
        sem, bump, val = c.sem, c.bump, c.val
        A = c.sb("A", [128, NT * D], F32)
        B = c.sb("B", [128, NT * D], F32)

        def f_ld(sy):
            sy.dma_start(out=A[:], in_=i_a).then_inc(sem("ldq"), 16)
            sy.dma_start(out=B[:], in_=i_b).then_inc(sem("ldq"), 16)
        c.sy(f_ld)
        ld = bump("ldq", 32)

        def f_add(v, ld=ld):
            v.wait_ge(sem("ldq"), ld)
            for j in range(4):
                mm = v.tensor_tensor(A[:, 4096 * j:4096 * (j + 1)], A[:, 4096 * j:4096 * (j + 1)],
                                     B[:, 4096 * j:4096 * (j + 1)], OP.add)
            mm.then_inc(sem("V"), 1)
        c.v(f_add)
        v_a = bump("V")

        def f_st(sy, v_a=v_a):
            sy.wait_ge(sem("V"), v_a)
            sy.dma_start(out=o_x, in_=A[:]).then_inc(sem("st"), 16)
            sy.wait_ge(sem("st"), 16)
        c.sy(f_st)
        bump("st", 16)
    return nc


import ml_dtypes
from concourse.bass_utils import run_bass_kernel_spmd

BFH = ml_dtypes.bfloat16


def pm(x):
    return np.ascontiguousarray(x.reshape(16, 128, 1024).transpose(1, 0, 2).reshape(128, 16384))


def ipm(x):
    return np.ascontiguousarray(x.reshape(128, 16, 1024).transpose(1, 0, 2).reshape(2048, 1024))


def fmz(x):
    return np.ascontiguousarray(x.T.reshape(8, 128, 2048).transpose(1, 0, 2).reshape(128, 16384))


def ifmz(z):
    return np.ascontiguousarray(z.reshape(128, 8, 2048).transpose(1, 0, 2).reshape(1024, 2048).T)


def mchunks(W):
    return np.ascontiguousarray(W.reshape(8, 128, 8, 128).transpose(0, 3, 2, 1).reshape(8, 128, 1024))


def l0a_rope_tables(pos):
    i = np.arange(32, dtype=np.float64)
    inv = 1.0 / (10000.0 ** (2 * i / 64.0))
    ang = pos[:, None] * inv[None, :]
    cos = np.cos(ang).T
    sin = np.sin(ang).T
    cos64 = np.concatenate([cos, cos], 0)
    sin64 = np.concatenate([-sin, sin], 0)
    return (np.concatenate([cos64, cos64], 0).astype(np.float32),
            np.concatenate([sin64, sin64], 0).astype(np.float32))


def xpos_tables(pos, is_q):
    j32 = np.arange(32, dtype=np.float64)
    invf = 10000.0 ** (-2 * j32 / 64.0)
    base = (2 * j32 + 25.6) / 89.6
    ang = pos[:, None] * invf[None, :]
    power = (pos - 2048.0) / 128.0
    scale = base[None, :] ** power[:, None]
    if not is_q:
        scale = 1.0 / scale
    cos = (np.cos(ang) * scale).T
    sinp = (np.sin(ang) * scale).T
    c64 = np.concatenate([cos, cos], 0)
    s64 = np.concatenate([-sinp, sinp], 0)
    return (np.tile(c64, (2, 1)).astype(np.float32),
            np.tile(s64, (2, 1)).astype(np.float32))


def l0f_layer_weights(d, i):
    DI, DIP = 2730, 2816
    Wqkv = d["L0_ffn_Wqkv"]
    Wq = Wqkv[i][:, 0:1024]; Wk = Wqkv[i][:, 1024:2048]; Wv = Wqkv[i][:, 2048:3072]
    mh = lambda W: np.ascontiguousarray(np.asarray(W).reshape(8, 128, 2, 512).transpose(2, 1, 0, 3).reshape(2, 128, 4096))
    W1A = d["L0_ffn_W1"]; b1A = d["L0_ffn_b1"]; W2A = d["L0_ffn_W2"]; b2A = d["L0_ffn_b2"]
    W1a = np.zeros((1024, DIP), np.float32); W1a[:, :DI] = W1A[i][:, :DI]
    W1g = np.zeros((1024, DIP), np.float32); W1g[:, :DI] = W1A[i][:, DI:]
    A = W1a.reshape(8, 128, 22, 128).transpose(2, 1, 0, 3)
    G = W1g.reshape(8, 128, 22, 128).transpose(2, 1, 0, 3)
    w1s = np.ascontiguousarray(np.stack([A, G], 3).reshape(22, 128, 2048))
    W2p = np.zeros((DIP, 1024), np.float32); W2p[:DI] = W2A[i]
    w2s = np.ascontiguousarray(W2p.reshape(22, 128, 1024).transpose(1, 0, 2).reshape(128, 22528))
    b1a = np.zeros(DIP, np.float32); b1a[:DI] = b1A[i][:DI]
    b1g = np.zeros(DIP, np.float32); b1g[:DI] = b1A[i][DI:]
    b1c = np.ascontiguousarray(np.concatenate([b1a.reshape(22, 128).T, b1g.reshape(22, 128).T], 1))
    qs = d["L0_ffn_q_scale"][i]; ks = d["L0_ffn_k_scale"][i]
    qsw = np.concatenate([qs[32:], qs[:32]]); ksw = np.concatenate([ks[32:], ks[:32]])
    qsc4 = np.ascontiguousarray(np.stack([np.tile(qs, 2), np.tile(qsw, 2),
                                          np.tile(ks, 2), np.tile(ksw, 2)], 1).astype(np.float32))
    return dict(
        wqh=mh(Wq), wkh=mh(Wk), wvh=mh(Wv),
        wot=np.ascontiguousarray(np.asarray(d["L0_ffn_Wo"][i]).reshape(16, 64, 1024).transpose(1, 0, 2).reshape(64, 16384)),
        w1s=w1s, w2s=w2s, b1c=b1c,
        b2c=np.ascontiguousarray(b2A[i].reshape(8, 128).T),
        gbc=np.ascontiguousarray(np.concatenate([d["L0_ffn_ln_g"][i].reshape(8, 128).T,
                                                 d["L0_ffn_ln_b"][i].reshape(8, 128).T], 1)),
        qsc4=qsc4,
    )


def l0f_core_tables(h):
    T0 = 2048 * h
    qcs, qss = xpos_tables(np.arange(T0 - 128, T0 + 2048).astype(np.float64), True)
    kcs, kss = xpos_tables(np.arange(T0 - 384, T0 + 2048).astype(np.float64), False)
    ii = np.arange(128)[:, None]; jj = np.arange(384)[None, :]
    band = (jj - ii >= 0) & (jj - ii <= 256)
    mb = np.zeros((4, 128, 384), np.float32)
    for v in range(4):
        ok = band.copy()
        if h == 0:
            if v == 1:
                ok &= (jj >= 256)
            if v == 2:
                ok &= (jj >= 128)
        mb[v] = np.where(ok, 0.0, -1e9)
    mb4 = np.ascontiguousarray(mb.transpose(1, 0, 2).reshape(128, 4 * 384).astype(BFH))
    return dict(qcs=qcs, qss=qss, kcs=kcs, kss=kss, mb4=mb4,
                shz=np.full((128, 1), float(h), np.float32),
                idn=np.eye(128, dtype=BFH), idnf=np.eye(128, dtype=np.float32),
                onec=np.ones((128, 1), BFH), onesr=np.ones((1, 128), BFH))


def l0f_stage_z(zlist, c, dt):
    b, h = c // 2, c % 2
    zloc = zlist[c]
    zv = zloc.reshape(128, 8, 2048)
    if h == 1:
        sib = zlist[2 * b].reshape(128, 8, 2048)
        zhalo = np.ascontiguousarray(sib[:, :, 1536:2048].reshape(128, 4096))
    else:
        zhalo = np.zeros((128, 4096), dt)
    zres = np.zeros((17, 128, 1024), dt)
    if h == 1:
        zres[0] = sib[:, :, 1920:2048].reshape(128, 1024)
    for qt in range(1, 17):
        zres[qt] = zv[:, :, 128 * (qt - 1):128 * qt].reshape(128, 1024)
    return zloc, np.ascontiguousarray(zhalo), np.ascontiguousarray(zres)


def l3_rope_tables(pos):
    j = np.arange(0, 64, 2, np.float64)
    inv = 10000.0 ** (j / 64.0)
    ang = pos[:, None] / inv[None, :]
    cos = np.cos(ang).T
    sin = np.sin(ang).T
    cos64 = np.concatenate([cos, cos], 0)
    sin64 = np.concatenate([-sin, sin], 0)
    return (np.concatenate([cos64, cos64], 0).astype(np.float32),
            np.concatenate([sin64, sin64], 0).astype(np.float32))


def kernel(**d):
    d = {k: np.asarray(v) for k, v in d.items()}
    x0 = d["L0_attn_hidden_states"].astype(np.float32)

    Wq, Wk, Wv, Wo = d["L0_attn_Wq"], d["L0_attn_Wk"], d["L0_attn_Wv"], d["L0_attn_Wo"]
    qi_ = np.arange(128)[:, None]; kj_ = np.arange(128)[None, :]
    mprev = np.where(kj_ >= qi_, 0.0, -1e9).astype(BFH)
    mown = np.where(kj_ <= qi_, 0.0, -1e9).astype(BFH)
    common = dict(
        wqm=mchunks(Wq), wkm=mchunks(Wk),
        wvg=np.ascontiguousarray(Wv.reshape(4, 256, 8, 128).transpose(0, 3, 2, 1).reshape(4, 128, 2048)),
        wot=np.ascontiguousarray(Wo.T.reshape(16, 64, 1024).transpose(1, 0, 2).reshape(64, 16384)),
        bqc=np.ascontiguousarray(d["L0_attn_bq"].reshape(8, 128).T),
        bkc=np.ascontiguousarray(d["L0_attn_bk"].reshape(8, 128).T),
        bvr=np.ascontiguousarray(d["L0_attn_bv"][None, :]),
        bor=np.ascontiguousarray(d["L0_attn_bo"][None, :]),
        mprev=mprev, mown=mown,
        idn=np.eye(128, dtype=BFH), onesr=np.ones((1, 128), BFH),
    )
    in_maps = []
    for c in range(8):
        b, h = c // 2, c % 2
        T0 = 2048 * h
        qcos, qsin = l0a_rope_tables(np.arange(T0, T0 + 2048))
        kcos, ksin = l0a_rope_tables(np.maximum(np.arange(T0 - 128, T0 + 2048), 0))
        xhk = x0[b, T0 - 128:T0] if h == 1 else np.zeros((128, 1024), np.float32)
        m = dict(common)
        m.update(xloc=pm(x0[b, T0:T0 + 2048]), xhk=np.ascontiguousarray(xhk),
                 qcos=qcos, qsin=qsin, kcos=kcos, ksin=ksin,
                 amsk0=(mprev if h == 1 else np.full((128, 128), -1e9, np.float32).astype(BFH)))
        in_maps.append(m)
    nc_l0a = build_L0a()
    res = run_bass_kernel_spmd(nc_l0a, in_maps, core_ids=list(range(8)))
    ypart = [np.ascontiguousarray(np.asarray(res.results[c]["ypart"])) for c in range(8)]
    zcur = [np.ascontiguousarray(np.asarray(res.results[c]["z0"])) for c in range(8)]

    tabs = [l0f_core_tables(c % 2) for c in range(8)]
    nc_f0 = build_L0f(True)
    nc_fr = build_L0f(False)
    zdt = BFH
    for li in range(6):
        lw = l0f_layer_weights(d, li)
        in_maps = []
        for c in range(8):
            zloc, zhalo, zres = l0f_stage_z(zcur, c, zdt)
            m = dict(lw); m.update(tabs[c])
            m.update(zloc=zloc, zhalo=zhalo, zres=zres)
            in_maps.append(m)
        nc = nc_f0 if li == 0 else nc_fr
        res = run_bass_kernel_spmd(nc, in_maps, core_ids=list(range(8)))
        zcur = [np.ascontiguousarray(np.asarray(res.results[c]["zout"])) for c in range(8)]
        zdt = np.float32

    nc_cb = build_comb()
    in_maps = [dict(ain=ypart[c], bin=pm(ifmz(zcur[c]))) for c in range(8)]
    res = run_bass_kernel_spmd(nc_cb, in_maps, core_ids=list(range(8)))
    x1 = np.zeros((4, 4096, 1024), np.float32)
    for c in range(8):
        b, h = c // 2, c % 2
        x1[b, 2048 * h:2048 * (h + 1)] = ipm(np.asarray(res.results[c]["xout"]))

    w1 = d["L1_ffn_w1"]; w3 = d["L1_ffn_w3"]; w2 = d["L1_ffn_w2"]
    b1 = d["L1_ffn_b1"]; b3 = d["L1_ffn_b3"]
    sw1 = d["L1_ffn_sw1"]; sw3 = d["L1_ffn_sw3"]; sw2 = d["L1_ffn_sw2"]
    sb1 = d["L1_ffn_sb1"]; sb3 = d["L1_ffn_sb3"]
    w1x = np.concatenate([w1, sw1[:, 0:512][None], sw1[:, 512:1024][None]], 0)
    w3x = np.concatenate([w3, sw3[:, 0:512][None], sw3[:, 512:1024][None]], 0)
    w2x = np.concatenate([w2, sw2[0:512][None], sw2[512:1024][None]], 0)
    b1x = np.concatenate([b1, sb1[0:512][None], sb1[512:1024][None]], 0)
    b3x = np.concatenate([b3, sb3[0:512][None], sb3[512:1024][None]], 0)
    cvw = d["L1_attn_conv_w"].reshape(8, 128, 3).transpose(1, 0, 2).reshape(128, 24)
    common = dict(
        gw1=d["L1_attn_gate_w1"],
        gb1=np.ascontiguousarray(d["L1_attn_gate_b1"].reshape(2, 128).T),
        gw2=d["L1_attn_gate_w2"],
        gb2=np.ascontiguousarray(d["L1_attn_gate_b2"][:, None]),
        uw=d["L1_attn_U_w"], vw=d["L1_attn_V_w"],
        cvw=np.ascontiguousarray(cvw),
        ow=d["L1_attn_out_w"], obr=np.ascontiguousarray(d["L1_attn_out_b"][None, :]),
        gwf=d["L1_ffn_gw"], gbr=np.ascontiguousarray(d["L1_ffn_gb"][None, :]),
        w1x=np.ascontiguousarray(w1x), w3x=np.ascontiguousarray(w3x),
        w2x=np.ascontiguousarray(w2x), b1x=np.ascontiguousarray(b1x),
        b3x=np.ascontiguousarray(b3x),
        b2s=d["L1_ffn_b2"], sb2r=np.ascontiguousarray(d["L1_ffn_sb2"][None, :]),
        idn=np.eye(128, dtype=BFH), idnf=np.eye(128, dtype=np.float32),
        onesr=np.ones((1, 128), BFH), onesc=np.ones((128, 1), BFH),
    )
    common = {k: np.ascontiguousarray(v) for k, v in common.items()}
    in_maps = []
    for c in range(8):
        b, h = c // 2, c % 2
        T0 = 2048 * h
        xloc = x1[b, T0:T0 + 2048]
        xhalo = np.zeros((128, 1024), np.float32)
        if h == 1:
            xhalo[0] = x1[b, T0 - 1]
            xpfx = x1[b, 0:2048]
        else:
            xhalo[1] = x1[b, 2048]
            xpfx = np.zeros((2048, 1024), np.float32)
        m = dict(common)
        m.update(xloc=pm(xloc), xhalo=np.ascontiguousarray(xhalo), xpfx=pm(xpfx))
        in_maps.append(m)
    nc_l1 = build_L1()
    res = run_bass_kernel_spmd(nc_l1, in_maps, core_ids=list(range(8)))
    x2 = np.zeros((4, 4096, 1024), np.float32)
    for c in range(8):
        b, h = c // 2, c % 2
        x2[b, 2048 * h:2048 * (h + 1)] = ipm(np.asarray(res.results[c]["xout"]))

    common = dict(
        wqm=mchunks(d["L2_attn_W_q"]), wkm=mchunks(d["L2_attn_W_k"]), wam=mchunks(d["L2_attn_W_a"]),
        wvn=np.ascontiguousarray(d["L2_attn_W_v"].reshape(2, 512, 8, 128).transpose(0, 3, 2, 1).reshape(2, 128, 4096)),
        bqc=np.ascontiguousarray(d["L2_attn_b_q"].reshape(8, 128).T),
        bkc=np.ascontiguousarray(d["L2_attn_b_k"].reshape(8, 128).T),
        bac=np.ascontiguousarray(d["L2_attn_b_a"].reshape(8, 128).T),
        bvr=np.ascontiguousarray(d["L2_attn_b_v"][None, :]),
        gwT=np.ascontiguousarray(d["L2_ffn_gate_w"].T),
        bbr=np.ascontiguousarray(d["L2_ffn_balance_bias"][None, :]),
        w1t=np.ascontiguousarray(d["L2_ffn_w1"].transpose(0, 2, 1)),
        w2t=np.ascontiguousarray(d["L2_ffn_w2"].transpose(0, 2, 1)),
        idn=np.eye(128, dtype=BFH),
        onesr=np.ones((1, 128), BFH), onesc=np.ones((128, 1), BFH),
        mcau=np.ascontiguousarray(np.triu(np.ones((128, 128))).astype(BFH)),
    )
    in_maps = []
    for c in range(8):
        b, h = c // 2, c % 2
        T0 = 2048 * h
        m = dict(common)
        m.update(
            xloc=pm(x2[b, T0:T0 + 2048]),
            xoth=pm(x2[b, 2048 - T0:4096 - T0]),
            pmsk=np.full((128, 1), float(h), np.float32),
            smsk=np.full((128, 1), float(1 - h), np.float32),
        )
        in_maps.append(m)
    nc_l2 = build_L2()
    res = run_bass_kernel_spmd(nc_l2, in_maps, core_ids=list(range(8)))
    x3 = np.zeros((4, 4096, 1024), np.float32)
    for c in range(8):
        b, h = c // 2, c % 2
        x3[b, 2048 * h:2048 * (h + 1)] = ipm(np.asarray(res.results[c]["xout"]))

    Wqkv = d["L3_attn_W_qkv"]; Wo3 = d["L3_attn_W_o"]
    perm = np.zeros(64, np.int64)
    perm[:32] = np.arange(0, 64, 2)
    perm[32:] = np.arange(1, 64, 2)

    def permute_heads(W):
        Wr = W.reshape(16, 64, 1024)
        return Wr[:, perm, :].reshape(1024, 1024)
    Wq3 = permute_heads(Wqkv[0:1024])
    Wk3 = permute_heads(Wqkv[1024:2048])
    Wv3 = Wqkv[2048:3072]
    wvg = np.ascontiguousarray(Wv3.reshape(4, 256, 8, 128).transpose(0, 3, 2, 1).reshape(4, 128, 2048))
    WoT = np.ascontiguousarray(Wo3.T)
    wog = np.ascontiguousarray(WoT.reshape(4, 4, 64, 1024).transpose(0, 2, 1, 3).reshape(4, 64, 4096))
    common = dict(
        wqm=mchunks(Wq3), wkm=mchunks(Wk3), wvg=wvg, wog=wog,
        lngr=np.ascontiguousarray(d["L3_ffn_ln_g"][None, :]),
        lnbr=np.ascontiguousarray(d["L3_ffn_ln_b"][None, :]),
        idn=np.eye(128, dtype=BFH), onesr=np.ones((1, 128), BFH),
        mcadd=np.ascontiguousarray(np.where(np.tril(np.ones((128, 128), bool)), 0.0, -1e9).astype(BFH)),
    )
    in_maps = []
    for c in range(8):
        b, h = c // 2, c % 2
        T0 = 2048 * h
        own_pos = np.arange(T0, T0 + 2048)
        oth_pos = np.arange(2048 - T0, 4096 - T0)
        qcos, qsin = l3_rope_tables(own_pos)
        kcos, ksin = l3_rope_tables(np.concatenate([oth_pos, own_pos]))
        m = dict(common)
        m.update(
            xloc=pm(x3[b, T0:T0 + 2048]),
            xoth=pm(x3[b, 2048 - T0:4096 - T0]),
            qcos=qcos, qsin=qsin, kcos=kcos, ksin=ksin,
            amsk=np.full((128, 1), 0.0 if h == 1 else -1e9, np.float32),
        )
        in_maps.append(m)
    nc_l3 = build_L3()
    res = run_bass_kernel_spmd(nc_l3, in_maps, core_ids=list(range(8)))
    x4 = np.zeros((4, 4096, 1024), np.float32)
    for c in range(8):
        b, h = c // 2, c % 2
        x4[b, 2048 * h:2048 * (h + 1)] = ipm(np.asarray(res.results[c]["xout"]))
    return x4


# version 03 (final)
# speedup vs baseline: 1.0001x; 1.0001x over previous
import numpy as np
import contextlib
import concourse.bass as bass
import concourse.mybir as mybir

F32 = mybir.dt.float32
BF16 = mybir.dt.bfloat16
U32 = mybir.dt.uint32
AF = mybir.ActivationFunctionType
OP = mybir.AluOpType
AX = mybir.AxisListType

D = 1024
NCH = 8
NT = 16
TOK = 2048


class Ctx:
    def __init__(self, nc, es):
        self.nc = nc
        self.es = es
        self.block = es.enter_context(nc.Block())
        self.sems = {}
        self.vals = {}

    def sem(self, name):
        if name not in self.sems:
            self.sems[name] = self.es.enter_context(self.nc.semaphore(name))
            self.vals[name] = 0
        return self.sems[name]

    def val(self, name):
        self.sem(name)
        return self.vals[name]

    def bump(self, name, amt=1):
        self.sem(name)
        self.vals[name] += amt
        return self.vals[name]

    def v(self, fn): self.block.vector(fn)
    def s(self, fn): self.block.scalar(fn)
    def t(self, fn): self.block.tensor(fn)
    def g(self, fn): self.block.gpsimd(fn)
    def sy(self, fn): self.block.sync(fn)

    def sb(self, name, shape, dt=F32):
        return self.es.enter_context(self.nc.sbuf_tensor(name, shape, dt))

    def ps(self, name, shape, dt=F32):
        return self.es.enter_context(self.nc.psum_tensor(name, shape, dt))


def build_L1():
    nc = bass.Bass("TRN2", target_bir_lowering=False)
    i_x = nc.dram_tensor("xloc", [128, NT * D], F32, kind="ExternalInput").ap()
    i_xh = nc.dram_tensor("xhalo", [128, D], F32, kind="ExternalInput").ap()
    i_xp = nc.dram_tensor("xpfx", [128, NT * D], F32, kind="ExternalInput").ap()
    i_gw1 = nc.dram_tensor("gw1", [D, 256], F32, kind="ExternalInput").ap()
    i_gb1 = nc.dram_tensor("gb1", [128, 2], F32, kind="ExternalInput").ap()
    i_gw2 = nc.dram_tensor("gw2", [256, 32], F32, kind="ExternalInput").ap()
    i_gb2 = nc.dram_tensor("gb2", [32, 1], F32, kind="ExternalInput").ap()
    i_uw = nc.dram_tensor("uw", [D, 32], F32, kind="ExternalInput").ap()
    i_vw = nc.dram_tensor("vw", [D, 32], F32, kind="ExternalInput").ap()
    i_cvw = nc.dram_tensor("cvw", [128, 24], F32, kind="ExternalInput").ap()
    i_ow = nc.dram_tensor("ow", [32, D], F32, kind="ExternalInput").ap()
    i_obr = nc.dram_tensor("obr", [1, D], F32, kind="ExternalInput").ap()
    i_gw = nc.dram_tensor("gwf", [D, 32], F32, kind="ExternalInput").ap()
    i_gbr = nc.dram_tensor("gbr", [1, 32], F32, kind="ExternalInput").ap()
    i_w1 = nc.dram_tensor("w1x", [34, D, 512], F32, kind="ExternalInput").ap()
    i_w3 = nc.dram_tensor("w3x", [34, D, 512], F32, kind="ExternalInput").ap()
    i_w2 = nc.dram_tensor("w2x", [34, 512, D], F32, kind="ExternalInput").ap()
    i_b1 = nc.dram_tensor("b1x", [34, 512], F32, kind="ExternalInput").ap()
    i_b3 = nc.dram_tensor("b3x", [34, 512], F32, kind="ExternalInput").ap()
    i_b2 = nc.dram_tensor("b2s", [32, D], F32, kind="ExternalInput").ap()
    i_sb2 = nc.dram_tensor("sb2r", [1, D], F32, kind="ExternalInput").ap()
    i_idn = nc.dram_tensor("idn", [128, 128], BF16, kind="ExternalInput").ap()
    i_onr = nc.dram_tensor("onesr", [1, 128], BF16, kind="ExternalInput").ap()
    i_onc = nc.dram_tensor("onesc", [128, 1], BF16, kind="ExternalInput").ap()
    i_idnf = nc.dram_tensor("idnf", [128, 128], F32, kind="ExternalInput").ap()
    o_x = nc.dram_tensor("xout", [128, NT * D], F32, kind="ExternalOutput").ap()
    o_wk = nc.dram_tensor("dbg_wk", [128, NCH * TOK], BF16, kind="ExternalOutput").ap()
    o_xn1 = nc.dram_tensor("dbg_xn1", [128, NCH * TOK], BF16, kind="ExternalOutput").ap()
    o_xn2 = nc.dram_tensor("dbg_xn2", [128, NCH * TOK], BF16, kind="ExternalOutput").ap()
    o_cb = nc.dram_tensor("dbg_cb", [128, NT * 8], F32, kind="ExternalOutput").ap()
    o_q = nc.dram_tensor("dbg_q", [128, NCH * TOK], BF16, kind="ExternalOutput").ap()
    o_xn = nc.dram_tensor("dbg_xn", [128, NCH * 2050], BF16, kind="ExternalOutput").ap()
    o_xa = nc.dram_tensor("dbg_xattn", [128, NT * D], F32, kind="ExternalOutput").ap()
    o_cw = nc.dram_tensor("dbg_cw", [128, NT * 32], F32, kind="ExternalOutput").ap()
    o_ss = nc.dram_tensor("dbg_ss", [32, TOK], BF16, kind="ExternalOutput").ap()
    o_gt = nc.dram_tensor("dbg_gt", [32, TOK], BF16, kind="ExternalOutput").ap()
    o_uu = nc.dram_tensor("dbg_uu", [32, TOK], BF16, kind="ExternalOutput").ap()
    o_vv = nc.dram_tensor("dbg_vv", [32, TOK], BF16, kind="ExternalOutput").ap()
    o_s0 = nc.dram_tensor("dbg_s0", [32, 1], F32, kind="ExternalOutput").ap()
    o_xsr = nc.dram_tensor("dbg_xsr", [1, D], BF16, kind="ExternalOutput").ap()
    o_xsf = nc.dram_tensor("dbg_xsf", [128, 8], BF16, kind="ExternalOutput").ap()
    o_s0r = nc.dram_tensor("dbg_s0r", [1, 32], F32, kind="ExternalOutput").ap()
    o_pxb = nc.dram_tensor("dbg_pxb", [128, D], BF16, kind="ExternalOutput").ap()
    o_onc = nc.dram_tensor("dbg_onc", [128, 1], BF16, kind="ExternalOutput").ap()

    es = contextlib.ExitStack()
    with es:
        c = Ctx(nc, es)
        W = 2050
        X = c.sb("X", [128, NT * D], F32)
        XN = c.sb("XN", [128, NCH * W], BF16)
        XNT = c.sb("XNT", [128, 2 * D], BF16)
        TRASH_AP = None
        MV = c.sb("MV", [128, 36], F32)
        RS = c.sb("RS", [128, 18], F32)
        SQ = c.sb("SQ", [128, 18 * 2], F32)
        PXT = c.sb("PXT", [128, D], F32)
        PXB = c.sb("PXB", [128, D], BF16)
        XSR = c.sb("XSR", [1, D], BF16)
        XSF = c.sb("XSF", [128, 8], BF16)
        S0R = c.sb("S0R", [1, 32], F32)
        S0 = c.sb("S0", [32, 1], F32)
        G1W = c.sb("G1W", [128, NCH * 256], BF16)
        GB1 = c.sb("GB1", [128, 2], F32)
        G2W = c.sb("G2W", [128, 2 * 32], BF16)
        GB2 = c.sb("GB2", [32, 1], F32)
        UW = c.sb("UW", [128, NCH * 32], BF16)
        VW = c.sb("VW", [128, NCH * 32], BF16)
        CVW = c.sb("CVW", [128, 24], F32)
        OW = c.sb("OW", [32, D], BF16)
        OBR = c.sb("OBR", [1, D], BF16)
        GWr = c.sb("GWr", [128, NCH * 32], BF16)
        GBr = c.sb("GBr", [1, 32], BF16)
        H1S = c.sb("H1S", [128, 2 * 512], BF16)
        GT = c.sb("GT", [32, TOK], BF16)
        UU = c.sb("UU", [32, TOK], BF16)
        VV = c.sb("VV", [32, TOK], BF16)
        SS = c.sb("SS", [32, TOK], BF16)
        T1G = VV
        GLBB = GT
        CT1 = c.sb("CT1", [128, 512], F32)
        CT2 = CT1
        CT3 = c.sb("CT3", [128, 512], BF16)
        SC = c.sb("SC", [128, 32], F32)
        NMX = c.sb("NMX", [128, 1], F32)
        DEN = c.sb("DEN", [128, 1], F32)
        RDEN = c.sb("RDEN", [128, 1], F32)
        PS6 = c.sb("PS6", [128, 8], F32)
        GS = c.sb("GS", [128, 8], F32)
        GS8 = c.sb("GS8", [128, 8], F32)
        IDX8 = c.sb("IDX8", [128, 8], U32)
        KM = c.sb("KM", [128, 32], mybir.dt.uint8)
        KG = c.sb("KG", [128, 8], mybir.dt.uint8)
        MSK = c.sb("MSK", [128, 32], F32)
        NEG = c.sb("NEG", [128, 32], F32)
        M8 = c.sb("M8", [128, 8], F32)
        IND = c.sb("IND", [128, 32], F32)
        CWall = c.sb("CWall", [128, NT * 32], F32)
        CWT = c.sb("CWT", [32, NT * 128], BF16)
        W1E = c.sb("W1E", [128, 2 * NCH * 512], BF16)
        W3E = c.sb("W3E", [128, 2 * NCH * 512], BF16)
        W2E = c.sb("W2E", [128, 2 * 4 * D], BF16)
        B1E = c.sb("B1E", [1, 2 * 512], BF16)
        B3E = c.sb("B3E", [1, 2 * 512], BF16)
        B2S = c.sb("B2S", [32, D], BF16)
        SB2R = c.sb("SB2R", [1, D], BF16)
        SA = c.sb("SA", [128, 512], BF16)
        HS = c.sb("HS", [128, 512], BF16)
        HF = c.sb("HF", [128, 4 * 128], BF16)
        IDN = c.sb("IDN", [128, 128], BF16)
        IDNF = c.sb("IDNF", [128, 128], F32)
        ONR = c.sb("ONR", [1, 128], BF16)
        ONC = c.sb("ONC", [128, 1], BF16)
        EPS = c.sb("EPS", [128, 1], F32)
        PP0 = c.ps("PP0", [128, 512], F32)
        PP1 = c.ps("PP1", [128, 512], F32)
        PT = c.ps("PT", [128, 1024], BF16)
        YP = c.ps("YP", [128, 1024], F32)
        SP = c.ps("SP", [128, 512], F32)
        PXS = c.ps("PXS", [128, 1024], F32)

        sem, bump, val = c.sem, c.bump, c.val

        def vedge(v, instr):
            n = bump("V")
            instr.then_inc(sem("V"), 1)
            v.wait_ge(sem("V"), n)

        def sedge(s_, instr):
            n = bump("S")
            instr.then_inc(sem("S"), 1)
            s_.wait_ge(sem("S"), n)

        def ph0_sy(sy):
            sy.dma_start(out=X[:], in_=i_x).then_inc(sem("ldx"), 16)
            sy.dma_start(out=PXT[:], in_=i_xh).then_inc(sem("ldx"), 16)
            sy.dma_start(out=GB1[:], in_=i_gb1).then_inc(sem("ldx"), 16)
            sy.dma_start(out=GB2[:], in_=i_gb2).then_inc(sem("ldx"), 16)
            sy.dma_start(out=CVW[:], in_=i_cvw).then_inc(sem("ldx"), 16)
        c.sy(ph0_sy)
        LDX = bump("ldx", 5 * 16)

        def ph0_g(g):
            for dst, src in [(IDN, i_idn), (IDNF, i_idnf), (ONR, i_onr), (ONC, i_onc), (OW, i_ow),
                             (OBR, i_obr), (GBr, i_gbr), (B2S, i_b2), (SB2R, i_sb2)]:
                g.dma_start(out=dst[:], in_=src).then_inc(sem("ldw"), 16)
            for dst, src, m in [(G1W, i_gw1, 256), (G2W, i_gw2, 32), (UW, i_uw, 32),
                                (VW, i_vw, 32), (GWr, i_gw, 32)]:
                g.dma_start(out=dst[:].rearrange("p (c m) -> p c m", m=m),
                            in_=src.rearrange("(c p) m -> p c m", p=128)).then_inc(sem("ldw"), 16)
            g.memset(NEG[:], -1e9)
            g.memset(EPS[:], 1e-5).then_inc(sem("ldz"), 1)
        c.g(ph0_g)
        LDW = bump("ldw", 14 * 16); LDZ = bump("ldz", 1)

        def get_xt(i):
            return X[:, D * i:D * (i + 1)] if i < 16 else PXT[:]

        def ph1_ssq(s_):
            s_.wait_ge(sem("ldx"), LDX)
            s_.wait_ge(sem("ldw"), LDW); s_.wait_ge(sem("ldz"), 1)
            for i in range(17):
                s_.activation(out=XNT[:, 0:D], in_=get_xt(i), func=AF.Square,
                              accum_out=SQ[:, 18 + i:19 + i])
            last = s_.copy(RS[:, 17:18], SQ[:, 34:35])
            last.then_inc(sem("S"), 1)
        c.s(ph1_ssq)
        s_sq = bump("S")

        def ph1_vstats(v):
            v.wait_ge(sem("ldx"), LDX)
            v.wait_ge(sem("ldw"), LDW); v.wait_ge(sem("ldz"), 1)
            for i in range(17):
                v.reduce_sum(SQ[:, i:i + 1], get_xt(i), axis=AX.X)
            i1 = v.tensor_scalar(MV[:, 0:17], SQ[:, 0:17], 1.0 / D, None, OP.mult)
            vedge(v, i1)
            v.wait_ge(sem("S"), s_sq)
            i2 = v.tensor_scalar(MV[:, 18:35], SQ[:, 18:35], 1.0 / D, None, OP.mult)
            vedge(v, i2)
            i3 = v.tensor_tensor(RS[:, 0:17], MV[:, 0:17], MV[:, 0:17], OP.mult)
            vedge(v, i3)
            v.tensor_tensor(MV[:, 18:35], MV[:, 18:35], RS[:, 0:17], OP.subtract).then_inc(sem("V"), 1)
        c.v(ph1_vstats)
        v_stats = val("V") + 1
        bump("V")

        def ph1_srs(s_):
            s_.wait_ge(sem("V"), v_stats)
            i1 = s_.activation(out=RS[:, 0:17], in_=MV[:, 18:35], func=AF.Ln, bias=EPS[:])
            sedge(s_, i1)
            s_.activation(out=RS[:, 0:17], in_=RS[:, 0:17], func=AF.Exp, scale=-0.5).then_inc(sem("S"), 1)
        c.s(ph1_srs)
        s_rs = bump("S")

        v_apply = [0] * 17
        t_tr = [0] * 17
        v_evac = [0] * 17
        XNv = XN[:].rearrange("p (ch w) -> p ch w", w=W)
        PTv = PT[:].rearrange("p (ch w) -> p ch w", w=128)
        for i in range(17):
            buf = i % 2

            def f_ap(v, i=i, buf=buf):
                v.wait_ge(sem("S"), s_rs)
                if i >= 2:
                    v.wait_ge(sem("T"), t_tr[i - 2])
                v.tensor_scalar(XNT[:, D * buf:D * (buf + 1)], get_xt(i),
                                MV[:, i:i + 1], RS[:, i:i + 1],
                                OP.subtract, OP.mult).then_inc(sem("V"), 1)
            c.v(f_ap)
            v_apply[i] = bump("V")

            def f_tr(t, i=i, buf=buf):
                t.wait_ge(sem("V"), v_apply[i])
                if i == 0:
                    t.wait_ge(sem("ldw"), LDW); t.wait_ge(sem("ldz"), 1)
                if i >= 1:
                    t.wait_ge(sem("V"), v_evac[i - 1])
                for ch in range(NCH):
                    mm = t.transpose(PT[:, 128 * ch:128 * (ch + 1)],
                                     XNT[:, D * buf + 128 * ch:D * buf + 128 * (ch + 1)], IDN[:])
                mm.then_inc(sem("T"), 1)
            c.t(f_tr)
            t_tr[i] = bump("T")

            def f_ev(v, i=i):
                v.wait_ge(sem("T"), t_tr[i])
                if i < 16:
                    v.tensor_copy(XNv[:, :, 1 + 128 * i:1 + 128 * (i + 1)], PTv).then_inc(sem("V"), 1)
                else:
                    v.tensor_copy(XNv[:, :, 0:1], PTv[:, :, 0:1])
                    v.tensor_copy(XNv[:, :, W - 1:W], PTv[:, :, 1:2]).then_inc(sem("V"), 1)
            c.v(f_ev)
            v_evac[i] = bump("V")

        def dbg1_sy(sy):
            sy.wait_ge(sem("V"), v_evac[16])
            sy.dma_start(out=o_xn, in_=XN[:]).then_inc(sem("st"), 16)
        c.sy(dbg1_sy)
        bump("st", 16)

        def xn(ch, a, b, off=1):
            return XN[:, W * ch + off + a:W * ch + off + b]

        pv_ap = [0] * 16
        pxs_t = [0] * 16
        for i in range(16):
            def f_dma(sy, i=i):
                if i == 0:
                    sy.wait_ge(sem("V"), v_apply[16])
                if i >= 1:
                    sy.wait_ge(sem("V"), pv_ap[i - 1])
                sy.dma_start(out=PXT[:],
                             in_=i_xp[:, D * i:D * (i + 1)]).then_inc(sem("ldp"), 16)
            c.sy(f_dma)
            ld_i = bump("ldp", 16)

            def f_s0(s_, ld_i=ld_i, i=i):
                if i == 0:
                    s_.wait_ge(sem("T"), t_tr[16])
                s_.wait_ge(sem("ldp"), ld_i)
                s_.activation(out=XNT[:, 0:D], in_=PXT[:], func=AF.Square,
                              accum_out=SQ[:, 35:36])
                s_.copy(RS[:, 17:18], SQ[:, 35:36]).then_inc(sem("S"), 1)
            c.s(f_s0)
            s_q = bump("S")

            def f_v1(v, ld_i=ld_i, s_q=s_q):
                v.wait_ge(sem("ldp"), ld_i)
                v.wait_ge(sem("S"), s_q)
                i0 = v.reduce_sum(SQ[:, 17:18], PXT[:], axis=AX.X)
                vedge(v, i0)
                i1 = v.tensor_scalar(MV[:, 17:18], SQ[:, 17:18], 1.0 / D, None, OP.mult)
                i2 = v.tensor_scalar(MV[:, 35:36], SQ[:, 35:36], 1.0 / D, None, OP.mult)
                vedge(v, i2)
                i3 = v.tensor_tensor(RS[:, 17:18], MV[:, 17:18], MV[:, 17:18], OP.mult)
                vedge(v, i3)
                v.tensor_tensor(MV[:, 35:36], MV[:, 35:36], RS[:, 17:18], OP.subtract).then_inc(sem("V"), 1)
            c.v(f_v1)
            v_st = bump("V")

            def f_s(s_, v_st=v_st):
                s_.wait_ge(sem("V"), v_st)
                i1 = s_.activation(out=RS[:, 17:18], in_=MV[:, 35:36], func=AF.Ln, bias=EPS[:])
                sedge(s_, i1)
                s_.activation(out=RS[:, 17:18], in_=RS[:, 17:18], func=AF.Exp, scale=-0.5).then_inc(sem("S"), 1)
            c.s(f_s)
            s_i = bump("S")

            def f_v2(v, s_i=s_i, i=i):
                v.wait_ge(sem("S"), s_i)
                if i >= 1:
                    v.wait_ge(sem("T"), pxs_t[i - 1])
                v.tensor_scalar(PXB[:], PXT[:], MV[:, 17:18], RS[:, 17:18],
                                OP.subtract, OP.mult).then_inc(sem("V"), 1)
            c.v(f_v2)
            pv_ap[i] = bump("V")

            def f_t(t, i=i):
                t.wait_ge(sem("V"), pv_ap[i])
                t.matmul(PXS[0:1, 0:512], ONC[:], PXB[:, 0:512], start=(i == 0), stop=(i == 15))
                mm = t.matmul(PXS[0:1, 512:1024], ONC[:], PXB[:, 512:1024], start=(i == 0), stop=(i == 15))
                mm.then_inc(sem("T"), 1)
            c.t(f_t)
            pxs_t[i] = bump("T")
        t_pxs = val("T")

        def ph2c_v(v):
            v.wait_ge(sem("T"), t_pxs)
            v.tensor_copy(XSR[:], PXS[0:1, :]).then_inc(sem("V"), 1)
        c.v(ph2c_v)
        v_xsr = bump("V")

        def ph2c_t(t):
            t.wait_ge(sem("V"), v_xsr)
            for ch in range(NCH):
                mm = t.transpose(PT[:, 2 * ch:2 * ch + 1],
                                 XSR[:, 128 * ch:128 * (ch + 1)], IDN[0:1, 0:1])
            mm.then_inc(sem("T"), 1)
        c.t(ph2c_t)
        t_xf = bump("T")

        def ph2c_v2(v):
            v.wait_ge(sem("T"), t_xf)
            v.tensor_copy(XSF[:], PT[:].rearrange("p (a b) -> p a b", b=2)[:, 0:8, 0], allow_slow_non_contiguous=True) if False else None
            v.tensor_copy(XSF[:].rearrange("p (a b) -> p a b", b=1), PT[:].rearrange("p (a b) -> p a b", b=2)[:, 0:8, 0:1]).then_inc(sem("V"), 1)
        c.v(ph2c_v2)
        v_xsf = bump("V")

        def ph2c_t2(t):
            t.wait_ge(sem("V"), v_xsf)
            for ch in range(NCH):
                mm = t.matmul(SP[0:1, 0:32], XSF[:, ch:ch + 1], VW[:, 32 * ch:32 * (ch + 1)],
                              start=(ch == 0), stop=(ch == 7))
            mm.then_inc(sem("T"), 1)
        c.t(ph2c_t2)
        t_s0r = bump("T")

        def ph2c_v3(v):
            v.wait_ge(sem("T"), t_s0r)
            v.tensor_copy(S0R[:], SP[0:1, 0:32]).then_inc(sem("V"), 1)
        c.v(ph2c_v3)
        v_s0r = bump("V")

        def ph2c_t3(t):
            t.wait_ge(sem("V"), v_s0r)
            t.transpose(SP[0:32, 0:1], S0R[:], IDNF[0:1, 0:1]).then_inc(sem("T"), 1)
        c.t(ph2c_t3)
        t_s0 = bump("T")

        def ph2c_v4(v):
            v.wait_ge(sem("T"), t_s0)
            v.tensor_copy(S0[:], SP[0:32, 0:1]).then_inc(sem("V"), 1)
        c.v(ph2c_v4)
        v_s0 = bump("V")

        ev_done = {}
        for s4 in range(4):
            a, b = 512 * s4, 512 * (s4 + 1)
            for m in range(2):
                PB = PP0 if m == 0 else PP1

                def f_h1(t, m=m, s4=s4, a=a, b=b, PB=PB):
                    if (s4, m) != (0, 0):
                        pass
                    key = ("h1e", s4 - 1, m)
                    if key in ev_done:
                        t.wait_ge(sem("S"), ev_done[key])
                    for ch in range(NCH):
                        mm = t.matmul(PB[:], G1W[:, 256 * ch + 128 * m:256 * ch + 128 * (m + 1)],
                                      xn(ch, a, b), start=(ch == 0), stop=(ch == 7))
                    mm.then_inc(sem("T"), 1)
                c.t(f_h1)
                t_h1 = bump("T")

                def f_h1e(s, m=m, t_h1=t_h1, s4=s4):
                    if (s4, m) == (0, 0):
                        s.wait_ge(sem("ldx"), LDX)
                    s.wait_ge(sem("T"), t_h1)
                    s.activation(out=H1S[:, 512 * m:512 * (m + 1)], in_=(PP0 if m == 0 else PP1)[:],
                                 func=AF.Relu, bias=GB1[:, m:m + 1]).then_inc(sem("S"), 1)
                c.s(f_h1e)
                ev_done[("h1e", s4, m)] = bump("S")

            def f_g(t, s4=s4):
                t.wait_ge(sem("S"), ev_done[("h1e", s4, 1)])
                if s4 == 0:
                    t.wait_ge(sem("V"), v_s0)
                else:
                    t.wait_ge(sem("V"), ev_done[("ve", s4 - 1)])
                for ch in range(2):
                    mm = t.matmul(SP[0:32, 0:512], G2W[:, 32 * ch:32 * (ch + 1)],
                                  H1S[:, 512 * ch:512 * (ch + 1)], start=(ch == 0), stop=(ch == 1))
                mm.then_inc(sem("T"), 1)
            c.t(f_g)
            t_g = bump("T")

            def f_ge(s, s4=s4, a=a, t_g=t_g):
                s.wait_ge(sem("T"), t_g)
                s.activation(out=GT[0:32, a:a + 512], in_=SP[0:32, 0:512],
                             func=AF.Sigmoid, bias=GB2[:]).then_inc(sem("S"), 1)
            c.s(f_ge)
            ev_done[("ge", s4)] = bump("S")

            for nm, WT, DST in (("u", UW, UU), ("v", VW, VV)):
                def f_uv(t, WT=WT, a=a, b=b, s4=s4, nm=nm):
                    t.wait_ge(sem("S" if nm == "u" else "V"),
                              ev_done[("ge", s4)] if nm == "u" else ev_done[("ue", s4)])
                    for ch in range(NCH):
                        mm = t.matmul(SP[0:32, 0:512], WT[:, 32 * ch:32 * (ch + 1)],
                                      xn(ch, a, b), start=(ch == 0), stop=(ch == 7))
                    mm.then_inc(sem("T"), 1)
                c.t(f_uv)
                t_uv = bump("T")

                def f_uve(v, DST=DST, a=a, t_uv=t_uv):
                    v.wait_ge(sem("T"), t_uv)
                    v.tensor_copy(DST[0:32, a:a + 512], SP[0:32, 0:512]).then_inc(sem("V"), 1)
                c.v(f_uve)
                ev_done[(nm + "e", s4)] = bump("V")

        def dbg2_sy(sy):
            sy.wait_ge(sem("V"), ev_done[("ve", 3)])
            sy.wait_ge(sem("V"), v_s0)
            sy.dma_start(out=o_gt, in_=GT[0:32, :]).then_inc(sem("st"), 16)
            sy.dma_start(out=o_uu, in_=UU[0:32, :]).then_inc(sem("st"), 16)
            sy.dma_start(out=o_vv, in_=VV[0:32, :]).then_inc(sem("st"), 16)
            sy.dma_start(out=o_s0, in_=S0[:]).then_inc(sem("st"), 16)
            sy.dma_start(out=o_xsr, in_=XSR[:]).then_inc(sem("st"), 16)
            sy.dma_start(out=o_xsf, in_=XSF[:]).then_inc(sem("st"), 16)
            sy.dma_start(out=o_s0r, in_=S0R[:]).then_inc(sem("st"), 16)
            sy.dma_start(out=o_pxb, in_=PXB[:]).then_inc(sem("st"), 16)
            sy.dma_start(out=o_onc, in_=ONC[:]).then_inc(sem("st"), 16)
        c.sy(dbg2_sy)
        st_dbg2 = bump("st", 144)

        def f_scan(v):
            v.wait_ge(sem("V"), ev_done[("ve", 3)])
            v.wait_ge(sem("V"), v_s0)
            v.wait_ge(sem("st"), st_dbg2)
            i1 = v.tensor_tensor_scan(SS[:], VV[:], VV[:], S0[:], OP.add, OP.bypass)
            i2 = v.tensor_tensor(T1G[:], GT[:], UU[:], OP.mult)
            vedge(v, i1)
            vedge(v, i2)
            v.tensor_tensor(GLBB[:], T1G[:], SS[:], OP.mult).then_inc(sem("V"), 1)
        c.v(f_scan)
        v_glb = bump("V")

        yg_ev = {}
        for i in range(NT):
            for n in range(2):
                PB = PP0 if n == 0 else PP1

                def f_yg(t, i=i, n=n, PB=PB):
                    if (i, n) == (0, 0):
                        t.wait_ge(sem("V"), v_glb)
                    key = (i - 1, n)
                    if key in yg_ev:
                        t.wait_ge(sem("V"), yg_ev[key])
                    t.matmul(PB[:], GLBB[:, 128 * i:128 * (i + 1)], OW[:, 512 * n:512 * (n + 1)],
                             start=True, stop=False)
                    t.matmul(PB[:], ONR[:], OBR[:, 512 * n:512 * (n + 1)],
                             start=False, stop=True).then_inc(sem("T"), 1)
                c.t(f_yg)
                t_yg = bump("T")

                def f_yge(v, i=i, n=n, PB=PB, t_yg=t_yg):
                    v.wait_ge(sem("T"), t_yg)
                    sl = X[:, D * i + 512 * n:D * i + 512 * (n + 1)]
                    v.tensor_tensor(sl, sl, PB[:], OP.add).then_inc(sem("V"), 1)
                c.v(f_yge)
                yg_ev[(i, n)] = bump("V")

        cv_ev = {}
        for ch in range(NCH):
            for s4 in range(4):
                def f_cv(v, ch=ch, s4=s4):
                    a = 512 * s4
                    i1 = v.tensor_scalar(CT1[:], xn(ch, a, a + 512, off=0), CVW[:, 3 * ch:3 * ch + 1], None, OP.mult)
                    vedge(v, i1)
                    i2 = v.scalar_tensor_tensor(CT2[:], xn(ch, a, a + 512, off=1), CVW[:, 3 * ch + 1:3 * ch + 2],
                                                CT1[:], OP.mult, OP.add)
                    vedge(v, i2)
                    v.scalar_tensor_tensor(CT3[:], xn(ch, a, a + 512, off=2), CVW[:, 3 * ch + 2:3 * ch + 3],
                                           CT2[:], OP.mult, OP.add).then_inc(sem("V"), 1)
                c.v(f_cv)
                v_ct = bump("V")

                def f_cvt(t, ch=ch, s4=s4, v_ct=v_ct):
                    t.wait_ge(sem("V"), v_ct)
                    key = (ch, s4 - 1) if s4 > 0 else (ch - 1, 3)
                    if key in cv_ev:
                        t.wait_ge(sem("V"), cv_ev[key])
                    for j in range(4):
                        mm = t.transpose(PT[:, 128 * j:128 * (j + 1)],
                                         CT3[:, 128 * j:128 * (j + 1)], IDN[:])
                    mm.then_inc(sem("T"), 1)
                c.t(f_cvt)
                t_ct = bump("T")

                def f_cva(v, ch=ch, s4=s4, t_ct=t_ct):
                    v.wait_ge(sem("T"), t_ct)
                    dst = X[:].rearrange("p (t f) -> p t f", f=D)[:, 4 * s4:4 * s4 + 4, 128 * ch:128 * (ch + 1)]
                    src = PT[:].rearrange("p (j w) -> p j w", w=128)[:, 0:4, :]
                    v.tensor_tensor(dst, dst, src, OP.add).then_inc(sem("V"), 1)
                c.v(f_cva)
                cv_ev[(ch, s4)] = bump("V")

        rt_sexp = {}
        for i in range(NT):
            def f_lg(t, i=i):
                if i == 0:
                    t.wait_ge(sem("V"), cv_ev[(NCH - 1, 3)])
                else:
                    t.wait_ge(sem("S"), rt_sexp[i - 1])
                for ch in range(NCH):
                    t.matmul(SP[:, 0:32], xn(ch, 128 * i, 128 * (i + 1)),
                             GWr[:, 32 * ch:32 * (ch + 1)], start=(ch == 0), stop=False)
                t.matmul(SP[:, 0:32], ONR[:], GBr[:], start=False, stop=True).then_inc(sem("T"), 1)
            c.t(f_lg)
            t_lg = bump("T")

            def f_rmax(v, t_lg=t_lg):
                v.wait_ge(sem("T"), t_lg)
                v.reduce_max(NMX[:], SP[:, 0:32], axis=AX.X, negate=True).then_inc(sem("V"), 1)
            c.v(f_rmax)
            v_mx = bump("V")

            def f_exp(s, v_mx=v_mx):
                s.wait_ge(sem("V"), v_mx)
                s.activation(out=SC[:], in_=SP[:, 0:32], func=AF.Exp, bias=NMX[:],
                             accum_out=DEN[:])
                s.activation(out=NMX[:], in_=NMX[:], func=AF.Exp).then_inc(sem("S"), 1)
            c.s(f_exp)
            s_exp = bump("S")
            rt_sexp[i] = s_exp

            def f_rt(v, i=i, s_exp=s_exp):
                v.wait_ge(sem("S"), s_exp)
                i0 = v.reciprocal(RDEN[:], DEN[:])
                vedge(v, i0)
                i1 = v.tensor_scalar(SC[:], SC[:], RDEN[:], None, OP.mult)
                vedge(v, i1)
                SCg = SC[:].rearrange("p (g k) -> p g k", k=4)
                i2 = v.tensor_tensor(PS6[:], SCg[:, :, 0], SCg[:, :, 1], OP.add)
                i3 = v.tensor_tensor(GS[:], SCg[:, :, 0], SCg[:, :, 2], OP.add)
                vedge(v, i2); vedge(v, i3)
                i4 = v.tensor_tensor(GS[:], GS[:], PS6[:], OP.max)
                i5 = v.tensor_tensor(PS6[:], SCg[:, :, 0], SCg[:, :, 3], OP.add)
                vedge(v, i4); vedge(v, i5)
                i4 = v.tensor_tensor(GS[:], GS[:], PS6[:], OP.max)
                i5 = v.tensor_tensor(PS6[:], SCg[:, :, 1], SCg[:, :, 2], OP.add)
                vedge(v, i4); vedge(v, i5)
                i4 = v.tensor_tensor(GS[:], GS[:], PS6[:], OP.max)
                i5 = v.tensor_tensor(PS6[:], SCg[:, :, 1], SCg[:, :, 3], OP.add)
                vedge(v, i4); vedge(v, i5)
                i4 = v.tensor_tensor(GS[:], GS[:], PS6[:], OP.max)
                i5 = v.tensor_tensor(PS6[:], SCg[:, :, 2], SCg[:, :, 3], OP.add)
                vedge(v, i4); vedge(v, i5)
                i4 = v.tensor_tensor(GS[:], GS[:], PS6[:], OP.max)
                vedge(v, i4)
                v.max_with_indices(GS8[:], IDX8[:], GS[:])
                i6 = v.tensor_copy(PS6[:, 0:1], GS8[:, 0:1])
                vedge(v, i6)
                i7 = v.tensor_scalar(KG[:], GS[:], GS8[:, 3:4], None, OP.is_ge)
                vedge(v, i7)
                KMg = KM[:].rearrange("p (g k) -> p g k", k=4)
                for k in range(4):
                    i8 = v.tensor_copy(KMg[:, :, k], KG[:])
                vedge(v, i8)
                i9 = v.select(MSK[:], KM[:], SC[:], NEG[:], add_drain=True)
                vedge(v, i9)
                v.max_with_indices(M8[:], IDX8[:], MSK[:])
                i10 = v.tensor_copy(PS6[:, 0:1], M8[:, 0:1])
                vedge(v, i10)
                i11 = v.tensor_scalar(IND[:], MSK[:], M8[:, 3:4], None, OP.is_ge)
                vedge(v, i11)
                v.tensor_tensor(CWall[:, 32 * i:32 * (i + 1)], SC[:], IND[:], OP.mult).then_inc(sem("V"), 1)
            c.v(f_rt)
            bump("V", 1)
            v_cw = val("V")

            def f_cwt(t, i=i, v_cw=v_cw):
                t.wait_ge(sem("V"), v_cw)
                t.transpose(PXS[0:32, 0:128], CWall[:, 32 * i:32 * (i + 1)], IDNF[:]).then_inc(sem("T"), 1)
            c.t(f_cwt)
            t_cw = bump("T")

            def f_cwte(v, i=i, t_cw=t_cw):
                v.wait_ge(sem("T"), t_cw)
                v.tensor_copy(CWT[:, 128 * i:128 * (i + 1)], PXS[0:32, 0:128]).then_inc(sem("V"), 1)
            c.v(f_cwte)
            bump("V")
        v_route = val("V")

        def dbg3_sy(sy):
            sy.wait_ge(sem("V"), v_route)
            sy.dma_start(out=o_ss, in_=SS[0:32, :]).then_inc(sem("st"), 16)
            sy.dma_start(out=o_xa, in_=X[:]).then_inc(sem("st"), 16)
            sy.dma_start(out=o_cw, in_=CWall[:]).then_inc(sem("st"), 16)
        c.sy(dbg3_sy)
        st_dbg3 = bump("st", 48)

        t_lastuse = [0] * 34
        prev = {}
        g_emitted = [0]

        def emit_wdma(e):
            def f_wd(g, e=e):
                if e >= 2:
                    g.wait_ge(sem("T"), t_lastuse[e - 2])
                sl = e % 2
                g.dma_start(out=W1E[:, 4096 * sl:4096 * (sl + 1)].rearrange("p (ch m) -> p ch m", m=512),
                            in_=i_w1[e].rearrange("(ch p) m -> p ch m", p=128)).then_inc(sem("lde"), 16)
                g.dma_start(out=W3E[:, 4096 * sl:4096 * (sl + 1)].rearrange("p (ch m) -> p ch m", m=512),
                            in_=i_w3[e].rearrange("(ch p) m -> p ch m", p=128)).then_inc(sem("lde"), 16)
                g.dma_start(out=W2E[:, 4096 * sl:4096 * (sl + 1)].rearrange("p (ic m) -> p ic m", m=D),
                            in_=i_w2[e].rearrange("(ic p) m -> p ic m", p=128)).then_inc(sem("lde"), 16)
                g.dma_start(out=B1E[:, 512 * sl:512 * (sl + 1)], in_=i_b1[e:e + 1, :]).then_inc(sem("lde"), 16)
                g.dma_start(out=B3E[:, 512 * sl:512 * (sl + 1)], in_=i_b3[e:e + 1, :]).then_inc(sem("lde"), 16)
            c.g(f_wd)
            bump("lde", 5 * 16)

        def emit_expert(e):
            sl = e % 2
            shared = e >= 32
            lde_need = 80 * (e + 1)
            for i in range(NT):
                def f_a(t, e=e, i=i, sl=sl, lde_need=lde_need):
                    if i == 0:
                        t.wait_ge(sem("lde"), lde_need)
                        if e == 0:
                            t.wait_ge(sem("V"), v_route)
                            t.wait_ge(sem("st"), st_dbg3)
                    k = ("sil", e, i - 1) if i > 0 else ("sil", e - 1, NT - 1)
                    if k in prev:
                        t.wait_ge(sem("S"), prev[k])
                    k2 = ("hs", e, i - 1) if i > 0 else ("hs", e - 1, NT - 1)
                    if k2 in prev:
                        t.wait_ge(sem("V"), prev[k2])
                    for ch in range(NCH):
                        t.matmul(PP0[:], xn(ch, 128 * i, 128 * (i + 1)),
                                 W1E[:, 4096 * sl + 512 * ch:4096 * sl + 512 * (ch + 1)],
                                 start=(ch == 0), stop=False)
                    t.matmul(PP0[:], ONR[:], B1E[:, 512 * sl:512 * (sl + 1)],
                             start=False, stop=True).then_inc(sem("T"), 1)
                c.t(f_a)
                t_a = bump("T")

                def f_sil(s, t_a=t_a):
                    s.wait_ge(sem("T"), t_a)
                    s.activation(out=SA[:], in_=PP0[:], func=AF.Silu).then_inc(sem("S"), 1)
                c.s(f_sil)
                prev[("sil", e, i)] = bump("S")

                def f_b(t, e=e, i=i, sl=sl):
                    k = ("hs", e, i - 1) if i > 0 else ("hs", e - 1, NT - 1)
                    if k in prev:
                        t.wait_ge(sem("V"), prev[k])
                    for ch in range(NCH):
                        t.matmul(PP1[:], xn(ch, 128 * i, 128 * (i + 1)),
                                 W3E[:, 4096 * sl + 512 * ch:4096 * sl + 512 * (ch + 1)],
                                 start=(ch == 0), stop=False)
                    t.matmul(PP1[:], ONR[:], B3E[:, 512 * sl:512 * (sl + 1)],
                             start=False, stop=True).then_inc(sem("T"), 1)
                c.t(f_b)
                t_b = bump("T")

                def f_hs(v, e=e, i=i, t_b=t_b, shared=shared):
                    v.wait_ge(sem("T"), t_b)
                    v.wait_ge(sem("S"), prev[("sil", e, i)])
                    if shared:
                        v.tensor_tensor(HS[:], SA[:], PP1[:], OP.mult).then_inc(sem("V"), 1)
                    else:
                        v.scalar_tensor_tensor(HS[:], PP1[:], CWall[:, 32 * i + e:32 * i + e + 1],
                                               SA[:], OP.mult, OP.mult).then_inc(sem("V"), 1)
                c.v(f_hs)
                prev[("hs", e, i)] = bump("V")

                def f_ht(t, e=e, i=i):
                    t.wait_ge(sem("V"), prev[("hs", e, i)])
                    k = ("hf", e, i - 1) if i > 0 else ("hf", e - 1, NT - 1)
                    if k in prev:
                        t.wait_ge(sem("V"), prev[k])
                    for j in range(4):
                        mm = t.transpose(PT[:, 128 * j:128 * (j + 1)],
                                         HS[:, 128 * j:128 * (j + 1)], IDN[:])
                    mm.then_inc(sem("T"), 1)
                c.t(f_ht)
                t_ht = bump("T")

                def f_hf(v, t_ht=t_ht):
                    v.wait_ge(sem("T"), t_ht)
                    v.tensor_copy(HF[:], PT[:, 0:512]).then_inc(sem("V"), 1)
                c.v(f_hf)
                prev[("hf", e, i)] = bump("V")

                def f_y(t, e=e, i=i, sl=sl, shared=shared):
                    t.wait_ge(sem("V"), prev[("hf", e, i)])
                    k = ("ya", e, i - 1) if i > 0 else ("ya", e - 1, NT - 1)
                    if k in prev:
                        t.wait_ge(sem("V"), prev[k])
                    for n in range(2):
                        for ic in range(4):
                            mm = t.matmul(YP[:, 512 * n:512 * (n + 1)], HF[:, 128 * ic:128 * (ic + 1)],
                                          W2E[:, 4096 * sl + 1024 * ic + 512 * n:4096 * sl + 1024 * ic + 512 * (n + 1)],
                                          start=(ic == 0), stop=(e != 33 and ic == 3))
                        if e == 33:
                            t.matmul(YP[:, 512 * n:512 * (n + 1)], CWT[:, 128 * i:128 * (i + 1)],
                                     B2S[:, 512 * n:512 * (n + 1)], start=False, stop=False)
                            mm = t.matmul(YP[:, 512 * n:512 * (n + 1)], ONR[:],
                                          SB2R[:, 512 * n:512 * (n + 1)], start=False, stop=True)
                    mm.then_inc(sem("T"), 1)
                c.t(f_y)
                t_y = bump("T")

                def f_ya(v, i=i, t_y=t_y):
                    v.wait_ge(sem("T"), t_y)
                    for n in range(2):
                        sl_ = X[:, D * i + 512 * n:D * i + 512 * (n + 1)]
                        mmv = v.tensor_tensor(sl_, sl_, YP[:, 512 * n:512 * (n + 1)], OP.add)
                    mmv.then_inc(sem("V"), 1)
                c.v(f_ya)
                prev[("ya", e, i)] = bump("V")
            t_lastuse[e] = val("T")

        emit_wdma(0)
        emit_wdma(1)
        for e in range(34):
            if e + 2 < 34:
                emit_expert(e)
                emit_wdma(e + 2)
            else:
                emit_expert(e)

        v_fin = val("V")

        FINAL_ST = [0]
        def ph6_sy(sy):
            sy.wait_ge(sem("V"), v_fin)
            sy.dma_start(out=o_x, in_=X[:]).then_inc(sem("st"), 16)
            sy.wait_ge(sem("st"), FINAL_ST[0])
        FINAL_ST[0] = val("st") + 16
        c.sy(ph6_sy)
        bump("st", 16)
    return nc


def build_L2():
    nc = bass.Bass("TRN2", target_bir_lowering=False)
    i_x = nc.dram_tensor("xloc", [128, NT * D], F32, kind="ExternalInput").ap()
    i_xo = nc.dram_tensor("xoth", [128, NT * D], F32, kind="ExternalInput").ap()
    i_wqm = nc.dram_tensor("wqm", [NCH, 128, NCH * 128], F32, kind="ExternalInput").ap()
    i_wkm = nc.dram_tensor("wkm", [NCH, 128, NCH * 128], F32, kind="ExternalInput").ap()
    i_wam = nc.dram_tensor("wam", [NCH, 128, NCH * 128], F32, kind="ExternalInput").ap()
    i_wvn = nc.dram_tensor("wvn", [2, 128, NCH * 512], F32, kind="ExternalInput").ap()
    i_bq = nc.dram_tensor("bqc", [128, 8], F32, kind="ExternalInput").ap()
    i_bk = nc.dram_tensor("bkc", [128, 8], F32, kind="ExternalInput").ap()
    i_ba = nc.dram_tensor("bac", [128, 8], F32, kind="ExternalInput").ap()
    i_bvr = nc.dram_tensor("bvr", [1, D], F32, kind="ExternalInput").ap()
    i_gwT = nc.dram_tensor("gwT", [D, 8], F32, kind="ExternalInput").ap()
    i_bbr = nc.dram_tensor("bbr", [1, 8], F32, kind="ExternalInput").ap()
    i_w1 = nc.dram_tensor("w1t", [8, D, 2048], F32, kind="ExternalInput").ap()
    i_w2t = nc.dram_tensor("w2t", [8, 2048, D], F32, kind="ExternalInput").ap()
    i_idn = nc.dram_tensor("idn", [128, 128], BF16, kind="ExternalInput").ap()
    i_onr = nc.dram_tensor("onesr", [1, 128], BF16, kind="ExternalInput").ap()
    i_onc = nc.dram_tensor("onesc", [128, 1], BF16, kind="ExternalInput").ap()
    i_pmsk = nc.dram_tensor("pmsk", [128, 1], F32, kind="ExternalInput").ap()
    i_smsk = nc.dram_tensor("smsk", [128, 1], F32, kind="ExternalInput").ap()
    i_mcau = nc.dram_tensor("mcau", [128, 128], BF16, kind="ExternalInput").ap()
    o_x = nc.dram_tensor("xout", [128, NT * D], F32, kind="ExternalOutput").ap()
    o_wk = nc.dram_tensor("dbg_wk", [128, NCH * TOK], BF16, kind="ExternalOutput").ap()
    o_xn1 = nc.dram_tensor("dbg_xn1", [128, NCH * TOK], BF16, kind="ExternalOutput").ap()
    o_xn2 = nc.dram_tensor("dbg_xn2", [128, NCH * TOK], BF16, kind="ExternalOutput").ap()
    o_cb = nc.dram_tensor("dbg_cb", [128, NT * 8], F32, kind="ExternalOutput").ap()
    o_q = nc.dram_tensor("dbg_q", [128, NCH * TOK], BF16, kind="ExternalOutput").ap()
    d_np = nc.dram_tensor("nscr", [128, 16 * (2 * D + 16)], BF16, kind="ExternalOutput").ap()
    d_no = nc.dram_tensor("noscr", [128, NT * D], BF16, kind="ExternalOutput").ap()

    es = contextlib.ExitStack()
    with es:
        c = Ctx(nc, es)
        sem, bump, val = c.sem, c.bump, c.val

        def vedge(v, instr):
            n = bump("V"); instr.then_inc(sem("V"), 1); v.wait_ge(sem("V"), n)

        def sedge(s_, instr):
            n = bump("S"); instr.then_inc(sem("S"), 1); s_.wait_ge(sem("S"), n)

        XN = c.sb("XN", [128, NCH * TOK], BF16)
        Q = c.sb("Q", [128, NCH * TOK], BF16)
        BQ = c.sb("BQ", [128, 8], F32)
        BK = c.sb("BK", [128, 8], F32)
        BA = c.sb("BA", [128, 8], F32)
        BVR = c.sb("BVR", [1, D], BF16)
        GWT = c.sb("GWT", [128, NCH * 8], BF16)
        BBR = c.sb("BBR", [1, 8], BF16)
        CSO = c.sb("CSO", [128, 8], F32)
        CSX = c.sb("CSX", [128, 8], F32)
        SFX = c.sb("SFX", [128, 8], F32)
        MV = c.sb("MV", [128, 36], F32)
        RS = c.sb("RS", [128, 18], F32)
        SQ = c.sb("SQ", [128, 36], F32)
        XNT = c.sb("XNT", [128, 2 * D], BF16)
        IDN = c.sb("IDN", [128, 128], BF16)
        ONR = c.sb("ONR", [1, 128], BF16)
        ONC = c.sb("ONC", [128, 1], BF16)
        PMSK = c.sb("PMSK", [128, 1], F32)
        SMSK = c.sb("SMSK", [128, 1], F32)
        MCAU = c.sb("MCAU", [128, 128], BF16)
        EPS = c.sb("EPS", [128, 1], F32)
        EPS6 = c.sb("EPS6", [128, 1], F32)
        M87 = c.sb("M87", [128, 1], F32)
        NOB = c.sb("NOB", [128, 2 * D + 16], BF16)
        DENR = c.sb("DENR", [128, 8], F32)
        FINT = c.sb("FINT", [128, 512], F32)
        IDXS = c.sb("IDXS", [128, 8], U32)
        LACs = c.sb("LACs", [128, TOK], F32)
        KCs = c.sb("KCs", [128, TOK], BF16)
        EGCs = c.sb("EGCs", [128, TOK], BF16)
        WMB = c.sb("WMB", [128, 2 * NCH * 128], BF16)
        WMB2 = c.sb("WMB2", [128, 2 * NCH * 128], BF16)
        WVB = c.sb("WVB", [128, NCH * 512], BF16)
        PP0 = c.ps("PP0", [128, 512], F32)
        PP1 = c.ps("PP1", [128, 512], F32)
        NUM0 = c.ps("NUM0", [128, 1024], F32)
        NUM1 = c.ps("NUM1", [128, 1024], F32)
        DEN01 = c.ps("DEN01", [128, 512], F32)
        SPS = c.ps("SPS", [128, 512], F32)
        PT = NUM0[:].bitcast(BF16)

        def ph0_sy(sy):
            for dst, src in [(BQ, i_bq), (BK, i_bk), (BA, i_ba), (PMSK, i_pmsk), (SMSK, i_smsk)]:
                sy.dma_start(out=dst[:], in_=src).then_inc(sem("ldx"), 16)
        c.sy(ph0_sy)
        LDX = bump("ldx", 5 * 16)

        def ph0_g(g):
            for dst, src in [(IDN, i_idn), (ONR, i_onr), (ONC, i_onc), (BVR, i_bvr),
                             (BBR, i_bbr), (MCAU, i_mcau)]:
                g.dma_start(out=dst[:], in_=src).then_inc(sem("ldw"), 16)
            g.dma_start(out=GWT[:].rearrange("p (c m) -> p c m", m=8),
                        in_=i_gwT.rearrange("(c p) m -> p c m", p=128)).then_inc(sem("ldw"), 16)
            g.memset(EPS[:], 1e-5)
            g.memset(EPS6[:], 1e-6)
            g.memset(M87[:], -87.0).then_inc(sem("ldz"), 1)
        c.g(ph0_g)
        LDW = bump("ldw", 7 * 16); LDZ = bump("ldz", 1)

        def emit_ln(XSRC, ldsem, ldval, DEST):
            def f_ssq(s_, tw=val("T")):
                s_.wait_ge(sem(ldsem), ldval)
                s_.wait_ge(sem("ldw"), LDW); s_.wait_ge(sem("ldz"), 1)
                s_.wait_ge(sem("T"), tw)
                for i in range(16):
                    s_.activation(out=XNT[:, 0:D], in_=XSRC[:, D * i:D * (i + 1)],
                                  func=AF.Square, accum_out=SQ[:, 18 + i:19 + i])
                s_.copy(RS[:, 17:18], SQ[:, 33:34]).then_inc(sem("S"), 1)
            c.s(f_ssq)
            s_sq = bump("S")

            def f_vst(v):
                v.wait_ge(sem(ldsem), ldval)
                v.wait_ge(sem("ldw"), LDW); v.wait_ge(sem("ldz"), 1)
                for i in range(16):
                    v.reduce_sum(SQ[:, i:i + 1], XSRC[:, D * i:D * (i + 1)], axis=AX.X)
                i1 = v.tensor_scalar(MV[:, 0:16], SQ[:, 0:16], 1.0 / D, None, OP.mult)
                vedge(v, i1)
                v.wait_ge(sem("S"), s_sq)
                i2 = v.tensor_scalar(MV[:, 18:34], SQ[:, 18:34], 1.0 / D, None, OP.mult)
                vedge(v, i2)
                i3 = v.tensor_tensor(RS[:, 0:16], MV[:, 0:16], MV[:, 0:16], OP.mult)
                vedge(v, i3)
                v.tensor_tensor(MV[:, 18:34], MV[:, 18:34], RS[:, 0:16], OP.subtract).then_inc(sem("V"), 1)
            c.v(f_vst)
            v_st = bump("V")

            def f_rs(s_):
                s_.wait_ge(sem("V"), v_st)
                i1 = s_.activation(out=RS[:, 0:16], in_=MV[:, 18:34], func=AF.Ln, bias=EPS[:])
                sedge(s_, i1)
                s_.activation(out=RS[:, 0:16], in_=RS[:, 0:16], func=AF.Exp, scale=-0.5).then_inc(sem("S"), 1)
            c.s(f_rs)
            s_rs = bump("S")

            v_ap = [0] * 16
            t_tr = [0] * 16
            v_ev = [0] * 16
            DV = DEST[:].rearrange("p (ch w) -> p ch w", w=TOK)
            PTv = PT.rearrange("p (ch w) -> p ch w", w=128)
            for i in range(16):
                buf = i % 2

                def f_ap(v, i=i, buf=buf):
                    v.wait_ge(sem("S"), s_rs)
                    if i >= 2:
                        v.wait_ge(sem("T"), t_tr[i - 2])
                    v.tensor_scalar(XNT[:, D * buf:D * (buf + 1)], XSRC[:, D * i:D * (i + 1)],
                                    MV[:, i:i + 1], RS[:, i:i + 1],
                                    OP.subtract, OP.mult).then_inc(sem("V"), 1)
                c.v(f_ap)
                v_ap[i] = bump("V")

                def f_tr(t, i=i, buf=buf):
                    t.wait_ge(sem("V"), v_ap[i])
                    if i >= 1:
                        t.wait_ge(sem("V"), v_ev[i - 1])
                    for ch in range(NCH):
                        mm = t.transpose(PT[:, 128 * ch:128 * (ch + 1)],
                                         XNT[:, D * buf + 128 * ch:D * buf + 128 * (ch + 1)], IDN[:])
                    mm.then_inc(sem("T"), 1)
                c.t(f_tr)
                t_tr[i] = bump("T")

                def f_ev(v, i=i):
                    v.wait_ge(sem("T"), t_tr[i])
                    v.tensor_copy(DV[:, :, 128 * i:128 * (i + 1)], PTv[:, 0:NCH, :]).then_inc(sem("V"), 1)
                c.v(f_ev)
                v_ev[i] = bump("V")
            return val("V")

        def emit_proj_fm_stream(SRC, i_wm, WBUF, BCOL, DEST_FN, act, lds):
            ev = {}
            tm = {}
            for mch in range(NCH):
                def f_wd(g, mch=mch, lds=lds):
                    if mch >= 2 and (lds, mch - 2) in tm:
                        g.wait_ge(sem("T"), tm[(lds, mch - 2)])
                    sl = mch % 2
                    g.dma_start(out=WBUF[:, 1024 * sl:1024 * (sl + 1)],
                                in_=i_wm[mch]).then_inc(sem(lds), 16)
                c.g(f_wd)
                ld_m = bump(lds, 16)
                for s4 in range(4):
                    PB = PP0 if (mch * 4 + s4) % 2 == 0 else PP1

                    def f_mm(t, mch=mch, s4=s4, PB=PB, ld_m=ld_m):
                        key = mch * 4 + s4 - 2
                        if key in ev:
                            t.wait_ge(sem("S"), ev[key])
                        if s4 == 0:
                            t.wait_ge(sem(lds), ld_m)
                        sl = mch % 2
                        for cc in range(NCH):
                            mm = t.matmul(PB[:], WBUF[:, 1024 * sl + 128 * cc:1024 * sl + 128 * (cc + 1)],
                                          SRC[:, TOK * cc + 512 * s4:TOK * cc + 512 * (s4 + 1)],
                                          start=(cc == 0), stop=(cc == 7))
                        mm.then_inc(sem("T"), 1)
                    c.t(f_mm)
                    tm[(lds, mch)] = bump("T")

                    def f_ev(s_, mch=mch, s4=s4, PB=PB):
                        s_.wait_ge(sem("T"), tm[(lds, mch)])
                        s_.activation(out=DEST_FN(mch, s4), in_=PB[:], func=act,
                                      bias=BCOL[:, mch:mch + 1]).then_inc(sem("S"), 1)
                    c.s(f_ev)
                    ev[mch * 4 + s4] = bump("S")
            return val("S"), val("T")

        def emit_half(SRC_FM, WK, wk_off, VT, vt_off, CS_OUT, sfx_fn, lds):
            tm = {}
            sv = {}
            for ch in range(NCH):
                def f_wd(g, ch=ch):
                    if ch >= 2 and ("h", ch - 2) in tm:
                        g.wait_ge(sem("T"), tm[("h", ch - 2)])
                    sl = ch % 2
                    g.dma_start(out=WMB[:, 1024 * sl:1024 * (sl + 1)], in_=i_wam[ch]).then_inc(sem(lds), 16)
                    g.dma_start(out=WMB2[:, 1024 * sl:1024 * (sl + 1)], in_=i_wkm[ch]).then_inc(sem(lds), 16)
                c.g(f_wd)
                ld_m = bump(lds, 32)
                sl = ch % 2
                s_la0 = val("S")
                for s4 in range(4):
                    def f_amm(t, ch=ch, s4=s4, sl=sl, ld_m=ld_m, vw=val("V")):
                        t.wait_ge(sem("S"), val("S"))
                        if s4 == 0:
                            t.wait_ge(sem(lds), ld_m)
                            t.wait_ge(sem("V"), vw)
                        for cc in range(NCH):
                            mm = t.matmul(PP0[:], WMB[:, 1024 * sl + 128 * cc:1024 * sl + 128 * (cc + 1)],
                                          SRC_FM[:, TOK * cc + 512 * s4:TOK * cc + 512 * (s4 + 1)],
                                          start=(cc == 0), stop=(cc == 7))
                        mm.then_inc(sem("T"), 1)
                    c.t(f_amm)
                    t_a = bump("T")

                    def f_sig(s_, ch=ch, s4=s4, t_a=t_a):
                        s_.wait_ge(sem("T"), t_a)
                        i1 = s_.activation(out=LACs[:, 512 * s4:512 * (s4 + 1)], in_=PP0[:],
                                           func=AF.Sigmoid, bias=BA[:, ch:ch + 1])
                        sedge(s_, i1)
                        s_.activation(out=LACs[:, 512 * s4:512 * (s4 + 1)],
                                      in_=LACs[:, 512 * s4:512 * (s4 + 1)],
                                      func=AF.Ln, bias=EPS6[:]).then_inc(sem("S"), 1)
                    c.s(f_sig)
                    bump("S")
                s_la = val("S")

                def f_cs(v, ch=ch, s_la=s_la):
                    v.wait_ge(sem("S"), s_la)
                    i0 = v.reduce_sum(CS_OUT[:, ch:ch + 1], LACs[:], axis=AX.X)
                    vedge(v, i0)
                    i1 = v.tensor_tensor_scan(LACs[:, ::-1], LACs[:, ::-1], LACs[:, ::-1],
                                              0.0, OP.add, OP.bypass)
                    vedge(v, i1)
                    sfx = sfx_fn(v, ch)
                    v.tensor_scalar(LACs[:], LACs[:], sfx, M87[:], OP.add, OP.max).then_inc(sem("V"), 1)
                c.v(f_cs)
                v_lg = bump("V")

                def f_eg(s_, v_lg=v_lg):
                    s_.wait_ge(sem("V"), v_lg)
                    s_.activation(out=EGCs[:], in_=LACs[:], func=AF.Exp).then_inc(sem("S"), 1)
                c.s(f_eg)
                s_eg = bump("S")

                for s4 in range(4):
                    def f_kmm(t, ch=ch, s4=s4, sl=sl, vw=val("V")):
                        t.wait_ge(sem("S"), val("S"))
                        if s4 == 0:
                            t.wait_ge(sem("V"), vw)
                        for cc in range(NCH):
                            mm = t.matmul(PP1[:], WMB2[:, 1024 * sl + 128 * cc:1024 * sl + 128 * (cc + 1)],
                                          SRC_FM[:, TOK * cc + 512 * s4:TOK * cc + 512 * (s4 + 1)],
                                          start=(cc == 0), stop=(cc == 7))
                        mm.then_inc(sem("T"), 1)
                    c.t(f_kmm)
                    t_k = bump("T")

                    def f_rel(s_, ch=ch, s4=s4, t_k=t_k):
                        s_.wait_ge(sem("T"), t_k)
                        s_.activation(out=KCs[:, 512 * s4:512 * (s4 + 1)], in_=PP1[:],
                                      func=AF.Relu, bias=BK[:, ch:ch + 1]).then_inc(sem("S"), 1)
                    c.s(f_rel)
                    bump("S")
                s_k = val("S")

                def f_wk(v, ch=ch, s_k=s_k, s_eg=s_eg):
                    v.wait_ge(sem("S"), s_k)
                    v.wait_ge(sem("S"), s_eg)
                    v.tensor_tensor(WK[:, TOK * ch + wk_off:TOK * ch + wk_off + TOK],
                                    KCs[:], EGCs[:], OP.mult).then_inc(sem("V"), 1)
                c.v(f_wk)
                tm[("h", ch)] = val("T")
                bump("V")
            v_wk = val("V")

            for n in range(2):
                def f_wvd(g, n=n):
                    if n == 1:
                        g.wait_ge(sem("T"), sv[(15, 0)])
                    g.dma_start(out=WVB[:], in_=i_wvn[n]).then_inc(sem(lds), 16)
                c.g(f_wvd)
                ld_v = bump(lds, 16)
                for i in range(16):
                    PB = PP0 if i % 2 == 0 else PP1

                    def f_vmm(t, i=i, n=n, PB=PB, ld_v=ld_v):
                        if i == 0:
                            t.wait_ge(sem(lds), ld_v)
                        key = (i - 2, n)
                        if key in sv:
                            pass
                        t.wait_ge(sem("V"), val("V"))
                        for cc in range(NCH):
                            t.matmul(PB[:], SRC_FM[:, TOK * cc + 128 * i:TOK * cc + 128 * (i + 1)],
                                     WVB[:, 512 * cc:512 * (cc + 1)],
                                     start=(cc == 0), stop=False)
                        mm = t.matmul(PB[:], ONR[:], BVR[:, 512 * n:512 * (n + 1)],
                                      start=False, stop=True)
                        mm.then_inc(sem("T"), 1)
                    c.t(f_vmm)
                    t_v = bump("T")
                    sv[(i, n)] = t_v

                    def f_vev(v, i=i, n=n, PB=PB, t_v=t_v):
                        v.wait_ge(sem("T"), t_v)
                        v.tensor_copy(VT[:, D * (vt_off + i) + 512 * n:D * (vt_off + i) + 512 * (n + 1)],
                                      PB[:]).then_inc(sem("V"), 1)
                    c.v(f_vev)
                    bump("V")
            return val("V")

        def emit_attn_pass(WK, VT, nk_fn, is_final, diag_base_fn, PPART=None):
            for qg in range(8):
                nk = nk_fn(qg)
                v_sct_prev = [None]
                t_num_prev = [None]
                for kt in range(nk):
                    def f_sc(t, qg=qg, kt=kt):
                        if v_sct_prev[0] is not None:
                            t.wait_ge(sem("V"), v_sct_prev[0])
                        else:
                            t.wait_ge(sem("V"), val("V"))
                        for cc in range(NCH):
                            mm = t.matmul(SPS[:, 0:256], WK[:, TOK * cc + 128 * kt:TOK * cc + 128 * (kt + 1)],
                                          Q[:, TOK * cc + 256 * qg:TOK * cc + 256 * (qg + 1)],
                                          start=(cc == 0), stop=(cc == 7))
                        mm.then_inc(sem("T"), 1)
                    c.t(f_sc)
                    t_sc = bump("T")

                    def f_sev(v, qg=qg, kt=kt, t_sc=t_sc):
                        v.wait_ge(sem("T"), t_sc)
                        if t_num_prev[0] is not None:
                            v.wait_ge(sem("T"), t_num_prev[0])
                        SCT = XNT
                        db = diag_base_fn(qg)
                        if db is None:
                            i1 = v.tensor_scalar(SCT[:, 0:256], SPS[:, 0:256], PMSK[:], None, OP.mult)
                        else:
                            m = kt
                            for j in range(2):
                                sub = SPS[:, 128 * j:128 * (j + 1)]
                                dst = SCT[:, 128 * j:128 * (j + 1)]
                                if m < 2 * qg + j:
                                    i1 = v.tensor_copy(dst, sub)
                                elif m == 2 * qg + j:
                                    i1 = v.tensor_tensor(dst, sub, MCAU[:], OP.mult)
                                else:
                                    i1 = v.memset(dst, 0.0)
                        i1.then_inc(sem("V"), 1)
                    c.v(f_sev)
                    v_sct_prev[0] = bump("V")

                    def f_num(t, qg=qg, kt=kt, nk=nk, vw=v_sct_prev[0]):
                        t.wait_ge(sem("V"), vw)
                        SCT = XNT
                        for j in range(2):
                            NP = NUM0 if j == 0 else NUM1
                            for n in range(2):
                                t.matmul(NP[:, 512 * n:512 * (n + 1)], SCT[:, 128 * j:128 * (j + 1)],
                                         VT[:, D * kt + 512 * n:D * kt + 512 * (n + 1)],
                                         start=(kt == 0), stop=(kt == nk - 1))
                            mm = t.matmul(DEN01[:, 256 * j:256 * j + 1], SCT[:, 128 * j:128 * (j + 1)],
                                          ONC[:], start=(kt == 0 and j == 0), stop=(kt == nk - 1),
                                          skip_group_check=True)
                        mm.then_inc(sem("T"), 1)
                    c.t(f_num)
                    t_num_prev[0] = bump("T")

                t_last = val("T")
                if not is_final:
                    def f_p1ev(v, qg=qg, t_last=t_last):
                        v.wait_ge(sem("T"), t_last)
                        for j in range(2):
                            NP = NUM0 if j == 0 else NUM1
                            v.tensor_copy(NOB[:, D * j:D * (j + 1)], NP[:])
                            i1 = v.tensor_copy(NOB[:, 2 * D + j:2 * D + j + 1], DEN01[:, 256 * j:256 * j + 1])
                        i1.then_inc(sem("V"), 1)
                    c.v(f_p1ev)
                    v_nob = bump("V")

                    def f_nst(sy, qg=qg, v_nob=v_nob):
                        sy.wait_ge(sem("V"), v_nob)
                        sy.dma_start(out=d_np[:, (2 * D + 16) * qg:(2 * D + 16) * qg + 2 * D],
                                     in_=NOB[:, 0:2 * D]).then_inc(sem("st"), 16)
                        sy.dma_start(out=d_np[:, (2 * D + 16) * qg + 2 * D:(2 * D + 16) * qg + 2 * D + 2],
                                     in_=NOB[:, 2 * D:2 * D + 2]).then_inc(sem("st"), 16)
                    c.sy(f_nst)
                    bump("st", 32)
                else:
                    def f_ppl(sy, qg=qg, t_last=t_last):
                        sy.wait_ge(sem("V"), val("V"))
                        sy.dma_start(out=PPART[:, 0:2 * D],
                                     in_=d_np[:, (2 * D + 16) * qg:(2 * D + 16) * qg + 2 * D]).then_inc(sem("ldp"), 16)
                        sy.dma_start(out=PPART[:, 2 * D:2 * D + 2],
                                     in_=d_np[:, (2 * D + 16) * qg + 2 * D:(2 * D + 16) * qg + 2 * D + 2]).then_inc(sem("ldp"), 16)
                    c.sy(f_ppl)
                    ld_pp = bump("ldp", 32)

                    def f_fin(v, qg=qg, t_last=t_last, ld_pp=ld_pp):
                        v.wait_ge(sem("T"), t_last)
                        v.wait_ge(sem("ldp"), ld_pp)
                        for j in range(2):
                            i0 = v.tensor_tensor(DENR[:, j:j + 1], DEN01[:, 256 * j:256 * j + 1],
                                                 PPART[:, 2 * D + j:2 * D + j + 1], OP.add)
                            vedge(v, i0)
                            i1 = v.tensor_scalar(DENR[:, 2 + j:3 + j], DENR[:, j:j + 1], EPS6[:], None, OP.add)
                            vedge(v, i1)
                            i2 = v.reciprocal(DENR[:, 4 + j:5 + j], DENR[:, 2 + j:3 + j])
                            vedge(v, i2)
                        for j in range(2):
                            NP = NUM0 if j == 0 else NUM1
                            for n in range(2):
                                i3 = v.tensor_tensor(FINT[:], NP[:, 512 * n:512 * (n + 1)],
                                                     PPART[:, D * j + 512 * n:D * j + 512 * (n + 1)], OP.add)
                                vedge(v, i3)
                                i4 = v.tensor_scalar(NOB[:, D * j + 512 * n:D * j + 512 * (n + 1)],
                                                     FINT[:], DENR[:, 4 + j:5 + j], None, OP.mult)
                                vedge(v, i4)
                        v.tensor_copy(DENR[:, 6:7], DENR[:, 4:5]).then_inc(sem("V"), 1)
                    c.v(f_fin)
                    v_fin = bump("V")

                    def f_fst(sy, qg=qg, v_fin=v_fin):
                        sy.wait_ge(sem("V"), v_fin)
                        sy.dma_start(out=d_no[:, 2 * D * qg:2 * D * (qg + 1)],
                                     in_=NOB[:, 0:2 * D]).then_inc(sem("st"), 16)
                    c.sy(f_fst)
                    bump("st", 16)

        with nc.sbuf_tensor("X", [128, NT * D], F32) as X:
            def ldx_sy(sy):
                sy.dma_start(out=X[:], in_=i_x).then_inc(sem("ldx"), 16)
            c.sy(ldx_sy)
            LDX2 = bump("ldx", 16)
            v_lnx = emit_ln(X, "ldx", LDX2, XN)

            def dbgxn1(sy):
                sy.wait_ge(sem("V"), v_lnx)
                sy.dma_start(out=o_xn1, in_=XN[:]).then_inc(sem("st"), 16)
            c.sy(dbgxn1)
            bump("st", 16)

        s_q, t_q = emit_proj_fm_stream(
            XN, i_wqm, WMB, BQ,
            lambda mch, s4: Q[:, TOK * mch + 512 * s4:TOK * mch + 512 * (s4 + 1)],
            AF.Relu, "ldq")

        def dest_null(mch, s4):
            return LACs[:, 512 * s4:512 * (s4 + 1)]
        ev_cso = {}
        for mch in range(NCH):
            def f_wd(g, mch=mch):
                if ("cso", mch - 2) in ev_cso:
                    g.wait_ge(sem("T"), ev_cso[("cso", mch - 2)])
                sl = mch % 2
                g.dma_start(out=WMB2[:, 1024 * sl:1024 * (sl + 1)], in_=i_wam[mch]).then_inc(sem("lda"), 16)
            c.g(f_wd)
            ld_m = bump("lda", 16)
            sl = mch % 2
            for s4 in range(4):
                def f_mm(t, mch=mch, s4=s4, sl=sl, ld_m=ld_m, vw=val("V")):
                    t.wait_ge(sem("S"), val("S"))
                    if s4 == 0:
                        t.wait_ge(sem("lda"), ld_m)
                        t.wait_ge(sem("V"), vw)
                    for cc in range(NCH):
                        mm = t.matmul(PP0[:], WMB2[:, 1024 * sl + 128 * cc:1024 * sl + 128 * (cc + 1)],
                                      XN[:, TOK * cc + 512 * s4:TOK * cc + 512 * (s4 + 1)],
                                      start=(cc == 0), stop=(cc == 7))
                    mm.then_inc(sem("T"), 1)
                c.t(f_mm)
                ev_cso[("cso", mch)] = bump("T")
                t_m = val("T")

                def f_sig(s_, mch=mch, s4=s4, t_m=t_m):
                    s_.wait_ge(sem("T"), t_m)
                    i1 = s_.activation(out=LACs[:, 512 * s4:512 * (s4 + 1)], in_=PP0[:],
                                       func=AF.Sigmoid, bias=BA[:, mch:mch + 1])
                    sedge(s_, i1)
                    s_.activation(out=LACs[:, 512 * s4:512 * (s4 + 1)],
                                  in_=LACs[:, 512 * s4:512 * (s4 + 1)],
                                  func=AF.Ln, bias=EPS6[:]).then_inc(sem("S"), 1)
                c.s(f_sig)
                bump("S")
            s_la = val("S")

            def f_cs(v, mch=mch, s_la=s_la):
                v.wait_ge(sem("S"), s_la)
                v.reduce_sum(CSO[:, mch:mch + 1], LACs[:], axis=AX.X).then_inc(sem("V"), 1)
            c.v(f_cs)
            bump("V")
        v_cso = val("V")

        CSD = c.sb("CSD", [128, 8], F32)
        with nc.sbuf_tensor("XNO", [128, NCH * TOK], BF16) as XNO:
            with nc.sbuf_tensor("XO", [128, NT * D], F32) as XO:
                def ldxo_sy(sy):
                    sy.wait_ge(sem("V"), v_cso)
                    sy.dma_start(out=XO[:], in_=i_xo).then_inc(sem("ldx"), 16)
                c.sy(ldxo_sy)
                LDX3 = bump("ldx", 16)
                emit_ln(XO, "ldx", LDX3, XNO)
            import os as _os
            if _os.environ.get("L3CUT") == "2":
                def fcut(sy, vv=val("V")):
                    sy.wait_ge(sem("V"), vv)
                    sy.dma_start(out=o_xn0, in_=XN[:]).then_inc(sem("st"), 16)
                    sy.wait_ge(sem("st"), val("st") + 16)
                c.sy(fcut)
                bump("st", 16)
                return nc
            with nc.sbuf_tensor("WKP", [128, NCH * TOK], BF16) as WKP, \
                 nc.sbuf_tensor("VTP", [128, 16 * D], BF16) as VTP:
                def sfx_oth(v, ch):
                    return CSO[:, ch:ch + 1]
                emit_half(XNO, WKP, 0, VTP, 0, CSX, sfx_oth, "ldh1")
                emit_attn_pass(WKP, VTP, lambda qg: 16, False, lambda qg: None)
        v_p1 = val("V")

        with nc.sbuf_tensor("WKO", [128, NCH * TOK], BF16) as WKO, \
             nc.sbuf_tensor("VTO", [128, 16 * D], BF16) as VTO, \
             nc.sbuf_tensor("PPART", [128, 2 * D + 16], BF16) as PPART:
            def sfx_own(v, ch):
                i1 = v.tensor_scalar(SFX[:, ch:ch + 1], CSX[:, ch:ch + 1], SMSK[:], None, OP.mult)
                vedge(v, i1)
                return SFX[:, ch:ch + 1]
            emit_half(XN, WKO, 0, VTO, 0, CSD, sfx_own, "ldh2")
            v_wko = val("V")

            def f_dbgwk(sy):
                sy.wait_ge(sem("V"), v_wko)
                sy.dma_start(out=o_wk, in_=WKO[:]).then_inc(sem("st"), 16)
                sy.dma_start(out=o_q, in_=Q[:]).then_inc(sem("st"), 16)
            c.sy(f_dbgwk)
            st_wk = bump("st", 32)

            def f_dbgwk_hold(t):
                t.wait_ge(sem("st"), st_wk)
            c.t(f_dbgwk_hold)
            emit_attn_pass(WKO, VTO, lambda qg: 2 * qg + 2, True, lambda qg: qg, PPART=PPART)
        v_attn = val("V")
        st_attn = val("st")

        with nc.sbuf_tensor("X2", [128, NT * D], F32) as X2, \
             nc.sbuf_tensor("NOL", [128, 2 * D], BF16) as NOL, \
             nc.sbuf_tensor("W1S", [128, NCH * 512], BF16) as W1S, \
             nc.sbuf_tensor("W2S", [128, 4 * 1024], BF16) as W2S, \
             nc.sbuf_tensor("CBall", [128, NT * 8], F32) as CBall, \
             nc.sbuf_tensor("SA2", [128, 512], BF16) as SA2, \
             nc.sbuf_tensor("HS2", [128, 512], BF16) as HS2, \
             nc.sbuf_tensor("HF2", [128, 512], BF16) as HF2, \
             nc.sbuf_tensor("RT", [128, 16], F32) as RT:
            def ldx2_sy(sy):
                sy.wait_ge(sem("st"), st_attn)
                sy.wait_ge(sem("V"), v_attn)
                sy.dma_start(out=X2[:], in_=i_x).then_inc(sem("ldf"), 16)
            c.sy(ldx2_sy)
            LDF = bump("ldf", 16)
            v_na = [0]
            for i in range(NT):
                def f_nld(sy, i=i):
                    sy.wait_ge(sem("ldf"), LDF)
                    if i >= 2:
                        sy.wait_ge(sem("V"), v_na[0] - 1)
                    sy.dma_start(out=NOL[:, D * (i % 2):D * (i % 2 + 1)],
                                 in_=d_no[:, D * i:D * (i + 1)]).then_inc(sem("ldf"), 16)
                c.sy(f_nld)
                ld_n = bump("ldf", 16)

                def f_nadd(v, i=i, ld_n=ld_n):
                    v.wait_ge(sem("ldf"), ld_n)
                    v.tensor_tensor(X2[:, D * i:D * (i + 1)], X2[:, D * i:D * (i + 1)],
                                    NOL[:, D * (i % 2):D * (i % 2 + 1)], OP.add).then_inc(sem("V"), 1)
                c.v(f_nadd)
                v_na[0] = bump("V")
            v_xadd = val("V")

            rt_prev = {}
            rt_vtop = {}
            for i in range(NT):
                def f_lg(t, i=i):
                    if i == 0:
                        t.wait_ge(sem("V"), v_xadd)
                    if (i - 1) in rt_prev:
                        t.wait_ge(sem("S"), rt_prev[i - 1])
                    if (i - 1) in rt_vtop:
                        t.wait_ge(sem("V"), rt_vtop[i - 1])
                    for cc in range(NCH):
                        t.matmul(SPS[:, 0:8], XN[:, TOK * cc + 128 * i:TOK * cc + 128 * (i + 1)],
                                 GWT[:, 8 * cc:8 * (cc + 1)], start=(cc == 0), stop=False)
                    mm = t.matmul(SPS[:, 0:8], ONR[:], BBR[:], start=False, stop=True)
                    mm.then_inc(sem("T"), 1)
                c.t(f_lg)
                t_lg = bump("T")

                def f_sg(s_, i=i, t_lg=t_lg):
                    s_.wait_ge(sem("T"), t_lg)
                    s_.activation(out=RT[:, 0:8], in_=SPS[:, 0:8], func=AF.Sigmoid).then_inc(sem("S"), 1)
                c.s(f_sg)
                rt_prev[i] = bump("S")
                s_sg = rt_prev[i]

                def f_top(v, i=i, s_sg=s_sg):
                    v.wait_ge(sem("S"), s_sg)
                    v.max_with_indices(MV[:, 0:8], IDXS[:], RT[:, 0:8])
                    i1 = v.tensor_copy(DENR[:, 7:8], MV[:, 1:2])
                    vedge(v, i1)
                    i2 = v.tensor_scalar(RT[:, 8:16], RT[:, 0:8], MV[:, 1:2], None, OP.is_ge)
                    vedge(v, i2)
                    i3 = v.tensor_tensor(RT[:, 8:16], RT[:, 0:8], RT[:, 8:16], OP.mult)
                    vedge(v, i3)
                    i4 = v.reduce_sum(DENR[:, 0:1], RT[:, 8:16], axis=AX.X)
                    vedge(v, i4)
                    i5 = v.tensor_scalar(DENR[:, 1:2], DENR[:, 0:1], EPS6[:], None, OP.add)
                    vedge(v, i5)
                    i6 = v.reciprocal(DENR[:, 2:3], DENR[:, 1:2])
                    vedge(v, i6)
                    v.tensor_scalar(CBall[:, 8 * i:8 * (i + 1)], RT[:, 8:16], DENR[:, 2:3],
                                    None, OP.mult).then_inc(sem("V"), 1)
                c.v(f_top)
                rt_vtop[i] = bump("V")
            v_route = val("V")

            prevv = {}
            for e in range(8):
                for isl in range(4):
                    def f_wde(g, e=e, isl=isl):
                        k2 = e * 4 + isl - 1
                        if k2 in prevv:
                            g.wait_ge(sem("T"), prevv[k2])
                        g.dma_start(out=W1S[:].rearrange("p (cc m) -> p cc m", m=512),
                                    in_=i_w1.rearrange("e (cc p) (g i) -> e g p cc i", p=128, g=4)[e, isl]).then_inc(sem("lde"), 16)
                        g.dma_start(out=W2S[:].rearrange("p (ic m) -> p ic m", m=1024),
                                    in_=i_w2t.rearrange("e (g ic p) d -> e g p ic d", g=4, p=128)[e, isl]).then_inc(sem("lde"), 16)
                    c.g(f_wde)
                    ld_e = bump("lde", 32)
                    for i in range(NT):
                        def f_a(t, e=e, isl=isl, i=i, ld_e=ld_e):
                            if (e, isl, i) == (0, 0, 0):
                                t.wait_ge(sem("V"), v_route)
                            if i == 0:
                                t.wait_ge(sem("lde"), ld_e)
                            k = ("sil", e * 64 + isl * 16 + i - 1)
                            if k in prevv:
                                t.wait_ge(sem("S"), prevv[k])
                            k2 = ("hs", e * 64 + isl * 16 + i - 1)
                            if k2 in prevv:
                                t.wait_ge(sem("V"), prevv[k2])
                            for cc in range(NCH):
                                mm = t.matmul(PP0[:], XN[:, TOK * cc + 128 * i:TOK * cc + 128 * (i + 1)],
                                              W1S[:, 512 * cc:512 * (cc + 1)],
                                              start=(cc == 0), stop=(cc == 7))
                            mm.then_inc(sem("T"), 1)
                        c.t(f_a)
                        t_a = bump("T")

                        def f_sil(s_, t_a=t_a):
                            s_.wait_ge(sem("T"), t_a)
                            s_.activation(out=SA2[:], in_=PP0[:], func=AF.Silu).then_inc(sem("S"), 1)
                        c.s(f_sil)
                        prevv[("sil", e * 64 + isl * 16 + i)] = bump("S")
                        s_sil = prevv[("sil", e * 64 + isl * 16 + i)]

                        def f_hs(v, e=e, i=i, s_sil=s_sil):
                            v.wait_ge(sem("S"), s_sil)
                            v.tensor_scalar(HS2[:], SA2[:], CBall[:, 8 * i + e:8 * i + e + 1],
                                            None, OP.mult).then_inc(sem("V"), 1)
                        c.v(f_hs)
                        v_hs = bump("V")
                        prevv[("hs", e * 64 + isl * 16 + i)] = v_hs

                        def f_ht(t, v_hs=v_hs):
                            t.wait_ge(sem("V"), v_hs)
                            t.wait_ge(sem("V"), val("V"))
                            for j in range(4):
                                mm = t.transpose(PT[:, 128 * j:128 * (j + 1)],
                                                 HS2[:, 128 * j:128 * (j + 1)], IDN[:])
                            mm.then_inc(sem("T"), 1)
                        c.t(f_ht)
                        t_ht = bump("T")

                        def f_hf(v, t_ht=t_ht):
                            v.wait_ge(sem("T"), t_ht)
                            v.tensor_copy(HF2[:], PT[:, 0:512]).then_inc(sem("V"), 1)
                        c.v(f_hf)
                        v_hf = bump("V")

                        def f_y(t, e=e, isl=isl, i=i, v_hf=v_hf):
                            t.wait_ge(sem("V"), v_hf)
                            t.wait_ge(sem("V"), val("V"))
                            for n in range(2):
                                for ic in range(4):
                                    mm = t.matmul(NUM1[:, 512 * n:512 * (n + 1)], HF2[:, 128 * ic:128 * (ic + 1)],
                                                  W2S[:, 1024 * ic + 512 * n:1024 * ic + 512 * (n + 1)],
                                                  start=(ic == 0), stop=(ic == 3))
                            mm.then_inc(sem("T"), 1)
                        c.t(f_y)
                        t_y = bump("T")
                        prevv[e * 4 + isl] = t_y

                        def f_yev(v, i=i, t_y=t_y):
                            v.wait_ge(sem("T"), t_y)
                            v.tensor_tensor(X2[:, D * i:D * (i + 1)], X2[:, D * i:D * (i + 1)],
                                            NUM1[:, 0:1024], OP.add).then_inc(sem("V"), 1)
                        c.v(f_yev)
                        bump("V")
            v_ffn = val("V")

            def ph9_sy(sy):
                sy.wait_ge(sem("V"), v_ffn)
                sy.dma_start(out=o_xn2, in_=XN[:]).then_inc(sem("st"), 16)
                sy.dma_start(out=o_cb, in_=CBall[:]).then_inc(sem("st"), 16)
                sy.dma_start(out=o_x, in_=X2[:]).then_inc(sem("st"), 16)
                sy.wait_ge(sem("st"), st_attn + 48)
            c.sy(ph9_sy)
            bump("st", 16)
    return nc


def build_L3():
    nc = bass.Bass("TRN2", target_bir_lowering=False)
    i_x = nc.dram_tensor("xloc", [128, NT * D], F32, kind="ExternalInput").ap()
    i_xo = nc.dram_tensor("xoth", [128, NT * D], F32, kind="ExternalInput").ap()
    i_wqm = nc.dram_tensor("wqm", [NCH, 128, NCH * 128], F32, kind="ExternalInput").ap()
    i_wkm = nc.dram_tensor("wkm", [NCH, 128, NCH * 128], F32, kind="ExternalInput").ap()
    i_wvg = nc.dram_tensor("wvg", [4, 128, NCH * 256], F32, kind="ExternalInput").ap()
    i_wog = nc.dram_tensor("wog", [4, 64, 4 * D], F32, kind="ExternalInput").ap()
    i_qcos = nc.dram_tensor("qcos", [128, TOK], F32, kind="ExternalInput").ap()
    i_qsin = nc.dram_tensor("qsin", [128, TOK], F32, kind="ExternalInput").ap()
    i_kcos = nc.dram_tensor("kcos", [128, 2 * TOK], F32, kind="ExternalInput").ap()
    i_ksin = nc.dram_tensor("ksin", [128, 2 * TOK], F32, kind="ExternalInput").ap()
    i_lng = nc.dram_tensor("lngr", [1, D], F32, kind="ExternalInput").ap()
    i_lnb = nc.dram_tensor("lnbr", [1, D], F32, kind="ExternalInput").ap()
    i_idn = nc.dram_tensor("idn", [128, 128], BF16, kind="ExternalInput").ap()
    i_onr = nc.dram_tensor("onesr", [1, 128], BF16, kind="ExternalInput").ap()
    i_amsk = nc.dram_tensor("amsk", [128, 1], F32, kind="ExternalInput").ap()
    i_mca = nc.dram_tensor("mcadd", [128, 128], BF16, kind="ExternalInput").ap()
    o_x = nc.dram_tensor("xout", [128, NT * D], F32, kind="ExternalOutput").ap()
    d_ao = nc.dram_tensor("aoscr", [128, NT * D], BF16, kind="ExternalOutput").ap()
    o_scc = nc.dram_tensor("dbg_scc", [128, 32 * 128], BF16, kind="ExternalOutput").ap()
    o_den = nc.dram_tensor("dbg_den", [128, 4], F32, kind="ExternalOutput").ap()
    o_qf = nc.dram_tensor("dbg_qf", [128, 2 * TOK], BF16, kind="ExternalOutput").ap()
    o_kf = nc.dram_tensor("dbg_kf", [128, 4 * TOK], BF16, kind="ExternalOutput").ap()
    o_qpre = nc.dram_tensor("dbg_qpre", [128, 2 * TOK], BF16, kind="ExternalOutput").ap()
    o_xn3 = nc.dram_tensor("dbg_xn3", [128, NCH * TOK], BF16, kind="ExternalOutput").ap()
    o_xn0 = nc.dram_tensor("dbg_xn0", [128, NCH * TOK], BF16, kind="ExternalOutput").ap()

    es = contextlib.ExitStack()
    with es:
        c = Ctx(nc, es)
        sem, bump, val = c.sem, c.bump, c.val

        def vedge(v, instr):
            n = bump("V"); instr.then_inc(sem("V"), 1); v.wait_ge(sem("V"), n)

        def sedge(s_, instr):
            n = bump("S"); instr.then_inc(sem("S"), 1); s_.wait_ge(sem("S"), n)

        MV = c.sb("MV", [128, 36], F32)
        RS = c.sb("RS", [128, 18], F32)
        SQ = c.sb("SQ", [128, 36], F32)
        XNT = c.sb("XNT", [128, 2 * D], BF16)
        IDN = c.sb("IDN", [128, 128], BF16)
        ONR = c.sb("ONR", [1, 128], BF16)
        AMSK = c.sb("AMSK", [128, 1], F32)
        MCA = c.sb("MCA", [128, 128], BF16)
        EPS = c.sb("EPS", [128, 1], F32)
        Z0 = c.sb("Z0", [128, 1], F32)
        NMX = c.sb("NMX", [128, 1], F32)
        DEN = c.sb("DEN", [128, 1], F32)
        RDEN = c.sb("RDEN", [128, 1], F32)
        ATT = c.sb("ATT", [128, 512], BF16)
        CTXS = c.sb("CTXS", [64, 128], BF16)
        LNGR = c.sb("LNGR", [1, D], F32)
        LNBR = c.sb("LNBR", [1, D], F32)
        LNGB = c.sb("LNGB", [1, D], BF16)
        LNBB = c.sb("LNBB", [1, D], BF16)
        PP0 = c.ps("PP0", [128, 512], F32)
        PP1 = c.ps("PP1", [128, 512], F32)
        SPS = c.ps("SPS", [128, 512], F32)
        OUTP = c.ps("OUTP", [128, 1024], F32)
        CTXP = c.ps("CTXP", [128, 128], F32)
        PTT = c.ps("PTT", [128, 1024], BF16)
        PT = PTT[:]

        def ph0_sy(sy):
            for dst, src in [(AMSK, i_amsk), (LNGR, i_lng), (LNBR, i_lnb)]:
                sy.dma_start(out=dst[:], in_=src).then_inc(sem("ldx"), 16)
        c.sy(ph0_sy)
        LDX = bump("ldx", 3 * 16)

        def ph0_g(g):
            for dst, src in [(IDN, i_idn), (ONR, i_onr), (MCA, i_mca)]:
                g.dma_start(out=dst[:], in_=src).then_inc(sem("ldw"), 16)
            g.memset(EPS[:], 1e-5)
            g.memset(Z0[:], 0.0).then_inc(sem("ldz"), 1)
        c.g(ph0_g)
        LDW = bump("ldw", 3 * 16); LDZ = bump("ldz", 1)

        def emit_ln(XSRC, ldsem, ldval, DEST):
            def f_ssq(s_, tw=val("T")):
                s_.wait_ge(sem(ldsem), ldval)
                s_.wait_ge(sem("ldw"), LDW); s_.wait_ge(sem("ldz"), 1)
                s_.wait_ge(sem("T"), tw)
                for i in range(16):
                    s_.activation(out=XNT[:, 0:D], in_=XSRC[:, D * i:D * (i + 1)],
                                  func=AF.Square, accum_out=SQ[:, 18 + i:19 + i])
                s_.copy(RS[:, 17:18], SQ[:, 33:34]).then_inc(sem("S"), 1)
            c.s(f_ssq)
            s_sq = bump("S")

            def f_vst(v):
                v.wait_ge(sem(ldsem), ldval)
                v.wait_ge(sem("ldw"), LDW); v.wait_ge(sem("ldz"), 1)
                for i in range(16):
                    v.reduce_sum(SQ[:, i:i + 1], XSRC[:, D * i:D * (i + 1)], axis=AX.X)
                i1 = v.tensor_scalar(MV[:, 0:16], SQ[:, 0:16], 1.0 / D, None, OP.mult)
                vedge(v, i1)
                v.wait_ge(sem("S"), s_sq)
                i2 = v.tensor_scalar(MV[:, 18:34], SQ[:, 18:34], 1.0 / D, None, OP.mult)
                vedge(v, i2)
                i3 = v.tensor_tensor(RS[:, 0:16], MV[:, 0:16], MV[:, 0:16], OP.mult)
                vedge(v, i3)
                v.tensor_tensor(MV[:, 18:34], MV[:, 18:34], RS[:, 0:16], OP.subtract).then_inc(sem("V"), 1)
            c.v(f_vst)
            v_st = bump("V")

            def f_rs(s_):
                s_.wait_ge(sem("V"), v_st)
                i1 = s_.activation(out=RS[:, 0:16], in_=MV[:, 18:34], func=AF.Ln, bias=EPS[:])
                sedge(s_, i1)
                s_.activation(out=RS[:, 0:16], in_=RS[:, 0:16], func=AF.Exp, scale=-0.5).then_inc(sem("S"), 1)
            c.s(f_rs)
            s_rs = bump("S")

            v_ap = [0] * 16
            t_tr = [0] * 16
            v_ev = [0] * 16
            DV = DEST[:].rearrange("p (ch w) -> p ch w", w=TOK)
            PTv = PT.rearrange("p (ch w) -> p ch w", w=128)
            for i in range(16):
                buf = i % 2

                def f_ap(v, i=i, buf=buf):
                    v.wait_ge(sem("S"), s_rs)
                    if i >= 2:
                        v.wait_ge(sem("T"), t_tr[i - 2])
                    v.tensor_scalar(XNT[:, D * buf:D * (buf + 1)], XSRC[:, D * i:D * (i + 1)],
                                    MV[:, i:i + 1], RS[:, i:i + 1],
                                    OP.subtract, OP.mult).then_inc(sem("V"), 1)
                c.v(f_ap)
                v_ap[i] = bump("V")

                def f_tr(t, i=i, buf=buf):
                    t.wait_ge(sem("V"), v_ap[i])
                    if i >= 1:
                        t.wait_ge(sem("V"), v_ev[i - 1])
                    for ch in range(NCH):
                        mm = t.transpose(PT[:, 128 * ch:128 * (ch + 1)],
                                         XNT[:, D * buf + 128 * ch:D * buf + 128 * (ch + 1)], IDN[:])
                    mm.then_inc(sem("T"), 1)
                c.t(f_tr)
                t_tr[i] = bump("T")

                def f_ev(v, i=i):
                    v.wait_ge(sem("T"), t_tr[i])
                    v.tensor_copy(DV[:, :, 128 * i:128 * (i + 1)], PTv[:, 0:NCH, :]).then_inc(sem("V"), 1)
                c.v(f_ev)
                v_ev[i] = bump("V")
            return val("V")

        with nc.sbuf_tensor("XN", [128, NCH * TOK], BF16) as XN, \
             nc.sbuf_tensor("XNO", [128, NCH * TOK], BF16) as XNO:
            with nc.sbuf_tensor("X", [128, NT * D], F32) as X:
                def ldx_sy(sy):
                    sy.dma_start(out=X[:], in_=i_x).then_inc(sem("ldx"), 16)
                c.sy(ldx_sy)
                LDX2 = bump("ldx", 16)
                emit_ln(X, "ldx", LDX2, XN)

                def f_dxn0(sy, vv=val("V")):
                    sy.wait_ge(sem("V"), vv)
                    sy.dma_start(out=o_xn0, in_=XN[:]).then_inc(sem("st"), 16)
                c.sy(f_dxn0)
                st0 = bump("st", 16)

                def f_dxn0h(v):
                    v.wait_ge(sem("st"), st0)
                c.v(f_dxn0h)
                import os
                if os.environ.get("L3CUT") == "1":
                    return nc
            with nc.sbuf_tensor("XO", [128, NT * D], F32) as XO:
                def ldxo_sy(sy, vv=val("V")):
                    sy.wait_ge(sem("V"), vv)
                    sy.dma_start(out=XO[:], in_=i_xo).then_inc(sem("ldx"), 16)
                c.sy(ldxo_sy)
                LDX3 = bump("ldx", 16)
                emit_ln(XO, "ldx", LDX3, XNO)
            import os as _os
            if _os.environ.get("L3CUT") == "2":
                def fcut(sy, vv=val("V")):
                    sy.wait_ge(sem("V"), vv)
                    sy.dma_start(out=o_xn0, in_=XN[:]).then_inc(sem("st"), 16)
                    sy.wait_ge(sem("st"), val("st") + 16)
                c.sy(fcut)
                bump("st", 16)
                return nc

            with nc.sbuf_tensor("QCOS", [128, TOK], BF16) as QCOS, \
                 nc.sbuf_tensor("QSIN", [128, TOK], BF16) as QSIN, \
                 nc.sbuf_tensor("KCOS", [128, 2 * TOK], BF16) as KCOS, \
                 nc.sbuf_tensor("KSIN", [128, 2 * TOK], BF16) as KSIN, \
                 nc.sbuf_tensor("SCC", [128, 32 * 128], BF16) as SCC, \
                 nc.sbuf_tensor("WMB", [128, 2 * NCH * 128], BF16) as WMB, \
                 nc.sbuf_tensor("WVB", [128, NCH * 256], BF16) as WVB, \
                 nc.sbuf_tensor("WOTG", [64, 4 * D], BF16) as WOTG, \
                 nc.sbuf_tensor("RSC", [128, 2 * TOK], BF16) as RSC, \
                 nc.sbuf_tensor("RC2", [128, 2 * TOK], BF16) as RC2, \
                 nc.sbuf_tensor("AOB", [128, D], BF16) as AOB, \
                 nc.sbuf_tensor("AOL", [128, D], BF16) as AOL:
                def f_tbl(g, vv=val("V")):
                    g.wait_ge(sem("V"), vv)
                    for dst, src in [(QCOS, i_qcos), (QSIN, i_qsin), (KCOS, i_kcos), (KSIN, i_ksin)]:
                        g.dma_start(out=dst[:], in_=src).then_inc(sem("ldw"), 16)
                c.g(f_tbl)
                LDT = bump("ldw", 4 * 16)
                import os as _osA
                if _osA.environ.get("L3CUT") == "3a":
                    def fcutA(sy, vv=val("V")):
                        sy.wait_ge(sem("V"), vv)
                        sy.wait_ge(sem("ldw"), LDT)
                        sy.dma_start(out=o_xn0, in_=XN[:]).then_inc(sem("st"), 16)
                        sy.wait_ge(sem("st"), val("st") + 16)
                    c.sy(fcutA)
                    bump("st", 16)
                    return nc

                for hg in range(4):
                        hg_es = contextlib.ExitStack()
                        QF = hg_es.enter_context(nc.sbuf_tensor(f"QF{hg}", [128, 2 * TOK], BF16))
                        KF = hg_es.enter_context(nc.sbuf_tensor(f"KF{hg}", [128, 2 * 2 * TOK], BF16))
                        VT = hg_es.enter_context(nc.sbuf_tensor(f"VT{hg}", [128, 32 * 256], BF16))
                        for which, i_wm in (("q", i_wqm), ("k", i_wkm)):
                            for mci in range(2):
                                mch = 2 * hg + mci

                                def f_wd(g, mch=mch, mci=mci, i_wm=i_wm, tw=val("T")):
                                    g.wait_ge(sem("T"), tw)
                                    g.dma_start(out=WMB[:, 1024 * mci:1024 * (mci + 1)],
                                                in_=i_wm[mch]).then_inc(sem("ldm"), 16)
                                c.g(f_wd)
                                ld_m = bump("ldm", 16)
                                srcs = [(XN, TOK, TOK)] if which == "q" else [(XNO, 0, TOK), (XN, TOK, TOK)]
                                if which == "q":
                                    srcs = [(XN, 0, TOK)]
                                for SRC, off, wid in srcs:
                                    for s4 in range(wid // 512):
                                        PB = PP0 if s4 % 2 == 0 else PP1

                                        def f_mm(t, mci=mci, s4=s4, PB=PB, ld_m=ld_m, SRC=SRC, sw=val("S")):
                                            t.wait_ge(sem("S"), sw)
                                            if s4 == 0:
                                                t.wait_ge(sem("ldm"), ld_m)
                                            for cc in range(NCH):
                                                mm = t.matmul(PB[:], WMB[:, 1024 * mci + 128 * cc:1024 * mci + 128 * (cc + 1)],
                                                              SRC[:, TOK * cc + 512 * s4:TOK * cc + 512 * (s4 + 1)],
                                                              start=(cc == 0), stop=(cc == 7))
                                            mm.then_inc(sem("T"), 1)
                                        c.t(f_mm)
                                        t_mm = bump("T")

                                        def f_ev(s_, mci=mci, s4=s4, PB=PB, t_mm=t_mm, which=which, off=off):
                                            s_.wait_ge(sem("T"), t_mm)
                                            if which == "q":
                                                dst = QF[:, TOK * mci + 512 * s4:TOK * mci + 512 * (s4 + 1)]
                                            else:
                                                dst = KF[:, 2 * TOK * mci + off + 512 * s4:2 * TOK * mci + off + 512 * (s4 + 1)]
                                            s_.activation(out=dst, in_=PB[:], func=AF.Identity,
                                                          bias=Z0[:]).then_inc(sem("S"), 1)
                                        c.s(f_ev)
                                        bump("S")
                        s_prj = val("S")
                        import os as _os2
                        if _os2.environ.get("L3CUT") == "3" and hg == 0:
                            def fcut3(sy, sp=s_prj):
                                sy.wait_ge(sem("S"), sp)
                                sy.dma_start(out=o_xn0, in_=XN[:]).then_inc(sem("st"), 16)
                                sy.wait_ge(sem("st"), val("st") + 16)
                            c.sy(fcut3)
                            bump("st", 16)
                            hg_es.close()
                            return nc
                        if hg == 0:
                            def f_dbgq(sy, s_prj=s_prj):
                                sy.wait_ge(sem("S"), s_prj)
                                sy.dma_start(out=o_qpre, in_=QF[:]).then_inc(sem("st"), 16)
                                sy.dma_start(out=o_xn3, in_=XN[:]).then_inc(sem("st"), 16)
                            c.sy(f_dbgq)
                            stq = bump("st", 32)

                            def f_dbgq_hold(v):
                                v.wait_ge(sem("st"), stq)
                            c.v(f_dbgq_hold)

                        def f_rope(v):
                            v.wait_ge(sem("S"), s_prj)
                            v.wait_ge(sem("ldw"), LDT)
                            last = None
                            for mci in range(2):
                                for pb in (0, 64):
                                    qv = QF[pb:pb + 64, TOK * mci:TOK * (mci + 1)]
                                    i0 = v.tensor_copy(RSC[pb:pb + 32, 0:TOK],
                                                       QF[pb + 32:pb + 64, TOK * mci:TOK * (mci + 1)])
                                    i1 = v.tensor_copy(RSC[pb + 32:pb + 64, 0:TOK],
                                                       QF[pb:pb + 32, TOK * mci:TOK * (mci + 1)])
                                    vedge(v, i0); vedge(v, i1)
                                    i2 = v.tensor_tensor(RSC[pb:pb + 64, 0:TOK], RSC[pb:pb + 64, 0:TOK],
                                                         QSIN[pb:pb + 64, :], OP.mult)
                                    i3 = v.tensor_tensor(RC2[pb:pb + 64, 0:TOK], qv, QCOS[pb:pb + 64, :], OP.mult)
                                    vedge(v, i2); vedge(v, i3)
                                    i4 = v.tensor_tensor(qv, RC2[pb:pb + 64, 0:TOK], RSC[pb:pb + 64, 0:TOK], OP.add)
                                    vedge(v, i4)
                                    kv = KF[pb:pb + 64, 2 * TOK * mci:2 * TOK * (mci + 1)]
                                    i5 = v.tensor_copy(RSC[pb:pb + 32, :],
                                                       KF[pb + 32:pb + 64, 2 * TOK * mci:2 * TOK * (mci + 1)])
                                    i6 = v.tensor_copy(RSC[pb + 32:pb + 64, :],
                                                       KF[pb:pb + 32, 2 * TOK * mci:2 * TOK * (mci + 1)])
                                    vedge(v, i5); vedge(v, i6)
                                    i7 = v.tensor_tensor(RSC[pb:pb + 64, :], RSC[pb:pb + 64, :],
                                                         KSIN[pb:pb + 64, :], OP.mult)
                                    i8 = v.tensor_tensor(RC2[pb:pb + 64, :], kv, KCOS[pb:pb + 64, :], OP.mult)
                                    vedge(v, i7); vedge(v, i8)
                                    i9 = v.tensor_tensor(kv, RC2[pb:pb + 64, :], RSC[pb:pb + 64, :], OP.add)
                                    vedge(v, i9)
                            v.tensor_copy(MV[:, 0:1], MV[:, 0:1]).then_inc(sem("V"), 1)
                        c.v(f_rope)
                        v_rope = bump("V")
                        import os as _os3
                        if _os3.environ.get("L3CUT") == "4" and hg == 0:
                            def fcut4(sy, vr=v_rope):
                                sy.wait_ge(sem("V"), vr)
                                sy.dma_start(out=o_xn0, in_=XN[:]).then_inc(sem("st"), 16)
                                sy.dma_start(out=o_qf, in_=QF[:]).then_inc(sem("st"), 16)
                                sy.dma_start(out=o_kf, in_=KF[:]).then_inc(sem("st"), 16)
                                sy.wait_ge(sem("st"), val("st") + 48)
                            c.sy(fcut4)
                            bump("st", 48)
                            hg_es.close()
                            return nc

                        def f_wvd(g, hg=hg, tw=val("T")):
                            g.wait_ge(sem("T"), tw)
                            g.dma_start(out=WVB[:].rearrange("p (cc m) -> p cc m", m=256),
                                        in_=i_wvg[hg]).then_inc(sem("ldm"), 16)
                            g.dma_start(out=WOTG[:], in_=i_wog[hg]).then_inc(sem("ldm"), 16)
                        c.g(f_wvd)
                        ld_v = bump("ldm", 32)
                        for half, SRC, toff in (("pfx", XNO, 0), ("own", XN, 16)):
                            for i in range(16):
                                PB = PP0 if i % 2 == 0 else PP1

                                def f_vmm(t, i=i, PB=PB, SRC=SRC, ld_v=ld_v, sw=val("S")):
                                    t.wait_ge(sem("S"), sw)
                                    if i == 0:
                                        t.wait_ge(sem("ldm"), ld_v)
                                    for cc in range(NCH):
                                        mm = t.matmul(PB[:, 0:256], SRC[:, TOK * cc + 128 * i:TOK * cc + 128 * (i + 1)],
                                                      WVB[:, 256 * cc:256 * (cc + 1)],
                                                      start=(cc == 0), stop=(cc == 7))
                                    mm.then_inc(sem("T"), 1)
                                c.t(f_vmm)
                                t_v = bump("T")

                                def f_vev(s_, i=i, toff=toff, PB=PB, t_v=t_v):
                                    s_.wait_ge(sem("T"), t_v)
                                    s_.activation(out=VT[:, 256 * (toff + i):256 * (toff + i + 1)], in_=PB[:, 0:256],
                                                  func=AF.Identity, bias=Z0[:]).then_inc(sem("S"), 1)
                                c.s(f_vev)
                                bump("S")
                        s_kv = val("S")

                        for qt in range(16):
                            nkt = 16 + qt + 1
                            if hg > 0:
                                def f_aol(sy, qt=qt, vv=val("V")):
                                    sy.wait_ge(sem("V"), vv)
                                    sy.dma_start(out=AOL[:],
                                                 in_=d_ao[:, D * qt:D * (qt + 1)]).then_inc(sem("lda"), 16)
                                c.sy(f_aol)
                                ld_ao = bump("lda", 16)
                            else:
                                ld_ao = None
                            for hi in range(4):
                                mci = hi // 2
                                pb = 64 * (hi % 2)
                                prev_scev = [None]
                                for kb in range((nkt + 3) // 4):
                                    k0 = 4 * kb
                                    kw = min(4, nkt - k0) * 128

                                    def f_sc(t, qt=qt, mci=mci, pb=pb, k0=k0, kw=kw, pv=prev_scev[0], s_kv=s_kv, v_rope=v_rope):
                                        if pv is not None:
                                            t.wait_ge(sem("V"), pv)
                                        else:
                                            t.wait_ge(sem("V"), v_rope)
                                            t.wait_ge(sem("S"), s_kv)
                                        mm = t.matmul(SPS[:, 0:kw],
                                                      QF[pb:pb + 64, TOK * mci + 128 * qt:TOK * mci + 128 * (qt + 1)],
                                                      KF[pb:pb + 64, 2 * TOK * mci + 128 * k0:2 * TOK * mci + 128 * k0 + kw],
                                                      start=True, stop=True)
                                        mm.then_inc(sem("T"), 1)
                                    c.t(f_sc)
                                    t_sc = bump("T")

                                    def f_scev(v, k0=k0, kw=kw, t_sc=t_sc, nkt=nkt):
                                        v.wait_ge(sem("T"), t_sc)
                                        for j in range(kw // 128):
                                            kt = k0 + j
                                            src = SPS[:, 128 * j:128 * (j + 1)]
                                            dst = SCC[:, 128 * kt:128 * (kt + 1)]
                                            if kt < 16:
                                                i1 = v.tensor_scalar(dst, src, AMSK[:], None, OP.add)
                                            elif kt == nkt - 1:
                                                i1 = v.tensor_tensor(dst, src, MCA[:], OP.add)
                                            else:
                                                i1 = v.tensor_copy(dst, src)
                                        i1.then_inc(sem("V"), 1)
                                    c.v(f_scev)
                                    prev_scev[0] = bump("V")

                                def f_sm(v, nkt=nkt, pv=prev_scev[0]):
                                    v.wait_ge(sem("V"), pv)
                                    i1 = v.reduce_max(NMX[:], SCC[:, 0:128 * nkt], axis=AX.X, negate=True)
                                    vedge(v, i1)
                                    v.tensor_scalar(NMX[:], NMX[:], 0.125, None, OP.mult).then_inc(sem("V"), 1)
                                c.v(f_sm)
                                v_mx = bump("V")

                                def f_exp(s_, nkt=nkt, v_mx=v_mx):
                                    s_.wait_ge(sem("V"), v_mx)
                                    s_.activation(out=SCC[:, 0:128 * nkt], in_=SCC[:, 0:128 * nkt], func=AF.Exp,
                                                  bias=NMX[:], scale=0.125, accum_out=DEN[:])
                                    s_.activation(out=NMX[:], in_=NMX[:], func=AF.Exp).then_inc(sem("S"), 1)
                                c.s(f_exp)
                                s_exp = bump("S")

                                def f_nrm(v, nkt=nkt, s_exp=s_exp):
                                    v.wait_ge(sem("S"), s_exp)
                                    i1 = v.reciprocal(RDEN[:], DEN[:])
                                    vedge(v, i1)
                                    v.tensor_scalar(SCC[:, 0:128 * nkt], SCC[:, 0:128 * nkt], RDEN[:],
                                                    None, OP.mult).then_inc(sem("V"), 1)
                                c.v(f_nrm)
                                v_nrm = bump("V")

                                prev_ctx = [None]
                                for kb in range((nkt + 3) // 4):
                                    k0 = 4 * kb
                                    nt_ = min(4, nkt - k0)

                                    def f_att(t, k0=k0, nt_=nt_, v_nrm=v_nrm, pv=prev_ctx[0]):
                                        t.wait_ge(sem("V"), pv if pv is not None else v_nrm)
                                        for j in range(nt_):
                                            mm = t.transpose(PT[:, 128 * j:128 * (j + 1)],
                                                             SCC[:, 128 * (k0 + j):128 * (k0 + j + 1)], IDN[:])
                                        mm.then_inc(sem("T"), 1)
                                    c.t(f_att)
                                    t_at = bump("T")

                                    def f_ate(v, nt_=nt_, t_at=t_at):
                                        v.wait_ge(sem("T"), t_at)
                                        v.tensor_copy(ATT[:, 0:128 * nt_], PT[:, 0:128 * nt_]).then_inc(sem("V"), 1)
                                    c.v(f_ate)
                                    v_ate = bump("V")
                                    prev_ctx[0] = v_ate

                                    def f_ctx(t, k0=k0, nt_=nt_, hi=hi, v_ate=v_ate, nkt=nkt):
                                        t.wait_ge(sem("V"), v_ate)
                                        for j in range(nt_):
                                            kt = k0 + j
                                            mm = t.matmul(CTXP[0:64, 0:128],
                                                          VT[:, 256 * kt + 64 * hi:256 * kt + 64 * (hi + 1)],
                                                          ATT[:, 128 * j:128 * (j + 1)],
                                                          start=(kt == 0), stop=(kt == nkt - 1))
                                        mm.then_inc(sem("T"), 1)
                                    c.t(f_ctx)
                                    t_ctx = bump("T")

                                def f_cte(v, t_ctx=val("T")):
                                    v.wait_ge(sem("T"), t_ctx)
                                    v.tensor_copy(CTXS[:], CTXP[0:64, 0:128]).then_inc(sem("V"), 1)
                                c.v(f_cte)
                                v_cte = bump("V")

                                def f_wo(t, hi=hi, v_cte=v_cte):
                                    t.wait_ge(sem("V"), v_cte)
                                    for n in range(2):
                                        mm = t.matmul(OUTP[:, 512 * n:512 * (n + 1)], CTXS[:],
                                                      WOTG[:, 1024 * hi + 512 * n:1024 * hi + 512 * (n + 1)],
                                                      start=(hi == 0), stop=(hi == 3 and n == 1),
                                                      skip_group_check=True)
                                    mm.then_inc(sem("T"), 1)
                                c.t(f_wo)
                                bump("T")
                            t_wo = val("T")

                            def f_oev(v, qt=qt, hg=hg, t_wo=t_wo, ld_ao=ld_ao):
                                v.wait_ge(sem("T"), t_wo)
                                if hg == 0:
                                    mmv = v.tensor_copy(AOB[:], OUTP[:, 0:1024])
                                else:
                                    v.wait_ge(sem("lda"), ld_ao)
                                    mmv = v.tensor_tensor(AOB[:], AOL[:], OUTP[:, 0:1024], OP.add)
                                mmv.then_inc(sem("V"), 1)
                            c.v(f_oev)
                            v_ao = bump("V")

                            def f_ast(sy, qt=qt, v_ao=v_ao):
                                sy.wait_ge(sem("V"), v_ao)
                                sy.dma_start(out=d_ao[:, D * qt:D * (qt + 1)],
                                             in_=AOB[:]).then_inc(sem("st"), 16)
                            c.sy(f_ast)
                            bump("st", 16)
                        if hg == 0:
                            def f_dbg(sy, vv=val("V")):
                                sy.wait_ge(sem("V"), vv)
                                sy.dma_start(out=o_scc, in_=SCC[:]).then_inc(sem("st"), 16)
                                sy.dma_start(out=o_den[:, 0:1], in_=DEN[:], allow_slow_non_contiguous=True).then_inc(sem("st"), 16)
                                sy.dma_start(out=o_den[:, 1:2], in_=NMX[:], allow_slow_non_contiguous=True).then_inc(sem("st"), 16)
                                sy.dma_start(out=o_qf, in_=QF[:]).then_inc(sem("st"), 16)
                                sy.dma_start(out=o_kf, in_=KF[:]).then_inc(sem("st"), 16)
                            c.sy(f_dbg)
                            stdbg = bump("st", 80)

                            def f_dbg_hold(t):
                                t.wait_ge(sem("st"), stdbg)
                            c.t(f_dbg_hold)
                        hg_es.close()
        v_attn = val("V")
        st_attn = val("st")

        with nc.sbuf_tensor("X3", [128, NT * D], F32) as X3, \
             nc.sbuf_tensor("GREP", [128, D], F32) as GREP, \
             nc.sbuf_tensor("BREP", [128, D], F32) as BREP, \
             nc.sbuf_tensor("FFT", [128, D], F32) as FFT, \
             nc.sbuf_tensor("FFT2", [128, D], F32) as FFT2, \
             nc.sbuf_tensor("AOL2", [128, 2 * D], BF16) as AOL2:
            def ldx3_sy(sy):
                sy.wait_ge(sem("st"), st_attn)
                sy.dma_start(out=X3[:], in_=i_x).then_inc(sem("ldf"), 16)
            c.sy(ldx3_sy)
            LDF = bump("ldf", 16)

            def f_rowcast(v):
                v.wait_ge(sem("ldx"), LDX)
                i1 = v.tensor_copy(LNGB[:], LNGR[:])
                i2 = v.tensor_copy(LNBB[:], LNBR[:])
                vedge(v, i1)
                i2.then_inc(sem("V"), 1)
            c.v(f_rowcast)
            v_rows = bump("V")

            def f_rep2(t):
                t.wait_ge(sem("V"), v_rows)
                t.wait_ge(sem("V"), v_attn)
                t.matmul(OUTP[:, 0:512], ONR[:], LNGB[:, 0:512], start=True, stop=True)
                t.matmul(OUTP[:, 512:1024], ONR[:], LNGB[:, 512:1024], start=True, stop=True)
                t.matmul(PTT[:, 0:512].bitcast(F32) if False else CTXP[:, 0:128], ONR[:], LNBB[:, 0:128], start=True, stop=True)
                mm = t.matmul(SPS[:, 0:512], ONR[:], LNBB[:, 0:512], start=True, stop=True)
                mm.then_inc(sem("T"), 1)
            def f_rep3(t):
                t.wait_ge(sem("V"), v_rows)
                t.wait_ge(sem("V"), v_attn)
                t.matmul(OUTP[:, 0:512], ONR[:], LNGB[:, 0:512], start=True, stop=True)
                t.matmul(OUTP[:, 512:1024], ONR[:], LNGB[:, 512:1024], start=True, stop=True)
                t.matmul(SPS[:, 0:512], ONR[:], LNBB[:, 0:512], start=True, stop=True)
                mm = t.matmul(PP0[:, 0:512], ONR[:], LNBB[:, 512:1024], start=True, stop=True)
                mm.then_inc(sem("T"), 1)
            c.t(f_rep3)
            t_rep = bump("T")

            def f_grep(v):
                v.wait_ge(sem("T"), t_rep)
                i1 = v.tensor_copy(GREP[:], OUTP[:, 0:1024])
                i2 = v.tensor_copy(BREP[:, 0:512], SPS[:, 0:512])
                i3 = v.tensor_copy(BREP[:, 512:1024], PP0[:, 0:512])
                vedge(v, i1); vedge(v, i2)
                i3.then_inc(sem("V"), 1)
            c.v(f_grep)
            v_grep = bump("V")

            v_pa = [0]
            for i in range(16):
                def f_aol2(sy, i=i):
                    sy.wait_ge(sem("ldf"), LDF)
                    if i >= 2:
                        sy.wait_ge(sem("V"), v_pa[0] - 1)
                    sy.dma_start(out=AOL2[:, D * (i % 2):D * (i % 2 + 1)],
                                 in_=d_ao[:, D * i:D * (i + 1)]).then_inc(sem("ldf"), 16)
                c.sy(f_aol2)
                ld_n = bump("ldf", 16)

                def f_f3(v, i=i, ld_n=ld_n):
                    if i == 0:
                        v.wait_ge(sem("V"), v_grep)
                    v.wait_ge(sem("ldf"), ld_n)
                    x = X3[:, D * i:D * (i + 1)]
                    i0 = v.reduce_sum(SQ[:, 0:1], x, axis=AX.X)
                    vedge(v, i0)
                    i1 = v.tensor_tensor(FFT[:], x, x, OP.mult)
                    vedge(v, i1)
                    i2 = v.reduce_sum(SQ[:, 1:2], FFT[:], axis=AX.X)
                    vedge(v, i2)
                    i3 = v.tensor_scalar(MV[:, 0:2], SQ[:, 0:2], 1.0 / D, None, OP.mult)
                    vedge(v, i3)
                    i4 = v.tensor_tensor(RS[:, 0:1], MV[:, 0:1], MV[:, 0:1], OP.mult)
                    vedge(v, i4)
                    v.tensor_tensor(MV[:, 1:2], MV[:, 1:2], RS[:, 0:1], OP.subtract).then_inc(sem("V"), 1)
                c.v(f_f3)
                v_st1 = bump("V")

                def f_f3s(s_, v_st1=v_st1):
                    s_.wait_ge(sem("V"), v_st1)
                    i1 = s_.activation(out=RS[:, 0:1], in_=MV[:, 1:2], func=AF.Ln, bias=EPS[:])
                    sedge(s_, i1)
                    s_.activation(out=RS[:, 0:1], in_=RS[:, 0:1], func=AF.Exp, scale=-0.5).then_inc(sem("S"), 1)
                c.s(f_f3s)
                s_r1 = bump("S")

                def f_f3b(v, i=i, s_r1=s_r1):
                    v.wait_ge(sem("S"), s_r1)
                    x = X3[:, D * i:D * (i + 1)]
                    i0 = v.tensor_scalar(FFT[:], x, MV[:, 0:1], RS[:, 0:1], OP.subtract, OP.mult)
                    vedge(v, i0)
                    i1 = v.reduce_sum(SQ[:, 2:3], FFT[:], axis=AX.X)
                    vedge(v, i1)
                    i2 = v.tensor_tensor(FFT2[:], FFT[:], FFT[:], OP.mult)
                    vedge(v, i2)
                    i3 = v.reduce_sum(SQ[:, 3:4], FFT2[:], axis=AX.X)
                    vedge(v, i3)
                    i4 = v.tensor_scalar(MV[:, 2:4], SQ[:, 2:4], 1.0 / D, None, OP.mult)
                    vedge(v, i4)
                    i5 = v.tensor_tensor(RS[:, 1:2], MV[:, 2:3], MV[:, 2:3], OP.mult)
                    vedge(v, i5)
                    v.tensor_tensor(MV[:, 3:4], MV[:, 3:4], RS[:, 1:2], OP.subtract).then_inc(sem("V"), 1)
                c.v(f_f3b)
                v_st2 = bump("V")

                def f_f3s2(s_, v_st2=v_st2):
                    s_.wait_ge(sem("V"), v_st2)
                    i1 = s_.activation(out=RS[:, 1:2], in_=MV[:, 3:4], func=AF.Ln, bias=EPS[:])
                    sedge(s_, i1)
                    s_.activation(out=RS[:, 1:2], in_=RS[:, 1:2], func=AF.Exp, scale=-0.5).then_inc(sem("S"), 1)
                c.s(f_f3s2)
                s_r2 = bump("S")

                def f_f3c(v, i=i, s_r2=s_r2):
                    v.wait_ge(sem("S"), s_r2)
                    x = X3[:, D * i:D * (i + 1)]
                    i0 = v.tensor_scalar(FFT[:], FFT[:], MV[:, 2:3], RS[:, 1:2], OP.subtract, OP.mult)
                    vedge(v, i0)
                    i1 = v.tensor_tensor(FFT[:], FFT[:], GREP[:], OP.mult)
                    vedge(v, i1)
                    i2 = v.tensor_tensor(FFT[:], FFT[:], BREP[:], OP.add)
                    vedge(v, i2)
                    i3 = v.tensor_tensor(x, x, FFT[:], OP.add)
                    vedge(v, i3)
                    v.tensor_tensor(x, x, AOL2[:, D * (i % 2):D * (i % 2 + 1)], OP.add).then_inc(sem("V"), 1)
                c.v(f_f3c)
                v_pa[0] = bump("V")
            v_fin = val("V")

            def ph9_sy(sy):
                sy.wait_ge(sem("V"), v_fin)
                sy.dma_start(out=o_x, in_=X3[:]).then_inc(sem("st"), 16)
                sy.wait_ge(sem("st"), val("st") + 16)
            c.sy(ph9_sy)
            bump("st", 16)
    return nc


DBG0 = False


def build_L0a():
    nc = bass.Bass("TRN2", target_bir_lowering=False)
    i_x = nc.dram_tensor("xloc", [128, NT * D], F32, kind="ExternalInput").ap()
    i_xh = nc.dram_tensor("xhk", [128, D], F32, kind="ExternalInput").ap()
    i_wqm = nc.dram_tensor("wqm", [NCH, 128, NCH * 128], F32, kind="ExternalInput").ap()
    i_wkm = nc.dram_tensor("wkm", [NCH, 128, NCH * 128], F32, kind="ExternalInput").ap()
    i_wvg = nc.dram_tensor("wvg", [4, 128, NCH * 256], F32, kind="ExternalInput").ap()
    i_wot = nc.dram_tensor("wot", [64, 16 * D], F32, kind="ExternalInput").ap()
    i_bq = nc.dram_tensor("bqc", [128, 8], F32, kind="ExternalInput").ap()
    i_bk = nc.dram_tensor("bkc", [128, 8], F32, kind="ExternalInput").ap()
    i_bvr = nc.dram_tensor("bvr", [1, D], F32, kind="ExternalInput").ap()
    i_bor = nc.dram_tensor("bor", [1, D], F32, kind="ExternalInput").ap()
    i_qcos = nc.dram_tensor("qcos", [128, TOK], F32, kind="ExternalInput").ap()
    i_qsin = nc.dram_tensor("qsin", [128, TOK], F32, kind="ExternalInput").ap()
    i_kcos = nc.dram_tensor("kcos", [128, TOK + 128], F32, kind="ExternalInput").ap()
    i_ksin = nc.dram_tensor("ksin", [128, TOK + 128], F32, kind="ExternalInput").ap()
    i_mprev = nc.dram_tensor("mprev", [128, 128], BF16, kind="ExternalInput").ap()
    i_mown = nc.dram_tensor("mown", [128, 128], BF16, kind="ExternalInput").ap()
    i_am0 = nc.dram_tensor("amsk0", [128, 128], BF16, kind="ExternalInput").ap()
    i_idn = nc.dram_tensor("idn", [128, 128], BF16, kind="ExternalInput").ap()
    i_onr = nc.dram_tensor("onesr", [1, 128], BF16, kind="ExternalInput").ap()
    o_y = nc.dram_tensor("ypart", [128, NT * D], F32, kind="ExternalOutput").ap()
    o_z = nc.dram_tensor("z0", [128, NCH * TOK], BF16, kind="ExternalOutput").ap()

    KVW = TOK + 128

    es = contextlib.ExitStack()
    with es:
        c = Ctx(nc, es)
        sem, bump, val = c.sem, c.bump, c.val

        def vedge(v, instr):
            n = bump("V"); instr.then_inc(sem("V"), 1); v.wait_ge(sem("V"), n)

        def sedge(s_, instr):
            n = bump("S"); instr.then_inc(sem("S"), 1); s_.wait_ge(sem("S"), n)

        MV = c.sb("MV", [128, 40], F32)
        RS = c.sb("RS", [128, 19], F32)
        SQ = c.sb("SQ", [128, 40], F32)
        XNT = c.sb("XNT", [128, 2 * D], BF16)
        IDN = c.sb("IDN", [128, 128], BF16)
        ONR = c.sb("ONR", [1, 128], BF16)
        EPS = c.sb("EPS", [128, 1], F32)
        Z0C = c.sb("Z0C", [128, 1], F32)
        NMX = c.sb("NMX", [128, 1], F32)
        DEN = c.sb("DEN", [128, 1], F32)
        RDEN = c.sb("RDEN", [128, 1], F32)
        ATT = c.sb("ATT", [128, 256], BF16)
        CTXS = c.sb("CTXS", [64, 128], BF16)
        SCB = c.sb("SCB", [128, 256], BF16)
        MPREV = c.sb("MPREV", [128, 128], BF16)
        MOWN = c.sb("MOWN", [128, 128], BF16)
        AM0 = c.sb("AM0", [128, 128], BF16)
        XN = c.sb("XN", [128, NCH * KVW], BF16)
        BQ = c.sb("BQ", [128, 8], F32)
        BK = c.sb("BK", [128, 8], F32)
        BVR = c.sb("BVR", [1, D], BF16)
        BOR = c.sb("BOR", [1, D], BF16)
        YSTG = c.sb("YSTG", [128, 2 * D], F32)
        PP0 = c.ps("PP0", [128, 512], F32)
        PP1 = c.ps("PP1", [128, 512], F32)
        SPS = c.ps("SPS", [128, 512], F32)
        OUTP = c.ps("OUTP", [128, 1024], F32)
        CTXP = c.ps("CTXP", [128, 128], F32)
        PTT = c.ps("PTT", [128, 1024], BF16)
        PT = PTT[:]

        def ph0_g(g):
            for dst, src in [(IDN, i_idn), (ONR, i_onr), (MPREV, i_mprev), (MOWN, i_mown),
                             (AM0, i_am0), (BVR, i_bvr), (BOR, i_bor)]:
                g.dma_start(out=dst[:], in_=src).then_inc(sem("ldw"), 16)
            g.memset(EPS[:], 1e-5)
            g.memset(Z0C[:], 0.0).then_inc(sem("ldz"), 1)
        c.g(ph0_g)
        LDW = bump("ldw", 7 * 16); LDZ = bump("ldz", 1)

        def ph0_sy(sy):
            sy.dma_start(out=BQ[:], in_=i_bq).then_inc(sem("ldx"), 16)
            sy.dma_start(out=BK[:], in_=i_bk).then_inc(sem("ldx"), 16)
        c.sy(ph0_sy)
        LDX = bump("ldx", 32)

        with nc.sbuf_tensor("X", [128, NT * D], F32) as X, \
             nc.sbuf_tensor("XH", [128, D], F32) as XH:
            def ldx_sy(sy):
                sy.dma_start(out=X[:], in_=i_x).then_inc(sem("ldx"), 16)
                sy.dma_start(out=XH[:], in_=i_xh).then_inc(sem("ldx"), 16)
            c.sy(ldx_sy)
            LDX2 = bump("ldx", 32)

            def get_xt(i):
                return XH[:] if i == 0 else X[:, D * (i - 1):D * i]

            def f_ssq(s_):
                s_.wait_ge(sem("ldx"), LDX2)
                s_.wait_ge(sem("ldw"), LDW); s_.wait_ge(sem("ldz"), 1)
                for i in range(17):
                    s_.activation(out=XNT[:, 0:D], in_=get_xt(i), func=AF.Square,
                                  accum_out=SQ[:, 20 + i:21 + i])
                s_.copy(RS[:, 18:19], SQ[:, 36:37]).then_inc(sem("S"), 1)
            c.s(f_ssq)
            s_sq = bump("S")

            def f_vst(v):
                v.wait_ge(sem("ldx"), LDX2)
                v.wait_ge(sem("ldw"), LDW); v.wait_ge(sem("ldz"), 1)
                for i in range(17):
                    v.reduce_sum(SQ[:, i:i + 1], get_xt(i), axis=AX.X)
                i1 = v.tensor_scalar(MV[:, 0:17], SQ[:, 0:17], 1.0 / D, None, OP.mult)
                vedge(v, i1)
                v.wait_ge(sem("S"), s_sq)
                i2 = v.tensor_scalar(MV[:, 20:37], SQ[:, 20:37], 1.0 / D, None, OP.mult)
                vedge(v, i2)
                i3 = v.tensor_tensor(RS[:, 0:17], MV[:, 0:17], MV[:, 0:17], OP.mult)
                vedge(v, i3)
                v.tensor_tensor(MV[:, 20:37], MV[:, 20:37], RS[:, 0:17], OP.subtract).then_inc(sem("V"), 1)
            c.v(f_vst)
            v_st = bump("V")

            def f_rs(s_):
                s_.wait_ge(sem("V"), v_st)
                i1 = s_.activation(out=RS[:, 0:17], in_=MV[:, 20:37], func=AF.Ln, bias=EPS[:])
                sedge(s_, i1)
                s_.activation(out=RS[:, 0:17], in_=RS[:, 0:17], func=AF.Exp, scale=-0.5).then_inc(sem("S"), 1)
            c.s(f_rs)
            s_rs = bump("S")

            v_ap = [0] * 17
            t_tr = [0] * 17
            v_ev = [0] * 17
            XNv = XN[:].rearrange("p (ch w) -> p ch w", w=KVW)
            PTv = PT.rearrange("p (ch w) -> p ch w", w=128)
            for i in range(17):
                buf = i % 2

                def f_ap(v, i=i, buf=buf):
                    v.wait_ge(sem("S"), s_rs)
                    if i >= 2:
                        v.wait_ge(sem("T"), t_tr[i - 2])
                    v.tensor_scalar(XNT[:, D * buf:D * (buf + 1)], get_xt(i),
                                    MV[:, i:i + 1], RS[:, i:i + 1],
                                    OP.subtract, OP.mult).then_inc(sem("V"), 1)
                c.v(f_ap)
                v_ap[i] = bump("V")

                def f_tr(t, i=i, buf=buf):
                    t.wait_ge(sem("V"), v_ap[i])
                    if i >= 1:
                        t.wait_ge(sem("V"), v_ev[i - 1])
                    for ch in range(NCH):
                        mm = t.transpose(PT[:, 128 * ch:128 * (ch + 1)],
                                         XNT[:, D * buf + 128 * ch:D * buf + 128 * (ch + 1)], IDN[:])
                    mm.then_inc(sem("T"), 1)
                c.t(f_tr)
                t_tr[i] = bump("T")

                def f_ev(v, i=i):
                    v.wait_ge(sem("T"), t_tr[i])
                    v.tensor_copy(XNv[:, :, 128 * i:128 * (i + 1)], PTv[:, 0:NCH, :]).then_inc(sem("V"), 1)
                c.v(f_ev)
                v_ev[i] = bump("V")
            v_ln = val("V")

        QF = c.sb("QF", [128, NCH * TOK], BF16)
        KF = c.sb("KF", [128, NCH * KVW], BF16)
        VT = c.sb("VT", [128, 17 * D], BF16)
        prod_es = contextlib.ExitStack()
        QCOS = prod_es.enter_context(nc.sbuf_tensor("QCOS", [128, TOK], BF16))
        QSIN = prod_es.enter_context(nc.sbuf_tensor("QSIN", [128, TOK], BF16))
        KCOS = prod_es.enter_context(nc.sbuf_tensor("KCOS", [128, KVW], BF16))
        KSIN = prod_es.enter_context(nc.sbuf_tensor("KSIN", [128, KVW], BF16))
        RSC = prod_es.enter_context(nc.sbuf_tensor("RSC", [128, KVW], BF16))
        RC2 = prod_es.enter_context(nc.sbuf_tensor("RC2", [128, KVW], BF16))
        WMB = prod_es.enter_context(nc.sbuf_tensor("WMB", [128, 2 * NCH * 128], BF16))
        WVB = prod_es.enter_context(nc.sbuf_tensor("WVB", [128, NCH * 256], BF16))

        def f_tbl2(g, vv=val("V")):
            g.wait_ge(sem("V"), vv)
            for dst, src in [(QCOS, i_qcos), (QSIN, i_qsin), (KCOS, i_kcos), (KSIN, i_ksin)]:
                g.dma_start(out=dst[:], in_=src).then_inc(sem("ldw"), 16)
        c.g(f_tbl2)
        LDT = bump("ldw", 64)

        def f_z0(sy):
            sy.wait_ge(sem("V"), v_ln)
            sy.dma_start(out=o_z.rearrange("p (ch w) -> p ch w", w=TOK),
                         in_=XN[:].rearrange("p (ch w) -> p ch w", w=KVW)[:, :, 128:KVW]).then_inc(sem("st"), 16)
        c.sy(f_z0)
        bump("st", 16)

        prev_ev = {}
        for which, i_wm, DST, off, wid, BC in (("q", i_wqm, QF, 128, TOK, BQ), ("k", i_wkm, KF, 0, KVW, BK)):
            for mch in range(NCH):
                def f_wd(g, mch=mch, i_wm=i_wm, tw=val("T"), vv=val("V")):
                    g.wait_ge(sem("T"), tw)
                    if mch == 0:
                        g.wait_ge(sem("V"), vv)
                    g.dma_start(out=WMB[:, 1024 * (mch % 2):1024 * (mch % 2 + 1)],
                                in_=i_wm[mch]).then_inc(sem("ldm"), 16)
                c.g(f_wd)
                ld_m = bump("ldm", 16)
                ns = (wid + 511) // 512
                for s4 in range(ns):
                    a = 512 * s4
                    w_ = min(512, wid - a)
                    PB = PP0 if s4 % 2 == 0 else PP1

                    def f_mm(t, mch=mch, s4=s4, a=a, w_=w_, PB=PB, ld_m=ld_m, off=off, sw=val("S")):
                        t.wait_ge(sem("S"), sw)
                        if s4 == 0:
                            t.wait_ge(sem("ldm"), ld_m)
                        src_a = a if off == 0 else a + 128
                        for cc in range(NCH):
                            mm = t.matmul(PB[:, 0:w_],
                                          WMB[:, 1024 * (mch % 2) + 128 * cc:1024 * (mch % 2) + 128 * (cc + 1)],
                                          XN[:, KVW * cc + (0 if off == 0 else 128) + a:KVW * cc + (0 if off == 0 else 128) + a + w_],
                                          start=(cc == 0), stop=(cc == 7))
                        mm.then_inc(sem("T"), 1)
                    c.t(f_mm)
                    t_mm = bump("T")

                    def f_ev(s_, mch=mch, a=a, w_=w_, PB=PB, t_mm=t_mm, DST=DST, wid=wid, BC=BC):
                        s_.wait_ge(sem("T"), t_mm)
                        s_.activation(out=DST[:, wid * mch + a:wid * mch + a + w_], in_=PB[:, 0:w_],
                                      func=AF.Identity, bias=BC[:, mch:mch + 1]).then_inc(sem("S"), 1)
                    c.s(f_ev)
                    bump("S")
        s_prj = val("S")

        def f_rope(v):
            v.wait_ge(sem("S"), s_prj)
            v.wait_ge(sem("ldw"), LDT)
            for mci in range(NCH):
                for pb in (0, 64):
                    qv = QF[pb:pb + 64, TOK * mci:TOK * (mci + 1)]
                    i0 = v.tensor_copy(RSC[pb:pb + 32, 0:TOK], QF[pb + 32:pb + 64, TOK * mci:TOK * (mci + 1)])
                    i1 = v.tensor_copy(RSC[pb + 32:pb + 64, 0:TOK], QF[pb:pb + 32, TOK * mci:TOK * (mci + 1)])
                    vedge(v, i0); vedge(v, i1)
                    i2 = v.tensor_tensor(RSC[pb:pb + 64, 0:TOK], RSC[pb:pb + 64, 0:TOK], QSIN[pb:pb + 64, :], OP.mult)
                    i3 = v.tensor_tensor(RC2[pb:pb + 64, 0:TOK], qv, QCOS[pb:pb + 64, :], OP.mult)
                    vedge(v, i2); vedge(v, i3)
                    i4 = v.tensor_tensor(qv, RC2[pb:pb + 64, 0:TOK], RSC[pb:pb + 64, 0:TOK], OP.add)
                    vedge(v, i4)
                    kv = KF[pb:pb + 64, KVW * mci:KVW * (mci + 1)]
                    i5 = v.tensor_copy(RSC[pb:pb + 32, :], KF[pb + 32:pb + 64, KVW * mci:KVW * (mci + 1)])
                    i6 = v.tensor_copy(RSC[pb + 32:pb + 64, :], KF[pb:pb + 32, KVW * mci:KVW * (mci + 1)])
                    vedge(v, i5); vedge(v, i6)
                    i7 = v.tensor_tensor(RSC[pb:pb + 64, :], RSC[pb:pb + 64, :], KSIN[pb:pb + 64, :], OP.mult)
                    i8 = v.tensor_tensor(RC2[pb:pb + 64, :], kv, KCOS[pb:pb + 64, :], OP.mult)
                    vedge(v, i7); vedge(v, i8)
                    i9 = v.tensor_tensor(kv, RC2[pb:pb + 64, :], RSC[pb:pb + 64, :], OP.add)
                    vedge(v, i9)
            v.tensor_copy(MV[:, 0:1], MV[:, 0:1]).then_inc(sem("V"), 1)
        c.v(f_rope)
        v_rope = bump("V")

        for hg in range(4):
            def f_wvd(g, hg=hg, tw=val("T")):
                g.wait_ge(sem("T"), tw)
                g.dma_start(out=WVB[:].rearrange("p (cc m) -> p cc m", m=256),
                            in_=i_wvg[hg]).then_inc(sem("ldm"), 16)
            c.g(f_wvd)
            ld_v = bump("ldm", 16)
            for i in range(17):
                PB = PP0 if i % 2 == 0 else PP1

                def f_vmm(t, i=i, hg=hg, PB=PB, ld_v=ld_v, sw=val("S")):
                    t.wait_ge(sem("S"), sw)
                    if i == 0:
                        t.wait_ge(sem("ldm"), ld_v)
                    for cc in range(NCH):
                        t.matmul(PB[:, 0:256], XN[:, KVW * cc + 128 * i:KVW * cc + 128 * (i + 1)],
                                 WVB[:, 256 * cc:256 * (cc + 1)],
                                 start=(cc == 0), stop=False)
                    mm = t.matmul(PB[:, 0:256], ONR[:], BVR[:, 256 * hg:256 * (hg + 1)],
                                  start=False, stop=True)
                    mm.then_inc(sem("T"), 1)
                c.t(f_vmm)
                t_v = bump("T")

                def f_vev(s_, i=i, hg=hg, PB=PB, t_v=t_v):
                    s_.wait_ge(sem("T"), t_v)
                    s_.activation(out=VT[:, D * i + 256 * hg:D * i + 256 * (hg + 1)], in_=PB[:, 0:256],
                                  func=AF.Identity, bias=Z0C[:]).then_inc(sem("S"), 1)
                c.s(f_vev)
                bump("S")
        s_kv = val("S")
        prod_es.close()
        WOT = c.sb("WOT", [64, 16 * D], BF16)

        def f_wotld(g, tw=val("T")):
            g.wait_ge(sem("T"), tw)
            g.wait_ge(sem("V"), v_rope)
            g.dma_start(out=WOT[:], in_=i_wot).then_inc(sem("ldw"), 16)
        c.g(f_wotld)
        LDWOT = bump("ldw", 16)

        with nc.sbuf_tensor("X2", [128, 2 * D], F32) as X2:
            v_ao = [0]
            styst = {}
            for qt in range(16):
                def f_x2(sy, qt=qt, vv=val("V"), tv=val("T")):
                    if qt < 2:
                        sy.wait_ge(sem("V"), vv)
                        sy.wait_ge(sem("T"), tv)
                    if qt >= 2:
                        sy.wait_ge(sem("V"), v_ao[0] - 1)
                    sy.dma_start(out=X2[:, D * (qt % 2):D * (qt % 2 + 1)],
                                 in_=i_x[:, D * qt:D * (qt + 1)]).then_inc(sem("ldq"), 16)
                c.sy(f_x2)
                ld_x2 = bump("ldq", 16)
                for hi in range(16):
                    mci = hi // 2
                    pb = 64 * (hi % 2)

                    def f_sc(t, qt=qt, mci=mci, pb=pb, vw=v_rope, sw=s_kv, pv=val("V")):
                        t.wait_ge(sem("V"), vw)
                        t.wait_ge(sem("S"), sw)
                        t.wait_ge(sem("V"), pv)
                        mm = t.matmul(SPS[:, 0:256],
                                      QF[pb:pb + 64, TOK * mci + 128 * qt:TOK * mci + 128 * (qt + 1)],
                                      KF[pb:pb + 64, KVW * mci + 128 * qt:KVW * mci + 128 * qt + 256],
                                      start=True, stop=True)
                        mm.then_inc(sem("T"), 1)
                    c.t(f_sc)
                    t_sc = bump("T")

                    def f_scev(v, qt=qt, t_sc=t_sc):
                        v.wait_ge(sem("T"), t_sc)
                        MP = AM0 if qt == 0 else MPREV
                        i1 = v.tensor_tensor(SCB[:, 0:128], SPS[:, 0:128], MP[:], OP.add)
                        i2 = v.tensor_tensor(SCB[:, 128:256], SPS[:, 128:256], MOWN[:], OP.add)
                        vedge(v, i1); vedge(v, i2)
                        i3 = v.reduce_max(NMX[:], SCB[:, 0:256], axis=AX.X, negate=True)
                        vedge(v, i3)
                        v.tensor_scalar(NMX[:], NMX[:], 0.125, None, OP.mult).then_inc(sem("V"), 1)
                    c.v(f_scev)
                    v_mx = bump("V")

                    def f_exp(s_, v_mx=v_mx):
                        s_.wait_ge(sem("V"), v_mx)
                        s_.activation(out=SCB[:, 0:256], in_=SCB[:, 0:256], func=AF.Exp,
                                      bias=NMX[:], scale=0.125, accum_out=DEN[:])
                        s_.activation(out=NMX[:], in_=NMX[:], func=AF.Exp).then_inc(sem("S"), 1)
                    c.s(f_exp)
                    s_exp = bump("S")

                    def f_nrm(v, s_exp=s_exp):
                        v.wait_ge(sem("S"), s_exp)
                        i1 = v.reciprocal(RDEN[:], DEN[:])
                        vedge(v, i1)
                        v.tensor_scalar(SCB[:, 0:256], SCB[:, 0:256], RDEN[:], None, OP.mult).then_inc(sem("V"), 1)
                    c.v(f_nrm)
                    v_nrm = bump("V")

                    def f_att(t, v_nrm=v_nrm):
                        t.wait_ge(sem("V"), v_nrm)
                        t.transpose(PT[:, 0:128], SCB[:, 0:128], IDN[:])
                        mm = t.transpose(PT[:, 128:256], SCB[:, 128:256], IDN[:])
                        mm.then_inc(sem("T"), 1)
                    c.t(f_att)
                    t_at = bump("T")

                    def f_ate(v, t_at=t_at):
                        v.wait_ge(sem("T"), t_at)
                        v.tensor_copy(ATT[:], PT[:, 0:256]).then_inc(sem("V"), 1)
                    c.v(f_ate)
                    v_at = bump("V")

                    def f_ctx(t, qt=qt, hi=hi, v_at=v_at):
                        t.wait_ge(sem("V"), v_at)
                        t.matmul(CTXP[0:64, 0:128], VT[:, D * qt + 64 * hi:D * qt + 64 * (hi + 1)],
                                 ATT[:, 0:128], start=True, stop=False)
                        mm = t.matmul(CTXP[0:64, 0:128], VT[:, D * (qt + 1) + 64 * hi:D * (qt + 1) + 64 * (hi + 1)],
                                      ATT[:, 128:256], start=False, stop=True)
                        mm.then_inc(sem("T"), 1)
                    c.t(f_ctx)
                    t_ctx = bump("T")

                    def f_cte(v, t_ctx=t_ctx):
                        v.wait_ge(sem("T"), t_ctx)
                        v.tensor_copy(CTXS[:], CTXP[0:64, 0:128]).then_inc(sem("V"), 1)
                    c.v(f_cte)
                    v_cte = bump("V")

                    def f_wo(t, hi=hi, v_cte=v_cte):
                        t.wait_ge(sem("V"), v_cte)
                        if hi == 0:
                            t.wait_ge(sem("ldw"), LDWOT)
                        mm = None
                        for n in range(2):
                            mm = t.matmul(OUTP[:, 512 * n:512 * (n + 1)], CTXS[:],
                                          WOT[:, D * hi + 512 * n:D * hi + 512 * (n + 1)],
                                          start=(hi == 0), stop=False, skip_group_check=True)
                        if hi == 15:
                            t.matmul(OUTP[:, 0:512], ONR[:], BOR[:, 0:512],
                                     start=False, stop=False, skip_group_check=True)
                            mm = t.matmul(OUTP[:, 512:1024], ONR[:], BOR[:, 512:1024],
                                          start=False, stop=True, skip_group_check=True)
                            mm.then_inc(sem("T"), 1)
                    c.t(f_wo)
                    if hi == 15:
                        t_wo = bump("T")

                def f_yev(v, qt=qt, t_wo=t_wo, ld_x2=ld_x2):
                    v.wait_ge(sem("T"), t_wo)
                    v.wait_ge(sem("ldq"), ld_x2)
                    if qt >= 2 and (qt - 2) in styst:
                        v.wait_ge(sem("st"), styst[qt - 2])
                    v.tensor_tensor(YSTG[:, D * (qt % 2):D * (qt % 2 + 1)],
                                    X2[:, D * (qt % 2):D * (qt % 2 + 1)],
                                    OUTP[:, 0:1024], OP.add).then_inc(sem("V"), 1)
                c.v(f_yev)
                v_ao[0] = bump("V")

                def f_yst(sy, qt=qt, vv=v_ao[0]):
                    sy.wait_ge(sem("V"), vv)
                    sy.dma_start(out=o_y[:, D * qt:D * (qt + 1)],
                                 in_=YSTG[:, D * (qt % 2):D * (qt % 2 + 1)]).then_inc(sem("st"), 16)
                c.sy(f_yst)
                styst[qt] = bump("st", 16)

        if DBG0:
            o_dxn = nc.dram_tensor("dxn", [128, NCH * KVW], BF16, kind="ExternalOutput").ap()
            o_dqf = nc.dram_tensor("dqf", [128, NCH * TOK], BF16, kind="ExternalOutput").ap()
            o_dkf = nc.dram_tensor("dkf", [128, NCH * KVW], BF16, kind="ExternalOutput").ap()
            o_dvt = nc.dram_tensor("dvt", [128, 17 * D], BF16, kind="ExternalOutput").ap()

            def dbg_sy(sy, sv=val("st")):
                sy.wait_ge(sem("st"), sv)
                sy.dma_start(out=o_dxn, in_=XN[:]).then_inc(sem("st"), 16)
                sy.dma_start(out=o_dqf, in_=QF[:]).then_inc(sem("st"), 16)
                sy.dma_start(out=o_dkf, in_=KF[:]).then_inc(sem("st"), 16)
                sy.dma_start(out=o_dvt, in_=VT[:]).then_inc(sem("st"), 16)
            c.sy(dbg_sy)
            bump("st", 64)

        def fin_sy(sy):
            sy.wait_ge(sem("st"), val("st"))
        c.sy(fin_sy)
    return nc


L0F_PHASE = 4
L0F_NCI = 22
L0F_GELU = True


def build_L0f(first=False):
    nc = bass.Bass("TRN2", target_bir_lowering=False)
    ZDT = BF16 if first else F32
    QW = 17 * 128
    KW = 19 * 128
    ZW = 20 * 128
    i_zl = nc.dram_tensor("zloc", [128, NCH * TOK], ZDT, kind="ExternalInput").ap()
    i_zh = nc.dram_tensor("zhalo", [128, NCH * 512], ZDT, kind="ExternalInput").ap()
    i_zr = nc.dram_tensor("zres", [17, 128, D], ZDT, kind="ExternalInput").ap()
    i_wqh = nc.dram_tensor("wqh", [2, 128, 4096], F32, kind="ExternalInput").ap()
    i_wkh = nc.dram_tensor("wkh", [2, 128, 4096], F32, kind="ExternalInput").ap()
    i_wvh = nc.dram_tensor("wvh", [2, 128, 4096], F32, kind="ExternalInput").ap()
    i_wot = nc.dram_tensor("wot", [64, 16 * D], F32, kind="ExternalInput").ap()
    i_w1s = nc.dram_tensor("w1s", [22, 128, 2048], F32, kind="ExternalInput").ap()
    i_w2s = nc.dram_tensor("w2s", [128, 22 * D], F32, kind="ExternalInput").ap()
    i_b1c = nc.dram_tensor("b1c", [128, 44], F32, kind="ExternalInput").ap()
    i_b2c = nc.dram_tensor("b2c", [128, 8], F32, kind="ExternalInput").ap()
    i_gbc = nc.dram_tensor("gbc", [128, 16], F32, kind="ExternalInput").ap()
    i_qsc4 = nc.dram_tensor("qsc4", [128, 4], F32, kind="ExternalInput").ap()
    i_qcs = nc.dram_tensor("qcs", [128, QW], F32, kind="ExternalInput").ap()
    i_qss = nc.dram_tensor("qss", [128, QW], F32, kind="ExternalInput").ap()
    i_kcs = nc.dram_tensor("kcs", [128, KW], F32, kind="ExternalInput").ap()
    i_kss = nc.dram_tensor("kss", [128, KW], F32, kind="ExternalInput").ap()
    i_mb4 = nc.dram_tensor("mb4", [128, 4 * 384], BF16, kind="ExternalInput").ap()
    i_shz = nc.dram_tensor("shz", [128, 1], F32, kind="ExternalInput").ap()
    i_idn = nc.dram_tensor("idn", [128, 128], BF16, kind="ExternalInput").ap()
    i_idnf = nc.dram_tensor("idnf", [128, 128], F32, kind="ExternalInput").ap()
    i_onec = nc.dram_tensor("onec", [128, 1], BF16, kind="ExternalInput").ap()
    i_onr = nc.dram_tensor("onesr", [1, 128], BF16, kind="ExternalInput").ap()
    o_z = nc.dram_tensor("zout", [128, NCH * TOK], F32, kind="ExternalOutput").ap()
    d_ysc = nc.dram_tensor("yscr", [128, NCH * QW], F32, kind="Internal").ap()
    yscv = d_ysc.rearrange("p (c w) -> p c w", w=QW)

    es = contextlib.ExitStack()
    with es:
        c = Ctx(nc, es)
        sem, bump, val = c.sem, c.bump, c.val

        def vedge(v, instr):
            n = bump("V"); instr.then_inc(sem("V"), 1); v.wait_ge(sem("V"), n)

        MB4 = c.sb("MB4", [128, 4 * 384], BF16)
        QSC4 = c.sb("QSC4", [128, 4], F32)
        GBC = c.sb("GBC", [128, 16], F32)
        B1C = c.sb("B1C", [128, 44], F32)
        B2C = c.sb("B2C", [128, 8], F32)
        SHZ = c.sb("SHZ", [128, 1], F32)
        IDN = c.sb("IDN", [128, 128], BF16)
        IDNF = c.sb("IDNF", [128, 128], F32)
        ONEC = c.sb("ONEC", [128, 1], BF16)
        ONR = c.sb("ONR", [1, 128], BF16)
        EPS = c.sb("EPS", [128, 1], F32)
        BTINY = c.sb("BTINY", [128, 1], F32)
        NMX = c.sb("NMX", [128, 1], F32)
        DEN = c.sb("DEN", [128, 1], F32)
        RDEN = c.sb("RDEN", [128, 1], F32)
        SSQ = c.sb("SSQ", [128, 16], F32)
        NRM = c.sb("NRM", [128, 16], F32)
        NRM2 = c.sb("NRM2", [128, 16], F32)
        TRW = c.sb("TRW", [128, 64], F32)
        CTXS = c.sb("CTXS", [64, 128], BF16)
        SCB = c.sb("SCB", [128, 384], BF16)
        ATT = c.sb("ATT", [128, 384], BF16)

        PP0 = c.ps("PP0", [128, 512], F32)
        PP1 = c.ps("PP1", [128, 512], F32)
        OUTP = c.ps("OUTP", [128, 1024], F32)
        PTT = c.ps("PTT", [128, 1024], BF16)
        PTF = c.ps("PTF", [128, 1024], F32)
        PSB = c.ps("PSB", [128, 512], F32)
        SPS = PP0[:, 0:384]
        CTXP = PP1[0:64, 0:128]
        PTB = PTT[:, 0:384]

        def ph0_g(g):
            for dst, src in [(MB4, i_mb4), (QSC4, i_qsc4), (GBC, i_gbc), (B1C, i_b1c),
                             (B2C, i_b2c), (SHZ, i_shz), (IDN, i_idn), (IDNF, i_idnf),
                             (ONEC, i_onec), (ONR, i_onr)]:
                g.dma_start(out=dst[:], in_=src).then_inc(sem("ldw"), 16)
            g.memset(EPS[:], 1e-5)
            g.memset(BTINY[:], 1e-30).then_inc(sem("ldz"), 1)
        c.g(ph0_g)
        LDW0 = bump("ldw", 10 * 16); LDZ = bump("ldz", 1)

        scopeQ = contextlib.ExitStack()
        QF = scopeQ.enter_context(nc.sbuf_tensor("QF", [128, NCH * QW], BF16))
        KF = scopeQ.enter_context(nc.sbuf_tensor("KF", [128, NCH * KW], BF16))
        VT = scopeQ.enter_context(nc.sbuf_tensor("VT", [128, 19 * D], BF16))
        QFv = QF[:].rearrange("p (c w) -> p c w", w=QW)
        KFv = KF[:].rearrange("p (c w) -> p c w", w=KW)

        with nc.sbuf_tensor("ZB", [128, NCH * ZW], BF16) as ZB, \
             nc.sbuf_tensor("WQKB", [128, 2 * 4096], BF16) as WQKB:
            ZBv = ZB[:].rearrange("p (c w) -> p c w", w=ZW)

            def ph0z(g):
                g.dma_start(out=ZBv[:, :, 0:512], in_=i_zh.rearrange("p (c w) -> p c w", w=512)).then_inc(sem("ldw"), 16)
                g.dma_start(out=ZBv[:, :, 512:ZW], in_=i_zl.rearrange("p (c w) -> p c w", w=TOK)).then_inc(sem("ldw"), 16)
            c.g(ph0z)
            LDZB = bump("ldw", 32)

            def zsh(fc, c0, w):
                off = 0 if fc < 4 else -1
                return ZBv[:, fc, c0 + off:c0 + off + w]

            projs = [("q", i_wqh, 17, 384), ("k", i_wkh, 19, 128), ("v", i_wvh, 19, 128)]
            bi = 0
            last_mm_for_buf = {0: 0, 1: 0}
            last_ev_for_pp = {0: 0, 1: 0}
            ppi = 0
            for nm, i_w, NTIL, BASE in projs:
                for n in (0, 1):
                    buf = bi % 2

                    def f_wld(g, i_w=i_w, n=n, buf=buf, tw=last_mm_for_buf[buf]):
                        g.wait_ge(sem("T"), tw)
                        g.dma_start(out=WQKB[:, 4096 * buf:4096 * (buf + 1)],
                                    in_=i_w[n]).then_inc(sem("ldm"), 16)
                    c.g(f_wld)
                    ld = bump("ldm", 16)
                    for m in range(NTIL):
                        PB = PP0 if ppi % 2 == 0 else PP1
                        pkey = ppi % 2

                        def f_mm(t, nm=nm, n=n, buf=buf, m=m, PB=PB, ld=ld, BASE=BASE,
                                 sw=last_ev_for_pp[pkey]):
                            if m == 0:
                                t.wait_ge(sem("ldm"), ld)
                                t.wait_ge(sem("ldw"), LDZB)
                            t.wait_ge(sem("S"), sw)
                            mm = None
                            for fc in range(NCH):
                                mm = t.matmul(PB[:, 0:512], zsh(fc, BASE + 128 * m, 128),
                                              WQKB[:, 4096 * buf + 512 * fc:4096 * buf + 512 * (fc + 1)],
                                              start=(fc == 0), stop=(fc == 7))
                            mm.then_inc(sem("T"), 1)
                        c.t(f_mm)
                        t_mm = bump("T")
                        last_mm_for_buf[buf] = t_mm

                        def f_ev(s_, nm=nm, n=n, m=m, PB=PB, t_mm=t_mm):
                            s_.wait_ge(sem("T"), t_mm)
                            if nm == "v":
                                s_.activation(out=VT[:, D * m + 512 * n:D * m + 512 * (n + 1)],
                                              in_=PB[:, 0:512], func=AF.Identity).then_inc(sem("S"), 1)
                            else:
                                DV = QFv if nm == "q" else KFv
                                s_.activation(out=DV[:, 4 * n:4 * (n + 1), 128 * m:128 * (m + 1)],
                                              in_=PB[:].rearrange("p (a b) -> p a b", b=128),
                                              func=AF.Identity).then_inc(sem("S"), 1)
                        c.s(f_ev)
                        last_ev_for_pp[pkey] = bump("S")
                        ppi += 1
                    bi += 1
            s_st1 = val("S")
            t_st1 = val("T")

        if L0F_PHASE == 1:
            def dbg1(g, sw=val("S")):
                g.wait_ge(sem("S"), sw)
                g.dma_start(out=o_z, in_=QF[:, 0:NCH * TOK]).then_inc(sem("st"), 16)
            c.g(dbg1)
            bump("st", 16)

            def fin1(sy):
                sy.wait_ge(sem("st"), val("st"))
            c.sy(fin1)
            scopeQ.close()
            return nc

        with nc.sbuf_tensor("QCS", [128, QW], BF16) as QCS, \
             nc.sbuf_tensor("QSS", [128, QW], BF16) as QSS, \
             nc.sbuf_tensor("KCS", [128, KW], BF16) as KCS, \
             nc.sbuf_tensor("KSS", [128, KW], BF16) as KSS, \
             nc.sbuf_tensor("RSC", [128, KW], BF16) as RSC, \
             nc.sbuf_tensor("RC2", [128, KW], BF16) as RC2, \
             nc.sbuf_tensor("QTB", [128, D], BF16) as QTB:

            def f_tbl(g, tw=t_st1, sw=s_st1):
                g.wait_ge(sem("T"), tw)
                g.wait_ge(sem("S"), sw)
                for dst, src in [(QCS, i_qcs), (QSS, i_qss), (KCS, i_kcs), (KSS, i_kss)]:
                    g.dma_start(out=dst[:], in_=src).then_inc(sem("ldm"), 16)
            c.g(f_tbl)
            ld_tb = bump("ldm", 64)

            v_ts_prev = 0
            t_tr_prev = 0
            v_ev_prev = 0
            for nm, SRCv, WID, NTIL in (("q", QFv, QW, 17), ("k", KFv, KW, 19)):
                for m in range(NTIL):
                    def f_sq(s_, SRCv=SRCv, m=m, first_it=(nm == "q" and m == 0), vp=v_ts_prev):
                        if first_it:
                            s_.wait_ge(sem("S"), s_st1)
                        s_.wait_ge(sem("V"), vp)
                        for h in range(16):
                            s_.activation(out=TRW[:], in_=SRCv[:, h // 2, 128 * m + 64 * (h % 2):128 * m + 64 * (h % 2) + 64],
                                          func=AF.Square, accum_out=SSQ[:, h:h + 1])
                        i1 = s_.activation(out=NRM[:], in_=SSQ[:], func=AF.Ln, bias=BTINY[:])
                        n0 = bump("S"); i1.then_inc(sem("S"), 1); s_.wait_ge(sem("S"), n0)
                        s_.activation(out=NRM[:], in_=NRM[:], func=AF.Exp, scale=0.5).then_inc(sem("S"), 1)
                    c.s(f_sq)
                    s_nr = bump("S")

                    def f_ts(v, SRCv=SRCv, m=m, s_nr=s_nr, tp=t_tr_prev):
                        v.wait_ge(sem("S"), s_nr)
                        v.wait_ge(sem("T"), tp)
                        i1 = v.tensor_scalar(NRM2[:], NRM[:], 1e-12, None, OP.max)
                        vedge(v, i1)
                        i2 = v.reciprocal(NRM2[:], NRM2[:])
                        vedge(v, i2)
                        for h in range(16):
                            i3 = v.tensor_scalar(QTB[:, 64 * h:64 * (h + 1)],
                                                 SRCv[:, h // 2, 128 * m + 64 * (h % 2):128 * m + 64 * (h % 2) + 64],
                                                 NRM2[:, h:h + 1], None, OP.mult)
                        i3.then_inc(sem("V"), 1)
                    c.v(f_ts)
                    v_ts_prev = bump("V")

                    def f_tr(t, v_ts=v_ts_prev, vep=v_ev_prev):
                        t.wait_ge(sem("V"), v_ts)
                        t.wait_ge(sem("V"), vep)
                        t.wait_ge(sem("ldw"), LDW0); t.wait_ge(sem("ldz"), 1)
                        for ch in range(NCH):
                            mm = t.transpose(PTT[:, 128 * ch:128 * (ch + 1)],
                                             QTB[:, 128 * ch:128 * (ch + 1)], IDN[:])
                        mm.then_inc(sem("T"), 1)
                    c.t(f_tr)
                    t_tr_prev = bump("T")

                    def f_ev(v, SRCv=SRCv, m=m, t_tr=t_tr_prev):
                        v.wait_ge(sem("T"), t_tr)
                        v.tensor_copy(SRCv[:, :, 128 * m:128 * (m + 1)],
                                      PTT[:].rearrange("p (a b) -> p a b", b=128)[:, 0:NCH, :]).then_inc(sem("V"), 1)
                    c.v(f_ev)
                    v_ev_prev = bump("V")
            t_st2 = val("T")

            def f_fold(v, ld_tb=ld_tb):
                v.wait_ge(sem("ldm"), ld_tb)
                v.wait_ge(sem("ldw"), LDW0); v.wait_ge(sem("ldz"), 1)
                i1 = v.tensor_scalar(QCS[:], QCS[:], QSC4[:, 0:1], None, OP.mult)
                i2 = v.tensor_scalar(QSS[:], QSS[:], QSC4[:, 1:2], None, OP.mult)
                i3 = v.tensor_scalar(KCS[:], KCS[:], QSC4[:, 2:3], None, OP.mult)
                vedge(v, i1); vedge(v, i2); vedge(v, i3)
                v.tensor_scalar(KSS[:], KSS[:], QSC4[:, 3:4], None, OP.mult).then_inc(sem("V"), 1)
            c.v(f_fold)
            bump("V")

            for mci in range(NCH):
                def f_rope(v, mci=mci):
                    qv = QF[:, QW * mci:QW * (mci + 1)]
                    v.tensor_copy(RSC[0:32, 0:QW], qv[32:64])
                    v.tensor_copy(RSC[32:64, 0:QW], qv[0:32])
                    v.tensor_copy(RSC[64:96, 0:QW], qv[96:128])
                    i1 = v.tensor_copy(RSC[96:128, 0:QW], qv[64:96])
                    vedge(v, i1)
                    i2 = v.tensor_tensor(RSC[:, 0:QW], RSC[:, 0:QW], QSS[:], OP.mult)
                    i3 = v.tensor_tensor(RC2[:, 0:QW], qv, QCS[:], OP.mult)
                    vedge(v, i2); vedge(v, i3)
                    i4 = v.tensor_tensor(qv, RC2[:, 0:QW], RSC[:, 0:QW], OP.add)
                    vedge(v, i4)
                    kv = KF[:, KW * mci:KW * (mci + 1)]
                    v.tensor_copy(RSC[0:32, :], kv[32:64])
                    v.tensor_copy(RSC[32:64, :], kv[0:32])
                    v.tensor_copy(RSC[64:96, :], kv[96:128])
                    i5 = v.tensor_copy(RSC[96:128, :], kv[64:96])
                    vedge(v, i5)
                    i6 = v.tensor_tensor(RSC[:], RSC[:], KSS[:], OP.mult)
                    i7 = v.tensor_tensor(RC2[:], kv, KCS[:], OP.mult)
                    vedge(v, i6); vedge(v, i7)
                    v.tensor_tensor(kv, RC2[:], RSC[:], OP.add).then_inc(sem("V"), 1)
                c.v(f_rope)
                bump("V")
            v_rope = val("V")

        if L0F_PHASE == 2:
            def dbg2(g, vw=v_rope):
                g.wait_ge(sem("V"), vw)
                g.dma_start(out=o_z, in_=QF[:, 0:NCH * TOK]).then_inc(sem("st"), 16)
            c.g(dbg2)
            bump("st", 16)

            def fin2(sy):
                sy.wait_ge(sem("st"), val("st"))
            c.sy(fin2)
            scopeQ.close()
            return nc

        with nc.sbuf_tensor("WOT", [64, 16 * D], BF16) as WOT, \
             nc.sbuf_tensor("YTM", [128, 2 * D], F32) as YTM, \
             nc.sbuf_tensor("ZRES", [128, 2 * D], F32) as ZRES, \
             nc.sbuf_tensor("YFS", [128, 2 * D], F32) as YFS:

            def f_wotld(g, vw=v_rope, tw=t_st2):
                g.wait_ge(sem("V"), vw)
                g.wait_ge(sem("T"), tw)
                g.dma_start(out=WOT[:], in_=i_wot).then_inc(sem("ldm"), 16)
            c.g(f_wotld)
            ld_wot = bump("ldm", 16)

            styst = {}
            v_yfm = {}
            ld_zres = {}
            s_oev_prev = 0
            t_ytr = {}
            for qt in range(17):
                def f_zres(g, qt=qt, vw=v_rope, tw=t_st2):
                    if qt < 2:
                        g.wait_ge(sem("V"), vw)
                        g.wait_ge(sem("T"), tw)
                    else:
                        g.wait_ge(sem("V"), v_yfm.get(qt - 2, 0))
                    g.dma_start(out=ZRES[:, D * (qt % 2):D * (qt % 2 + 1)],
                                in_=i_zr[qt]).then_inc(sem("ldq"), 16)
                c.g(f_zres)
                ld_zres[qt] = bump("ldq", 16)

                mi = min(qt, 3)
                for hi in range(16):
                    mci = hi // 2
                    pb = 64 * (hi % 2)

                    def f_sc(t, qt=qt, mci=mci, pb=pb, vw=v_rope, pv=val("V")):
                        t.wait_ge(sem("V"), vw)
                        t.wait_ge(sem("V"), pv)
                        mm = t.matmul(SPS, QF[pb:pb + 64, QW * mci + 128 * qt:QW * mci + 128 * (qt + 1)],
                                      KF[pb:pb + 64, KW * mci + 128 * qt:KW * mci + 128 * qt + 384],
                                      start=True, stop=True)
                        mm.then_inc(sem("T"), 1)
                    c.t(f_sc)
                    t_sc = bump("T")

                    def f_scev(v, mi=mi, t_sc=t_sc):
                        v.wait_ge(sem("T"), t_sc)
                        i1 = v.tensor_tensor(SCB[:], SPS, MB4[:, 384 * mi:384 * (mi + 1)], OP.add)
                        vedge(v, i1)
                        i2 = v.reduce_max(NMX[:], SCB[:], axis=AX.X, negate=True)
                        vedge(v, i2)
                        v.tensor_scalar(NMX[:], NMX[:], 8.0, None, OP.mult).then_inc(sem("V"), 1)
                    c.v(f_scev)
                    v_mx = bump("V")

                    def f_exp(s_, v_mx=v_mx):
                        s_.wait_ge(sem("V"), v_mx)
                        s_.activation(out=SCB[:], in_=SCB[:], func=AF.Exp,
                                      bias=NMX[:], scale=8.0, accum_out=DEN[:])
                        s_.activation(out=NMX[:], in_=NMX[:], func=AF.Exp).then_inc(sem("S"), 1)
                    c.s(f_exp)
                    s_exp = bump("S")

                    def f_nrm(v, s_exp=s_exp):
                        v.wait_ge(sem("S"), s_exp)
                        i1 = v.reciprocal(RDEN[:], DEN[:])
                        vedge(v, i1)
                        v.tensor_scalar(SCB[:], SCB[:], RDEN[:], None, OP.mult).then_inc(sem("V"), 1)
                    c.v(f_nrm)
                    v_nrm = bump("V")

                    def f_att(t, v_nrm=v_nrm):
                        t.wait_ge(sem("V"), v_nrm)
                        for kk in range(3):
                            mm = t.transpose(PTB[:, 128 * kk:128 * (kk + 1)],
                                             SCB[:, 128 * kk:128 * (kk + 1)], IDN[:])
                        mm.then_inc(sem("T"), 1)
                    c.t(f_att)
                    t_at = bump("T")

                    def f_ate(v, t_at=t_at):
                        v.wait_ge(sem("T"), t_at)
                        v.tensor_copy(ATT[:], PTB).then_inc(sem("V"), 1)
                    c.v(f_ate)
                    v_at = bump("V")

                    def f_ctx(t, qt=qt, hi=hi, v_at=v_at):
                        t.wait_ge(sem("V"), v_at)
                        for kk in range(3):
                            mm = t.matmul(CTXP, VT[:, D * (qt + kk) + 64 * hi:D * (qt + kk) + 64 * (hi + 1)],
                                          ATT[:, 128 * kk:128 * (kk + 1)],
                                          start=(kk == 0), stop=(kk == 2))
                        mm.then_inc(sem("T"), 1)
                    c.t(f_ctx)
                    t_ctx = bump("T")

                    def f_cte(v, t_ctx=t_ctx):
                        v.wait_ge(sem("T"), t_ctx)
                        v.tensor_copy(CTXS[:], CTXP).then_inc(sem("V"), 1)
                    c.v(f_cte)
                    v_cte = bump("V")

                    def f_wo(t, hi=hi, v_cte=v_cte, sp=s_oev_prev):
                        t.wait_ge(sem("V"), v_cte)
                        if hi == 0:
                            t.wait_ge(sem("ldm"), ld_wot)
                            t.wait_ge(sem("S"), sp)
                        for n in range(2):
                            mm = t.matmul(OUTP[:, 512 * n:512 * (n + 1)], CTXS[:],
                                          WOT[:, D * hi + 512 * n:D * hi + 512 * (n + 1)],
                                          start=(hi == 0), stop=(hi == 15), skip_group_check=True)
                        if hi == 15:
                            mm.then_inc(sem("T"), 1)
                    c.t(f_wo)
                    if hi == 15:
                        t_wo = bump("T")

                def f_oev(s_, qt=qt, t_wo=t_wo, tp=t_ytr.get(qt - 2, 0)):
                    s_.wait_ge(sem("T"), t_wo)
                    s_.wait_ge(sem("T"), tp)
                    s_.activation(out=YTM[:, D * (qt % 2):D * (qt % 2 + 1)], in_=OUTP[:],
                                  func=AF.Identity).then_inc(sem("S"), 1)
                c.s(f_oev)
                s_oev_prev = bump("S")

                def f_ytr(t, qt=qt, s_o=s_oev_prev, vp=v_yfm.get(qt - 1, 0)):
                    t.wait_ge(sem("S"), s_o)
                    t.wait_ge(sem("V"), vp)
                    for ch in range(NCH):
                        mm = t.transpose(PTF[:, 128 * ch:128 * (ch + 1)],
                                         YTM[:, D * (qt % 2) + 128 * ch:D * (qt % 2) + 128 * (ch + 1)],
                                         IDNF[:])
                    mm.then_inc(sem("T"), 1)
                c.t(f_ytr)
                t_ytr[qt] = bump("T")

                def f_yfm(v, qt=qt, t_y=t_ytr[qt]):
                    v.wait_ge(sem("T"), t_y)
                    v.wait_ge(sem("ldq"), ld_zres[qt])
                    if (qt - 2) in styst:
                        v.wait_ge(sem("st"), styst[qt - 2])
                    v.tensor_tensor(YFS[:, D * (qt % 2):D * (qt % 2 + 1)], PTF[:],
                                    ZRES[:, D * (qt % 2):D * (qt % 2 + 1)], OP.add).then_inc(sem("V"), 1)
                c.v(f_yfm)
                v_yfm[qt] = bump("V")

                def f_yst(sy, qt=qt, vv=v_yfm[qt]):
                    sy.wait_ge(sem("V"), vv)
                    sy.dma_start(out=yscv[:, :, 128 * qt:128 * (qt + 1)],
                                 in_=YFS[:, D * (qt % 2):D * (qt % 2 + 1)].rearrange(
                                     "p (a b) -> p a b", b=128)).then_inc(sem("st"), 16)
                c.sy(f_yst)
                styst[qt] = bump("st", 16)

        scopeQ.close()
        if L0F_PHASE == 3:
            def dbg3(g, sv=val("st")):
                g.wait_ge(sem("st"), sv)
                g.dma_start(out=o_z, in_=d_ysc[:, 0:NCH * TOK]).then_inc(sem("st"), 16)
            c.g(dbg3)
            bump("st", 16)

            def fin3(sy):
                sy.wait_ge(sem("st"), val("st"))
            c.sy(fin3)
            return nc

        bT, bV, bS, bst = val("T"), val("V"), val("S"), val("st")

        with nc.sbuf_tensor("XNF", [128, NCH * TOK], BF16) as XNF, \
             nc.sbuf_tensor("W2B", [128, 22 * D], BF16) as W2B:

            def f_bld(g):
                g.wait_ge(sem("st"), bst)
                g.wait_ge(sem("V"), bV)
                g.wait_ge(sem("S"), bS)
                g.wait_ge(sem("T"), bT)
                g.dma_start(out=W2B[:], in_=i_w2s).then_inc(sem("ldm"), 16)
            c.g(f_bld)
            ld_w2 = bump("ldm", 16)

            scopeB1 = contextlib.ExitStack()
            YB = scopeB1.enter_context(nc.sbuf_tensor("YB", [128, NCH * 2052], BF16))
            SQB = scopeB1.enter_context(nc.sbuf_tensor("SQB", [128, 2 * 512], BF16))
            MUR = scopeB1.enter_context(nc.sbuf_tensor("MUR", [128, TOK], BF16))
            RSR = scopeB1.enter_context(nc.sbuf_tensor("RSR", [128, TOK], BF16))
            MUROW = scopeB1.enter_context(nc.sbuf_tensor("MUROW", [1, TOK], BF16))
            RSROW = scopeB1.enter_context(nc.sbuf_tensor("RSROW", [1, TOK], BF16))
            MURF = scopeB1.enter_context(nc.sbuf_tensor("MURF", [1, TOK], F32))
            VRW = scopeB1.enter_context(nc.sbuf_tensor("VRW", [1, TOK], F32))
            TVR = scopeB1.enter_context(nc.sbuf_tensor("TVR", [1, 512], F32))
            XT1 = scopeB1.enter_context(nc.sbuf_tensor("XT1", [128, 512], BF16))
            YBv = YB[:].rearrange("p (c w) -> p c w", w=2052)

            def f_ybld(g):
                g.dma_start(out=YB[:], in_=yscv[:, :, 124:QW]).then_inc(sem("ldm"), 16)
            c.g(f_ybld)
            ld_yb = bump("ldm", 16)

            def f_shz(v, ld_yb=ld_yb):
                v.wait_ge(sem("ldm"), ld_yb)
                v.wait_ge(sem("ldw"), LDW0); v.wait_ge(sem("ldz"), 1)
                for ch in range(4, 8):
                    i1 = v.tensor_scalar(YBv[:, ch, 3:4], YBv[:, ch, 3:4], SHZ[:], None, OP.mult)
                i1.then_inc(sem("V"), 1)
            c.v(f_shz)
            v_shz = bump("V")

            def ybsh(fc, a, w):
                off = 4 if fc < 4 else 3
                return YBv[:, fc, off + a:off + a + w]

            SUMP = PTF[0:1, 0:512]
            SSQP = PTF[0:1, 512:1024]
            v_repe2 = {}
            s_sqr = {}
            t_stat = {}
            v_mu_prev = {}
            for t4 in range(4):
                for fc in range(NCH):
                    def f_sqr(s_, t4=t4, fc=fc, tp=t_stat.get((t4 * 8 + fc) - 2, 0)):
                        if t4 == 0 and fc == 0:
                            s_.wait_ge(sem("V"), v_shz)
                        s_.wait_ge(sem("T"), tp)
                        s_.activation(out=SQB[:, 512 * (fc % 2):512 * (fc % 2 + 1)],
                                      in_=ybsh(fc, 512 * t4, 512), func=AF.Square).then_inc(sem("S"), 1)
                    c.s(f_sqr)
                    s_sqr[(t4, fc)] = bump("S")

                    def f_stat(t, t4=t4, fc=fc, sw=s_sqr[(t4, fc)], vm=v_mu_prev.get(t4 - 1, 0)):
                        t.wait_ge(sem("S"), sw)
                        if fc == 0:
                            t.wait_ge(sem("V"), vm)
                        t.matmul(SUMP, ONEC[:], ybsh(fc, 512 * t4, 512),
                                 start=(fc == 0), stop=(fc == 7), skip_group_check=True)
                        mm = t.matmul(SSQP, ONEC[:], SQB[:, 512 * (fc % 2):512 * (fc % 2 + 1)],
                                      start=(fc == 0), stop=(fc == 7), skip_group_check=True)
                        if fc == 7:
                            mm.then_inc(sem("T"), 1)
                    c.t(f_stat)
                    t_stat[t4 * 8 + fc] = val("T") + (1 if fc == 7 else 0)
                    if fc == 7:
                        bump("T")
                t_st = val("T")

                def f_mu(v, t4=t4, t_st=t_st):
                    v.wait_ge(sem("T"), t_st)
                    a = 512 * t4
                    i1 = v.tensor_scalar(MURF[0:1, a:a + 512], SUMP, 1.0 / D, None, OP.mult)
                    i2 = v.tensor_scalar(VRW[0:1, a:a + 512], SSQP, 1.0 / D, None, OP.mult)
                    vedge(v, i1); vedge(v, i2)
                    i3 = v.tensor_copy(MUROW[0:1, a:a + 512], MURF[0:1, a:a + 512])
                    i4 = v.tensor_tensor(TVR[0:1, :], MURF[0:1, a:a + 512], MURF[0:1, a:a + 512], OP.mult)
                    vedge(v, i3); vedge(v, i4)
                    v.tensor_tensor(VRW[0:1, a:a + 512], VRW[0:1, a:a + 512], TVR[0:1, :],
                                    OP.subtract).then_inc(sem("V"), 1)
                c.v(f_mu)
                v_mu = bump("V")
                v_mu_prev[t4] = v_mu

                def f_rst(s_, t4=t4, v_mu=v_mu):
                    s_.wait_ge(sem("V"), v_mu)
                    a = 512 * t4
                    i1 = s_.activation(out=VRW[0:1, a:a + 512], in_=VRW[0:1, a:a + 512],
                                       func=AF.Ln, bias=EPS[0:1, :])
                    n0 = bump("S"); i1.then_inc(sem("S"), 1); s_.wait_ge(sem("S"), n0)
                    s_.activation(out=RSROW[0:1, a:a + 512], in_=VRW[0:1, a:a + 512],
                                  func=AF.Exp, scale=-0.5).then_inc(sem("S"), 1)
                c.s(f_rst)
                s_rst = bump("S")

                def f_rep1(t, t4=t4, sw=s_rst, vp=v_repe2.get(t4 - 1, 0)):
                    t.wait_ge(sem("S"), sw)
                    t.wait_ge(sem("V"), vp)
                    t.matmul(PSB[:], ONR[:], MUROW[0:1, 512 * t4:512 * (t4 + 1)],
                             start=True, stop=True).then_inc(sem("T"), 1)
                c.t(f_rep1)
                t_r1 = bump("T")

                def f_re1(v, t4=t4, t_r1=t_r1):
                    v.wait_ge(sem("T"), t_r1)
                    v.tensor_copy(MUR[:, 512 * t4:512 * (t4 + 1)], PSB[:]).then_inc(sem("V"), 1)
                c.v(f_re1)
                v_r1 = bump("V")

                def f_rep2(t, t4=t4, v_r1=v_r1):
                    t.wait_ge(sem("V"), v_r1)
                    t.matmul(PSB[:], ONR[:], RSROW[0:1, 512 * t4:512 * (t4 + 1)],
                             start=True, stop=True).then_inc(sem("T"), 1)
                c.t(f_rep2)
                t_r2 = bump("T")

                def f_re2(v, t4=t4, t_r2=t_r2):
                    v.wait_ge(sem("T"), t_r2)
                    v.tensor_copy(RSR[:, 512 * t4:512 * (t4 + 1)], PSB[:]).then_inc(sem("V"), 1)
                c.v(f_re2)
                v_repe2[t4] = bump("V")

            if L0F_PHASE == 35:
                def dbg35(g, vv=val("V")):
                    g.wait_ge(sem("V"), vv)
                    g.dma_start(out=o_z, in_=d_ysc[:, 0:NCH * TOK]).then_inc(sem("st"), 16)
                c.g(dbg35)
                bump("st", 16)

                def fin35(sy):
                    sy.wait_ge(sem("st"), val("st"))
                c.sy(fin35)
                scopeB1.close()
                return nc

            for t4 in range(4):
                for fc in range(NCH):
                    def f_xnf(v, t4=t4, fc=fc, vr=v_repe2[t4]):
                        v.wait_ge(sem("V"), vr)
                        a = 512 * t4
                        i1 = v.tensor_tensor(XT1[:], ybsh(fc, a, 512), MUR[:, a:a + 512], OP.subtract)
                        vedge(v, i1)
                        i2 = v.tensor_tensor(XT1[:], XT1[:], RSR[:, a:a + 512], OP.mult)
                        vedge(v, i2)
                        v.tensor_scalar(XNF[:, TOK * fc + a:TOK * fc + a + 512], XT1[:],
                                        GBC[:, fc:fc + 1], GBC[:, 8 + fc:8 + fc + 1],
                                        OP.mult, OP.add).then_inc(sem("V"), 1)
                    c.v(f_xnf)
                    bump("V")
            v_xnf_fin = val("V")
            scopeB1.close()
            if L0F_PHASE == 36:
                def dbg36(g, vv=val("V")):
                    g.wait_ge(sem("V"), vv)
                    g.dma_start(out=o_z, in_=d_ysc[:, 0:NCH * TOK]).then_inc(sem("st"), 16)
                c.g(dbg36)
                bump("st", 16)

                def fin36(sy):
                    sy.wait_ge(sem("st"), val("st"))
                c.sy(fin36)
                return nc

            with nc.sbuf_tensor("H", [128, 22 * TOK], BF16) as H, \
                 nc.sbuf_tensor("W1B", [128, 2 * TOK], BF16) as W1B, \
                 nc.sbuf_tensor("ABUF", [128, 2 * 512], BF16) as ABUF, \
                 nc.sbuf_tensor("GBUF", [128, 2 * 512], BF16) as GBUF, \
                 nc.sbuf_tensor("YFSL", [128, 2 * 512], F32) as YFSL, \
                 nc.sbuf_tensor("ZOW", [128, 2 * 512], F32) as ZOW:

                w1_last_mm = {0: 0, 1: 0}
                s_gel_prev = {0: 0, 1: 0}
                s_gel_last = [0]
                v_hev = {}
                for ci in range(L0F_NCI):
                    buf = ci % 2

                    def f_w1ld(g, ci=ci, buf=buf, tw=w1_last_mm[buf], vx=v_xnf_fin):
                        if ci < 2:
                            g.wait_ge(sem("V"), vx)
                        g.wait_ge(sem("T"), tw)
                        g.dma_start(out=W1B[:, TOK * buf:TOK * (buf + 1)],
                                    in_=i_w1s[ci]).then_inc(sem("ldm"), 16)
                    c.g(f_w1ld)
                    ld_w1 = bump("ldm", 16)
                    for t4 in range(4):
                        def f_hmm(t, ci=ci, buf=buf, t4=t4, ld_w1=ld_w1,
                                  sp=s_gel_last[0], vx=v_xnf_fin):
                            if t4 == 0:
                                t.wait_ge(sem("ldm"), ld_w1)
                            if ci == 0 and t4 == 0:
                                t.wait_ge(sem("V"), vx)
                            t.wait_ge(sem("S"), sp)
                            for fc in range(NCH):
                                t.matmul(PP0[:], W1B[:, TOK * buf + 256 * fc:TOK * buf + 256 * fc + 128],
                                         XNF[:, TOK * fc + 512 * t4:TOK * fc + 512 * (t4 + 1)],
                                         start=(fc == 0), stop=(fc == 7), skip_group_check=True)
                                mm = t.matmul(PP1[:], W1B[:, TOK * buf + 256 * fc + 128:TOK * buf + 256 * (fc + 1)],
                                              XNF[:, TOK * fc + 512 * t4:TOK * fc + 512 * (t4 + 1)],
                                              start=(fc == 0), stop=(fc == 7), skip_group_check=True)
                            mm.then_inc(sem("T"), 1)
                        c.t(f_hmm)
                        t_hmm = bump("T")
                        w1_last_mm[buf] = t_hmm

                        def f_gel(s_, ci=ci, t4=t4, t_hmm=t_hmm, vp=v_hev.get((ci, t4 - 2) if t4 >= 2 else (ci - 1, t4 + 2), 0)):
                            s_.wait_ge(sem("T"), t_hmm)
                            s_.wait_ge(sem("V"), vp)
                            i1 = s_.activation(out=ABUF[:, 512 * (t4 % 2):512 * (t4 % 2 + 1)], in_=PP0[:],
                                               func=AF.Identity, bias=B1C[:, ci:ci + 1])
                            n0 = bump("S"); i1.then_inc(sem("S"), 1); s_.wait_ge(sem("S"), n0)
                            s_.activation(out=GBUF[:, 512 * (t4 % 2):512 * (t4 % 2 + 1)], in_=PP1[:],
                                          func=(AF.Gelu if L0F_GELU else AF.Identity), bias=B1C[:, 22 + ci:23 + ci]).then_inc(sem("S"), 1)
                        c.s(f_gel)
                        s_gel_prev[t4 % 2] = bump("S")
                        s_gel_last[0] = s_gel_prev[t4 % 2]

                        def f_hev(v, ci=ci, t4=t4, sw=s_gel_prev[t4 % 2]):
                            v.wait_ge(sem("S"), sw)
                            v.tensor_tensor(H[:, TOK * ci + 512 * t4:TOK * ci + 512 * (t4 + 1)],
                                            ABUF[:, 512 * (t4 % 2):512 * (t4 % 2 + 1)],
                                            GBUF[:, 512 * (t4 % 2):512 * (t4 % 2 + 1)],
                                            OP.mult).then_inc(sem("V"), 1)
                        c.v(f_hev)
                        v_hev[(ci, t4)] = bump("V")
                v_hev_fin = val("V")
                if L0F_PHASE == 37:
                    def dbg37(g, vv=val("V")):
                        g.wait_ge(sem("V"), vv)
                        g.dma_start(out=o_z, in_=d_ysc[:, 0:NCH * TOK]).then_inc(sem("st"), 16)
                    c.g(dbg37)
                    bump("st", 16)

                    def fin37(sy):
                        sy.wait_ge(sem("st"), val("st"))
                    c.sy(fin37)
                    return nc

                v_zev = {}
                zst = {}
                ld_yfl = {}
                for fc in range(NCH):
                    for t4 in range(4):
                        gi = fc * 4 + t4
                        buf = gi % 2
                        PZ = OUTP[:, 512 * buf:512 * (buf + 1)]

                        def f_yfl(g, fc=fc, t4=t4, gi=gi, buf=buf, vh=v_hev_fin):
                            if gi < 2:
                                g.wait_ge(sem("V"), vh)
                            else:
                                g.wait_ge(sem("V"), v_zev.get(gi - 2, 0))
                            g.dma_start(out=YFSL[:, 512 * buf:512 * (buf + 1)],
                                        in_=yscv[:, fc, 128 + 512 * t4:128 + 512 * (t4 + 1)]).then_inc(sem("ldq"), 16)
                        c.g(f_yfl)
                        ld_yfl[gi] = bump("ldq", 16)

                        def f_zmm(t, fc=fc, t4=t4, gi=gi, PZ=PZ, vh=v_hev_fin, ld_w2=ld_w2):
                            if gi == 0:
                                t.wait_ge(sem("V"), vh)
                                t.wait_ge(sem("ldm"), ld_w2)
                            t.wait_ge(sem("V"), v_zev.get(gi - 2, 0))
                            for ci in range(22):
                                mm = t.matmul(PZ, W2B[:, D * ci + 128 * fc:D * ci + 128 * (fc + 1)],
                                              H[:, TOK * ci + 512 * t4:TOK * ci + 512 * (t4 + 1)],
                                              start=(ci == 0), stop=(ci == 21), skip_group_check=True)
                            mm.then_inc(sem("T"), 1)
                        c.t(f_zmm)
                        t_zmm = bump("T")

                        def f_zev(v, fc=fc, gi=gi, buf=buf, PZ=PZ, t_zmm=t_zmm):
                            v.wait_ge(sem("T"), t_zmm)
                            v.wait_ge(sem("ldq"), ld_yfl[gi])
                            if (gi - 2) in zst:
                                v.wait_ge(sem("st"), zst[gi - 2])
                            v.scalar_tensor_tensor(ZOW[:, 512 * buf:512 * (buf + 1)], PZ,
                                                   B2C[:, fc:fc + 1],
                                                   YFSL[:, 512 * buf:512 * (buf + 1)],
                                                   OP.add, OP.add).then_inc(sem("V"), 1)
                        c.v(f_zev)
                        v_zev[gi] = bump("V")

                        def f_zst(sy, fc=fc, t4=t4, buf=buf, vv=v_zev[gi]):
                            sy.wait_ge(sem("V"), vv)
                            sy.dma_start(out=o_z[:, TOK * fc + 512 * t4:TOK * fc + 512 * (t4 + 1)],
                                         in_=ZOW[:, 512 * buf:512 * (buf + 1)]).then_inc(sem("st"), 16)
                        c.sy(f_zst)
                        zst[gi] = bump("st", 16)

        def fin_sy(sy):
            sy.wait_ge(sem("st"), val("st"))
        c.sy(fin_sy)
    return nc


def build_comb():
    nc = bass.Bass("TRN2", target_bir_lowering=False)
    i_a = nc.dram_tensor("ain", [128, NT * D], F32, kind="ExternalInput").ap()
    i_b = nc.dram_tensor("bin", [128, NT * D], F32, kind="ExternalInput").ap()
    o_x = nc.dram_tensor("xout", [128, NT * D], F32, kind="ExternalOutput").ap()
    es = contextlib.ExitStack()
    with es:
        c = Ctx(nc, es)
        sem, bump, val = c.sem, c.bump, c.val
        A = c.sb("A", [128, NT * D], F32)
        B = c.sb("B", [128, NT * D], F32)

        def f_ld(sy):
            sy.dma_start(out=A[:], in_=i_a).then_inc(sem("ldq"), 16)
            sy.dma_start(out=B[:], in_=i_b).then_inc(sem("ldq"), 16)
        c.sy(f_ld)
        ld = bump("ldq", 32)

        def f_add(v, ld=ld):
            v.wait_ge(sem("ldq"), ld)
            for j in range(4):
                mm = v.tensor_tensor(A[:, 4096 * j:4096 * (j + 1)], A[:, 4096 * j:4096 * (j + 1)],
                                     B[:, 4096 * j:4096 * (j + 1)], OP.add)
            mm.then_inc(sem("V"), 1)
        c.v(f_add)
        v_a = bump("V")

        def f_st(sy, v_a=v_a):
            sy.wait_ge(sem("V"), v_a)
            sy.dma_start(out=o_x, in_=A[:]).then_inc(sem("st"), 16)
            sy.wait_ge(sem("st"), 16)
        c.sy(f_st)
        bump("st", 16)
    return nc


import ml_dtypes
from concourse.bass_utils import run_bass_kernel_spmd

BFH = ml_dtypes.bfloat16


def pm(x):
    return np.ascontiguousarray(x.reshape(16, 128, 1024).transpose(1, 0, 2).reshape(128, 16384))


def ipm(x):
    return np.ascontiguousarray(x.reshape(128, 16, 1024).transpose(1, 0, 2).reshape(2048, 1024))


def fmz(x):
    return np.ascontiguousarray(x.T.reshape(8, 128, 2048).transpose(1, 0, 2).reshape(128, 16384))


def ifmz(z):
    return np.ascontiguousarray(z.reshape(128, 8, 2048).transpose(1, 0, 2).reshape(1024, 2048).T)


def mchunks(W):
    return np.ascontiguousarray(W.reshape(8, 128, 8, 128).transpose(0, 3, 2, 1).reshape(8, 128, 1024))


def l0a_rope_tables(pos):
    i = np.arange(32, dtype=np.float64)
    inv = 1.0 / (10000.0 ** (2 * i / 64.0))
    ang = pos[:, None] * inv[None, :]
    cos = np.cos(ang).T
    sin = np.sin(ang).T
    cos64 = np.concatenate([cos, cos], 0)
    sin64 = np.concatenate([-sin, sin], 0)
    return (np.concatenate([cos64, cos64], 0).astype(np.float32),
            np.concatenate([sin64, sin64], 0).astype(np.float32))


def xpos_tables(pos, is_q):
    j32 = np.arange(32, dtype=np.float64)
    invf = 10000.0 ** (-2 * j32 / 64.0)
    base = (2 * j32 + 25.6) / 89.6
    ang = pos[:, None] * invf[None, :]
    power = (pos - 2048.0) / 128.0
    scale = base[None, :] ** power[:, None]
    if not is_q:
        scale = 1.0 / scale
    cos = (np.cos(ang) * scale).T
    sinp = (np.sin(ang) * scale).T
    c64 = np.concatenate([cos, cos], 0)
    s64 = np.concatenate([-sinp, sinp], 0)
    return (np.tile(c64, (2, 1)).astype(np.float32),
            np.tile(s64, (2, 1)).astype(np.float32))


def l0f_layer_weights(d, i):
    DI, DIP = 2730, 2816
    Wqkv = d["L0_ffn_Wqkv"]
    Wq = Wqkv[i][:, 0:1024]; Wk = Wqkv[i][:, 1024:2048]; Wv = Wqkv[i][:, 2048:3072]
    mh = lambda W: np.ascontiguousarray(np.asarray(W).reshape(8, 128, 2, 512).transpose(2, 1, 0, 3).reshape(2, 128, 4096))
    W1A = d["L0_ffn_W1"]; b1A = d["L0_ffn_b1"]; W2A = d["L0_ffn_W2"]; b2A = d["L0_ffn_b2"]
    W1a = np.zeros((1024, DIP), np.float32); W1a[:, :DI] = W1A[i][:, :DI]
    W1g = np.zeros((1024, DIP), np.float32); W1g[:, :DI] = W1A[i][:, DI:]
    A = W1a.reshape(8, 128, 22, 128).transpose(2, 1, 0, 3)
    G = W1g.reshape(8, 128, 22, 128).transpose(2, 1, 0, 3)
    w1s = np.ascontiguousarray(np.stack([A, G], 3).reshape(22, 128, 2048))
    W2p = np.zeros((DIP, 1024), np.float32); W2p[:DI] = W2A[i]
    w2s = np.ascontiguousarray(W2p.reshape(22, 128, 1024).transpose(1, 0, 2).reshape(128, 22528))
    b1a = np.zeros(DIP, np.float32); b1a[:DI] = b1A[i][:DI]
    b1g = np.zeros(DIP, np.float32); b1g[:DI] = b1A[i][DI:]
    b1c = np.ascontiguousarray(np.concatenate([b1a.reshape(22, 128).T, b1g.reshape(22, 128).T], 1))
    qs = d["L0_ffn_q_scale"][i]; ks = d["L0_ffn_k_scale"][i]
    qsw = np.concatenate([qs[32:], qs[:32]]); ksw = np.concatenate([ks[32:], ks[:32]])
    qsc4 = np.ascontiguousarray(np.stack([np.tile(qs, 2), np.tile(qsw, 2),
                                          np.tile(ks, 2), np.tile(ksw, 2)], 1).astype(np.float32))
    return dict(
        wqh=mh(Wq), wkh=mh(Wk), wvh=mh(Wv),
        wot=np.ascontiguousarray(np.asarray(d["L0_ffn_Wo"][i]).reshape(16, 64, 1024).transpose(1, 0, 2).reshape(64, 16384)),
        w1s=w1s, w2s=w2s, b1c=b1c,
        b2c=np.ascontiguousarray(b2A[i].reshape(8, 128).T),
        gbc=np.ascontiguousarray(np.concatenate([d["L0_ffn_ln_g"][i].reshape(8, 128).T,
                                                 d["L0_ffn_ln_b"][i].reshape(8, 128).T], 1)),
        qsc4=qsc4,
    )


def l0f_core_tables(h):
    T0 = 2048 * h
    qcs, qss = xpos_tables(np.arange(T0 - 128, T0 + 2048).astype(np.float64), True)
    kcs, kss = xpos_tables(np.arange(T0 - 384, T0 + 2048).astype(np.float64), False)
    ii = np.arange(128)[:, None]; jj = np.arange(384)[None, :]
    band = (jj - ii >= 0) & (jj - ii <= 256)
    mb = np.zeros((4, 128, 384), np.float32)
    for v in range(4):
        ok = band.copy()
        if h == 0:
            if v == 1:
                ok &= (jj >= 256)
            if v == 2:
                ok &= (jj >= 128)
        mb[v] = np.where(ok, 0.0, -1e9)
    mb4 = np.ascontiguousarray(mb.transpose(1, 0, 2).reshape(128, 4 * 384).astype(BFH))
    return dict(qcs=qcs, qss=qss, kcs=kcs, kss=kss, mb4=mb4,
                shz=np.full((128, 1), float(h), np.float32),
                idn=np.eye(128, dtype=BFH), idnf=np.eye(128, dtype=np.float32),
                onec=np.ones((128, 1), BFH), onesr=np.ones((1, 128), BFH))


def l0f_stage_z(zlist, c, dt):
    b, h = c // 2, c % 2
    zloc = zlist[c]
    zv = zloc.reshape(128, 8, 2048)
    if h == 1:
        sib = zlist[2 * b].reshape(128, 8, 2048)
        zhalo = np.ascontiguousarray(sib[:, :, 1536:2048].reshape(128, 4096))
    else:
        zhalo = np.zeros((128, 4096), dt)
    zres = np.zeros((17, 128, 1024), dt)
    if h == 1:
        zres[0] = sib[:, :, 1920:2048].reshape(128, 1024)
    for qt in range(1, 17):
        zres[qt] = zv[:, :, 128 * (qt - 1):128 * qt].reshape(128, 1024)
    return zloc, np.ascontiguousarray(zhalo), np.ascontiguousarray(zres)


def l3_rope_tables(pos):
    j = np.arange(0, 64, 2, np.float64)
    inv = 10000.0 ** (j / 64.0)
    ang = pos[:, None] / inv[None, :]
    cos = np.cos(ang).T
    sin = np.sin(ang).T
    cos64 = np.concatenate([cos, cos], 0)
    sin64 = np.concatenate([-sin, sin], 0)
    return (np.concatenate([cos64, cos64], 0).astype(np.float32),
            np.concatenate([sin64, sin64], 0).astype(np.float32))


def kernel(**d):
    d = {k: np.asarray(v) for k, v in d.items()}
    x0 = d["L0_attn_hidden_states"].astype(np.float32)

    Wq, Wk, Wv, Wo = d["L0_attn_Wq"], d["L0_attn_Wk"], d["L0_attn_Wv"], d["L0_attn_Wo"]
    qi_ = np.arange(128)[:, None]; kj_ = np.arange(128)[None, :]
    mprev = np.where(kj_ >= qi_, 0.0, -1e9).astype(BFH)
    mown = np.where(kj_ <= qi_, 0.0, -1e9).astype(BFH)
    common = dict(
        wqm=mchunks(Wq), wkm=mchunks(Wk),
        wvg=np.ascontiguousarray(Wv.reshape(4, 256, 8, 128).transpose(0, 3, 2, 1).reshape(4, 128, 2048)),
        wot=np.ascontiguousarray(Wo.T.reshape(16, 64, 1024).transpose(1, 0, 2).reshape(64, 16384)),
        bqc=np.ascontiguousarray(d["L0_attn_bq"].reshape(8, 128).T),
        bkc=np.ascontiguousarray(d["L0_attn_bk"].reshape(8, 128).T),
        bvr=np.ascontiguousarray(d["L0_attn_bv"][None, :]),
        bor=np.ascontiguousarray(d["L0_attn_bo"][None, :]),
        mprev=mprev, mown=mown,
        idn=np.eye(128, dtype=BFH), onesr=np.ones((1, 128), BFH),
    )
    in_maps = []
    for c in range(8):
        b, h = c // 2, c % 2
        T0 = 2048 * h
        qcos, qsin = l0a_rope_tables(np.arange(T0, T0 + 2048))
        kcos, ksin = l0a_rope_tables(np.maximum(np.arange(T0 - 128, T0 + 2048), 0))
        xhk = x0[b, T0 - 128:T0] if h == 1 else np.zeros((128, 1024), np.float32)
        m = dict(common)
        m.update(xloc=pm(x0[b, T0:T0 + 2048]), xhk=np.ascontiguousarray(xhk),
                 qcos=qcos, qsin=qsin, kcos=kcos, ksin=ksin,
                 amsk0=(mprev if h == 1 else np.full((128, 128), -1e9, np.float32).astype(BFH)))
        in_maps.append(m)
    nc_l0a = build_L0a()
    res = run_bass_kernel_spmd(nc_l0a, in_maps, core_ids=list(range(8)))
    ypart = [np.ascontiguousarray(np.asarray(res.results[c]["ypart"])) for c in range(8)]
    zcur = [np.ascontiguousarray(np.asarray(res.results[c]["z0"])) for c in range(8)]

    tabs = [l0f_core_tables(c % 2) for c in range(8)]
    nc_f0 = build_L0f(True)
    nc_fr = build_L0f(False)
    zdt = BFH
    for li in range(6):
        lw = l0f_layer_weights(d, li)
        in_maps = []
        for c in range(8):
            zloc, zhalo, zres = l0f_stage_z(zcur, c, zdt)
            m = dict(lw); m.update(tabs[c])
            m.update(zloc=zloc, zhalo=zhalo, zres=zres)
            in_maps.append(m)
        nc = nc_f0 if li == 0 else nc_fr
        res = run_bass_kernel_spmd(nc, in_maps, core_ids=list(range(8)))
        zcur = [np.ascontiguousarray(np.asarray(res.results[c]["zout"])) for c in range(8)]
        zdt = np.float32

    nc_cb = build_comb()
    in_maps = [dict(ain=ypart[c], bin=pm(ifmz(zcur[c]))) for c in range(8)]
    res = run_bass_kernel_spmd(nc_cb, in_maps, core_ids=list(range(8)))
    x1 = np.zeros((4, 4096, 1024), np.float32)
    for c in range(8):
        b, h = c // 2, c % 2
        x1[b, 2048 * h:2048 * (h + 1)] = ipm(np.asarray(res.results[c]["xout"]))

    w1 = d["L1_ffn_w1"]; w3 = d["L1_ffn_w3"]; w2 = d["L1_ffn_w2"]
    b1 = d["L1_ffn_b1"]; b3 = d["L1_ffn_b3"]
    sw1 = d["L1_ffn_sw1"]; sw3 = d["L1_ffn_sw3"]; sw2 = d["L1_ffn_sw2"]
    sb1 = d["L1_ffn_sb1"]; sb3 = d["L1_ffn_sb3"]
    w1x = np.concatenate([w1, sw1[:, 0:512][None], sw1[:, 512:1024][None]], 0)
    w3x = np.concatenate([w3, sw3[:, 0:512][None], sw3[:, 512:1024][None]], 0)
    w2x = np.concatenate([w2, sw2[0:512][None], sw2[512:1024][None]], 0)
    b1x = np.concatenate([b1, sb1[0:512][None], sb1[512:1024][None]], 0)
    b3x = np.concatenate([b3, sb3[0:512][None], sb3[512:1024][None]], 0)
    cvw = d["L1_attn_conv_w"].reshape(8, 128, 3).transpose(1, 0, 2).reshape(128, 24)
    common = dict(
        gw1=d["L1_attn_gate_w1"],
        gb1=np.ascontiguousarray(d["L1_attn_gate_b1"].reshape(2, 128).T),
        gw2=d["L1_attn_gate_w2"],
        gb2=np.ascontiguousarray(d["L1_attn_gate_b2"][:, None]),
        uw=d["L1_attn_U_w"], vw=d["L1_attn_V_w"],
        cvw=np.ascontiguousarray(cvw),
        ow=d["L1_attn_out_w"], obr=np.ascontiguousarray(d["L1_attn_out_b"][None, :]),
        gwf=d["L1_ffn_gw"], gbr=np.ascontiguousarray(d["L1_ffn_gb"][None, :]),
        w1x=np.ascontiguousarray(w1x), w3x=np.ascontiguousarray(w3x),
        w2x=np.ascontiguousarray(w2x), b1x=np.ascontiguousarray(b1x),
        b3x=np.ascontiguousarray(b3x),
        b2s=d["L1_ffn_b2"], sb2r=np.ascontiguousarray(d["L1_ffn_sb2"][None, :]),
        idn=np.eye(128, dtype=BFH), idnf=np.eye(128, dtype=np.float32),
        onesr=np.ones((1, 128), BFH), onesc=np.ones((128, 1), BFH),
    )
    common = {k: np.ascontiguousarray(v) for k, v in common.items()}
    in_maps = []
    for c in range(8):
        b, h = c // 2, c % 2
        T0 = 2048 * h
        xloc = x1[b, T0:T0 + 2048]
        xhalo = np.zeros((128, 1024), np.float32)
        if h == 1:
            xhalo[0] = x1[b, T0 - 1]
            xpfx = x1[b, 0:2048]
        else:
            xhalo[1] = x1[b, 2048]
            xpfx = np.zeros((2048, 1024), np.float32)
        m = dict(common)
        m.update(xloc=pm(xloc), xhalo=np.ascontiguousarray(xhalo), xpfx=pm(xpfx))
        in_maps.append(m)
    nc_l1 = build_L1()
    res = run_bass_kernel_spmd(nc_l1, in_maps, core_ids=list(range(8)))
    x2 = np.zeros((4, 4096, 1024), np.float32)
    for c in range(8):
        b, h = c // 2, c % 2
        x2[b, 2048 * h:2048 * (h + 1)] = ipm(np.asarray(res.results[c]["xout"]))

    common = dict(
        wqm=mchunks(d["L2_attn_W_q"]), wkm=mchunks(d["L2_attn_W_k"]), wam=mchunks(d["L2_attn_W_a"]),
        wvn=np.ascontiguousarray(d["L2_attn_W_v"].reshape(2, 512, 8, 128).transpose(0, 3, 2, 1).reshape(2, 128, 4096)),
        bqc=np.ascontiguousarray(d["L2_attn_b_q"].reshape(8, 128).T),
        bkc=np.ascontiguousarray(d["L2_attn_b_k"].reshape(8, 128).T),
        bac=np.ascontiguousarray(d["L2_attn_b_a"].reshape(8, 128).T),
        bvr=np.ascontiguousarray(d["L2_attn_b_v"][None, :]),
        gwT=np.ascontiguousarray(d["L2_ffn_gate_w"].T),
        bbr=np.ascontiguousarray(d["L2_ffn_balance_bias"][None, :]),
        w1t=np.ascontiguousarray(d["L2_ffn_w1"].transpose(0, 2, 1)),
        w2t=np.ascontiguousarray(d["L2_ffn_w2"].transpose(0, 2, 1)),
        idn=np.eye(128, dtype=BFH),
        onesr=np.ones((1, 128), BFH), onesc=np.ones((128, 1), BFH),
        mcau=np.ascontiguousarray(np.triu(np.ones((128, 128))).astype(BFH)),
    )
    in_maps = []
    for c in range(8):
        b, h = c // 2, c % 2
        T0 = 2048 * h
        m = dict(common)
        m.update(
            xloc=pm(x2[b, T0:T0 + 2048]),
            xoth=pm(x2[b, 2048 - T0:4096 - T0]),
            pmsk=np.full((128, 1), float(h), np.float32),
            smsk=np.full((128, 1), float(1 - h), np.float32),
        )
        in_maps.append(m)
    nc_l2 = build_L2()
    res = run_bass_kernel_spmd(nc_l2, in_maps, core_ids=list(range(8)))
    x3 = np.zeros((4, 4096, 1024), np.float32)
    for c in range(8):
        b, h = c // 2, c % 2
        x3[b, 2048 * h:2048 * (h + 1)] = ipm(np.asarray(res.results[c]["xout"]))

    Wqkv = d["L3_attn_W_qkv"]; Wo3 = d["L3_attn_W_o"]
    perm = np.zeros(64, np.int64)
    perm[:32] = np.arange(0, 64, 2)
    perm[32:] = np.arange(1, 64, 2)

    def permute_heads(W):
        Wr = W.reshape(16, 64, 1024)
        return Wr[:, perm, :].reshape(1024, 1024)
    Wq3 = permute_heads(Wqkv[0:1024])
    Wk3 = permute_heads(Wqkv[1024:2048])
    Wv3 = Wqkv[2048:3072]
    wvg = np.ascontiguousarray(Wv3.reshape(4, 256, 8, 128).transpose(0, 3, 2, 1).reshape(4, 128, 2048))
    WoT = np.ascontiguousarray(Wo3.T)
    wog = np.ascontiguousarray(WoT.reshape(4, 4, 64, 1024).transpose(0, 2, 1, 3).reshape(4, 64, 4096))
    common = dict(
        wqm=mchunks(Wq3), wkm=mchunks(Wk3), wvg=wvg, wog=wog,
        lngr=np.ascontiguousarray(d["L3_ffn_ln_g"][None, :]),
        lnbr=np.ascontiguousarray(d["L3_ffn_ln_b"][None, :]),
        idn=np.eye(128, dtype=BFH), onesr=np.ones((1, 128), BFH),
        mcadd=np.ascontiguousarray(np.where(np.tril(np.ones((128, 128), bool)), 0.0, -1e9).astype(BFH)),
    )
    in_maps = []
    for c in range(8):
        b, h = c // 2, c % 2
        T0 = 2048 * h
        own_pos = np.arange(T0, T0 + 2048)
        oth_pos = np.arange(2048 - T0, 4096 - T0)
        qcos, qsin = l3_rope_tables(own_pos)
        kcos, ksin = l3_rope_tables(np.concatenate([oth_pos, own_pos]))
        m = dict(common)
        m.update(
            xloc=pm(x3[b, T0:T0 + 2048]),
            xoth=pm(x3[b, 2048 - T0:4096 - T0]),
            qcos=qcos, qsin=qsin, kcos=kcos, ksin=ksin,
            amsk=np.full((128, 1), 0.0 if h == 1 else -1e9, np.float32),
        )
        in_maps.append(m)
    nc_l3 = build_L3()
    res = run_bass_kernel_spmd(nc_l3, in_maps, core_ids=list(range(8)))
    x4 = np.zeros((4, 4096, 1024), np.float32)
    for c in range(8):
        b, h = c // 2, c % 2
        x4[b, 2048 * h:2048 * (h + 1)] = ipm(np.asarray(res.results[c]["xout"]))
    return x4
